# version 01
# baseline (speedup 1.0000x reference)
import contextlib
import numpy as np
import ml_dtypes
import concourse.bass as bass
import concourse.mybir as mybir
from concourse.bass_utils import run_bass_kernel_spmd

F32 = mybir.dt.float32
BF16 = mybir.dt.bfloat16
AF = mybir.ActivationFunctionType
ALU = mybir.AluOpType
AX = mybir.AxisListType

D = 1024
NH = 16
HD = 64
FF = 4096
NE = 8
CAP = 4916.0
NCORES = 8
NTF = 128
NTO = 16
NTB = 32
EPS = 1e-5

CB_EPS, CB_NHALF, CB_SCALE, CB_EBIAS, CB_INVD, CB_NEG1 = 0, 1, 2, 3, 4, 5


class Plan:
    ORDER = (("q", "sync"), ("g", "gpsimd"), ("t", "tensor"), ("v", "vector"), ("a", "scalar"))

    def __init__(self, nc, es):
        self.nc = nc
        self.es = es
        self.steps = {e: [] for e, _ in self.ORDER}
        self.sems = {}
        self.cnt = {}

    def sem(self, name):
        if name not in self.sems:
            self.sems[name] = self.es.enter_context(self.nc.semaphore(name))
            self.cnt[name] = 0
        return self.sems[name]

    def step(self, eng, fn, waits=(), inc=None, amt=1):
        waits = tuple((s, v) for s, v in waits if v > 0)
        for s, _ in waits:
            self.sem(s)
        if inc is not None:
            self.sem(inc)
            self.cnt[inc] += amt
        self.steps[eng].append((fn, waits, inc, amt))
        return self.cnt[inc] if inc is not None else None

    def v(self, name):
        self.sem(name)
        return self.cnt[name]

    def barrier(self):
        for e in ("q", "g", "t", "v", "a"):
            w = []
            if e == "q":
                w.append(("qd", self.v("qd")))
            if e == "g":
                w.append(("gd", self.v("gd")))
            self.step(e, lambda en: en.nop(), waits=w, inc="bar")
        tot = self.v("bar")
        for e in ("q", "g", "t", "v", "a"):
            self.step(e, lambda en: en.nop(), waits=[("bar", tot)])

    def emit(self, blk):
        for engname, attr in self.ORDER:
            steps = self.steps[engname]

            def body(e, steps=steps):
                for fn, waits, inc, amt in steps:
                    for s, val in waits:
                        e.wait_ge(self.sems[s], val)
                    inst = fn(e)
                    if inc is not None:
                        assert inst is not None
                        inst.then_inc(self.sems[inc], amt)
            getattr(blk, attr)(body)


def build(debug=False):
    nc = bass.Bass("TRN2", target_bir_lowering=False)

    dt_in = lambda n, s, d=F32: nc.dram_tensor(n, s, d, kind="ExternalInput").ap()
    xrot = dt_in("xrot", [NTF, 128, D])
    xbatch = dt_in("xbatch", [NTB, 128, D])
    xown = dt_in("xown", [128, NTO, D])
    qkvw_st = dt_in("qkvw_st", [3, 8, 128, 8, 128])
    outw_st = dt_in("outw_st", [128, 8, D])
    qkvbT = dt_in("qkvbT", [128, 24])
    qkvbRow = dt_in("qkvbRow", [1, 3 * D])
    outbRow = dt_in("outbRow", [1, D])
    wg_st = dt_in("wg_st", [128, 64])
    bgRow = dt_in("bgRow", [1, 8])
    w1_st = dt_in("w1_st", [NE, 4, 128, 8, 1024])
    w2_st = dt_in("w2_st", [NE, 4, 128, 8, 1024])
    b1T_d = dt_in("b1T", [128, 256])
    b2Row = dt_in("b2Row", [1, NE * D])
    identf_d = dt_in("identf", [128, 128])
    identb_d = dt_in("identb", [128, 128], BF16)
    pswap_d = dt_in("pswap", [128, 128], BF16)
    triu_tok_d = dt_in("triu_tok", [128, 128])
    triu_rot_d = dt_in("triu_rot", [128, 128])
    cosq_d = dt_in("cosq", [128, 4096], BF16)
    sinq_d = dt_in("sinq", [128, 4096], BF16)
    outp = nc.dram_tensor("outp", [128, NTO, D], F32, kind="ExternalOutput").ap()
    scr_d = nc.dram_tensor("scr", [1, 4096], F32, kind="Internal").ap()
    dbg = {}
    if debug:
        for n, s, d in [("d_ccoef", [128, 128], F32), ("d_cnts", [128, 16], F32),
                        ("d_pref", [128, 16], F32), ("d_xnTq", [128, 2048], BF16),
                        ("d_attn", [128, NTO * D], BF16),
                        ("d_krot", [128, 4096], BF16), ("d_oh", [128, 32], F32),
                        ("d_l8", [128, 8], F32), ("d_top8", [128, 8], F32),
                        ("d_c12", [128, 32], F32), ("d_ktmp", [128, 32], F32),
                        ("d_gat", [128, 8], F32), ("d_tsc1", [128, 8], F32),
                        ("d_tsc2", [128, 8], F32), ("d_stat", [128, 24], F32),
                        ("d_xT32", [128, 1024], F32)]:
            dbg[n] = nc.dram_tensor(n, s, d, kind="ExternalOutput").ap()

    es_glob = contextlib.ExitStack()
    p = Plan(nc, es_glob)
    sb = lambda n, s, d=F32: es_glob.enter_context(nc.sbuf_tensor(n, s, d))

    identf = sb("identf_s", [128, 128])
    identb = sb("identb_s", [128, 128], BF16)
    pswap = sb("pswap_s", [128, 128], BF16)
    triu_tok = sb("triu_tok_s", [128, 128])
    triu_rot = sb("triu_rot_s", [128, 128])
    wg_sb = sb("wg_sb_s", [128, 64])
    bg_sb = sb("bg_sb_s", [1, 8])
    qkvbT_sb = sb("qkvbT_sb_s", [128, 24])
    qkvb_row = sb("qkvb_row_s", [1, 3 * D], BF16)
    outb_row = sb("outb_row_s", [1, D], BF16)
    b1T = sb("b1T_sb_s", [128, 256])
    b2_row = sb("b2_row_s", [1, NE * D], BF16)
    cbank = sb("cbank_s", [128, 8])
    ones_col = sb("ones_col_s", [128, 1])
    ones_rowf = sb("ones_rowf_s", [1, 128])
    ones_rowb = sb("ones_rowb_s", [1, 128], BF16)
    zeros8 = sb("zeros8_s", [128, 8])
    c_coef = sb("c_coef_s", [128, NTO * 8])
    xnTq = sb("xnTq_s", [128, 8, 2048], BF16)
    attn_tok = sb("attn_tok_s", [128, NTO, D], BF16)

    pb = [es_glob.enter_context(nc.psum_tensor(f"pb{i}", [128, 512], F32)) for i in range(8)]

    def qd(fn, waits=()):
        return p.step("q", fn, waits=waits, inc="qd", amt=16)

    def gdma(fn, waits=()):
        return p.step("g", fn, waits=waits, inc="gd", amt=16)

    def vseq(fns, waits=(), inc=None):
        for i, fn in enumerate(fns):
            w = list(waits) + [("vch", p.v("vch"))] if i == 0 else [("vch", p.v("vch"))]
            p.step("v", fn, waits=w, inc=(inc if (i == len(fns) - 1 and inc) else "vch"))

    qd(lambda q: q.dma_start(out=identf[:], in_=identf_d))
    qd(lambda q: q.dma_start(out=triu_tok[:], in_=triu_tok_d))
    qd(lambda q: q.dma_start(out=triu_rot[:], in_=triu_rot_d))
    qd(lambda q: q.dma_start(out=wg_sb[:], in_=wg_st))
    qd(lambda q: q.dma_start(out=bg_sb[:], in_=bgRow))
    qd(lambda q: q.dma_start(out=qkvbT_sb[:], in_=qkvbT))
    qd(lambda q: q.dma_start(out=b1T[:], in_=b1T_d))
    qd(lambda q: q.dma_start(out=identb[:], in_=identb_d))
    qd(lambda q: q.dma_start(out=pswap[:], in_=pswap_d))
    gdma(lambda g: g.dma_start(out=qkvb_row[:], in_=qkvbRow))
    gdma(lambda g: g.dma_start(out=outb_row[:], in_=outbRow))
    gdma(lambda g: g.dma_start(out=b2_row[:], in_=b2Row))

    def memsets(g):
        g.memset(cbank[:, CB_EPS:CB_EPS + 1], EPS)
        g.memset(cbank[:, CB_NHALF:CB_NHALF + 1], -0.5)
        g.memset(cbank[:, CB_SCALE:CB_SCALE + 1], 0.125)
        g.memset(cbank[:, CB_EBIAS:CB_EBIAS + 1], -12.0)
        g.memset(cbank[:, CB_INVD:CB_INVD + 1], 1.0 / D)
        g.memset(cbank[:, CB_NEG1:CB_NEG1 + 1], -1.0)
        g.memset(ones_col[:], 1.0)
        g.memset(ones_rowf[:], 1.0)
        g.memset(ones_rowb[:], 1.0)
        return g.memset(zeros8[:], 0.0)
    p.step("g", memsets, inc="gcst")
    p.barrier()

    es_XK = contextlib.ExitStack()
    xnTk = es_XK.enter_context(nc.sbuf_tensor("xnTk_s", [128, 8, 2048], BF16, side="right"))
    es_RX = contextlib.ExitStack()
    rsb = lambda n, s, d=F32: es_RX.enter_context(nc.sbuf_tensor(n, s, d, side="right"))
    xt2 = rsb("xt2_s", [128, 2, D])
    sqdump = rsb("sqdump_s", [128, D])
    stat = rsb("stat_s", [128, 2, 12])
    xn32 = rsb("xn32_s", [128, 2, D])
    xnbf2 = rsb("xnbf2_s", [128, 2, D], BF16)
    xT32 = rsb("xT32_s", [128, D])
    l8 = rsb("l8_s", [128, 8])
    top8 = rsb("top8_s", [128, 8])
    tsc1 = rsb("tsc1_s", [128, 8])
    tsc2 = rsb("tsc2_s", [128, 8])
    gat = rsb("gat_s", [128, 8])
    oh1_all = rsb("oh1_all_s", [128, NTF * 8])
    oh2_all = rsb("oh2_all_s", [128, NTF * 8])
    c1_all = rsb("c1_all_s", [128, NTF * 8])
    c2_all = rsb("c2_all_s", [128, NTF * 8])
    cnts_row = rsb("cnts_row_s", [1, NTF * 16])
    cnts_t = rsb("cnts_t_s", [128, 16])
    pref_sb = rsb("pref_sb_s", [128, 16])
    tot_sb = rsb("tot_sb_s", [1, 16])
    pref_rows = rsb("pref_rows_s", [1, NTF * 16])
    ktmp = rsb("ktmp_s", [128, 32])

    def ln_chain(n, src_tile):
        par = n % 2
        st = lambda c: stat[:, par, c:c + 1]

        wv = [("qd", p.v("qd")), ("lnarst", p.v("lnarst") - 1)]
        vseq([lambda v: v.reduce_sum(st(0), src_tile, axis=AX.X),
              lambda v: v.tensor_scalar(st(2), st(0), 1.0 / D, None, op0=ALU.mult),
              lambda v: v.scalar_tensor_tensor(st(3), st(2), -1.0, st(2),
                                               op0=ALU.mult, op1=ALU.mult)],
             waits=wv, inc="lnvst")
        p.step("a", lambda a: a.activation(sqdump[:], src_tile, AF.Square, accum_out=st(1)),
               waits=[("qd", p.v("qd"))], inc="lnassq")
        vseq([lambda v: v.scalar_tensor_tensor(st(4), st(1), 1.0 / D, st(3),
                                               op0=ALU.mult, op1=ALU.add)],
             waits=[("lnassq", p.v("lnassq"))], inc="lnvvar")

        def archain(a):
            a.activation(st(5), st(4), AF.Ln, bias=cbank[:, CB_EPS:CB_EPS + 1])
            return a.activation(st(6), st(5), AF.Exp, scale=cbank[:, CB_NHALF:CB_NHALF + 1])
        p.step("a", archain, waits=[("lnvvar", p.v("lnvvar"))], inc="lnarst")
        return st

    for g in range(NTF):
        par = g % 2
        xt = xt2[:, par, :]
        qd(lambda q, g=g, par=par: q.dma_start(out=xt2[:, par, :], in_=xrot[g]),
           waits=[("lnaxn", p.v("lnaxn") - 1)])
        st = ln_chain(g, xt)
        vseq([lambda v, xt=xt, par=par, st=st:
              v.tensor_scalar(xn32[:, par, :], xt, st(2), st(6),
                              op0=ALU.subtract, op1=ALU.mult)],
             waits=[("rttr", p.v("rttr") - 1), ("lnarst", p.v("lnarst"))], inc="lnaxn")
        b0 = 0 if par == 0 else 2

        def trans(t, b0=b0, par=par):
            for ci in range(8):
                bank = pb[b0 + ci // 4]
                inst = t.transpose(bank[:, (ci % 4) * 128:(ci % 4) * 128 + 128],
                                   xn32[:, par, ci * 128:ci * 128 + 128], identf[:])
            return inst
        p.step("t", trans, waits=[("lnaxn", p.v("lnaxn")), ("rvev", p.v("rvev") - 1)],
               inc="rttr")

        def evtr(v, b0=b0):
            v.tensor_copy(xT32[:, 0:512], pb[b0][:])
            return v.tensor_copy(xT32[:, 512:1024], pb[b0 + 1][:])
        p.step("v", evtr, waits=[("rttr", p.v("rttr")), ("rtlg", p.v("rtlg"))], inc="rvev")
        plg = pb[4 + par]

        def lgmm(t, plg=plg):
            for ci in range(8):
                t.matmul(plg[:, 0:8], xT32[:, ci * 128:ci * 128 + 128],
                         wg_sb[:, ci * 8:ci * 8 + 8], start=(ci == 0), stop=False)
            return t.matmul(plg[:, 0:8], ones_rowf[:], bg_sb[:], start=False, stop=True)
        p.step("t", lgmm, waits=[("rvev", p.v("rvev")), ("rvl8", p.v("rvl8") - 1)], inc="rtlg")

        p.step("v", lambda v, plg=plg: v.tensor_copy(l8[:], plg[:, 0:8]),
               waits=[("rtlg", p.v("rtlg")), ("raeg", p.v("raeg") - 1)], inc="rvcp")
        p.step("v", lambda v: v.max(top8[:], l8[:]),
               waits=[("rvcp", p.v("rvcp"))], inc="rvmx")
        p.step("v", lambda v: v.tensor_sub(gat[:, 0:1], top8[:, 1:2], top8[:, 0:1]),
               waits=[("rvmx", p.v("rvmx"))], inc="rvl8")
        p.step("a", lambda a: a.activation(gat[:, 3:4], gat[:, 0:1], AF.Exp),
               waits=[("rvl8", p.v("rvl8"))], inc="raeg")

        vseq([lambda v: (v.tensor_sub(tsc1[:], l8[:], top8[:, 0:1].to_broadcast([128, 8])),
                         v.tensor_scalar(gat[:, 1:2], gat[:, 0:1], 0.0, None, op0=ALU.is_equal))[-1],
              lambda v: (v.tensor_scalar(tsc1[:], tsc1[:], 0.0, None, op0=ALU.is_equal),
                         v.tensor_scalar(gat[:, 2:3], gat[:, 1:2], 1.0, None, op0=ALU.add))[-1],
              lambda v: v.tensor_tensor_scan(tsc2[:], tsc1[:], zeros8[:], 0.0, op0=ALU.add, op1=ALU.add),
              lambda v: v.tensor_scalar(tsc2[:], tsc2[:], 1.0, None, op0=ALU.is_equal),
              lambda v, g=g: v.tensor_tensor(oh1_all[:, g * 8:g * 8 + 8], tsc1[:], tsc2[:], op=ALU.mult)],
             waits=[("raeg", p.v("raeg"))], inc="rvo1")

        vseq([lambda v: (v.tensor_sub(tsc1[:], l8[:], top8[:, 1:2].to_broadcast([128, 8])),
                         v.tensor_scalar(gat[:, 4:5], gat[:, 3:4], 1.0, None, op0=ALU.add))[-1],
              lambda v: (v.tensor_scalar(tsc1[:], tsc1[:], 0.0, None, op0=ALU.is_equal),
                         v.reciprocal(gat[:, 5:6], gat[:, 4:5]))[-1],
              lambda v: v.tensor_tensor_scan(tsc2[:], tsc1[:], zeros8[:], 0.0, op0=ALU.add, op1=ALU.add),
              lambda v: v.tensor_sub(tsc2[:], tsc2[:], gat[:, 2:3].to_broadcast([128, 8])),
              lambda v: v.tensor_scalar(tsc2[:], tsc2[:], 0.0, None, op0=ALU.is_equal),
              lambda v, g=g: v.tensor_tensor(oh2_all[:, g * 8:g * 8 + 8], tsc1[:], tsc2[:], op=ALU.mult)],
             waits=[("rvo1", p.v("rvo1"))], inc="rvo2")

        vseq([lambda v: v.tensor_tensor(gat[:, 6:7], gat[:, 3:4], gat[:, 5:6], op=ALU.mult),
              lambda v, g=g: (v.tensor_mul(c1_all[:, g * 8:g * 8 + 8], oh1_all[:, g * 8:g * 8 + 8],
                                           gat[:, 5:6].to_broadcast([128, 8])),
                              v.tensor_mul(c2_all[:, g * 8:g * 8 + 8], oh2_all[:, g * 8:g * 8 + 8],
                                           gat[:, 6:7].to_broadcast([128, 8])))[-1]],
             waits=[("rvo2", p.v("rvo2"))], inc="rvoh")
        pcnt = pb[6 + par]

        def cntmm(t, pcnt=pcnt, g=g):
            t.matmul(pcnt[0:1, 0:8], ones_col[:], oh1_all[:, g * 8:g * 8 + 8], start=True, stop=True)
            return t.matmul(pcnt[0:1, 8:16], ones_col[:], oh2_all[:, g * 8:g * 8 + 8], start=True, stop=True)
        p.step("t", cntmm, waits=[("rvoh", p.v("rvoh")), ("rvcv", p.v("rvcv") - 1)], inc="rtcnt")
        p.step("v", lambda v, pcnt=pcnt, g=g:
               v.tensor_copy(cnts_row[0:1, g * 16:g * 16 + 16], pcnt[0:1, 0:16]),
               waits=[("rtcnt", p.v("rtcnt"))], inc="rvcv")

    gdma(lambda g: g.dma_start(out=scr_d[0:1, 0:2048], in_=cnts_row[:]),
         waits=[("rvcv", p.v("rvcv"))])
    gdma(lambda g: g.dma_start(out=cnts_t[:],
                               in_=scr_d[0:1, 0:2048].rearrange("o (g j) -> (o g) j", j=16)),
         waits=[("gd", p.v("gd"))])

    def prefmm(t):
        t.matmul(pb[0][:, 0:16], triu_rot[:], cnts_t[:], start=True, stop=True)
        return t.matmul(pb[1][0:1, 0:16], ones_col[:], cnts_t[:], start=True, stop=True)
    p.step("t", prefmm, waits=[("gd", p.v("gd"))], inc="rtpre")

    def evpre(v):
        v.tensor_copy(pref_sb[:], pb[0][:, 0:16])
        return v.tensor_copy(tot_sb[:], pb[1][0:1, 0:16])
    p.step("v", evpre, waits=[("rtpre", p.v("rtpre"))], inc="rvpre")
    gdma(lambda g: g.dma_start(out=scr_d[0:1, 0:2048].rearrange("o (g j) -> (o g) j", j=16),
                               in_=pref_sb[:]),
         waits=[("rvpre", p.v("rvpre"))])
    gdma(lambda g: g.dma_start(out=pref_rows[0:1, 0:2048], in_=scr_d[0:1, 0:2048]),
         waits=[("gd", p.v("gd"))])
    gd_pref = p.v("gd")

    for ti in range(NTO):
        b1b = pb[2 + ti % 2]
        b2b = pb[4 + ti % 2]

        def posmm(t, ti=ti, b1b=b1b, b2b=b2b):
            t.matmul(b1b[:, 0:8], triu_tok[:], oh1_all[:, ti * 8:ti * 8 + 8], start=True, stop=False)
            t.matmul(b1b[:, 0:8], ones_rowf[:], pref_rows[0:1, ti * 16:ti * 16 + 8], start=False, stop=True)
            t.matmul(b2b[:, 0:8], triu_tok[:], oh2_all[:, ti * 8:ti * 8 + 8], start=True, stop=False)
            t.matmul(b2b[:, 0:8], ones_rowf[:], pref_rows[0:1, ti * 16 + 8:ti * 16 + 16], start=False, stop=False)
            return t.matmul(b2b[:, 0:8], ones_rowf[:], tot_sb[0:1, 0:8], start=False, stop=True)
        p.step("t", posmm, waits=[("gd", gd_pref), ("rvcc", p.v("rvcc") - 1)], inc="rtpos")

        vseq([lambda v, ti=ti, b1b=b1b, b2b=b2b:
              (v.tensor_scalar(ktmp[:, 0:8], b1b[:, 0:8], CAP, None, op0=ALU.is_lt),
               v.tensor_scalar(ktmp[:, 8:16], b2b[:, 0:8], CAP, None, op0=ALU.is_lt))[-1],
              lambda v, ti=ti: (v.tensor_tensor(ktmp[:, 16:24], c1_all[:, ti * 8:ti * 8 + 8],
                                                ktmp[:, 0:8], op=ALU.mult),
                                v.tensor_tensor(ktmp[:, 24:32], c2_all[:, ti * 8:ti * 8 + 8],
                                                ktmp[:, 8:16], op=ALU.mult))[-1],
              lambda v, ti=ti: v.tensor_tensor(c_coef[:, ti * 8:ti * 8 + 8],
                                               ktmp[:, 16:24], ktmp[:, 24:32], op=ALU.add)],
             waits=[("rtpos", p.v("rtpos"))], inc="rvcc")

    if debug:
        qd(lambda q: q.dma_start(out=dbg["d_l8"], in_=l8[:]),
           waits=[("rvcc", p.v("rvcc"))])
        qd(lambda q: q.dma_start(out=dbg["d_top8"], in_=top8[:]))
        qd(lambda q: q.dma_start(out=dbg["d_gat"], in_=gat[:]))
        qd(lambda q: q.dma_start(out=dbg["d_tsc1"], in_=tsc1[:]))
        qd(lambda q: q.dma_start(out=dbg["d_tsc2"], in_=tsc2[:]))
        qd(lambda q: q.dma_start(out=dbg["d_stat"], in_=stat[:].rearrange("p a b -> p (a b)")))
        qd(lambda q: q.dma_start(out=dbg["d_xT32"], in_=xT32[:]))
        qd(lambda q: q.dma_start(out=dbg["d_cnts"], in_=cnts_t[:]))
        qd(lambda q: q.dma_start(out=dbg["d_pref"], in_=pref_sb[:]))
        qd(lambda q: q.dma_start(out=dbg["d_oh"], in_=oh1_all[:, 0:32]))
        qd(lambda q: q.dma_start(out=dbg["d_c12"][:, 0:16], in_=c1_all[:, 0:16]))
        qd(lambda q: q.dma_start(out=dbg["d_c12"][:, 16:32], in_=c2_all[:, 0:16]))
        qd(lambda q: q.dma_start(out=dbg["d_ktmp"], in_=ktmp[:]))

    rvcc_done = p.v("rvcc")

    for bt in range(NTB):
        n = NTF + bt
        par = n % 2
        xt = xt2[:, par, :]
        qd(lambda q, bt=bt, par=par: q.dma_start(out=xt2[:, par, :], in_=xbatch[bt]),
           waits=[("lnaxn", p.v("lnaxn") - 1)])
        st = ln_chain(n, xt)
        vseq([lambda v, xt=xt, par=par, st=st:
              v.tensor_scalar(xnbf2[:, par, :], xt, st(2), st(6),
                              op0=ALU.subtract, op1=ALU.mult)],
             waits=[("xttr", p.v("xttr") - 1), ("lnarst", p.v("lnarst"))], inc="lnaxn")
        b0 = 0 if par == 0 else 2

        def transb(t, b0=b0, par=par):
            pbB = pb[b0][:].bitcast(BF16)
            for ci in range(8):
                inst = t.transpose(pbB[:, ci * 128:ci * 128 + 128],
                                   xnbf2[:, par, ci * 128:ci * 128 + 128], identb[:])
            return inst
        tw = [("lnaxn", p.v("lnaxn")), ("xvev", p.v("xvev") - 1)]
        if bt < 2:
            tw.append(("rvcc", rvcc_done))
        p.step("t", transb, waits=tw, inc="xttr")
        tgt = xnTq if bt < NTO else xnTk
        off = (bt % NTO) * 128

        def evtrb(v, b0=b0, tgt=tgt, off=off):
            return v.tensor_copy(tgt[:, :, off:off + 128],
                                 pb[b0][:].bitcast(BF16).rearrange("p (c k) -> p c k", k=128))
        p.step("v", evtrb, waits=[("xttr", p.v("xttr"))], inc="xvev")

    if debug:
        qd(lambda q: q.dma_start(out=dbg["d_xnTq"], in_=xnTq[:, 0, :]),
           waits=[("xvev", p.v("xvev"))])
        qd(lambda q: q.dma_start(out=dbg["d_ccoef"], in_=c_coef[:]))

    es_RX.close()
    p.barrier()

    es_A = contextlib.ExitStack()
    asb = lambda n, s, d=F32: es_A.enter_context(nc.sbuf_tensor(n, s, d, side="right"))
    cosq = asb("cosq_s", [128, 4096], BF16)
    sinq = asb("sinq_s", [128, 4096], BF16)
    wq_b = asb("wq_b_s", [128, 2, 8, 128], BF16)
    wk_b = asb("wk_b_s", [128, 2, 8, 128], BF16)
    wv_b = asb("wv_b_s", [128, 2, 8, 128], BF16)
    kq_sb = asb("kq_sb_s", [128, 2, 512], BF16)
    ttmp = asb("ttmp_s", [128, 2, 512], BF16)
    krot = asb("krot_s", [128, 4096], BF16)
    qrot = asb("qrot_s", [128, 2048], BF16)
    vt = asb("vt_s", [128, 32, 130], BF16)
    pt_sb = asb("pt_sb_s", [128, 2, 2, 512], BF16)
    recip_sb = asb("recip_sb_s", [128, 8])

    qd(lambda q: q.dma_start(out=cosq[:], in_=cosq_d))
    qd(lambda q: q.dma_start(out=sinq[:], in_=sinq_d))
    p.step("v", lambda v: v.memset(vt[:], 1.0), inc="avones")
    qd_tbl = p.v("qd")

    def xnT_slice(ci, t0, nn):
        if t0 < 2048:
            return xnTq[:, ci, t0:t0 + nn]
        return xnTk[:, ci, t0 - 2048:t0 - 2048 + nn]

    for hp in range(8):
        par = hp % 2
        wdw = [("atdone", p.v("atdone") - 1)]
        for s, wt in ((0, wq_b), (1, wk_b), (2, wv_b)):
            gdma(lambda g, s=s, wt=wt, par=par, hp=hp:
                 g.dma_start(out=wt[:, par], in_=qkvw_st[s, hp]), waits=wdw)
        gd_w = p.v("gd")

        for kind in range(12):
            isq = kind >= 8
            it = kind - 8 if isq else kind
            pbm = pb[kind % 2]
            pbs = pb[2 + kind % 2]

            def kqmm(t, isq=isq, it=it, par=par, pbm=pbm):
                w = wq_b if isq else wk_b
                for ci in range(8):
                    rhs = xnTq[:, ci, it * 512:it * 512 + 512] if isq else xnT_slice(ci, it * 512, 512)
                    inst = t.matmul(pbm[:], w[:, par, ci, :], rhs,
                                    start=(ci == 0), stop=(ci == 7))
                return inst
            tw = [("gd", gd_w), ("aaev", p.v("aaev") - 1), ("avrp", p.v("avrp") - 1)]
            p.step("t", kqmm, waits=tw, inc="atkm")
            bcol = hp if isq else 8 + hp
            p.step("a", lambda a, kind=kind, pbm=pbm, bcol=bcol:
                   a.activation(kq_sb[:, kind % 2, :], pbm[:], AF.Identity,
                                bias=qkvbT_sb[:, bcol:bcol + 1]),
                   waits=[("atkm", p.v("atkm"))], inc="aaev")

            def swmm(t, kind=kind, pbs=pbs):
                return t.matmul(pbs[:], pswap[:], kq_sb[:, kind % 2, :], start=True, stop=True)
            p.step("t", swmm, waits=[("aaev", p.v("aaev")), ("avrp", p.v("avrp") - 1)], inc="atsw")
            dst = qrot if isq else krot

            vseq([lambda v, kind=kind, it=it, pbs=pbs:
                  (v.tensor_tensor(ttmp[:, 0, :], kq_sb[:, kind % 2, :],
                                   cosq[:, it * 512:it * 512 + 512], op=ALU.mult),
                   v.tensor_tensor(ttmp[:, 1, :], pbs[:],
                                   sinq[:, it * 512:it * 512 + 512], op=ALU.mult))[-1],
                  lambda v, kind=kind, it=it, dst=dst:
                  v.tensor_tensor(dst[:, it * 512:it * 512 + 512],
                                  ttmp[:, 0, :], ttmp[:, 1, :], op=ALU.add)],
                 waits=[("atsw", p.v("atsw"))], inc="avrp")

        for tt in range(32):
            pbv = pb[4 + tt % 2]

            def vmm(t, tt=tt, par=par, pbv=pbv, hp=hp):
                for ci in range(8):
                    t.matmul(pbv[:, 0:128], xnT_slice(ci, tt * 128, 128), wv_b[:, par, ci, :],
                             start=(ci == 0), stop=False)
                return t.matmul(pbv[:, 0:128], ones_rowb[:],
                                qkvb_row[0:1, 2 * D + hp * 128:2 * D + hp * 128 + 128],
                                start=False, stop=True)
            p.step("t", vmm, waits=[("gd", gd_w), ("avve", p.v("avve") - 1)], inc="atvm")

            def vev(v, tt=tt, pbv=pbv):
                v.tensor_copy(vt[:, tt, 0:64], pbv[:, 0:64])
                return v.tensor_copy(vt[:, tt, 65:129], pbv[:, 64:128])
            p.step("v", vev, waits=[("atvm", p.v("atvm")), ("avones", p.v("avones"))], inc="avve")

        avrp_done = p.v("avrp")
        avve_done = p.v("avve")
        for qt in range(4):
            fs0 = p.v("afx")
            fa0 = p.v("afa")
            for kc in range(32):
                sb0 = pb[kc % 2]
                sb1 = pb[2 + kc % 2]

                def smm(t, kc=kc, qt=qt, sb0=sb0, sb1=sb1):
                    t.matmul(sb0[:], krot[0:64, kc * 128:kc * 128 + 128],
                             qrot[0:64, qt * 512:qt * 512 + 512], start=True, stop=True,
                             skip_group_check=True)
                    return t.matmul(sb1[:], krot[64:128, kc * 128:kc * 128 + 128],
                                    qrot[64:128, qt * 512:qt * 512 + 512], start=True, stop=True,
                                    skip_group_check=True)
                tw = [("avrp", avrp_done), ("avve", avve_done),
                      ("afx", fs0 + 2 * (kc - 1))]
                if kc == 0 and qt > 0:
                    tw.append(("avno", p.v("avno")))
                p.step("t", smm, waits=tw, inc="afs", amt=2)

                def expev(a, kc=kc, sb0=sb0, sb1=sb1):
                    a.activation(pt_sb[:, 0, kc % 2, :], sb0[:], AF.Exp,
                                 scale=cbank[:, CB_SCALE:CB_SCALE + 1],
                                 bias=cbank[:, CB_EBIAS:CB_EBIAS + 1])
                    return a.activation(pt_sb[:, 1, kc % 2, :], sb1[:], AF.Exp,
                                        scale=cbank[:, CB_SCALE:CB_SCALE + 1],
                                        bias=cbank[:, CB_EBIAS:CB_EBIAS + 1])
                p.step("a", expev,
                       waits=[("afs", p.v("afs")), ("afa", p.v("afa") - 1)], inc="afx", amt=2)

                def avmm(t, kc=kc):
                    for h in range(2):
                        pba = pb[4 + h]
                        for qs in range(4):
                            inst = t.matmul(pba[:, qs * 65:qs * 65 + 65],
                                            pt_sb[:, h, kc % 2, qs * 128:qs * 128 + 128],
                                            vt[:, kc, h * 65:h * 65 + 65],
                                            start=(kc == 0), stop=(kc == 31),
                                            skip_group_check=True)
                    return inst
                p.step("t", avmm, waits=[("afx", p.v("afx"))], inc="afa")

            def norm1(v):
                for h in range(2):
                    pba = pb[4 + h]
                    for qs in range(4):
                        inst = v.reciprocal(recip_sb[:, h * 4 + qs:h * 4 + qs + 1],
                                            pba[:, qs * 65 + 64:qs * 65 + 65])
                return inst
            p.step("v", norm1, waits=[("afa", p.v("afa"))], inc="avn1")

            def norm2(v, qt=qt, hp=hp):
                for h in range(2):
                    pba = pb[4 + h]
                    for qs in range(4):
                        tt = qt * 4 + qs
                        inst = v.tensor_mul(
                            attn_tok[:, tt, (2 * hp + h) * 64:(2 * hp + h) * 64 + 64],
                            pba[:, qs * 65:qs * 65 + 64],
                            recip_sb[:, h * 4 + qs:h * 4 + qs + 1].to_broadcast([128, 64]))
                return inst
            p.step("v", norm2, waits=[("avn1", p.v("avn1"))], inc="avno")
        p.step("t", lambda t: t.nop(), waits=[("avno", p.v("avno"))], inc="atdone")

    if debug:
        qd(lambda q: q.dma_start(out=dbg["d_krot"], in_=krot[:]),
           waits=[("atdone", p.v("atdone"))])
        qd(lambda q: q.dma_start(out=dbg["d_attn"],
                                 in_=attn_tok[:].rearrange("p a b -> p (a b)")),
           waits=[("avno", p.v("avno"))])

    es_A.close()
    es_XK.close()
    p.barrier()

    es_ACC = contextlib.ExitStack()
    acc = es_ACC.enter_context(nc.sbuf_tensor("acc_s", [128, NTO, D], F32, side="right"))
    es_OP = contextlib.ExitStack()
    ow_b = es_OP.enter_context(nc.sbuf_tensor("ow_b_s", [128, 8, D], BF16, side="right"))
    attnF_t = es_OP.enter_context(nc.sbuf_tensor("attnF_t_s", [128, 8, 128], BF16, side="right"))

    gdma(lambda g: g.dma_start(out=ow_b[:], in_=outw_st))
    gd_ow = p.v("gd")
    qd(lambda q: q.dma_start(out=acc[:], in_=xown))
    qd_acc = p.v("qd")

    for tt in range(NTO):
        def optr(t, tt=tt):
            pbB = pb[0][:].bitcast(BF16)
            for ci in range(8):
                inst = t.transpose(pbB[:, ci * 128:ci * 128 + 128],
                                   attn_tok[:, tt, ci * 128:ci * 128 + 128], identb[:])
            return inst
        p.step("t", optr, waits=[("opve", p.v("opve"))], inc="optr")

        def opev(v):
            return v.tensor_copy(attnF_t[:, :, :],
                                 pb[0][:].bitcast(BF16).rearrange("p (c k) -> p c k", k=128))
        p.step("v", opev, waits=[("optr", p.v("optr"))], inc="opve")

        def opmm(t, tt=tt):
            for nh in range(2):
                pbo = pb[2 + nh]
                t.matmul(pbo[:], ones_rowb[:], outb_row[0:1, nh * 512:nh * 512 + 512],
                         start=True, stop=False, skip_group_check=True)
                for ci in range(8):
                    inst = t.matmul(pbo[:], attnF_t[:, ci, :], ow_b[:, ci, nh * 512:nh * 512 + 512],
                                    start=False, stop=(ci == 7), skip_group_check=True)
            return inst
        p.step("t", opmm, waits=[("opve", p.v("opve")), ("gd", gd_ow),
                                 ("opac", p.v("opac"))], inc="opmm")

        def opadd(v, tt=tt):
            v.tensor_tensor(acc[:, tt, 0:512], acc[:, tt, 0:512], pb[2][:], op=ALU.add)
            return v.tensor_tensor(acc[:, tt, 512:1024], acc[:, tt, 512:1024], pb[3][:], op=ALU.add)
        p.step("v", opadd, waits=[("opmm", p.v("opmm")), ("qd", qd_acc)], inc="opac")

    es_OP.close()
    p.barrier()

    es_F = contextlib.ExitStack()
    w1q = es_F.enter_context(nc.sbuf_tensor("w1q_s", [128, 8, 1024], BF16, side="right"))
    w2q = es_F.enter_context(nc.sbuf_tensor("w2q_s", [128, 8, 1024], BF16, side="right"))
    h_sb = es_F.enter_context(nc.sbuf_tensor("h_sb_s", [128, 2, 256], BF16, side="right"))

    for e in range(NE):
        for fq in range(4):
            wdw = [("fm2", p.v("fm2")), ("fmm", p.v("fmm"))]
            gdma(lambda g, e=e, fq=fq: g.dma_start(out=w1q[:], in_=w1_st[e, fq]), waits=wdw)
            gdma(lambda g, e=e, fq=fq: g.dma_start(out=w2q[:], in_=w2_st[e, fq]))
            gd_f = p.v("gd")
            for tq in range(8):
                for fc in range(8):
                    pbh = pb[fc % 2]

                    def mm1(t, tq=tq, fc=fc, pbh=pbh):
                        for ci in range(8):
                            inst = t.matmul(pbh[:, 0:256], w1q[:, ci, fc * 128:fc * 128 + 128],
                                            xnTq[:, ci, tq * 256:tq * 256 + 256],
                                            start=(ci == 0), stop=(ci == 7),
                                            skip_group_check=True)
                        return inst
                    tw = [("gd", gd_f), ("fge", p.v("fge") - 1)]
                    if fc == 0 and tq > 0:
                        tw.append(("fev", p.v("fev")))
                    p.step("t", mm1, waits=tw, inc="fmm")

                    def gelu(a, e=e, fq=fq, fc=fc, pbh=pbh):
                        j = e * 32 + fq * 8 + fc
                        return a.activation(h_sb[:, fc % 2, :], pbh[:, 0:256], AF.Gelu_apprx_tanh,
                                            bias=b1T[:, j:j + 1])
                    p.step("a", gelu, waits=[("fmm", p.v("fmm")), ("fm2", p.v("fm2") - 1)],
                           inc="fge")

                    def mm2(t, e=e, fq=fq, fc=fc):
                        inst = None
                        for stt in range(2):
                            for nh in range(2):
                                pby = pb[2 + stt * 2 + nh]
                                if fq == 0 and fc == 0:
                                    t.matmul(pby[:], ones_rowb[:],
                                             b2_row[0:1, e * D + nh * 512:e * D + nh * 512 + 512],
                                             start=True, stop=False, skip_group_check=True)
                                inst = t.matmul(pby[:], h_sb[:, fc % 2, stt * 128:stt * 128 + 128],
                                                w2q[:, fc, nh * 512:nh * 512 + 512],
                                                start=(fq > 0 and fc == 0), stop=(fc == 7),
                                                skip_group_check=True)
                        return inst
                    p.step("t", mm2, waits=[("fge", p.v("fge"))], inc="fm2")

                def fevac(v, e=e, tq=tq):
                    inst = None
                    for stx in range(2):
                        tt = tq * 2 + stx
                        cav = c_coef[:, tt * 8 + e:tt * 8 + e + 1]
                        for nh in range(2):
                            pby = pb[2 + stx * 2 + nh]
                            inst = v.scalar_tensor_tensor(
                                acc[:, tt, nh * 512:nh * 512 + 512], pby[:], cav,
                                acc[:, tt, nh * 512:nh * 512 + 512],
                                op0=ALU.mult, op1=ALU.add)
                    return inst
                p.step("v", fevac, waits=[("fm2", p.v("fm2"))], inc="fev")

    es_F.close()
    p.barrier()

    qd(lambda q: q.dma_start(out=outp, in_=acc[:]))
    p.step("q", lambda q: q.nop(), waits=[("qd", p.v("qd"))])

    with nc.Block() as blk:
        p.emit(blk)
    es_ACC.close()
    es_glob.close()
    return nc


def _stage(core, x, qkv_w, qkv_b, out_w, out_b, Wg, bg, W1, b1, W2, b2):
    b = core // 2
    h = core % 2
    xf = np.ascontiguousarray(x.reshape(128, 128, D))
    rot = np.roll(np.arange(NTF), -16 * core)
    xrot = np.ascontiguousarray(xf[rot])
    xb = x.reshape(4, 4096, D)[b]
    own = xb[h * 2048:(h + 1) * 2048]
    oth = xb[(1 - h) * 2048:(2 - h) * 2048]
    xbatch = np.ascontiguousarray(
        np.concatenate([own, oth], axis=0).reshape(NTB, 128, D))
    xown = np.ascontiguousarray(own.reshape(NTO, 128, D).transpose(1, 0, 2))

    perm = np.arange(3 * D)
    for hh in range(NH):
        base = hh * HD
        for sec in (0, D):
            perm[sec + base:sec + base + 32] = sec + base + 2 * np.arange(32)
            perm[sec + base + 32:sec + base + 64] = sec + base + 2 * np.arange(32) + 1
    qkvw_p = qkv_w[:, perm]
    qkvb_p = qkv_b[perm]
    qkvw_st = np.ascontiguousarray(
        qkvw_p.reshape(8, 128, 3, 8, 128).transpose(2, 3, 1, 0, 4))
    qkvbT = np.ascontiguousarray(qkvb_p.reshape(24, 128).T)
    qkvbRow = np.ascontiguousarray(qkvb_p.reshape(1, 3 * D))
    outw_st = np.ascontiguousarray(out_w.reshape(8, 128, D).transpose(1, 0, 2))
    outbRow = np.ascontiguousarray(out_b.reshape(1, D))
    wg_stv = np.ascontiguousarray(Wg.reshape(8, 128, 8).transpose(1, 0, 2).reshape(128, 64))
    bgRow = np.ascontiguousarray(bg.reshape(1, 8))
    w1_stv = np.ascontiguousarray(
        W1.reshape(NE, 8, 128, 4, 1024).transpose(0, 3, 2, 1, 4))
    w2_stv = np.ascontiguousarray(
        W2.reshape(NE, 4, 8, 128, D).transpose(0, 1, 3, 2, 4))
    b1T = np.ascontiguousarray(b1.reshape(NE, 32, 128).transpose(2, 0, 1).reshape(128, 256))
    b2Row = np.ascontiguousarray(b2.reshape(1, NE * D))

    identf = np.eye(128, dtype=np.float32)
    identb = np.eye(128, dtype=np.float32).astype(ml_dtypes.bfloat16)
    swap = np.arange(128)
    swap = np.where((swap % 64) < 32, swap + 32, swap - 32)
    psw = np.zeros((128, 128), np.float32)
    psw[swap, np.arange(128)] = 1.0
    pswap = psw.astype(ml_dtypes.bfloat16)
    triu_tok = np.triu(np.ones((128, 128), np.float32), 1)
    canon = rot
    triu_rot = (canon[:, None] < canon[None, :]).astype(np.float32)
    freqs = (10000.0 ** (np.arange(0, HD, 2.0, dtype=np.float32) / HD)).astype(np.float32)
    tpos = np.arange(4096, dtype=np.float32)
    stagedpos = np.concatenate([tpos[h * 2048:(h + 1) * 2048],
                                tpos[(1 - h) * 2048:(2 - h) * 2048]])
    ang = (stagedpos[:, None] * freqs[None, :]).astype(np.float32)
    cosv = np.cos(ang.astype(np.float64)).astype(np.float32)
    sinv = np.sin(ang.astype(np.float64)).astype(np.float32)
    prow = np.arange(128)
    cosq = np.ascontiguousarray(cosv[:, prow % 32].T).astype(ml_dtypes.bfloat16)
    sgn = np.where((prow % 64) < 32, -1.0, 1.0).astype(np.float32)
    sinq = np.ascontiguousarray((sinv[:, prow % 32] * sgn[None, :]).T).astype(ml_dtypes.bfloat16)

    return {
        "xrot": xrot, "xbatch": xbatch, "xown": xown,
        "qkvw_st": qkvw_st, "outw_st": outw_st, "qkvbT": qkvbT,
        "qkvbRow": qkvbRow, "outbRow": outbRow, "wg_st": wg_stv, "bgRow": bgRow,
        "w1_st": w1_stv, "w2_st": w2_stv, "b1T": b1T, "b2Row": b2Row,
        "identf": identf, "identb": identb, "pswap": pswap,
        "triu_tok": triu_tok, "triu_rot": triu_rot, "cosq": cosq, "sinq": sinq,
    }


def kernel(L0_attn_x, L0_attn_qkv_w, L0_attn_qkv_b, L0_attn_out_w, L0_attn_out_b,
           L0_ffn_Wg, L0_ffn_bg, L0_ffn_W1, L0_ffn_b1, L0_ffn_W2, L0_ffn_b2):
    x = np.ascontiguousarray(np.asarray(L0_attn_x, np.float32).reshape(16384, D))
    args = [np.asarray(a, np.float32) for a in
            (L0_attn_qkv_w, L0_attn_qkv_b, L0_attn_out_w, L0_attn_out_b,
             L0_ffn_Wg, L0_ffn_bg, L0_ffn_W1, L0_ffn_b1, L0_ffn_W2, L0_ffn_b2)]
    nc = build()
    in_maps = [_stage(c, x, *args) for c in range(NCORES)]
    res = run_bass_kernel_spmd(nc, in_maps, core_ids=list(range(NCORES)))
    shards = []
    for c in range(NCORES):
        o = np.asarray(res.results[c]["outp"])
        shards.append(o.transpose(1, 0, 2).reshape(2048, D))
    return np.concatenate(shards, axis=0).reshape(4, 4096, D).astype(np.float32)


# version 02
# speedup vs baseline: 1.3142x; 1.3142x over previous
import contextlib
import numpy as np
import ml_dtypes
import concourse.bass as bass
import concourse.mybir as mybir
from concourse.bass_utils import run_bass_kernel_spmd

F32 = mybir.dt.float32
BF16 = mybir.dt.bfloat16
AF = mybir.ActivationFunctionType
ALU = mybir.AluOpType
AX = mybir.AxisListType

D = 1024
NH = 16
HD = 64
FF = 4096
NE = 8
CAP = 4916.0
NCORES = 8
NTF = 128
NTO = 16
NTB = 32
EPS = 1e-5

CB_EPS, CB_NHALF, CB_SCALE, CB_EBIAS, CB_INVD, CB_NEG1 = 0, 1, 2, 3, 4, 5


class Plan:
    ORDER = (("q", "sync"), ("g", "gpsimd"), ("t", "tensor"), ("v", "vector"), ("a", "scalar"))

    def __init__(self, nc, es):
        self.nc = nc
        self.es = es
        self.steps = {e: [] for e, _ in self.ORDER}
        self.sems = {}
        self.cnt = {}

    def sem(self, name):
        if name not in self.sems:
            self.sems[name] = self.es.enter_context(self.nc.semaphore(name))
            self.cnt[name] = 0
        return self.sems[name]

    def step(self, eng, fn, waits=(), inc=None, amt=1):
        waits = tuple((s, v) for s, v in waits if v > 0)
        for s, _ in waits:
            self.sem(s)
        if inc is not None:
            self.sem(inc)
            self.cnt[inc] += amt
        self.steps[eng].append((fn, waits, inc, amt))
        return self.cnt[inc] if inc is not None else None

    def v(self, name):
        self.sem(name)
        return self.cnt[name]

    def barrier(self):
        for e in ("q", "g", "t", "v", "a"):
            w = []
            if e == "q":
                w.append(("qd", self.v("qd")))
            if e == "g":
                w.append(("gd", self.v("gd")))
            self.step(e, lambda en: en.nop(), waits=w, inc="bar")
        tot = self.v("bar")
        for e in ("q", "g", "t", "v", "a"):
            self.step(e, lambda en: en.nop(), waits=[("bar", tot)])

    def emit(self, blk):
        for engname, attr in self.ORDER:
            steps = self.steps[engname]

            def body(e, steps=steps):
                for fn, waits, inc, amt in steps:
                    for s, val in waits:
                        e.wait_ge(self.sems[s], val)
                    inst = fn(e)
                    if inc is not None:
                        assert inst is not None
                        inst.then_inc(self.sems[inc], amt)
            getattr(blk, attr)(body)


def build(debug=False):
    nc = bass.Bass("TRN2", target_bir_lowering=False)

    dt_in = lambda n, s, d=F32: nc.dram_tensor(n, s, d, kind="ExternalInput").ap()
    xrot = dt_in("xrot", [NTF, 128, D])
    xbatch = dt_in("xbatch", [NTB, 128, D])
    xown = dt_in("xown", [128, NTO, D])
    qkvw_st = dt_in("qkvw_st", [3, 8, 128, 8, 128])
    outw_st = dt_in("outw_st", [128, 8, D])
    qkvbT = dt_in("qkvbT", [128, 24])
    qkvbRow = dt_in("qkvbRow", [1, 3 * D])
    outbRow = dt_in("outbRow", [1, D])
    wg_st = dt_in("wg_st", [128, 64])
    bgRow = dt_in("bgRow", [1, 8])
    w1_st = dt_in("w1_st", [NE, 4, 128, 8, 1024])
    w2_st = dt_in("w2_st", [NE, 4, 128, 8, 1024])
    b1T_d = dt_in("b1T", [128, 256])
    b2Row = dt_in("b2Row", [1, NE * D])
    identf_d = dt_in("identf", [128, 128])
    identb_d = dt_in("identb", [128, 128], BF16)
    pswap_d = dt_in("pswap", [128, 128], BF16)
    triu_tok_d = dt_in("triu_tok", [128, 128])
    triu_rot_d = dt_in("triu_rot", [128, 128])
    cosq_d = dt_in("cosq", [128, 4096], BF16)
    sinq_d = dt_in("sinq", [128, 4096], BF16)
    outp = nc.dram_tensor("outp", [128, NTO, D], F32, kind="ExternalOutput").ap()
    scr_d = nc.dram_tensor("scr", [1, 4096], F32, kind="Internal").ap()
    dbg = {}
    if debug:
        for n, s, d in [("d_ccoef", [128, 128], F32), ("d_cnts", [128, 16], F32),
                        ("d_pref", [128, 16], F32), ("d_xnTq", [128, 2048], BF16),
                        ("d_attn", [128, NTO * D], BF16),
                        ("d_krot", [128, 4096], BF16), ("d_oh", [128, 32], F32),
                        ("d_l8", [128, 8], F32), ("d_top8", [128, 8], F32),
                        ("d_c12", [128, 32], F32), ("d_ktmp", [128, 32], F32),
                        ("d_gat", [128, 8], F32), ("d_tsc1", [128, 8], F32),
                        ("d_tsc2", [128, 8], F32), ("d_stat", [128, 24], F32),
                        ("d_xT32", [128, 1024], F32)]:
            dbg[n] = nc.dram_tensor(n, s, d, kind="ExternalOutput").ap()

    es_glob = contextlib.ExitStack()
    p = Plan(nc, es_glob)
    sb = lambda n, s, d=F32: es_glob.enter_context(nc.sbuf_tensor(n, s, d))

    identf = sb("identf_s", [128, 128])
    identb = sb("identb_s", [128, 128], BF16)
    pswap = sb("pswap_s", [128, 128], BF16)
    triu_tok = sb("triu_tok_s", [128, 128])
    triu_rot = sb("triu_rot_s", [128, 128])
    wg_sb = sb("wg_sb_s", [128, 64])
    bg_sb = sb("bg_sb_s", [1, 8])
    qkvbT_sb = sb("qkvbT_sb_s", [128, 24])
    qkvb_row = sb("qkvb_row_s", [1, 3 * D], BF16)
    outb_row = sb("outb_row_s", [1, D], BF16)
    b1T = sb("b1T_sb_s", [128, 256])
    b2_row = sb("b2_row_s", [1, NE * D], BF16)
    cbank = sb("cbank_s", [128, 8])
    ones_col = sb("ones_col_s", [128, 1])
    ones_rowf = sb("ones_rowf_s", [1, 128])
    ones_rowb = sb("ones_rowb_s", [1, 128], BF16)
    zeros8 = sb("zeros8_s", [128, 8])
    c_coef = sb("c_coef_s", [128, NTO * 8])
    xnTq = sb("xnTq_s", [128, 8, 2048], BF16)
    attn_tok = sb("attn_tok_s", [128, NTO, D], BF16)

    pb = [es_glob.enter_context(nc.psum_tensor(f"pb{i}", [128, 512], F32)) for i in range(8)]

    def qd(fn, waits=()):
        return p.step("q", fn, waits=waits, inc="qd", amt=16)

    def gdma(fn, waits=()):
        return p.step("g", fn, waits=waits, inc="gd", amt=16)

    def vseq(fns, waits=(), inc=None):
        for i, fn in enumerate(fns):
            w = list(waits) + [("vch", p.v("vch"))] if i == 0 else [("vch", p.v("vch"))]
            p.step("v", fn, waits=w, inc=(inc if (i == len(fns) - 1 and inc) else "vch"))

    qd(lambda q: q.dma_start(out=identf[:], in_=identf_d))
    qd(lambda q: q.dma_start(out=triu_tok[:], in_=triu_tok_d))
    qd(lambda q: q.dma_start(out=triu_rot[:], in_=triu_rot_d))
    qd(lambda q: q.dma_start(out=wg_sb[:], in_=wg_st))
    qd(lambda q: q.dma_start(out=bg_sb[:], in_=bgRow))
    qd(lambda q: q.dma_start(out=qkvbT_sb[:], in_=qkvbT))
    qd(lambda q: q.dma_start(out=b1T[:], in_=b1T_d))
    qd(lambda q: q.dma_start(out=identb[:], in_=identb_d))
    qd(lambda q: q.dma_start(out=pswap[:], in_=pswap_d))
    gdma(lambda g: g.dma_start(out=qkvb_row[:], in_=qkvbRow))
    gdma(lambda g: g.dma_start(out=outb_row[:], in_=outbRow))
    gdma(lambda g: g.dma_start(out=b2_row[:], in_=b2Row))

    def memsets(g):
        g.memset(cbank[:, CB_EPS:CB_EPS + 1], EPS)
        g.memset(cbank[:, CB_NHALF:CB_NHALF + 1], -0.5)
        g.memset(cbank[:, CB_SCALE:CB_SCALE + 1], 0.125)
        g.memset(cbank[:, CB_EBIAS:CB_EBIAS + 1], -12.0)
        g.memset(cbank[:, CB_INVD:CB_INVD + 1], 1.0 / D)
        g.memset(cbank[:, CB_NEG1:CB_NEG1 + 1], -1.0)
        g.memset(ones_col[:], 1.0)
        g.memset(ones_rowf[:], 1.0)
        g.memset(ones_rowb[:], 1.0)
        return g.memset(zeros8[:], 0.0)
    p.step("g", memsets, inc="gcst")
    p.barrier()

    es_XK = contextlib.ExitStack()
    xnTk = es_XK.enter_context(nc.sbuf_tensor("xnTk_s", [128, 8, 2048], BF16, side="right"))
    es_RX = contextlib.ExitStack()
    rsb = lambda n, s, d=F32: es_RX.enter_context(nc.sbuf_tensor(n, s, d, side="right"))
    xt2 = rsb("xt2_s", [128, 2, D])
    sqdump = rsb("sqdump_s", [128, D])
    stat = rsb("stat_s", [128, 2, 12])
    xn32 = rsb("xn32_s", [128, 2, D])
    xnbf2 = rsb("xnbf2_s", [128, 2, D], BF16)
    xT32 = rsb("xT32_s", [128, D])
    l8 = rsb("l8_s", [128, 8])
    top8 = rsb("top8_s", [128, 8])
    tsc1 = rsb("tsc1_s", [128, 8])
    tsc2 = rsb("tsc2_s", [128, 8])
    gat = rsb("gat_s", [128, 8])
    oh1_all = rsb("oh1_all_s", [128, NTF * 8])
    oh2_all = rsb("oh2_all_s", [128, NTF * 8])
    c1_all = rsb("c1_all_s", [128, NTF * 8])
    c2_all = rsb("c2_all_s", [128, NTF * 8])
    cnts_row = rsb("cnts_row_s", [1, NTF * 16])
    cnts_t = rsb("cnts_t_s", [128, 16])
    pref_sb = rsb("pref_sb_s", [128, 16])
    tot_sb = rsb("tot_sb_s", [1, 16])
    pref_rows = rsb("pref_rows_s", [1, NTF * 16])
    ktmp = rsb("ktmp_s", [128, 32])

    def ln_chain(n, src_tile):
        par = n % 2
        st = lambda c: stat[:, par, c:c + 1]

        wv = [("qd", p.v("qd")), ("lnarst", p.v("lnarst") - 1)]
        vseq([lambda v: v.reduce_sum(st(0), src_tile, axis=AX.X),
              lambda v: v.tensor_scalar(st(2), st(0), 1.0 / D, None, op0=ALU.mult),
              lambda v: v.scalar_tensor_tensor(st(3), st(2), -1.0, st(2),
                                               op0=ALU.mult, op1=ALU.mult)],
             waits=wv, inc="lnvst")
        p.step("a", lambda a: a.activation(sqdump[:], src_tile, AF.Square, accum_out=st(1)),
               waits=[("qd", p.v("qd"))], inc="lnassq")
        vseq([lambda v: v.scalar_tensor_tensor(st(4), st(1), 1.0 / D, st(3),
                                               op0=ALU.mult, op1=ALU.add)],
             waits=[("lnassq", p.v("lnassq"))], inc="lnvvar")

        def archain(a):
            a.activation(st(5), st(4), AF.Ln, bias=cbank[:, CB_EPS:CB_EPS + 1])
            return a.activation(st(6), st(5), AF.Exp, scale=cbank[:, CB_NHALF:CB_NHALF + 1])
        p.step("a", archain, waits=[("lnvvar", p.v("lnvvar"))], inc="lnarst")
        return st

    for g in range(NTF):
        par = g % 2
        xt = xt2[:, par, :]
        qd(lambda q, g=g, par=par: q.dma_start(out=xt2[:, par, :], in_=xrot[g]),
           waits=[("lnaxn", p.v("lnaxn") - 1)])
        st = ln_chain(g, xt)
        vseq([lambda v, xt=xt, par=par, st=st:
              v.tensor_scalar(xn32[:, par, :], xt, st(2), st(6),
                              op0=ALU.subtract, op1=ALU.mult)],
             waits=[("rttr", p.v("rttr") - 1), ("lnarst", p.v("lnarst"))], inc="lnaxn")
        b0 = 0 if par == 0 else 2

        def trans(t, b0=b0, par=par):
            for ci in range(8):
                bank = pb[b0 + ci // 4]
                inst = t.transpose(bank[:, (ci % 4) * 128:(ci % 4) * 128 + 128],
                                   xn32[:, par, ci * 128:ci * 128 + 128], identf[:])
            return inst
        p.step("t", trans, waits=[("lnaxn", p.v("lnaxn")), ("rvev", p.v("rvev") - 1)],
               inc="rttr")

        def evtr(v, b0=b0):
            v.tensor_copy(xT32[:, 0:512], pb[b0][:])
            return v.tensor_copy(xT32[:, 512:1024], pb[b0 + 1][:])
        p.step("v", evtr, waits=[("rttr", p.v("rttr")), ("rtlg", p.v("rtlg"))], inc="rvev")
        plg = pb[4 + par]

        def lgmm(t, plg=plg):
            for ci in range(8):
                t.matmul(plg[:, 0:8], xT32[:, ci * 128:ci * 128 + 128],
                         wg_sb[:, ci * 8:ci * 8 + 8], start=(ci == 0), stop=False)
            return t.matmul(plg[:, 0:8], ones_rowf[:], bg_sb[:], start=False, stop=True)
        p.step("t", lgmm, waits=[("rvev", p.v("rvev")), ("rvl8", p.v("rvl8") - 1)], inc="rtlg")

        p.step("v", lambda v, plg=plg: v.tensor_copy(l8[:], plg[:, 0:8]),
               waits=[("rtlg", p.v("rtlg")), ("raeg", p.v("raeg") - 1)], inc="rvcp")
        p.step("v", lambda v: v.max(top8[:], l8[:]),
               waits=[("rvcp", p.v("rvcp"))], inc="rvmx")
        p.step("v", lambda v: v.tensor_sub(gat[:, 0:1], top8[:, 1:2], top8[:, 0:1]),
               waits=[("rvmx", p.v("rvmx"))], inc="rvl8")
        p.step("a", lambda a: a.activation(gat[:, 3:4], gat[:, 0:1], AF.Exp),
               waits=[("rvl8", p.v("rvl8"))], inc="raeg")

        vseq([lambda v: (v.tensor_sub(tsc1[:], l8[:], top8[:, 0:1].to_broadcast([128, 8])),
                         v.tensor_scalar(gat[:, 1:2], gat[:, 0:1], 0.0, None, op0=ALU.is_equal))[-1],
              lambda v: (v.tensor_scalar(tsc1[:], tsc1[:], 0.0, None, op0=ALU.is_equal),
                         v.tensor_scalar(gat[:, 2:3], gat[:, 1:2], 1.0, None, op0=ALU.add))[-1],
              lambda v: v.tensor_tensor_scan(tsc2[:], tsc1[:], zeros8[:], 0.0, op0=ALU.add, op1=ALU.add),
              lambda v: v.tensor_scalar(tsc2[:], tsc2[:], 1.0, None, op0=ALU.is_equal),
              lambda v, g=g: v.tensor_tensor(oh1_all[:, g * 8:g * 8 + 8], tsc1[:], tsc2[:], op=ALU.mult)],
             waits=[("raeg", p.v("raeg"))], inc="rvo1")

        vseq([lambda v: (v.tensor_sub(tsc1[:], l8[:], top8[:, 1:2].to_broadcast([128, 8])),
                         v.tensor_scalar(gat[:, 4:5], gat[:, 3:4], 1.0, None, op0=ALU.add))[-1],
              lambda v: (v.tensor_scalar(tsc1[:], tsc1[:], 0.0, None, op0=ALU.is_equal),
                         v.reciprocal(gat[:, 5:6], gat[:, 4:5]))[-1],
              lambda v: v.tensor_tensor_scan(tsc2[:], tsc1[:], zeros8[:], 0.0, op0=ALU.add, op1=ALU.add),
              lambda v: v.tensor_sub(tsc2[:], tsc2[:], gat[:, 2:3].to_broadcast([128, 8])),
              lambda v: v.tensor_scalar(tsc2[:], tsc2[:], 0.0, None, op0=ALU.is_equal),
              lambda v, g=g: v.tensor_tensor(oh2_all[:, g * 8:g * 8 + 8], tsc1[:], tsc2[:], op=ALU.mult)],
             waits=[("rvo1", p.v("rvo1"))], inc="rvo2")

        vseq([lambda v: v.tensor_tensor(gat[:, 6:7], gat[:, 3:4], gat[:, 5:6], op=ALU.mult),
              lambda v, g=g: (v.tensor_mul(c1_all[:, g * 8:g * 8 + 8], oh1_all[:, g * 8:g * 8 + 8],
                                           gat[:, 5:6].to_broadcast([128, 8])),
                              v.tensor_mul(c2_all[:, g * 8:g * 8 + 8], oh2_all[:, g * 8:g * 8 + 8],
                                           gat[:, 6:7].to_broadcast([128, 8])))[-1]],
             waits=[("rvo2", p.v("rvo2"))], inc="rvoh")
        pcnt = pb[6 + par]

        def cntmm(t, pcnt=pcnt, g=g):
            t.matmul(pcnt[0:1, 0:8], ones_col[:], oh1_all[:, g * 8:g * 8 + 8], start=True, stop=True)
            return t.matmul(pcnt[0:1, 8:16], ones_col[:], oh2_all[:, g * 8:g * 8 + 8], start=True, stop=True)
        p.step("t", cntmm, waits=[("rvoh", p.v("rvoh")), ("rvcv", p.v("rvcv") - 1)], inc="rtcnt")
        p.step("v", lambda v, pcnt=pcnt, g=g:
               v.tensor_copy(cnts_row[0:1, g * 16:g * 16 + 16], pcnt[0:1, 0:16]),
               waits=[("rtcnt", p.v("rtcnt"))], inc="rvcv")

    gdma(lambda g: g.dma_start(out=scr_d[0:1, 0:2048], in_=cnts_row[:]),
         waits=[("rvcv", p.v("rvcv"))])
    gdma(lambda g: g.dma_start(out=cnts_t[:],
                               in_=scr_d[0:1, 0:2048].rearrange("o (g j) -> (o g) j", j=16)),
         waits=[("gd", p.v("gd"))])

    def prefmm(t):
        t.matmul(pb[0][:, 0:16], triu_rot[:], cnts_t[:], start=True, stop=True)
        return t.matmul(pb[1][0:1, 0:16], ones_col[:], cnts_t[:], start=True, stop=True)
    p.step("t", prefmm, waits=[("gd", p.v("gd"))], inc="rtpre")

    def evpre(v):
        v.tensor_copy(pref_sb[:], pb[0][:, 0:16])
        return v.tensor_copy(tot_sb[:], pb[1][0:1, 0:16])
    p.step("v", evpre, waits=[("rtpre", p.v("rtpre"))], inc="rvpre")
    gdma(lambda g: g.dma_start(out=scr_d[0:1, 0:2048].rearrange("o (g j) -> (o g) j", j=16),
                               in_=pref_sb[:]),
         waits=[("rvpre", p.v("rvpre"))])
    gdma(lambda g: g.dma_start(out=pref_rows[0:1, 0:2048], in_=scr_d[0:1, 0:2048]),
         waits=[("gd", p.v("gd"))])
    gd_pref = p.v("gd")

    for ti in range(NTO):
        b1b = pb[2 + ti % 2]
        b2b = pb[4 + ti % 2]

        def posmm(t, ti=ti, b1b=b1b, b2b=b2b):
            t.matmul(b1b[:, 0:8], triu_tok[:], oh1_all[:, ti * 8:ti * 8 + 8], start=True, stop=False)
            t.matmul(b1b[:, 0:8], ones_rowf[:], pref_rows[0:1, ti * 16:ti * 16 + 8], start=False, stop=True)
            t.matmul(b2b[:, 0:8], triu_tok[:], oh2_all[:, ti * 8:ti * 8 + 8], start=True, stop=False)
            t.matmul(b2b[:, 0:8], ones_rowf[:], pref_rows[0:1, ti * 16 + 8:ti * 16 + 16], start=False, stop=False)
            return t.matmul(b2b[:, 0:8], ones_rowf[:], tot_sb[0:1, 0:8], start=False, stop=True)
        p.step("t", posmm, waits=[("gd", gd_pref), ("rvcc", p.v("rvcc") - 1)], inc="rtpos")

        vseq([lambda v, ti=ti, b1b=b1b, b2b=b2b:
              (v.tensor_scalar(ktmp[:, 0:8], b1b[:, 0:8], CAP, None, op0=ALU.is_lt),
               v.tensor_scalar(ktmp[:, 8:16], b2b[:, 0:8], CAP, None, op0=ALU.is_lt))[-1],
              lambda v, ti=ti: (v.tensor_tensor(ktmp[:, 16:24], c1_all[:, ti * 8:ti * 8 + 8],
                                                ktmp[:, 0:8], op=ALU.mult),
                                v.tensor_tensor(ktmp[:, 24:32], c2_all[:, ti * 8:ti * 8 + 8],
                                                ktmp[:, 8:16], op=ALU.mult))[-1],
              lambda v, ti=ti: v.tensor_tensor(c_coef[:, ti * 8:ti * 8 + 8],
                                               ktmp[:, 16:24], ktmp[:, 24:32], op=ALU.add)],
             waits=[("rtpos", p.v("rtpos"))], inc="rvcc")

    if debug:
        qd(lambda q: q.dma_start(out=dbg["d_l8"], in_=l8[:]),
           waits=[("rvcc", p.v("rvcc"))])
        qd(lambda q: q.dma_start(out=dbg["d_top8"], in_=top8[:]))
        qd(lambda q: q.dma_start(out=dbg["d_gat"], in_=gat[:]))
        qd(lambda q: q.dma_start(out=dbg["d_tsc1"], in_=tsc1[:]))
        qd(lambda q: q.dma_start(out=dbg["d_tsc2"], in_=tsc2[:]))
        qd(lambda q: q.dma_start(out=dbg["d_stat"], in_=stat[:].rearrange("p a b -> p (a b)")))
        qd(lambda q: q.dma_start(out=dbg["d_xT32"], in_=xT32[:]))
        qd(lambda q: q.dma_start(out=dbg["d_cnts"], in_=cnts_t[:]))
        qd(lambda q: q.dma_start(out=dbg["d_pref"], in_=pref_sb[:]))
        qd(lambda q: q.dma_start(out=dbg["d_oh"], in_=oh1_all[:, 0:32]))
        qd(lambda q: q.dma_start(out=dbg["d_c12"][:, 0:16], in_=c1_all[:, 0:16]))
        qd(lambda q: q.dma_start(out=dbg["d_c12"][:, 16:32], in_=c2_all[:, 0:16]))
        qd(lambda q: q.dma_start(out=dbg["d_ktmp"], in_=ktmp[:]))

    rvcc_done = p.v("rvcc")

    for bt in range(NTB):
        n = NTF + bt
        par = n % 2
        xt = xt2[:, par, :]
        qd(lambda q, bt=bt, par=par: q.dma_start(out=xt2[:, par, :], in_=xbatch[bt]),
           waits=[("lnaxn", p.v("lnaxn") - 1)])
        st = ln_chain(n, xt)
        vseq([lambda v, xt=xt, par=par, st=st:
              v.tensor_scalar(xnbf2[:, par, :], xt, st(2), st(6),
                              op0=ALU.subtract, op1=ALU.mult)],
             waits=[("xttr", p.v("xttr") - 1), ("lnarst", p.v("lnarst"))], inc="lnaxn")
        b0 = 0 if par == 0 else 2

        def transb(t, b0=b0, par=par):
            pbB = pb[b0][:].bitcast(BF16)
            for ci in range(8):
                inst = t.transpose(pbB[:, ci * 128:ci * 128 + 128],
                                   xnbf2[:, par, ci * 128:ci * 128 + 128], identb[:])
            return inst
        tw = [("lnaxn", p.v("lnaxn")), ("xvev", p.v("xvev") - 1)]
        if bt < 2:
            tw.append(("rvcc", rvcc_done))
        p.step("t", transb, waits=tw, inc="xttr")
        tgt = xnTq if bt < NTO else xnTk
        off = (bt % NTO) * 128

        def evtrb(v, b0=b0, tgt=tgt, off=off):
            return v.tensor_copy(tgt[:, :, off:off + 128],
                                 pb[b0][:].bitcast(BF16).rearrange("p (c k) -> p c k", k=128))
        p.step("v", evtrb, waits=[("xttr", p.v("xttr"))], inc="xvev")

    if debug:
        qd(lambda q: q.dma_start(out=dbg["d_xnTq"], in_=xnTq[:, 0, :]),
           waits=[("xvev", p.v("xvev"))])
        qd(lambda q: q.dma_start(out=dbg["d_ccoef"], in_=c_coef[:]))

    es_RX.close()
    p.barrier()

    es_A = contextlib.ExitStack()
    asb = lambda n, s, d=F32: es_A.enter_context(nc.sbuf_tensor(n, s, d, side="right"))
    cosq = asb("cosq_s", [128, 4096], BF16)
    sinq = asb("sinq_s", [128, 4096], BF16)
    wq_b = asb("wq_b_s", [128, 2, 8, 128], BF16)
    wk_b = asb("wk_b_s", [128, 2, 8, 128], BF16)
    wv_b = asb("wv_b_s", [128, 2, 8, 128], BF16)
    kq_sb = asb("kq_sb_s", [128, 2, 512], BF16)
    ttmp = asb("ttmp_s", [128, 2, 512], BF16)
    krot = asb("krot_s", [128, 4096], BF16)
    qrot = asb("qrot_s", [128, 2048], BF16)
    vt = asb("vt_s", [128, 32, 130], BF16)
    pt_sb = asb("pt_sb_s", [128, 2, 2, 512], BF16)
    recip_sb = asb("recip_sb_s", [128, 8])

    qd(lambda q: q.dma_start(out=cosq[:], in_=cosq_d))
    qd(lambda q: q.dma_start(out=sinq[:], in_=sinq_d))
    p.step("v", lambda v: v.memset(vt[:], 1.0), inc="avones")
    qd_tbl = p.v("qd")

    def xnT_slice(ci, t0, nn):
        if t0 < 2048:
            return xnTq[:, ci, t0:t0 + nn]
        return xnTk[:, ci, t0 - 2048:t0 - 2048 + nn]

    for hp in range(8):
        par = hp % 2
        wdw = [("atdone", p.v("atdone") - 1)]
        for s, wt in ((0, wq_b), (1, wk_b), (2, wv_b)):
            gdma(lambda g, s=s, wt=wt, par=par, hp=hp:
                 g.dma_start(out=wt[:, par], in_=qkvw_st[s, hp]), waits=wdw)
        gd_w = p.v("gd")

        for kind in range(12):
            isq = kind >= 8
            it = kind - 8 if isq else kind
            pbm = pb[kind % 2]
            pbs = pb[2 + kind % 2]

            def kqmm(t, isq=isq, it=it, par=par, pbm=pbm):
                w = wq_b if isq else wk_b
                for ci in range(8):
                    rhs = xnTq[:, ci, it * 512:it * 512 + 512] if isq else xnT_slice(ci, it * 512, 512)
                    inst = t.matmul(pbm[:], w[:, par, ci, :], rhs,
                                    start=(ci == 0), stop=(ci == 7))
                return inst
            tw = [("gd", gd_w), ("aaev", p.v("aaev") - 1), ("avrp", p.v("avrp") - 1)]
            p.step("t", kqmm, waits=tw, inc="atkm")
            bcol = hp if isq else 8 + hp
            p.step("a", lambda a, kind=kind, pbm=pbm, bcol=bcol:
                   a.activation(kq_sb[:, kind % 2, :], pbm[:], AF.Identity,
                                bias=qkvbT_sb[:, bcol:bcol + 1]),
                   waits=[("atkm", p.v("atkm"))], inc="aaev")

            def swmm(t, kind=kind, pbs=pbs):
                return t.matmul(pbs[:], pswap[:], kq_sb[:, kind % 2, :], start=True, stop=True)
            p.step("t", swmm, waits=[("aaev", p.v("aaev")), ("avrp", p.v("avrp") - 1)], inc="atsw")
            dst = qrot if isq else krot

            vseq([lambda v, kind=kind, it=it, pbs=pbs:
                  (v.tensor_tensor(ttmp[:, 0, :], kq_sb[:, kind % 2, :],
                                   cosq[:, it * 512:it * 512 + 512], op=ALU.mult),
                   v.tensor_tensor(ttmp[:, 1, :], pbs[:],
                                   sinq[:, it * 512:it * 512 + 512], op=ALU.mult))[-1],
                  lambda v, kind=kind, it=it, dst=dst:
                  v.tensor_tensor(dst[:, it * 512:it * 512 + 512],
                                  ttmp[:, 0, :], ttmp[:, 1, :], op=ALU.add)],
                 waits=[("atsw", p.v("atsw"))], inc="avrp")

        for tt in range(32):
            pbv = pb[4 + tt % 2]

            def vmm(t, tt=tt, par=par, pbv=pbv, hp=hp):
                for ci in range(8):
                    t.matmul(pbv[:, 0:128], xnT_slice(ci, tt * 128, 128), wv_b[:, par, ci, :],
                             start=(ci == 0), stop=False)
                return t.matmul(pbv[:, 0:128], ones_rowb[:],
                                qkvb_row[0:1, 2 * D + hp * 128:2 * D + hp * 128 + 128],
                                start=False, stop=True)
            p.step("t", vmm, waits=[("gd", gd_w), ("avve", p.v("avve") - 1)], inc="atvm")

            def vev(v, tt=tt, pbv=pbv):
                v.tensor_copy(vt[:, tt, 0:64], pbv[:, 0:64])
                return v.tensor_copy(vt[:, tt, 65:129], pbv[:, 64:128])
            p.step("v", vev, waits=[("atvm", p.v("atvm")), ("avones", p.v("avones"))], inc="avve")

        avrp_done = p.v("avrp")
        avve_done = p.v("avve")
        for qt in range(4):
            fs0 = p.v("afx")
            fa0 = p.v("afa")
            for kc in range(32):
                sb0 = pb[kc % 2]
                sb1 = pb[2 + kc % 2]

                def smm(t, kc=kc, qt=qt, sb0=sb0, sb1=sb1):
                    t.matmul(sb0[:], krot[0:64, kc * 128:kc * 128 + 128],
                             qrot[0:64, qt * 512:qt * 512 + 512], start=True, stop=True,
                             skip_group_check=True)
                    return t.matmul(sb1[:], krot[64:128, kc * 128:kc * 128 + 128],
                                    qrot[64:128, qt * 512:qt * 512 + 512], start=True, stop=True,
                                    skip_group_check=True)
                tw = [("avrp", avrp_done), ("avve", avve_done),
                      ("afx", fs0 + 2 * (kc - 1))]
                if kc == 0 and qt > 0:
                    tw.append(("avno", p.v("avno")))
                p.step("t", smm, waits=tw, inc="afs", amt=2)

                def expev(a, kc=kc, sb0=sb0, sb1=sb1):
                    a.activation(pt_sb[:, 0, kc % 2, :], sb0[:], AF.Exp,
                                 scale=cbank[:, CB_SCALE:CB_SCALE + 1],
                                 bias=cbank[:, CB_EBIAS:CB_EBIAS + 1])
                    return a.activation(pt_sb[:, 1, kc % 2, :], sb1[:], AF.Exp,
                                        scale=cbank[:, CB_SCALE:CB_SCALE + 1],
                                        bias=cbank[:, CB_EBIAS:CB_EBIAS + 1])
                p.step("a", expev,
                       waits=[("afs", p.v("afs")), ("afa", p.v("afa") - 1)], inc="afx", amt=2)

                def avmm(t, kc=kc):
                    for h in range(2):
                        pba = pb[4 + h]
                        for qs in range(4):
                            inst = t.matmul(pba[:, qs * 65:qs * 65 + 65],
                                            pt_sb[:, h, kc % 2, qs * 128:qs * 128 + 128],
                                            vt[:, kc, h * 65:h * 65 + 65],
                                            start=(kc == 0), stop=(kc == 31),
                                            skip_group_check=True)
                    return inst
                p.step("t", avmm, waits=[("afx", p.v("afx"))], inc="afa")

            def norm1(v):
                for h in range(2):
                    pba = pb[4 + h]
                    for qs in range(4):
                        inst = v.reciprocal(recip_sb[:, h * 4 + qs:h * 4 + qs + 1],
                                            pba[:, qs * 65 + 64:qs * 65 + 65])
                return inst
            p.step("v", norm1, waits=[("afa", p.v("afa"))], inc="avn1")

            def norm2(v, qt=qt, hp=hp):
                for h in range(2):
                    pba = pb[4 + h]
                    for qs in range(4):
                        tt = qt * 4 + qs
                        inst = v.tensor_mul(
                            attn_tok[:, tt, (2 * hp + h) * 64:(2 * hp + h) * 64 + 64],
                            pba[:, qs * 65:qs * 65 + 64],
                            recip_sb[:, h * 4 + qs:h * 4 + qs + 1].to_broadcast([128, 64]))
                return inst
            p.step("v", norm2, waits=[("avn1", p.v("avn1"))], inc="avno")
        p.step("t", lambda t: t.nop(), waits=[("avno", p.v("avno"))], inc="atdone")

    if debug:
        qd(lambda q: q.dma_start(out=dbg["d_krot"], in_=krot[:]),
           waits=[("atdone", p.v("atdone"))])
        qd(lambda q: q.dma_start(out=dbg["d_attn"],
                                 in_=attn_tok[:].rearrange("p a b -> p (a b)")),
           waits=[("avno", p.v("avno"))])

    es_A.close()
    es_XK.close()
    p.barrier()

    es_ACC = contextlib.ExitStack()
    acc = es_ACC.enter_context(nc.sbuf_tensor("acc_s", [128, NTO, D], F32, side="right"))
    es_OP = contextlib.ExitStack()
    ow_b = es_OP.enter_context(nc.sbuf_tensor("ow_b_s", [128, 8, D], BF16, side="right"))
    attnF_t = es_OP.enter_context(nc.sbuf_tensor("attnF_t_s", [128, 8, 128], BF16, side="right"))

    gdma(lambda g: g.dma_start(out=ow_b[:], in_=outw_st))
    gd_ow = p.v("gd")
    qd(lambda q: q.dma_start(out=acc[:], in_=xown))
    qd_acc = p.v("qd")

    for tt in range(NTO):
        def optr(t, tt=tt):
            pbB = pb[0][:].bitcast(BF16)
            for ci in range(8):
                inst = t.transpose(pbB[:, ci * 128:ci * 128 + 128],
                                   attn_tok[:, tt, ci * 128:ci * 128 + 128], identb[:])
            return inst
        p.step("t", optr, waits=[("opve", p.v("opve"))], inc="optr")

        def opev(v):
            return v.tensor_copy(attnF_t[:, :, :],
                                 pb[0][:].bitcast(BF16).rearrange("p (c k) -> p c k", k=128))
        p.step("v", opev, waits=[("optr", p.v("optr"))], inc="opve")

        def opmm(t, tt=tt):
            for nh in range(2):
                pbo = pb[2 + nh]
                t.matmul(pbo[:], ones_rowb[:], outb_row[0:1, nh * 512:nh * 512 + 512],
                         start=True, stop=False, skip_group_check=True)
                for ci in range(8):
                    inst = t.matmul(pbo[:], attnF_t[:, ci, :], ow_b[:, ci, nh * 512:nh * 512 + 512],
                                    start=False, stop=(ci == 7), skip_group_check=True)
            return inst
        p.step("t", opmm, waits=[("opve", p.v("opve")), ("gd", gd_ow),
                                 ("opac", p.v("opac"))], inc="opmm")

        def opadd(v, tt=tt):
            v.tensor_tensor(acc[:, tt, 0:512], acc[:, tt, 0:512], pb[2][:], op=ALU.add)
            return v.tensor_tensor(acc[:, tt, 512:1024], acc[:, tt, 512:1024], pb[3][:], op=ALU.add)
        p.step("v", opadd, waits=[("opmm", p.v("opmm")), ("qd", qd_acc)], inc="opac")

    es_OP.close()
    p.barrier()

    es_F = contextlib.ExitStack()
    w1e = es_F.enter_context(nc.sbuf_tensor("w1e_s", [128, 2, 8, 512], BF16, side="right"))
    w2e = es_F.enter_context(nc.sbuf_tensor("w2e_s", [128, 2, 4, 1024], BF16, side="right"))
    h_sb = es_F.enter_context(nc.sbuf_tensor("h_sb_s", [128, 2, 256], BF16, side="right"))

    fmarks = {}
    gd_f = {}
    for e in range(NE):
        for fe in range(8):
            qi = e * 8 + fe
            wp = qi % 2
            fq, half = fe // 2, fe % 2
            wdw = [("fm2", fmarks[qi - 2])] if qi >= 2 else []
            gdma(lambda g, e=e, fq=fq, half=half, wp=wp:
                 g.dma_start(out=w1e[:, wp], in_=w1_st[e, fq][:, :, half * 512:half * 512 + 512]),
                 waits=wdw)
            gdma(lambda g, e=e, fq=fq, half=half, wp=wp:
                 g.dma_start(out=w2e[:, wp], in_=w2_st[e, fq][:, half * 4:half * 4 + 4, :]))
            gd_f[qi] = p.v("gd")
            for tq in range(8):
                def mk_mm1(fc, tq=tq, wp=wp):
                    def mm1(t, fc=fc, tq=tq, wp=wp):
                        pbh = pb[fc % 2]
                        for ci in range(8):
                            inst = t.matmul(pbh[:, 0:256], w1e[:, wp, ci, fc * 128:fc * 128 + 128],
                                            xnTq[:, ci, tq * 256:tq * 256 + 256],
                                            start=(ci == 0), stop=(ci == 7),
                                            skip_group_check=True)
                        return inst
                    return mm1

                def mk_mm2(fc, e=e, fe=fe, wp=wp):
                    def mm2(t, fc=fc, e=e, fe=fe, wp=wp):
                        inst = None
                        for stt in range(2):
                            for nh in range(2):
                                pby = pb[2 + stt * 2 + nh]
                                if fe == 0 and fc == 0:
                                    t.matmul(pby[:], ones_rowb[:],
                                             b2_row[0:1, e * D + nh * 512:e * D + nh * 512 + 512],
                                             start=True, stop=False, skip_group_check=True)
                                inst = t.matmul(pby[:], h_sb[:, fc % 2, stt * 128:stt * 128 + 128],
                                                w2e[:, wp, fc, nh * 512:nh * 512 + 512],
                                                start=(fe > 0 and fc == 0), stop=(fc == 3),
                                                skip_group_check=True)
                        return inst
                    return mm2

                def mk_gelu(fc, e=e, fe=fe):
                    def gelu(a, fc=fc, e=e, fe=fe):
                        j = e * 32 + fe * 4 + fc
                        return a.activation(h_sb[:, fc % 2, :], pb[fc % 2][:, 0:256],
                                            AF.Gelu_apprx_tanh, bias=b1T[:, j:j + 1])
                    return gelu

                for fc in range(4):
                    tw = [("gd", gd_f[qi])]
                    if fc >= 2:
                        tw.append(("fge", p.v("fge") - 1))
                    p.step("t", mk_mm1(fc), waits=tw, inc="fmm")
                    p.step("a", mk_gelu(fc), waits=[("fmm", p.v("fmm")), ("fm2", p.v("fm2") - 1)],
                           inc="fge")
                    if fc >= 1:
                        w2w = [("fge", p.v("fge") - 1)]
                        if fc == 1 and qi + tq > 0:
                            w2w.append(("fev", p.v("fev")))
                        p.step("t", mk_mm2(fc - 1), waits=w2w, inc="fm2")
                p.step("t", mk_mm2(3), waits=[("fge", p.v("fge"))], inc="fm2")

                def fevac(v, e=e, tq=tq):
                    inst = None
                    for stx in range(2):
                        tt = tq * 2 + stx
                        cav = c_coef[:, tt * 8 + e:tt * 8 + e + 1]
                        for nh in range(2):
                            pby = pb[2 + stx * 2 + nh]
                            inst = v.scalar_tensor_tensor(
                                acc[:, tt, nh * 512:nh * 512 + 512], pby[:], cav,
                                acc[:, tt, nh * 512:nh * 512 + 512],
                                op0=ALU.mult, op1=ALU.add)
                    return inst
                p.step("v", fevac, waits=[("fm2", p.v("fm2"))], inc="fev")
            fmarks[qi] = p.v("fm2")

    es_F.close()
    p.barrier()

    qd(lambda q: q.dma_start(out=outp, in_=acc[:]))
    p.step("q", lambda q: q.nop(), waits=[("qd", p.v("qd"))])

    with nc.Block() as blk:
        p.emit(blk)
    es_ACC.close()
    es_glob.close()
    return nc


def _stage(core, x, qkv_w, qkv_b, out_w, out_b, Wg, bg, W1, b1, W2, b2):
    b = core // 2
    h = core % 2
    xf = np.ascontiguousarray(x.reshape(128, 128, D))
    rot = np.roll(np.arange(NTF), -16 * core)
    xrot = np.ascontiguousarray(xf[rot])
    xb = x.reshape(4, 4096, D)[b]
    own = xb[h * 2048:(h + 1) * 2048]
    oth = xb[(1 - h) * 2048:(2 - h) * 2048]
    xbatch = np.ascontiguousarray(
        np.concatenate([own, oth], axis=0).reshape(NTB, 128, D))
    xown = np.ascontiguousarray(own.reshape(NTO, 128, D).transpose(1, 0, 2))

    perm = np.arange(3 * D)
    for hh in range(NH):
        base = hh * HD
        for sec in (0, D):
            perm[sec + base:sec + base + 32] = sec + base + 2 * np.arange(32)
            perm[sec + base + 32:sec + base + 64] = sec + base + 2 * np.arange(32) + 1
    qkvw_p = qkv_w[:, perm]
    qkvb_p = qkv_b[perm]
    qkvw_st = np.ascontiguousarray(
        qkvw_p.reshape(8, 128, 3, 8, 128).transpose(2, 3, 1, 0, 4))
    qkvbT = np.ascontiguousarray(qkvb_p.reshape(24, 128).T)
    qkvbRow = np.ascontiguousarray(qkvb_p.reshape(1, 3 * D))
    outw_st = np.ascontiguousarray(out_w.reshape(8, 128, D).transpose(1, 0, 2))
    outbRow = np.ascontiguousarray(out_b.reshape(1, D))
    wg_stv = np.ascontiguousarray(Wg.reshape(8, 128, 8).transpose(1, 0, 2).reshape(128, 64))
    bgRow = np.ascontiguousarray(bg.reshape(1, 8))
    w1_stv = np.ascontiguousarray(
        W1.reshape(NE, 8, 128, 4, 1024).transpose(0, 3, 2, 1, 4))
    w2_stv = np.ascontiguousarray(
        W2.reshape(NE, 4, 8, 128, D).transpose(0, 1, 3, 2, 4))
    b1T = np.ascontiguousarray(b1.reshape(NE, 32, 128).transpose(2, 0, 1).reshape(128, 256))
    b2Row = np.ascontiguousarray(b2.reshape(1, NE * D))

    identf = np.eye(128, dtype=np.float32)
    identb = np.eye(128, dtype=np.float32).astype(ml_dtypes.bfloat16)
    swap = np.arange(128)
    swap = np.where((swap % 64) < 32, swap + 32, swap - 32)
    psw = np.zeros((128, 128), np.float32)
    psw[swap, np.arange(128)] = 1.0
    pswap = psw.astype(ml_dtypes.bfloat16)
    triu_tok = np.triu(np.ones((128, 128), np.float32), 1)
    canon = rot
    triu_rot = (canon[:, None] < canon[None, :]).astype(np.float32)
    freqs = (10000.0 ** (np.arange(0, HD, 2.0, dtype=np.float32) / HD)).astype(np.float32)
    tpos = np.arange(4096, dtype=np.float32)
    stagedpos = np.concatenate([tpos[h * 2048:(h + 1) * 2048],
                                tpos[(1 - h) * 2048:(2 - h) * 2048]])
    ang = (stagedpos[:, None] * freqs[None, :]).astype(np.float32)
    cosv = np.cos(ang.astype(np.float64)).astype(np.float32)
    sinv = np.sin(ang.astype(np.float64)).astype(np.float32)
    prow = np.arange(128)
    cosq = np.ascontiguousarray(cosv[:, prow % 32].T).astype(ml_dtypes.bfloat16)
    sgn = np.where((prow % 64) < 32, -1.0, 1.0).astype(np.float32)
    sinq = np.ascontiguousarray((sinv[:, prow % 32] * sgn[None, :]).T).astype(ml_dtypes.bfloat16)

    return {
        "xrot": xrot, "xbatch": xbatch, "xown": xown,
        "qkvw_st": qkvw_st, "outw_st": outw_st, "qkvbT": qkvbT,
        "qkvbRow": qkvbRow, "outbRow": outbRow, "wg_st": wg_stv, "bgRow": bgRow,
        "w1_st": w1_stv, "w2_st": w2_stv, "b1T": b1T, "b2Row": b2Row,
        "identf": identf, "identb": identb, "pswap": pswap,
        "triu_tok": triu_tok, "triu_rot": triu_rot, "cosq": cosq, "sinq": sinq,
    }


def kernel(L0_attn_x, L0_attn_qkv_w, L0_attn_qkv_b, L0_attn_out_w, L0_attn_out_b,
           L0_ffn_Wg, L0_ffn_bg, L0_ffn_W1, L0_ffn_b1, L0_ffn_W2, L0_ffn_b2):
    x = np.ascontiguousarray(np.asarray(L0_attn_x, np.float32).reshape(16384, D))
    args = [np.asarray(a, np.float32) for a in
            (L0_attn_qkv_w, L0_attn_qkv_b, L0_attn_out_w, L0_attn_out_b,
             L0_ffn_Wg, L0_ffn_bg, L0_ffn_W1, L0_ffn_b1, L0_ffn_W2, L0_ffn_b2)]
    nc = build()
    in_maps = [_stage(c, x, *args) for c in range(NCORES)]
    res = run_bass_kernel_spmd(nc, in_maps, core_ids=list(range(NCORES)))
    shards = []
    for c in range(NCORES):
        o = np.asarray(res.results[c]["outp"])
        shards.append(o.transpose(1, 0, 2).reshape(2048, D))
    return np.concatenate(shards, axis=0).reshape(4, 4096, D).astype(np.float32)


# version 03
# speedup vs baseline: 1.3599x; 1.0348x over previous
import contextlib
import numpy as np
import ml_dtypes
import concourse.bass as bass
import concourse.mybir as mybir
from concourse.bass_utils import run_bass_kernel_spmd

F32 = mybir.dt.float32
BF16 = mybir.dt.bfloat16
AF = mybir.ActivationFunctionType
ALU = mybir.AluOpType
AX = mybir.AxisListType

D = 1024
NH = 16
HD = 64
FF = 4096
NE = 8
CAP = 4916.0
NCORES = 8
NTF = 128
NTO = 16
NTB = 32
EPS = 1e-5

CB_EPS, CB_NHALF, CB_SCALE, CB_EBIAS, CB_INVD, CB_NEG1 = 0, 1, 2, 3, 4, 5


class Plan:
    ORDER = (("q", "sync"), ("g", "gpsimd"), ("t", "tensor"), ("v", "vector"), ("a", "scalar"))

    def __init__(self, nc, es):
        self.nc = nc
        self.es = es
        self.steps = {e: [] for e, _ in self.ORDER}
        self.sems = {}
        self.cnt = {}

    def sem(self, name):
        if name not in self.sems:
            self.sems[name] = self.es.enter_context(self.nc.semaphore(name))
            self.cnt[name] = 0
        return self.sems[name]

    def step(self, eng, fn, waits=(), inc=None, amt=1):
        waits = tuple((s, v) for s, v in waits if v > 0)
        for s, _ in waits:
            self.sem(s)
        if inc is not None:
            self.sem(inc)
            self.cnt[inc] += amt
        self.steps[eng].append((fn, waits, inc, amt))
        return self.cnt[inc] if inc is not None else None

    def v(self, name):
        self.sem(name)
        return self.cnt[name]

    def barrier(self):
        for e in ("q", "g", "t", "v", "a"):
            w = []
            if e == "q":
                w.append(("qd", self.v("qd")))
            if e == "g":
                w.append(("gd", self.v("gd")))
            self.step(e, lambda en: en.nop(), waits=w, inc="bar")
        tot = self.v("bar")
        for e in ("q", "g", "t", "v", "a"):
            self.step(e, lambda en: en.nop(), waits=[("bar", tot)])

    def emit(self, blk):
        for engname, attr in self.ORDER:
            steps = self.steps[engname]

            def body(e, steps=steps):
                for fn, waits, inc, amt in steps:
                    for s, val in waits:
                        e.wait_ge(self.sems[s], val)
                    inst = fn(e)
                    if inc is not None:
                        assert inst is not None
                        inst.then_inc(self.sems[inc], amt)
            getattr(blk, attr)(body)


def build(debug=False):
    nc = bass.Bass("TRN2", target_bir_lowering=False)

    dt_in = lambda n, s, d=F32: nc.dram_tensor(n, s, d, kind="ExternalInput").ap()
    xrot = dt_in("xrot", [NTF, 128, D])
    xbatch = dt_in("xbatch", [NTB, 128, D])
    xown = dt_in("xown", [128, NTO, D])
    qkvw_st = dt_in("qkvw_st", [3, 8, 128, 8, 128])
    outw_st = dt_in("outw_st", [128, 8, D])
    qkvbT = dt_in("qkvbT", [128, 24])
    qkvbRow = dt_in("qkvbRow", [1, 3 * D])
    outbRow = dt_in("outbRow", [1, D])
    wg_st = dt_in("wg_st", [128, 64])
    bgRow = dt_in("bgRow", [1, 8])
    w1_st = dt_in("w1_st", [NE, 4, 128, 8, 1024])
    w2_st = dt_in("w2_st", [NE, 4, 128, 8, 1024])
    b1T_d = dt_in("b1T", [128, 256])
    b2Row = dt_in("b2Row", [1, NE * D])
    identf_d = dt_in("identf", [128, 128])
    identb_d = dt_in("identb", [128, 128], BF16)
    pswap_d = dt_in("pswap", [128, 128], BF16)
    triu_tok_d = dt_in("triu_tok", [128, 128])
    triu_rot_d = dt_in("triu_rot", [128, 128])
    cosq_d = dt_in("cosq", [128, 4096], BF16)
    sinq_d = dt_in("sinq", [128, 4096], BF16)
    outp = nc.dram_tensor("outp", [128, NTO, D], F32, kind="ExternalOutput").ap()
    scr_d = nc.dram_tensor("scr", [1, 4096], F32, kind="Internal").ap()
    dbg = {}
    if debug:
        for n, s, d in [("d_ccoef", [128, 128], F32), ("d_cnts", [128, 16], F32),
                        ("d_pref", [128, 16], F32), ("d_xnTq", [128, 2048], BF16),
                        ("d_attn", [128, NTO * D], BF16),
                        ("d_krot", [128, 4096], BF16), ("d_oh", [128, 32], F32),
                        ("d_l8", [128, 8], F32), ("d_top8", [128, 8], F32),
                        ("d_c12", [128, 32], F32), ("d_ktmp", [128, 32], F32),
                        ("d_gat", [128, 8], F32), ("d_tsc1", [128, 8], F32),
                        ("d_tsc2", [128, 8], F32), ("d_stat", [128, 24], F32),
                        ("d_xT32", [128, 1024], F32)]:
            dbg[n] = nc.dram_tensor(n, s, d, kind="ExternalOutput").ap()

    es_glob = contextlib.ExitStack()
    p = Plan(nc, es_glob)
    sb = lambda n, s, d=F32: es_glob.enter_context(nc.sbuf_tensor(n, s, d))

    identf = sb("identf_s", [128, 128])
    identb = sb("identb_s", [128, 128], BF16)
    pswap = sb("pswap_s", [128, 128], BF16)
    triu_tok = sb("triu_tok_s", [128, 128])
    triu_rot = sb("triu_rot_s", [128, 128])
    wg_sb = sb("wg_sb_s", [128, 64])
    bg_sb = sb("bg_sb_s", [1, 8])
    qkvbT_sb = sb("qkvbT_sb_s", [128, 24])
    qkvb_row = sb("qkvb_row_s", [1, 3 * D], BF16)
    outb_row = sb("outb_row_s", [1, D], BF16)
    b1T = sb("b1T_sb_s", [128, 256])
    b2_row = sb("b2_row_s", [1, NE * D], BF16)
    cbank = sb("cbank_s", [128, 8])
    ones_col = sb("ones_col_s", [128, 1])
    ones_rowf = sb("ones_rowf_s", [1, 128])
    ones_rowb = sb("ones_rowb_s", [1, 128], BF16)
    zeros8 = sb("zeros8_s", [128, 8])
    c_coef = sb("c_coef_s", [128, NTO * 8])
    xnTq = sb("xnTq_s", [128, 8, 2048], BF16)
    attn_tok = sb("attn_tok_s", [128, NTO, D], BF16)

    pbig = es_glob.enter_context(nc.psum_tensor("pbig", [128, 2048], F32))
    pb = [pbig[:, i * 512:(i + 1) * 512] for i in range(4)]
    pb += [es_glob.enter_context(nc.psum_tensor(f"pb{i}", [128, 512], F32)) for i in range(4, 8)]

    def qd(fn, waits=()):
        return p.step("q", fn, waits=waits, inc="qd", amt=16)

    def gdma(fn, waits=()):
        return p.step("g", fn, waits=waits, inc="gd", amt=16)

    def vseq(fns, waits=(), inc=None):
        for i, fn in enumerate(fns):
            w = list(waits) + [("vch", p.v("vch"))] if i == 0 else [("vch", p.v("vch"))]
            p.step("v", fn, waits=w, inc=(inc if (i == len(fns) - 1 and inc) else "vch"))

    qd(lambda q: q.dma_start(out=identf[:], in_=identf_d))
    qd(lambda q: q.dma_start(out=triu_tok[:], in_=triu_tok_d))
    qd(lambda q: q.dma_start(out=triu_rot[:], in_=triu_rot_d))
    qd(lambda q: q.dma_start(out=wg_sb[:], in_=wg_st))
    qd(lambda q: q.dma_start(out=bg_sb[:], in_=bgRow))
    qd(lambda q: q.dma_start(out=qkvbT_sb[:], in_=qkvbT))
    qd(lambda q: q.dma_start(out=b1T[:], in_=b1T_d))
    qd(lambda q: q.dma_start(out=identb[:], in_=identb_d))
    qd(lambda q: q.dma_start(out=pswap[:], in_=pswap_d))
    gdma(lambda g: g.dma_start(out=qkvb_row[:], in_=qkvbRow))
    gdma(lambda g: g.dma_start(out=outb_row[:], in_=outbRow))
    gdma(lambda g: g.dma_start(out=b2_row[:], in_=b2Row))

    def memsets(g):
        g.memset(cbank[:, CB_EPS:CB_EPS + 1], EPS)
        g.memset(cbank[:, CB_NHALF:CB_NHALF + 1], -0.5)
        g.memset(cbank[:, CB_SCALE:CB_SCALE + 1], 0.125)
        g.memset(cbank[:, CB_EBIAS:CB_EBIAS + 1], -12.0)
        g.memset(cbank[:, CB_INVD:CB_INVD + 1], 1.0 / D)
        g.memset(cbank[:, CB_NEG1:CB_NEG1 + 1], -1.0)
        g.memset(ones_col[:], 1.0)
        g.memset(ones_rowf[:], 1.0)
        g.memset(ones_rowb[:], 1.0)
        return g.memset(zeros8[:], 0.0)
    p.step("g", memsets, inc="gcst")
    p.barrier()

    es_XK = contextlib.ExitStack()
    xnTk = es_XK.enter_context(nc.sbuf_tensor("xnTk_s", [128, 8, 2048], BF16, side="right"))
    es_RX = contextlib.ExitStack()
    rsb = lambda n, s, d=F32: es_RX.enter_context(nc.sbuf_tensor(n, s, d, side="right"))
    xt2 = rsb("xt2_s", [128, 2, D])
    sqdump = rsb("sqdump_s", [128, D])
    stat = rsb("stat_s", [128, 2, 12])
    xn32 = rsb("xn32_s", [128, 2, D])
    xnbf2 = rsb("xnbf2_s", [128, 2, D], BF16)
    xT32 = rsb("xT32_s", [128, D])
    l8 = rsb("l8_s", [128, 8])
    top8 = rsb("top8_s", [128, 8])
    tsc1 = rsb("tsc1_s", [128, 8])
    tsc2 = rsb("tsc2_s", [128, 8])
    gat = rsb("gat_s", [128, 8])
    oh1_all = rsb("oh1_all_s", [128, NTF * 8])
    oh2_all = rsb("oh2_all_s", [128, NTF * 8])
    c1_all = rsb("c1_all_s", [128, NTF * 8])
    c2_all = rsb("c2_all_s", [128, NTF * 8])
    cnts_row = rsb("cnts_row_s", [1, NTF * 16])
    cnts_t = rsb("cnts_t_s", [128, 16])
    pref_sb = rsb("pref_sb_s", [128, 16])
    tot_sb = rsb("tot_sb_s", [1, 16])
    pref_rows = rsb("pref_rows_s", [1, NTF * 16])
    ktmp = rsb("ktmp_s", [128, 32])

    def ln_chain(n, src_tile):
        par = n % 2
        st = lambda c: stat[:, par, c:c + 1]

        wv = [("qd", p.v("qd")), ("lnarst", p.v("lnarst") - 1)]
        vseq([lambda v: v.reduce_sum(st(0), src_tile, axis=AX.X),
              lambda v: v.tensor_scalar(st(2), st(0), 1.0 / D, None, op0=ALU.mult),
              lambda v: v.scalar_tensor_tensor(st(3), st(2), -1.0, st(2),
                                               op0=ALU.mult, op1=ALU.mult)],
             waits=wv, inc="lnvst")
        p.step("a", lambda a: a.activation(sqdump[:], src_tile, AF.Square, accum_out=st(1)),
               waits=[("qd", p.v("qd"))], inc="lnassq")
        vseq([lambda v: v.scalar_tensor_tensor(st(4), st(1), 1.0 / D, st(3),
                                               op0=ALU.mult, op1=ALU.add)],
             waits=[("lnassq", p.v("lnassq"))], inc="lnvvar")

        def archain(a):
            a.activation(st(5), st(4), AF.Ln, bias=cbank[:, CB_EPS:CB_EPS + 1])
            return a.activation(st(6), st(5), AF.Exp, scale=cbank[:, CB_NHALF:CB_NHALF + 1])
        p.step("a", archain, waits=[("lnvvar", p.v("lnvvar"))], inc="lnarst")
        return st

    for g in range(NTF):
        par = g % 2
        xt = xt2[:, par, :]
        qd(lambda q, g=g, par=par: q.dma_start(out=xt2[:, par, :], in_=xrot[g]),
           waits=[("lnaxn", p.v("lnaxn") - 1)])
        st = ln_chain(g, xt)
        vseq([lambda v, xt=xt, par=par, st=st:
              v.tensor_scalar(xn32[:, par, :], xt, st(2), st(6),
                              op0=ALU.subtract, op1=ALU.mult)],
             waits=[("rttr", p.v("rttr") - 1), ("lnarst", p.v("lnarst"))], inc="lnaxn")
        b0 = 0 if par == 0 else 2

        def trans(t, b0=b0, par=par):
            for ci in range(8):
                bank = pb[b0 + ci // 4]
                inst = t.transpose(bank[:, (ci % 4) * 128:(ci % 4) * 128 + 128],
                                   xn32[:, par, ci * 128:ci * 128 + 128], identf[:])
            return inst
        p.step("t", trans, waits=[("lnaxn", p.v("lnaxn")), ("rvev", p.v("rvev") - 1)],
               inc="rttr")

        def evtr(v, b0=b0):
            v.tensor_copy(xT32[:, 0:512], pb[b0][:])
            return v.tensor_copy(xT32[:, 512:1024], pb[b0 + 1][:])
        p.step("v", evtr, waits=[("rttr", p.v("rttr")), ("rtlg", p.v("rtlg"))], inc="rvev")
        plg = pb[4 + par]

        def lgmm(t, plg=plg):
            for ci in range(8):
                t.matmul(plg[:, 0:8], xT32[:, ci * 128:ci * 128 + 128],
                         wg_sb[:, ci * 8:ci * 8 + 8], start=(ci == 0), stop=False)
            return t.matmul(plg[:, 0:8], ones_rowf[:], bg_sb[:], start=False, stop=True)
        p.step("t", lgmm, waits=[("rvev", p.v("rvev")), ("rvl8", p.v("rvl8") - 1)], inc="rtlg")

        p.step("v", lambda v, plg=plg: v.tensor_copy(l8[:], plg[:, 0:8]),
               waits=[("rtlg", p.v("rtlg")), ("raeg", p.v("raeg") - 1)], inc="rvcp")
        p.step("v", lambda v: v.max(top8[:], l8[:]),
               waits=[("rvcp", p.v("rvcp"))], inc="rvmx")
        p.step("v", lambda v: v.tensor_sub(gat[:, 0:1], top8[:, 1:2], top8[:, 0:1]),
               waits=[("rvmx", p.v("rvmx"))], inc="rvl8")
        p.step("a", lambda a: a.activation(gat[:, 3:4], gat[:, 0:1], AF.Exp),
               waits=[("rvl8", p.v("rvl8"))], inc="raeg")

        vseq([lambda v: (v.tensor_sub(tsc1[:], l8[:], top8[:, 0:1].to_broadcast([128, 8])),
                         v.tensor_scalar(gat[:, 1:2], gat[:, 0:1], 0.0, None, op0=ALU.is_equal))[-1],
              lambda v: (v.tensor_scalar(tsc1[:], tsc1[:], 0.0, None, op0=ALU.is_equal),
                         v.tensor_scalar(gat[:, 2:3], gat[:, 1:2], 1.0, None, op0=ALU.add))[-1],
              lambda v: v.tensor_tensor_scan(tsc2[:], tsc1[:], zeros8[:], 0.0, op0=ALU.add, op1=ALU.add),
              lambda v: v.tensor_scalar(tsc2[:], tsc2[:], 1.0, None, op0=ALU.is_equal),
              lambda v, g=g: v.tensor_tensor(oh1_all[:, g * 8:g * 8 + 8], tsc1[:], tsc2[:], op=ALU.mult)],
             waits=[("raeg", p.v("raeg"))], inc="rvo1")

        vseq([lambda v: (v.tensor_sub(tsc1[:], l8[:], top8[:, 1:2].to_broadcast([128, 8])),
                         v.tensor_scalar(gat[:, 4:5], gat[:, 3:4], 1.0, None, op0=ALU.add))[-1],
              lambda v: (v.tensor_scalar(tsc1[:], tsc1[:], 0.0, None, op0=ALU.is_equal),
                         v.reciprocal(gat[:, 5:6], gat[:, 4:5]))[-1],
              lambda v: v.tensor_tensor_scan(tsc2[:], tsc1[:], zeros8[:], 0.0, op0=ALU.add, op1=ALU.add),
              lambda v: v.tensor_sub(tsc2[:], tsc2[:], gat[:, 2:3].to_broadcast([128, 8])),
              lambda v: v.tensor_scalar(tsc2[:], tsc2[:], 0.0, None, op0=ALU.is_equal),
              lambda v, g=g: v.tensor_tensor(oh2_all[:, g * 8:g * 8 + 8], tsc1[:], tsc2[:], op=ALU.mult)],
             waits=[("rvo1", p.v("rvo1"))], inc="rvo2")

        vseq([lambda v: v.tensor_tensor(gat[:, 6:7], gat[:, 3:4], gat[:, 5:6], op=ALU.mult),
              lambda v, g=g: (v.tensor_mul(c1_all[:, g * 8:g * 8 + 8], oh1_all[:, g * 8:g * 8 + 8],
                                           gat[:, 5:6].to_broadcast([128, 8])),
                              v.tensor_mul(c2_all[:, g * 8:g * 8 + 8], oh2_all[:, g * 8:g * 8 + 8],
                                           gat[:, 6:7].to_broadcast([128, 8])))[-1]],
             waits=[("rvo2", p.v("rvo2"))], inc="rvoh")
        pcnt = pb[6 + par]

        def cntmm(t, pcnt=pcnt, g=g):
            t.matmul(pcnt[0:1, 0:8], ones_col[:], oh1_all[:, g * 8:g * 8 + 8], start=True, stop=True)
            return t.matmul(pcnt[0:1, 8:16], ones_col[:], oh2_all[:, g * 8:g * 8 + 8], start=True, stop=True)
        p.step("t", cntmm, waits=[("rvoh", p.v("rvoh")), ("rvcv", p.v("rvcv") - 1)], inc="rtcnt")
        p.step("v", lambda v, pcnt=pcnt, g=g:
               v.tensor_copy(cnts_row[0:1, g * 16:g * 16 + 16], pcnt[0:1, 0:16]),
               waits=[("rtcnt", p.v("rtcnt"))], inc="rvcv")

    gdma(lambda g: g.dma_start(out=scr_d[0:1, 0:2048], in_=cnts_row[:]),
         waits=[("rvcv", p.v("rvcv"))])
    gdma(lambda g: g.dma_start(out=cnts_t[:],
                               in_=scr_d[0:1, 0:2048].rearrange("o (g j) -> (o g) j", j=16)),
         waits=[("gd", p.v("gd"))])

    def prefmm(t):
        t.matmul(pb[0][:, 0:16], triu_rot[:], cnts_t[:], start=True, stop=True)
        return t.matmul(pb[1][0:1, 0:16], ones_col[:], cnts_t[:], start=True, stop=True)
    p.step("t", prefmm, waits=[("gd", p.v("gd"))], inc="rtpre")

    def evpre(v):
        v.tensor_copy(pref_sb[:], pb[0][:, 0:16])
        return v.tensor_copy(tot_sb[:], pb[1][0:1, 0:16])
    p.step("v", evpre, waits=[("rtpre", p.v("rtpre"))], inc="rvpre")
    gdma(lambda g: g.dma_start(out=scr_d[0:1, 0:2048].rearrange("o (g j) -> (o g) j", j=16),
                               in_=pref_sb[:]),
         waits=[("rvpre", p.v("rvpre"))])
    gdma(lambda g: g.dma_start(out=pref_rows[0:1, 0:2048], in_=scr_d[0:1, 0:2048]),
         waits=[("gd", p.v("gd"))])
    gd_pref = p.v("gd")

    for ti in range(NTO):
        b1b = pb[2 + ti % 2]
        b2b = pb[4 + ti % 2]

        def posmm(t, ti=ti, b1b=b1b, b2b=b2b):
            t.matmul(b1b[:, 0:8], triu_tok[:], oh1_all[:, ti * 8:ti * 8 + 8], start=True, stop=False)
            t.matmul(b1b[:, 0:8], ones_rowf[:], pref_rows[0:1, ti * 16:ti * 16 + 8], start=False, stop=True)
            t.matmul(b2b[:, 0:8], triu_tok[:], oh2_all[:, ti * 8:ti * 8 + 8], start=True, stop=False)
            t.matmul(b2b[:, 0:8], ones_rowf[:], pref_rows[0:1, ti * 16 + 8:ti * 16 + 16], start=False, stop=False)
            return t.matmul(b2b[:, 0:8], ones_rowf[:], tot_sb[0:1, 0:8], start=False, stop=True)
        p.step("t", posmm, waits=[("gd", gd_pref), ("rvcc", p.v("rvcc") - 1)], inc="rtpos")

        vseq([lambda v, ti=ti, b1b=b1b, b2b=b2b:
              (v.tensor_scalar(ktmp[:, 0:8], b1b[:, 0:8], CAP, None, op0=ALU.is_lt),
               v.tensor_scalar(ktmp[:, 8:16], b2b[:, 0:8], CAP, None, op0=ALU.is_lt))[-1],
              lambda v, ti=ti: (v.tensor_tensor(ktmp[:, 16:24], c1_all[:, ti * 8:ti * 8 + 8],
                                                ktmp[:, 0:8], op=ALU.mult),
                                v.tensor_tensor(ktmp[:, 24:32], c2_all[:, ti * 8:ti * 8 + 8],
                                                ktmp[:, 8:16], op=ALU.mult))[-1],
              lambda v, ti=ti: v.tensor_tensor(c_coef[:, ti * 8:ti * 8 + 8],
                                               ktmp[:, 16:24], ktmp[:, 24:32], op=ALU.add)],
             waits=[("rtpos", p.v("rtpos"))], inc="rvcc")

    if debug:
        qd(lambda q: q.dma_start(out=dbg["d_l8"], in_=l8[:]),
           waits=[("rvcc", p.v("rvcc"))])
        qd(lambda q: q.dma_start(out=dbg["d_top8"], in_=top8[:]))
        qd(lambda q: q.dma_start(out=dbg["d_gat"], in_=gat[:]))
        qd(lambda q: q.dma_start(out=dbg["d_tsc1"], in_=tsc1[:]))
        qd(lambda q: q.dma_start(out=dbg["d_tsc2"], in_=tsc2[:]))
        qd(lambda q: q.dma_start(out=dbg["d_stat"], in_=stat[:].rearrange("p a b -> p (a b)")))
        qd(lambda q: q.dma_start(out=dbg["d_xT32"], in_=xT32[:]))
        qd(lambda q: q.dma_start(out=dbg["d_cnts"], in_=cnts_t[:]))
        qd(lambda q: q.dma_start(out=dbg["d_pref"], in_=pref_sb[:]))
        qd(lambda q: q.dma_start(out=dbg["d_oh"], in_=oh1_all[:, 0:32]))
        qd(lambda q: q.dma_start(out=dbg["d_c12"][:, 0:16], in_=c1_all[:, 0:16]))
        qd(lambda q: q.dma_start(out=dbg["d_c12"][:, 16:32], in_=c2_all[:, 0:16]))
        qd(lambda q: q.dma_start(out=dbg["d_ktmp"], in_=ktmp[:]))

    rvcc_done = p.v("rvcc")

    for bt in range(NTB):
        n = NTF + bt
        par = n % 2
        xt = xt2[:, par, :]
        qd(lambda q, bt=bt, par=par: q.dma_start(out=xt2[:, par, :], in_=xbatch[bt]),
           waits=[("lnaxn", p.v("lnaxn") - 1)])
        st = ln_chain(n, xt)
        vseq([lambda v, xt=xt, par=par, st=st:
              v.tensor_scalar(xnbf2[:, par, :], xt, st(2), st(6),
                              op0=ALU.subtract, op1=ALU.mult)],
             waits=[("xttr", p.v("xttr") - 1), ("lnarst", p.v("lnarst"))], inc="lnaxn")
        b0 = 0 if par == 0 else 2

        def transb(t, b0=b0, par=par):
            pbB = pb[b0][:].bitcast(BF16)
            for ci in range(8):
                inst = t.transpose(pbB[:, ci * 128:ci * 128 + 128],
                                   xnbf2[:, par, ci * 128:ci * 128 + 128], identb[:])
            return inst
        tw = [("lnaxn", p.v("lnaxn")), ("xvev", p.v("xvev") - 1)]
        if bt < 2:
            tw.append(("rvcc", rvcc_done))
        p.step("t", transb, waits=tw, inc="xttr")
        tgt = xnTq if bt < NTO else xnTk
        off = (bt % NTO) * 128

        def evtrb(v, b0=b0, tgt=tgt, off=off):
            return v.tensor_copy(tgt[:, :, off:off + 128],
                                 pb[b0][:].bitcast(BF16).rearrange("p (c k) -> p c k", k=128))
        p.step("v", evtrb, waits=[("xttr", p.v("xttr"))], inc="xvev")

    if debug:
        qd(lambda q: q.dma_start(out=dbg["d_xnTq"], in_=xnTq[:, 0, :]),
           waits=[("xvev", p.v("xvev"))])
        qd(lambda q: q.dma_start(out=dbg["d_ccoef"], in_=c_coef[:]))

    es_RX.close()
    p.barrier()

    es_A = contextlib.ExitStack()
    asb = lambda n, s, d=F32: es_A.enter_context(nc.sbuf_tensor(n, s, d, side="right"))
    cosq = asb("cosq_s", [128, 4096], BF16)
    sinq = asb("sinq_s", [128, 4096], BF16)
    wq_b = asb("wq_b_s", [128, 2, 8, 128], BF16)
    wk_b = asb("wk_b_s", [128, 2, 8, 128], BF16)
    wv_b = asb("wv_b_s", [128, 2, 8, 128], BF16)
    kq_sb = asb("kq_sb_s", [128, 2, 512], BF16)
    ttmp = asb("ttmp_s", [128, 2, 512], BF16)
    krot = asb("krot_s", [128, 4096], BF16)
    qrot = asb("qrot_s", [128, 2048], BF16)
    vt = asb("vt_s", [128, 32, 130], BF16)
    pt_sb = asb("pt_sb_s", [128, 2, 2, 1024], BF16)
    recip_sb = asb("recip_sb_s", [128, 8])

    qd(lambda q: q.dma_start(out=cosq[:], in_=cosq_d))
    qd(lambda q: q.dma_start(out=sinq[:], in_=sinq_d))
    p.step("v", lambda v: v.memset(vt[:], 1.0), inc="avones")
    qd_tbl = p.v("qd")

    def xnT_slice(ci, t0, nn):
        if t0 < 2048:
            return xnTq[:, ci, t0:t0 + nn]
        return xnTk[:, ci, t0 - 2048:t0 - 2048 + nn]

    for hp in range(8):
        par = hp % 2
        wdw = [("atdone", p.v("atdone") - 1)]
        for s, wt in ((0, wq_b), (1, wk_b), (2, wv_b)):
            gdma(lambda g, s=s, wt=wt, par=par, hp=hp:
                 g.dma_start(out=wt[:, par], in_=qkvw_st[s, hp]), waits=wdw)
        gd_w = p.v("gd")

        for kind in range(12):
            isq = kind >= 8
            it = kind - 8 if isq else kind
            pbm = pb[kind % 2]
            pbs = pb[2 + kind % 2]

            def kqmm(t, isq=isq, it=it, par=par, pbm=pbm):
                w = wq_b if isq else wk_b
                for ci in range(8):
                    rhs = xnTq[:, ci, it * 512:it * 512 + 512] if isq else xnT_slice(ci, it * 512, 512)
                    inst = t.matmul(pbm[:], w[:, par, ci, :], rhs,
                                    start=(ci == 0), stop=(ci == 7))
                return inst
            tw = [("gd", gd_w), ("aaev", p.v("aaev") - 1), ("avrp", p.v("avrp") - 1)]
            p.step("t", kqmm, waits=tw, inc="atkm")
            bcol = hp if isq else 8 + hp
            p.step("a", lambda a, kind=kind, pbm=pbm, bcol=bcol:
                   a.activation(kq_sb[:, kind % 2, :], pbm[:], AF.Identity,
                                bias=qkvbT_sb[:, bcol:bcol + 1]),
                   waits=[("atkm", p.v("atkm"))], inc="aaev")

            def swmm(t, kind=kind, pbs=pbs):
                return t.matmul(pbs[:], pswap[:], kq_sb[:, kind % 2, :], start=True, stop=True)
            p.step("t", swmm, waits=[("aaev", p.v("aaev")), ("avrp", p.v("avrp") - 1)], inc="atsw")
            dst = qrot if isq else krot

            vseq([lambda v, kind=kind, it=it, pbs=pbs:
                  (v.tensor_tensor(ttmp[:, 0, :], kq_sb[:, kind % 2, :],
                                   cosq[:, it * 512:it * 512 + 512], op=ALU.mult),
                   v.tensor_tensor(ttmp[:, 1, :], pbs[:],
                                   sinq[:, it * 512:it * 512 + 512], op=ALU.mult))[-1],
                  lambda v, kind=kind, it=it, dst=dst:
                  v.tensor_tensor(dst[:, it * 512:it * 512 + 512],
                                  ttmp[:, 0, :], ttmp[:, 1, :], op=ALU.add)],
                 waits=[("atsw", p.v("atsw"))], inc="avrp")

        for tt in range(32):
            pbv = pb[4 + tt % 2]

            def vmm(t, tt=tt, par=par, pbv=pbv, hp=hp):
                for ci in range(8):
                    t.matmul(pbv[:, 0:128], xnT_slice(ci, tt * 128, 128), wv_b[:, par, ci, :],
                             start=(ci == 0), stop=False)
                return t.matmul(pbv[:, 0:128], ones_rowb[:],
                                qkvb_row[0:1, 2 * D + hp * 128:2 * D + hp * 128 + 128],
                                start=False, stop=True)
            p.step("t", vmm, waits=[("gd", gd_w), ("avve", p.v("avve") - 1)], inc="atvm")

            def vev(v, tt=tt, pbv=pbv):
                v.tensor_copy(vt[:, tt, 0:64], pbv[:, 0:64])
                return v.tensor_copy(vt[:, tt, 65:129], pbv[:, 64:128])
            p.step("v", vev, waits=[("atvm", p.v("atvm")), ("avones", p.v("avones"))], inc="avve")

        avrp_done = p.v("avrp")
        avve_done = p.v("avve")
        for qt in range(4):
            fs0 = p.v("afx")
            fa0 = p.v("afa")
            for kc in range(32):
                sb0 = pb[kc % 2]
                sb1 = pb[2 + kc % 2]

                def smm(t, kc=kc, qt=qt, sb0=sb0, sb1=sb1):
                    t.matmul(sb0[:], krot[0:64, kc * 128:kc * 128 + 128],
                             qrot[0:64, qt * 512:qt * 512 + 512], start=True, stop=True,
                             skip_group_check=True)
                    return t.matmul(sb1[:], krot[64:128, kc * 128:kc * 128 + 128],
                                    qrot[64:128, qt * 512:qt * 512 + 512], start=True, stop=True,
                                    skip_group_check=True)
                tw = [("avrp", avrp_done), ("avve", avve_done),
                      ("afx", fs0 + 2 * ((kc - 2) // 2 + 1) if kc >= 2 else fs0)]
                if kc == 0 and qt > 0:
                    tw.append(("avno", p.v("avno")))
                p.step("t", smm, waits=tw, inc="afs", amt=2)

                if kc % 2 == 1:
                    pj = kc // 2
                    def expev(a, pj=pj):
                        a.activation(pt_sb[:, 0, pj % 2, :], pbig[:, 0:1024], AF.Exp,
                                     scale=cbank[:, CB_SCALE:CB_SCALE + 1],
                                     bias=cbank[:, CB_EBIAS:CB_EBIAS + 1])
                        return a.activation(pt_sb[:, 1, pj % 2, :], pbig[:, 1024:2048], AF.Exp,
                                            scale=cbank[:, CB_SCALE:CB_SCALE + 1],
                                            bias=cbank[:, CB_EBIAS:CB_EBIAS + 1])
                    p.step("a", expev,
                           waits=[("afs", p.v("afs")), ("afa", p.v("afa") - 1)], inc="afx", amt=2)

                    def avmm(t, kc=kc, pj=pj):
                        for k2 in (kc - 1, kc):
                            for h in range(2):
                                pba = pb[4 + h]
                                for qs in range(4):
                                    inst = t.matmul(pba[:, qs * 65:qs * 65 + 65],
                                                    pt_sb[:, h, pj % 2,
                                                          (k2 % 2) * 512 + qs * 128:(k2 % 2) * 512 + qs * 128 + 128],
                                                    vt[:, k2, h * 65:h * 65 + 65],
                                                    start=(k2 == 0), stop=(k2 == 31),
                                                    skip_group_check=True)
                        return inst
                    p.step("t", avmm, waits=[("afx", p.v("afx"))], inc="afa")

            def norm1(v):
                for h in range(2):
                    pba = pb[4 + h]
                    for qs in range(4):
                        inst = v.reciprocal(recip_sb[:, h * 4 + qs:h * 4 + qs + 1],
                                            pba[:, qs * 65 + 64:qs * 65 + 65])
                return inst
            p.step("v", norm1, waits=[("afa", p.v("afa"))], inc="avn1")

            def norm2(v, qt=qt, hp=hp):
                for h in range(2):
                    pba = pb[4 + h]
                    for qs in range(4):
                        tt = qt * 4 + qs
                        inst = v.tensor_mul(
                            attn_tok[:, tt, (2 * hp + h) * 64:(2 * hp + h) * 64 + 64],
                            pba[:, qs * 65:qs * 65 + 64],
                            recip_sb[:, h * 4 + qs:h * 4 + qs + 1].to_broadcast([128, 64]))
                return inst
            p.step("v", norm2, waits=[("avn1", p.v("avn1"))], inc="avno")
        p.step("t", lambda t: t.nop(), waits=[("avno", p.v("avno"))], inc="atdone")

    if debug:
        qd(lambda q: q.dma_start(out=dbg["d_krot"], in_=krot[:]),
           waits=[("atdone", p.v("atdone"))])
        qd(lambda q: q.dma_start(out=dbg["d_attn"],
                                 in_=attn_tok[:].rearrange("p a b -> p (a b)")),
           waits=[("avno", p.v("avno"))])

    es_A.close()
    es_XK.close()
    p.barrier()

    es_ACC = contextlib.ExitStack()
    acc = es_ACC.enter_context(nc.sbuf_tensor("acc_s", [128, NTO, D], F32, side="right"))
    es_OP = contextlib.ExitStack()
    ow_b = es_OP.enter_context(nc.sbuf_tensor("ow_b_s", [128, 8, D], BF16, side="right"))
    attnF_t = es_OP.enter_context(nc.sbuf_tensor("attnF_t_s", [128, 8, 128], BF16, side="right"))

    gdma(lambda g: g.dma_start(out=ow_b[:], in_=outw_st))
    gd_ow = p.v("gd")
    qd(lambda q: q.dma_start(out=acc[:], in_=xown))
    qd_acc = p.v("qd")

    for tt in range(NTO):
        def optr(t, tt=tt):
            pbB = pb[0][:].bitcast(BF16)
            for ci in range(8):
                inst = t.transpose(pbB[:, ci * 128:ci * 128 + 128],
                                   attn_tok[:, tt, ci * 128:ci * 128 + 128], identb[:])
            return inst
        p.step("t", optr, waits=[("opve", p.v("opve"))], inc="optr")

        def opev(v):
            return v.tensor_copy(attnF_t[:, :, :],
                                 pb[0][:].bitcast(BF16).rearrange("p (c k) -> p c k", k=128))
        p.step("v", opev, waits=[("optr", p.v("optr"))], inc="opve")

        def opmm(t, tt=tt):
            for nh in range(2):
                pbo = pb[2 + nh]
                t.matmul(pbo[:], ones_rowb[:], outb_row[0:1, nh * 512:nh * 512 + 512],
                         start=True, stop=False, skip_group_check=True)
                for ci in range(8):
                    inst = t.matmul(pbo[:], attnF_t[:, ci, :], ow_b[:, ci, nh * 512:nh * 512 + 512],
                                    start=False, stop=(ci == 7), skip_group_check=True)
            return inst
        p.step("t", opmm, waits=[("opve", p.v("opve")), ("gd", gd_ow),
                                 ("opac", p.v("opac"))], inc="opmm")

        def opadd(v, tt=tt):
            v.tensor_tensor(acc[:, tt, 0:512], acc[:, tt, 0:512], pb[2][:], op=ALU.add)
            return v.tensor_tensor(acc[:, tt, 512:1024], acc[:, tt, 512:1024], pb[3][:], op=ALU.add)
        p.step("v", opadd, waits=[("opmm", p.v("opmm")), ("qd", qd_acc)], inc="opac")

    es_OP.close()
    p.barrier()

    es_F = contextlib.ExitStack()
    w1e = es_F.enter_context(nc.sbuf_tensor("w1e_s", [128, 2, 8, 512], BF16, side="right"))
    w2e = es_F.enter_context(nc.sbuf_tensor("w2e_s", [128, 2, 4, 1024], BF16, side="right"))
    h_sb = es_F.enter_context(nc.sbuf_tensor("h_sb_s", [128, 2, 256], BF16, side="right"))

    fmarks = {}
    gd_f = {}
    for e in range(NE):
        for fe in range(8):
            qi = e * 8 + fe
            wp = qi % 2
            fq, half = fe // 2, fe % 2
            wdw = [("fm2", fmarks[qi - 2])] if qi >= 2 else []
            gdma(lambda g, e=e, fq=fq, half=half, wp=wp:
                 g.dma_start(out=w1e[:, wp], in_=w1_st[e, fq][:, :, half * 512:half * 512 + 512]),
                 waits=wdw)
            gdma(lambda g, e=e, fq=fq, half=half, wp=wp:
                 g.dma_start(out=w2e[:, wp], in_=w2_st[e, fq][:, half * 4:half * 4 + 4, :]))
            gd_f[qi] = p.v("gd")
            for tq in range(8):
                def mk_mm1(fc, tq=tq, wp=wp):
                    def mm1(t, fc=fc, tq=tq, wp=wp):
                        pbh = pb[fc % 2]
                        for ci in range(8):
                            inst = t.matmul(pbh[:, 0:256], w1e[:, wp, ci, fc * 128:fc * 128 + 128],
                                            xnTq[:, ci, tq * 256:tq * 256 + 256],
                                            start=(ci == 0), stop=(ci == 7),
                                            skip_group_check=True)
                        return inst
                    return mm1

                def mk_mm2(fc, e=e, fe=fe, wp=wp):
                    def mm2(t, fc=fc, e=e, fe=fe, wp=wp):
                        inst = None
                        for stt in range(2):
                            for nh in range(2):
                                pby = pb[2 + stt * 2 + nh]
                                if fe == 0 and fc == 0:
                                    t.matmul(pby[:], ones_rowb[:],
                                             b2_row[0:1, e * D + nh * 512:e * D + nh * 512 + 512],
                                             start=True, stop=False, skip_group_check=True)
                                inst = t.matmul(pby[:], h_sb[:, fc % 2, stt * 128:stt * 128 + 128],
                                                w2e[:, wp, fc, nh * 512:nh * 512 + 512],
                                                start=(fe > 0 and fc == 0), stop=(fc == 3),
                                                skip_group_check=True)
                        return inst
                    return mm2

                def mk_gelu(fc, e=e, fe=fe):
                    def gelu(a, fc=fc, e=e, fe=fe):
                        j = e * 32 + fe * 4 + fc
                        return a.activation(h_sb[:, fc % 2, :], pb[fc % 2][:, 0:256],
                                            AF.Gelu_apprx_tanh, bias=b1T[:, j:j + 1])
                    return gelu

                for fc in range(4):
                    tw = [("gd", gd_f[qi])]
                    if fc >= 2:
                        tw.append(("fge", p.v("fge") - 1))
                    p.step("t", mk_mm1(fc), waits=tw, inc="fmm")
                    p.step("a", mk_gelu(fc), waits=[("fmm", p.v("fmm")), ("fm2", p.v("fm2") - 1)],
                           inc="fge")
                    if fc >= 1:
                        w2w = [("fge", p.v("fge") - 1)]
                        if fc == 1 and qi + tq > 0:
                            w2w.append(("fev", p.v("fev")))
                        p.step("t", mk_mm2(fc - 1), waits=w2w, inc="fm2")
                p.step("t", mk_mm2(3), waits=[("fge", p.v("fge"))], inc="fm2")

                def fevac(v, e=e, tq=tq):
                    inst = None
                    for stx in range(2):
                        tt = tq * 2 + stx
                        cav = c_coef[:, tt * 8 + e:tt * 8 + e + 1]
                        for nh in range(2):
                            pby = pb[2 + stx * 2 + nh]
                            inst = v.scalar_tensor_tensor(
                                acc[:, tt, nh * 512:nh * 512 + 512], pby[:], cav,
                                acc[:, tt, nh * 512:nh * 512 + 512],
                                op0=ALU.mult, op1=ALU.add)
                    return inst
                p.step("v", fevac, waits=[("fm2", p.v("fm2"))], inc="fev")
            fmarks[qi] = p.v("fm2")

    es_F.close()
    p.barrier()

    qd(lambda q: q.dma_start(out=outp, in_=acc[:]))
    p.step("q", lambda q: q.nop(), waits=[("qd", p.v("qd"))])

    with nc.Block() as blk:
        p.emit(blk)
    es_ACC.close()
    es_glob.close()
    return nc


def _stage(core, x, qkv_w, qkv_b, out_w, out_b, Wg, bg, W1, b1, W2, b2):
    b = core // 2
    h = core % 2
    xf = np.ascontiguousarray(x.reshape(128, 128, D))
    rot = np.roll(np.arange(NTF), -16 * core)
    xrot = np.ascontiguousarray(xf[rot])
    xb = x.reshape(4, 4096, D)[b]
    own = xb[h * 2048:(h + 1) * 2048]
    oth = xb[(1 - h) * 2048:(2 - h) * 2048]
    xbatch = np.ascontiguousarray(
        np.concatenate([own, oth], axis=0).reshape(NTB, 128, D))
    xown = np.ascontiguousarray(own.reshape(NTO, 128, D).transpose(1, 0, 2))

    perm = np.arange(3 * D)
    for hh in range(NH):
        base = hh * HD
        for sec in (0, D):
            perm[sec + base:sec + base + 32] = sec + base + 2 * np.arange(32)
            perm[sec + base + 32:sec + base + 64] = sec + base + 2 * np.arange(32) + 1
    qkvw_p = qkv_w[:, perm]
    qkvb_p = qkv_b[perm]
    qkvw_st = np.ascontiguousarray(
        qkvw_p.reshape(8, 128, 3, 8, 128).transpose(2, 3, 1, 0, 4))
    qkvbT = np.ascontiguousarray(qkvb_p.reshape(24, 128).T)
    qkvbRow = np.ascontiguousarray(qkvb_p.reshape(1, 3 * D))
    outw_st = np.ascontiguousarray(out_w.reshape(8, 128, D).transpose(1, 0, 2))
    outbRow = np.ascontiguousarray(out_b.reshape(1, D))
    wg_stv = np.ascontiguousarray(Wg.reshape(8, 128, 8).transpose(1, 0, 2).reshape(128, 64))
    bgRow = np.ascontiguousarray(bg.reshape(1, 8))
    w1_stv = np.ascontiguousarray(
        W1.reshape(NE, 8, 128, 4, 1024).transpose(0, 3, 2, 1, 4))
    w2_stv = np.ascontiguousarray(
        W2.reshape(NE, 4, 8, 128, D).transpose(0, 1, 3, 2, 4))
    b1T = np.ascontiguousarray(b1.reshape(NE, 32, 128).transpose(2, 0, 1).reshape(128, 256))
    b2Row = np.ascontiguousarray(b2.reshape(1, NE * D))

    identf = np.eye(128, dtype=np.float32)
    identb = np.eye(128, dtype=np.float32).astype(ml_dtypes.bfloat16)
    swap = np.arange(128)
    swap = np.where((swap % 64) < 32, swap + 32, swap - 32)
    psw = np.zeros((128, 128), np.float32)
    psw[swap, np.arange(128)] = 1.0
    pswap = psw.astype(ml_dtypes.bfloat16)
    triu_tok = np.triu(np.ones((128, 128), np.float32), 1)
    canon = rot
    triu_rot = (canon[:, None] < canon[None, :]).astype(np.float32)
    freqs = (10000.0 ** (np.arange(0, HD, 2.0, dtype=np.float32) / HD)).astype(np.float32)
    tpos = np.arange(4096, dtype=np.float32)
    stagedpos = np.concatenate([tpos[h * 2048:(h + 1) * 2048],
                                tpos[(1 - h) * 2048:(2 - h) * 2048]])
    ang = (stagedpos[:, None] * freqs[None, :]).astype(np.float32)
    cosv = np.cos(ang.astype(np.float64)).astype(np.float32)
    sinv = np.sin(ang.astype(np.float64)).astype(np.float32)
    prow = np.arange(128)
    cosq = np.ascontiguousarray(cosv[:, prow % 32].T).astype(ml_dtypes.bfloat16)
    sgn = np.where((prow % 64) < 32, -1.0, 1.0).astype(np.float32)
    sinq = np.ascontiguousarray((sinv[:, prow % 32] * sgn[None, :]).T).astype(ml_dtypes.bfloat16)

    return {
        "xrot": xrot, "xbatch": xbatch, "xown": xown,
        "qkvw_st": qkvw_st, "outw_st": outw_st, "qkvbT": qkvbT,
        "qkvbRow": qkvbRow, "outbRow": outbRow, "wg_st": wg_stv, "bgRow": bgRow,
        "w1_st": w1_stv, "w2_st": w2_stv, "b1T": b1T, "b2Row": b2Row,
        "identf": identf, "identb": identb, "pswap": pswap,
        "triu_tok": triu_tok, "triu_rot": triu_rot, "cosq": cosq, "sinq": sinq,
    }


def kernel(L0_attn_x, L0_attn_qkv_w, L0_attn_qkv_b, L0_attn_out_w, L0_attn_out_b,
           L0_ffn_Wg, L0_ffn_bg, L0_ffn_W1, L0_ffn_b1, L0_ffn_W2, L0_ffn_b2):
    x = np.ascontiguousarray(np.asarray(L0_attn_x, np.float32).reshape(16384, D))
    args = [np.asarray(a, np.float32) for a in
            (L0_attn_qkv_w, L0_attn_qkv_b, L0_attn_out_w, L0_attn_out_b,
             L0_ffn_Wg, L0_ffn_bg, L0_ffn_W1, L0_ffn_b1, L0_ffn_W2, L0_ffn_b2)]
    nc = build()
    in_maps = [_stage(c, x, *args) for c in range(NCORES)]
    res = run_bass_kernel_spmd(nc, in_maps, core_ids=list(range(NCORES)))
    shards = []
    for c in range(NCORES):
        o = np.asarray(res.results[c]["outp"])
        shards.append(o.transpose(1, 0, 2).reshape(2048, D))
    return np.concatenate(shards, axis=0).reshape(4, 4096, D).astype(np.float32)


# version 04
# speedup vs baseline: 1.3773x; 1.0128x over previous
import contextlib
import numpy as np
import ml_dtypes
import concourse.bass as bass
import concourse.mybir as mybir
from concourse.bass_utils import run_bass_kernel_spmd

F32 = mybir.dt.float32
BF16 = mybir.dt.bfloat16
AF = mybir.ActivationFunctionType
ALU = mybir.AluOpType
AX = mybir.AxisListType

D = 1024
NH = 16
HD = 64
FF = 4096
NE = 8
CAP = 4916.0
NCORES = 8
NTF = 128
NTO = 16
NTB = 32
EPS = 1e-5

CB_EPS, CB_NHALF, CB_SCALE, CB_EBIAS, CB_INVD, CB_NEG1 = 0, 1, 2, 3, 4, 5


class Plan:
    ORDER = (("q", "sync"), ("g", "gpsimd"), ("t", "tensor"), ("v", "vector"), ("a", "scalar"))

    def __init__(self, nc, es):
        self.nc = nc
        self.es = es
        self.steps = {e: [] for e, _ in self.ORDER}
        self.sems = {}
        self.cnt = {}

    def sem(self, name):
        if name not in self.sems:
            self.sems[name] = self.es.enter_context(self.nc.semaphore(name))
            self.cnt[name] = 0
        return self.sems[name]

    def step(self, eng, fn, waits=(), inc=None, amt=1):
        waits = tuple((s, v) for s, v in waits if v > 0)
        for s, _ in waits:
            self.sem(s)
        if inc is not None:
            self.sem(inc)
            self.cnt[inc] += amt
        self.steps[eng].append((fn, waits, inc, amt))
        return self.cnt[inc] if inc is not None else None

    def v(self, name):
        self.sem(name)
        return self.cnt[name]

    def barrier(self):
        for e in ("q", "g", "t", "v", "a"):
            w = []
            if e == "q":
                w.append(("qd", self.v("qd")))
            if e == "g":
                w.append(("gd", self.v("gd")))
            self.step(e, lambda en: en.nop(), waits=w, inc="bar")
        tot = self.v("bar")
        for e in ("q", "g", "t", "v", "a"):
            self.step(e, lambda en: en.nop(), waits=[("bar", tot)])

    def emit(self, blk):
        for engname, attr in self.ORDER:
            steps = self.steps[engname]

            def body(e, steps=steps):
                for fn, waits, inc, amt in steps:
                    for s, val in waits:
                        e.wait_ge(self.sems[s], val)
                    inst = fn(e)
                    if inc is not None:
                        assert inst is not None
                        inst.then_inc(self.sems[inc], amt)
            getattr(blk, attr)(body)


def build(debug=False):
    nc = bass.Bass("TRN2", target_bir_lowering=False)

    dt_in = lambda n, s, d=F32: nc.dram_tensor(n, s, d, kind="ExternalInput").ap()
    xrot = dt_in("xrot", [NTF, 128, D])
    xbatch = dt_in("xbatch", [NTB, 128, D])
    xown = dt_in("xown", [128, NTO, D])
    qkvw_st = dt_in("qkvw_st", [3, 8, 128, 8, 128])
    outw_st = dt_in("outw_st", [128, 8, D])
    qkvbT = dt_in("qkvbT", [128, 24])
    qkvbRow = dt_in("qkvbRow", [1, 3 * D])
    outbRow = dt_in("outbRow", [1, D])
    wg_st = dt_in("wg_st", [128, 64])
    bgRow = dt_in("bgRow", [1, 8])
    w1_st = dt_in("w1_st", [NE, 4, 128, 8, 1024])
    w2_st = dt_in("w2_st", [NE, 4, 128, 8, 1024])
    b1T_d = dt_in("b1T", [128, 256])
    b2Row = dt_in("b2Row", [1, NE * D])
    identf_d = dt_in("identf", [128, 128])
    identb_d = dt_in("identb", [128, 128], BF16)
    pswap_d = dt_in("pswap", [128, 128], BF16)
    triu_tok_d = dt_in("triu_tok", [128, 128], BF16)
    triu_rot_d = dt_in("triu_rot", [128, 128])
    cosq_d = dt_in("cosq", [128, 4096], BF16)
    sinq_d = dt_in("sinq", [128, 4096], BF16)
    outp = nc.dram_tensor("outp", [128, NTO, D], F32, kind="ExternalOutput").ap()
    scr_d = nc.dram_tensor("scr", [1, 4096], F32, kind="Internal").ap()
    dbg = {}
    if debug:
        for n, s, d in [("d_ccoef", [128, 128], F32), ("d_cnts", [128, 16], F32),
                        ("d_pref", [128, 16], F32), ("d_xnTq", [128, 2048], BF16),
                        ("d_attn", [128, NTO * D], BF16),
                        ("d_krot", [128, 4096], BF16), ("d_oh", [128, 32], F32)]:
            dbg[n] = nc.dram_tensor(n, s, d, kind="ExternalOutput").ap()

    es_glob = contextlib.ExitStack()
    p = Plan(nc, es_glob)
    sb = lambda n, s, d=F32: es_glob.enter_context(nc.sbuf_tensor(n, s, d))

    identf = sb("identf_s", [128, 128])
    identb = sb("identb_s", [128, 128], BF16)
    pswap = sb("pswap_s", [128, 128], BF16)
    triu_tok = sb("triu_tok_s", [128, 128], BF16)
    triu_rot = sb("triu_rot_s", [128, 128])
    wg_sb = sb("wg_sb_s", [128, 64])
    bg_sb = sb("bg_sb_s", [1, 8])
    qkvbT_sb = sb("qkvbT_sb_s", [128, 24])
    b1T = sb("b1T_sb_s", [128, 256])
    cbank = sb("cbank_s", [128, 8])
    ones_col = sb("ones_col_s", [128, 1])
    ones_colb = sb("ones_colb_s", [128, 1], BF16)
    ones_rowf = sb("ones_rowf_s", [1, 128])
    ones_rowb = sb("ones_rowb_s", [1, 128], BF16)
    zeros8 = sb("zeros8_s", [128, 8])
    c_coef = sb("c_coef_s", [128, NTO * 8])
    xnTq = sb("xnTq_s", [128, 8, 2048], BF16)
    attn_tok = sb("attn_tok_s", [128, NTO, D], BF16)

    pbig = es_glob.enter_context(nc.psum_tensor("pbig", [128, 2048], F32))
    pb = [pbig[:, i * 512:(i + 1) * 512] for i in range(4)]
    pb += [es_glob.enter_context(nc.psum_tensor(f"pb{i}", [128, 512], F32))[:] for i in range(4, 8)]

    def qd(fn, waits=()):
        return p.step("q", fn, waits=waits, inc="qd", amt=16)

    def gdma(fn, waits=()):
        return p.step("g", fn, waits=waits, inc="gd", amt=16)

    def vseq(fns, waits=(), inc=None):
        for i, fn in enumerate(fns):
            w = list(waits) + [("vch", p.v("vch"))] if i == 0 else [("vch", p.v("vch"))]
            p.step("v", fn, waits=w, inc=(inc if (i == len(fns) - 1 and inc) else "vch"))

    qd(lambda q: q.dma_start(out=identf[:], in_=identf_d))
    qd(lambda q: q.dma_start(out=triu_tok[:], in_=triu_tok_d))
    qd(lambda q: q.dma_start(out=triu_rot[:], in_=triu_rot_d))
    qd(lambda q: q.dma_start(out=wg_sb[:], in_=wg_st))
    qd(lambda q: q.dma_start(out=bg_sb[:], in_=bgRow))
    qd(lambda q: q.dma_start(out=qkvbT_sb[:], in_=qkvbT))
    qd(lambda q: q.dma_start(out=b1T[:], in_=b1T_d))
    qd(lambda q: q.dma_start(out=identb[:], in_=identb_d))
    qd(lambda q: q.dma_start(out=pswap[:], in_=pswap_d))

    def memsets(g):
        g.memset(cbank[:, CB_EPS:CB_EPS + 1], EPS)
        g.memset(cbank[:, CB_NHALF:CB_NHALF + 1], -0.5)
        g.memset(cbank[:, CB_SCALE:CB_SCALE + 1], 0.125)
        g.memset(cbank[:, CB_EBIAS:CB_EBIAS + 1], -12.0)
        g.memset(cbank[:, CB_INVD:CB_INVD + 1], 1.0 / D)
        g.memset(cbank[:, CB_NEG1:CB_NEG1 + 1], -1.0)
        g.memset(ones_col[:], 1.0)
        g.memset(ones_colb[:], 1.0)
        g.memset(ones_rowf[:], 1.0)
        g.memset(ones_rowb[:], 1.0)
        return g.memset(zeros8[:], 0.0)
    p.step("g", memsets, inc="gcst")
    p.barrier()

    es_XK = contextlib.ExitStack()
    xnTk = es_XK.enter_context(nc.sbuf_tensor("xnTk_s", [128, 8, 2048], BF16, side="right"))
    es_RX = contextlib.ExitStack()
    rsb = lambda n, s, d=F32: es_RX.enter_context(nc.sbuf_tensor(n, s, d, side="right"))
    xt2 = rsb("xt2_s", [128, 2, D])
    stat = rsb("stat_s", [128, 2, 12])
    xn32 = rsb("xn32_s", [128, 2, D])
    xnbf2 = rsb("xnbf2_s", [128, 2, D], BF16)
    xT32 = rsb("xT32_s", [128, D])
    l8 = rsb("l8_s", [128, 8])
    top8 = rsb("top8_s", [128, 8])
    tsc1 = rsb("tsc1_s", [128, 8])
    tsc2 = rsb("tsc2_s", [128, 8])
    gat = rsb("gat_s", [128, 8])
    oh1_all = rsb("oh1_all_s", [128, NTF * 8], BF16)
    oh2_all = rsb("oh2_all_s", [128, NTF * 8], BF16)
    c1_all = rsb("c1_all_s", [128, NTF * 8], BF16)
    c2_all = rsb("c2_all_s", [128, NTF * 8], BF16)
    cnts_row = rsb("cnts_row_s", [1, NTF * 16])
    pref_rows = cnts_row
    cnts_t = rsb("cnts_t_s", [128, 16])
    pref_sb = rsb("pref_sb_s", [128, 16])
    tot_sb = rsb("tot_sb_s", [1, 16])
    ktmp = rsb("ktmp_s", [128, 32])

    def ln_chain(n, src_tile, dump, aw=()):
        par = n % 2
        st = lambda c: stat[:, par, c:c + 1]
        wv = [("qd", p.v("qd")), ("lnarst", p.v("lnarst") - 1)]
        vseq([lambda v: v.reduce_sum(st(0), src_tile, axis=AX.X),
              lambda v: v.tensor_scalar(st(2), st(0), 1.0 / D, None, op0=ALU.mult),
              lambda v: v.scalar_tensor_tensor(st(3), st(2), -1.0, st(2),
                                               op0=ALU.mult, op1=ALU.mult)],
             waits=wv, inc="lnvst")
        p.step("a", lambda a: a.activation(dump, src_tile, AF.Square, accum_out=st(1)),
               waits=[("qd", p.v("qd"))] + list(aw), inc="lnassq")
        vseq([lambda v: v.scalar_tensor_tensor(st(4), st(1), 1.0 / D, st(3),
                                               op0=ALU.mult, op1=ALU.add)],
             waits=[("lnassq", p.v("lnassq"))], inc="lnvvar")

        def archain(a):
            a.activation(st(5), st(4), AF.Ln, bias=cbank[:, CB_EPS:CB_EPS + 1])
            return a.activation(st(6), st(5), AF.Exp, scale=cbank[:, CB_NHALF:CB_NHALF + 1])
        p.step("a", archain, waits=[("lnvvar", p.v("lnvvar"))], inc="lnarst")
        return st

    for bt in range(NTB):
        par = bt % 2
        xt = xt2[:, par, :]
        qd(lambda q, bt=bt, par=par: q.dma_start(out=xt2[:, par, :], in_=xbatch[bt]),
           waits=[("lnaxn", p.v("lnaxn") - 1)])
        st = ln_chain(bt, xt, xn32[:, par, :])
        vseq([lambda v, xt=xt, par=par, st=st:
              v.tensor_scalar(xnbf2[:, par, :], xt, st(2), st(6),
                              op0=ALU.subtract, op1=ALU.mult)],
             waits=[("xttr", p.v("xttr") - 1), ("lnarst", p.v("lnarst"))], inc="lnaxn")
        b0 = 0 if par == 0 else 2

        def transb(t, b0=b0, par=par):
            pbB = pb[b0].bitcast(BF16)
            for ci in range(8):
                inst = t.transpose(pbB[:, ci * 128:ci * 128 + 128],
                                   xnbf2[:, par, ci * 128:ci * 128 + 128], identb[:])
            return inst
        p.step("t", transb, waits=[("lnaxn", p.v("lnaxn")), ("xvev", p.v("xvev") - 1)],
               inc="xttr")
        tgt = xnTq if bt < NTO else xnTk
        off = (bt % NTO) * 128

        def evtrb(v, b0=b0, tgt=tgt, off=off):
            return v.tensor_copy(tgt[:, :, off:off + 128],
                                 pb[b0].bitcast(BF16).rearrange("p (c k) -> p c k", k=128))
        p.step("v", evtrb, waits=[("xttr", p.v("xttr"))], inc="xvev")

    if debug:
        qd(lambda q: q.dma_start(out=dbg["d_xnTq"], in_=xnTq[:, 0, :]),
           waits=[("xvev", p.v("xvev"))])

    p.barrier()

    def emit_rtile(g):
        par = g % 2
        xt = xt2[:, par, :]
        qd(lambda q, g=g, par=par: q.dma_start(out=xt2[:, par, :], in_=xrot[g]),
           waits=[("lnaxn", p.v("lnaxn") - 1)])
        aw = [("xttr", p.v("xttr"))] if g < 2 else ()
        st = ln_chain(NTB + g, xt, xnbf2[:, par, :], aw)
        vseq([lambda v, xt=xt, par=par, st=st:
              v.tensor_scalar(xn32[:, par, :], xt, st(2), st(6),
                              op0=ALU.subtract, op1=ALU.mult)],
             waits=[("rttr", p.v("rttr") - 2), ("lnarst", p.v("lnarst"))], inc="lnaxn")

        for hfi in range(2):
            def trh(t, hfi=hfi, par=par):
                for ci in range(4):
                    cc = hfi * 4 + ci
                    inst = t.transpose(pb[6][:, ci * 128:ci * 128 + 128],
                                       xn32[:, par, cc * 128:cc * 128 + 128], identf[:])
                return inst
            tw = [("rvev", p.v("rvev"))]
            if hfi == 0:
                tw.append(("lnaxn", p.v("lnaxn")))
            p.step("t", trh, waits=tw, inc="rttr")
            p.step("v", lambda v, hfi=hfi: v.tensor_copy(xT32[:, hfi * 512:hfi * 512 + 512], pb[6]),
                   waits=[("rttr", p.v("rttr"))], inc="rvev")

        def lgmm(t):
            for ci in range(8):
                t.matmul(pb[7][:, 0:8], xT32[:, ci * 128:ci * 128 + 128],
                         wg_sb[:, ci * 8:ci * 8 + 8], start=(ci == 0), stop=False)
            return t.matmul(pb[7][:, 0:8], ones_rowf[:], bg_sb[:], start=False, stop=True)
        p.step("t", lgmm, waits=[("rvev", p.v("rvev")), ("rvcv", p.v("rvcv"))], inc="rtlg")

        p.step("v", lambda v: v.tensor_copy(l8[:], pb[7][:, 0:8]),
               waits=[("rtlg", p.v("rtlg"))], inc="rvcp")
        p.step("v", lambda v: v.max(top8[:], l8[:]),
               waits=[("rvcp", p.v("rvcp"))], inc="rvmx")
        p.step("v", lambda v: v.tensor_sub(gat[:, 0:1], top8[:, 1:2], top8[:, 0:1]),
               waits=[("rvmx", p.v("rvmx"))], inc="rvl8")
        p.step("a", lambda a: a.activation(gat[:, 3:4], gat[:, 0:1], AF.Exp),
               waits=[("rvl8", p.v("rvl8"))], inc="raeg")

        vseq([lambda v: (v.tensor_sub(tsc1[:], l8[:], top8[:, 0:1].to_broadcast([128, 8])),
                         v.tensor_scalar(gat[:, 1:2], gat[:, 0:1], 0.0, None, op0=ALU.is_equal))[-1],
              lambda v: (v.tensor_scalar(tsc1[:], tsc1[:], 0.0, None, op0=ALU.is_equal),
                         v.tensor_scalar(gat[:, 2:3], gat[:, 1:2], 1.0, None, op0=ALU.add))[-1],
              lambda v: v.tensor_tensor_scan(tsc2[:], tsc1[:], zeros8[:], 0.0, op0=ALU.add, op1=ALU.add),
              lambda v: v.tensor_scalar(tsc2[:], tsc2[:], 1.0, None, op0=ALU.is_equal),
              lambda v, g=g: v.tensor_tensor(oh1_all[:, g * 8:g * 8 + 8], tsc1[:], tsc2[:], op=ALU.mult)],
             waits=[("raeg", p.v("raeg"))], inc="rvo1")

        vseq([lambda v: (v.tensor_sub(tsc1[:], l8[:], top8[:, 1:2].to_broadcast([128, 8])),
                         v.tensor_scalar(gat[:, 4:5], gat[:, 3:4], 1.0, None, op0=ALU.add))[-1],
              lambda v: (v.tensor_scalar(tsc1[:], tsc1[:], 0.0, None, op0=ALU.is_equal),
                         v.reciprocal(gat[:, 5:6], gat[:, 4:5]))[-1],
              lambda v: v.tensor_tensor_scan(tsc2[:], tsc1[:], zeros8[:], 0.0, op0=ALU.add, op1=ALU.add),
              lambda v: v.tensor_sub(tsc2[:], tsc2[:], gat[:, 2:3].to_broadcast([128, 8])),
              lambda v: v.tensor_scalar(tsc2[:], tsc2[:], 0.0, None, op0=ALU.is_equal),
              lambda v, g=g: v.tensor_tensor(oh2_all[:, g * 8:g * 8 + 8], tsc1[:], tsc2[:], op=ALU.mult)],
             waits=[("rvo1", p.v("rvo1"))], inc="rvo2")

        vseq([lambda v: v.tensor_tensor(gat[:, 6:7], gat[:, 3:4], gat[:, 5:6], op=ALU.mult),
              lambda v, g=g: (v.tensor_mul(c1_all[:, g * 8:g * 8 + 8], oh1_all[:, g * 8:g * 8 + 8],
                                           gat[:, 5:6].to_broadcast([128, 8])),
                              v.tensor_mul(c2_all[:, g * 8:g * 8 + 8], oh2_all[:, g * 8:g * 8 + 8],
                                           gat[:, 6:7].to_broadcast([128, 8])))[-1]],
             waits=[("rvo2", p.v("rvo2"))], inc="rvoh")

        def cntmm(t, g=g):
            t.matmul(pb[7][:, 32:40][0:1], ones_colb[:], oh1_all[:, g * 8:g * 8 + 8],
                     start=True, stop=True, skip_group_check=True)
            return t.matmul(pb[7][:, 40:48][0:1], ones_colb[:], oh2_all[:, g * 8:g * 8 + 8],
                            start=True, stop=True, skip_group_check=True)
        p.step("t", cntmm, waits=[("rvoh", p.v("rvoh"))], inc="rtcnt")
        p.step("v", lambda v, g=g:
               v.tensor_copy(cnts_row[0:1, g * 16:g * 16 + 16], pb[7][0:1, 32:48]),
               waits=[("rtcnt", p.v("rtcnt"))], inc="rvcv")

    es_A = contextlib.ExitStack()
    asb = lambda n, s, d=F32: es_A.enter_context(nc.sbuf_tensor(n, s, d, side="right"))
    cosq = asb("cosq_s", [128, 4096], BF16)
    sinq = asb("sinq_s", [128, 4096], BF16)
    wq_b = asb("wq_b_s", [128, 8, 128], BF16)
    wk_b = asb("wk_b_s", [128, 8, 128], BF16)
    wv_b = asb("wv_b_s", [128, 8, 128], BF16)
    kq_sb = asb("kq_sb_s", [128, 2, 512], BF16)
    ttmp = asb("ttmp_s", [128, 512], BF16)
    krot = asb("krot_s", [128, 4096], BF16)
    qrot = asb("qrot_s", [128, 2048], BF16)
    vt = asb("vt_s", [128, 32, 130], BF16)
    pt_sb = asb("pt_sb_s", [128, 2, 2, 1024], BF16)
    recip_sb = asb("recip_sb_s", [128, 8])
    qkvb_vrow = asb("qkvb_vrow_s", [1, D], BF16)

    qd(lambda q: q.dma_start(out=cosq[:], in_=cosq_d))
    qd(lambda q: q.dma_start(out=sinq[:], in_=sinq_d))
    gdma(lambda g: g.dma_start(out=qkvb_vrow[:], in_=qkvbRow[0:1, 2 * D:3 * D]))
    p.step("v", lambda v: v.memset(vt[:], 1.0), inc="avones")

    def xnT_slice(ci, t0, nn):
        if t0 < 2048:
            return xnTq[:, ci, t0:t0 + nn]
        return xnTk[:, ci, t0 - 2048:t0 - 2048 + nn]

    rg = 0
    for hp in range(8):
        par = hp % 2
        wdw = [("atdone", p.v("atdone"))]
        for s, wt in ((0, wq_b), (1, wk_b), (2, wv_b)):
            gdma(lambda g, s=s, wt=wt, hp=hp:
                 g.dma_start(out=wt[:], in_=qkvw_st[s, hp]), waits=wdw)
        gd_w = p.v("gd")

        for kind in range(12):
            isq = kind >= 8
            it = kind - 8 if isq else kind
            pbm = pb[kind % 2]
            pbs = pb[2 + kind % 2]

            def kqmm(t, isq=isq, it=it, pbm=pbm):
                w = wq_b if isq else wk_b
                for ci in range(8):
                    rhs = xnTq[:, ci, it * 512:it * 512 + 512] if isq else xnT_slice(ci, it * 512, 512)
                    inst = t.matmul(pbm[:], w[:, ci, :], rhs,
                                    start=(ci == 0), stop=(ci == 7))
                return inst
            tw = [("gd", gd_w), ("aaev", p.v("aaev") - 1), ("avrp", p.v("avrp") - 1)]
            if hp == 0 and kind < 4:
                tw.append(("xvev", p.v("xvev")))
            p.step("t", kqmm, waits=tw, inc="atkm")
            bcol = hp if isq else 8 + hp
            p.step("a", lambda a, kind=kind, pbm=pbm, bcol=bcol:
                   a.activation(kq_sb[:, kind % 2, :], pbm[:], AF.Identity,
                                bias=qkvbT_sb[:, bcol:bcol + 1]),
                   waits=[("atkm", p.v("atkm"))], inc="aaev")

            def swmm(t, kind=kind, pbs=pbs):
                return t.matmul(pbs[:], pswap[:], kq_sb[:, kind % 2, :], start=True, stop=True)
            p.step("t", swmm, waits=[("aaev", p.v("aaev")), ("avrp", p.v("avrp") - 1)], inc="atsw")
            dst = qrot if isq else krot

            vseq([lambda v, kind=kind, it=it, pbs=pbs, dst=dst:
                  (v.tensor_tensor(ttmp[:], kq_sb[:, kind % 2, :],
                                   cosq[:, it * 512:it * 512 + 512], op=ALU.mult),
                   v.tensor_tensor(dst[:, it * 512:it * 512 + 512], pbs[:],
                                   sinq[:, it * 512:it * 512 + 512], op=ALU.mult))[-1],
                  lambda v, kind=kind, it=it, dst=dst:
                  v.tensor_tensor(dst[:, it * 512:it * 512 + 512],
                                  dst[:, it * 512:it * 512 + 512], ttmp[:], op=ALU.add)],
                 waits=[("atsw", p.v("atsw"))], inc="avrp")

        for tt in range(32):
            pbv = pb[4 + tt % 2]

            def vmm(t, tt=tt, pbv=pbv, hp=hp):
                for ci in range(8):
                    t.matmul(pbv[:, 0:128], xnT_slice(ci, tt * 128, 128), wv_b[:, ci, :],
                             start=(ci == 0), stop=False)
                return t.matmul(pbv[:, 0:128], ones_rowb[:],
                                qkvb_vrow[0:1, hp * 128:hp * 128 + 128],
                                start=False, stop=True)
            vw = [("gd", gd_w), ("avve", p.v("avve") - 1)]
            if hp == 0 and tt < 2:
                vw.append(("gd", p.v("gd")))
            p.step("t", vmm, waits=vw, inc="atvm")

            def vev(v, tt=tt, pbv=pbv):
                v.tensor_copy(vt[:, tt, 0:64], pbv[:, 0:64])
                return v.tensor_copy(vt[:, tt, 65:129], pbv[:, 64:128])
            p.step("v", vev, waits=[("atvm", p.v("atvm")), ("avones", p.v("avones"))], inc="avve")

        avrp_done = p.v("avrp")
        avve_done = p.v("avve")
        for qt in range(4):
            fs0 = p.v("afx")
            for kc in range(32):
                sb0 = pb[kc % 2]
                sb1 = pb[2 + kc % 2]

                def smm(t, kc=kc, qt=qt, sb0=sb0, sb1=sb1):
                    t.matmul(sb0[:], krot[0:64, kc * 128:kc * 128 + 128],
                             qrot[0:64, qt * 512:qt * 512 + 512], start=True, stop=True,
                             skip_group_check=True)
                    return t.matmul(sb1[:], krot[64:128, kc * 128:kc * 128 + 128],
                                    qrot[64:128, qt * 512:qt * 512 + 512], start=True, stop=True,
                                    skip_group_check=True)
                tw = [("avrp", avrp_done), ("avve", avve_done),
                      ("afx", fs0 + 2 * ((kc - 2) // 2 + 1) if kc >= 2 else fs0)]
                if kc == 0 and qt > 0:
                    tw.append(("avno", p.v("avno")))
                p.step("t", smm, waits=tw, inc="afs", amt=2)

                if kc % 2 == 1:
                    pj = kc // 2

                    def expev(a, pj=pj):
                        a.activation(pt_sb[:, 0, pj % 2, :], pbig[:, 0:1024], AF.Exp,
                                     scale=cbank[:, CB_SCALE:CB_SCALE + 1],
                                     bias=cbank[:, CB_EBIAS:CB_EBIAS + 1])
                        return a.activation(pt_sb[:, 1, pj % 2, :], pbig[:, 1024:2048], AF.Exp,
                                            scale=cbank[:, CB_SCALE:CB_SCALE + 1],
                                            bias=cbank[:, CB_EBIAS:CB_EBIAS + 1])
                    p.step("a", expev,
                           waits=[("afs", p.v("afs")), ("afa", p.v("afa") - 1)], inc="afx", amt=2)

                    def avmm(t, kc=kc, pj=pj):
                        for k2 in (kc - 1, kc):
                            for h in range(2):
                                pba = pb[4 + h]
                                for qs in range(4):
                                    inst = t.matmul(pba[:, qs * 65:qs * 65 + 65],
                                                    pt_sb[:, h, pj % 2,
                                                          (k2 % 2) * 512 + qs * 128:(k2 % 2) * 512 + qs * 128 + 128],
                                                    vt[:, k2, h * 65:h * 65 + 65],
                                                    start=(k2 == 0), stop=(k2 == 31),
                                                    skip_group_check=True)
                        return inst
                    p.step("t", avmm, waits=[("afx", p.v("afx"))], inc="afa")

            def norm1(v):
                for h in range(2):
                    pba = pb[4 + h]
                    for qs in range(4):
                        inst = v.reciprocal(recip_sb[:, h * 4 + qs:h * 4 + qs + 1],
                                            pba[:, qs * 65 + 64:qs * 65 + 65])
                return inst
            p.step("v", norm1, waits=[("afa", p.v("afa"))], inc="avn1")

            def norm2(v, qt=qt, hp=hp):
                for h in range(2):
                    pba = pb[4 + h]
                    for qs in range(4):
                        tt = qt * 4 + qs
                        inst = v.tensor_mul(
                            attn_tok[:, tt, (2 * hp + h) * 64:(2 * hp + h) * 64 + 64],
                            pba[:, qs * 65:qs * 65 + 64],
                            recip_sb[:, h * 4 + qs:h * 4 + qs + 1].to_broadcast([128, 64]))
                return inst
            p.step("v", norm2, waits=[("avn1", p.v("avn1"))], inc="avno")

            for _ in range(4):
                if rg < NTF:
                    emit_rtile(rg)
                    rg += 1
        p.step("t", lambda t: t.nop(), waits=[("avno", p.v("avno"))], inc="atdone")

    assert rg == NTF

    gdma(lambda g: g.dma_start(out=scr_d[0:1, 0:2048], in_=cnts_row[:]),
         waits=[("rvcv", p.v("rvcv"))])
    gdma(lambda g: g.dma_start(out=cnts_t[:],
                               in_=scr_d[0:1, 0:2048].rearrange("o (g j) -> (o g) j", j=16)),
         waits=[("gd", p.v("gd"))])

    def prefmm(t):
        t.matmul(pb[6][:, 0:16], triu_rot[:], cnts_t[:], start=True, stop=True,
                 skip_group_check=True)
        return t.matmul(pb[7][0:1, 0:16], ones_col[:], cnts_t[:], start=True, stop=True,
                        skip_group_check=True)
    p.step("t", prefmm, waits=[("gd", p.v("gd")), ("rvcv", p.v("rvcv"))], inc="rtpre")

    def evpre(v):
        v.tensor_copy(pref_sb[:], pb[6][:, 0:16])
        return v.tensor_copy(tot_sb[:], pb[7][0:1, 0:16])
    p.step("v", evpre, waits=[("rtpre", p.v("rtpre"))], inc="rvpre")
    gdma(lambda g: g.dma_start(out=scr_d[0:1, 0:2048].rearrange("o (g j) -> (o g) j", j=16),
                               in_=pref_sb[:]),
         waits=[("rvpre", p.v("rvpre"))])
    gdma(lambda g: g.dma_start(out=pref_rows[0:1, 0:2048], in_=scr_d[0:1, 0:2048]),
         waits=[("gd", p.v("gd"))])
    gd_pref = p.v("gd")

    for ti in range(NTO):
        bb = pb[6 + ti % 2]

        def posmm(t, ti=ti, bb=bb):
            t.matmul(bb[:, 0:8], triu_tok[:], oh1_all[:, ti * 8:ti * 8 + 8],
                     start=True, stop=False, skip_group_check=True)
            t.matmul(bb[:, 0:8], ones_rowf[:], pref_rows[0:1, ti * 16:ti * 16 + 8],
                     start=False, stop=True, skip_group_check=True)
            t.matmul(bb[:, 8:16], triu_tok[:], oh2_all[:, ti * 8:ti * 8 + 8],
                     start=True, stop=False, skip_group_check=True)
            t.matmul(bb[:, 8:16], ones_rowf[:], pref_rows[0:1, ti * 16 + 8:ti * 16 + 16],
                     start=False, stop=False, skip_group_check=True)
            return t.matmul(bb[:, 8:16], ones_rowf[:], tot_sb[0:1, 0:8],
                            start=False, stop=True, skip_group_check=True)
        p.step("t", posmm, waits=[("gd", gd_pref), ("rvcc", p.v("rvcc") - 1)], inc="rtpos")

        vseq([lambda v, ti=ti, bb=bb:
              (v.tensor_scalar(ktmp[:, 0:8], bb[:, 0:8], CAP, None, op0=ALU.is_lt),
               v.tensor_scalar(ktmp[:, 8:16], bb[:, 8:16], CAP, None, op0=ALU.is_lt))[-1],
              lambda v, ti=ti: (v.tensor_tensor(ktmp[:, 16:24], c1_all[:, ti * 8:ti * 8 + 8],
                                                ktmp[:, 0:8], op=ALU.mult),
                                v.tensor_tensor(ktmp[:, 24:32], c2_all[:, ti * 8:ti * 8 + 8],
                                                ktmp[:, 8:16], op=ALU.mult))[-1],
              lambda v, ti=ti: v.tensor_tensor(c_coef[:, ti * 8:ti * 8 + 8],
                                               ktmp[:, 16:24], ktmp[:, 24:32], op=ALU.add)],
             waits=[("rtpos", p.v("rtpos"))], inc="rvcc")

    if debug:
        qd(lambda q: q.dma_start(out=dbg["d_cnts"], in_=cnts_t[:]),
           waits=[("rvcc", p.v("rvcc"))])
        qd(lambda q: q.dma_start(out=dbg["d_pref"], in_=pref_sb[:]))
        qd(lambda q: q.dma_start(out=dbg["d_oh"], in_=oh1_all[:, 0:32]))
        qd(lambda q: q.dma_start(out=dbg["d_ccoef"], in_=c_coef[:]))
        qd(lambda q: q.dma_start(out=dbg["d_krot"], in_=krot[:]))
        qd(lambda q: q.dma_start(out=dbg["d_attn"],
                                 in_=attn_tok[:].rearrange("p a b -> p (a b)")),
           waits=[("avno", p.v("avno"))])

    es_A.close()
    es_RX.close()
    es_XK.close()
    p.barrier()

    es_ACC = contextlib.ExitStack()
    acc = es_ACC.enter_context(nc.sbuf_tensor("acc_s", [128, NTO, D], F32, side="right"))
    es_OP = contextlib.ExitStack()
    ow_b = es_OP.enter_context(nc.sbuf_tensor("ow_b_s", [128, 8, D], BF16, side="right"))
    attnF_t = es_OP.enter_context(nc.sbuf_tensor("attnF_t_s", [128, 8, 128], BF16, side="right"))
    outb_row = es_OP.enter_context(nc.sbuf_tensor("outb_row_s", [1, D], BF16, side="right"))

    gdma(lambda g: g.dma_start(out=ow_b[:], in_=outw_st))
    gdma(lambda g: g.dma_start(out=outb_row[:], in_=outbRow))
    gd_ow = p.v("gd")
    qd(lambda q: q.dma_start(out=acc[:], in_=xown))
    qd_acc = p.v("qd")

    for tt in range(NTO):
        def optr(t, tt=tt):
            pbB = pb[0].bitcast(BF16)
            for ci in range(8):
                inst = t.transpose(pbB[:, ci * 128:ci * 128 + 128],
                                   attn_tok[:, tt, ci * 128:ci * 128 + 128], identb[:])
            return inst
        p.step("t", optr, waits=[("opve", p.v("opve"))], inc="optr")

        def opev(v):
            return v.tensor_copy(attnF_t[:, :, :],
                                 pb[0].bitcast(BF16).rearrange("p (c k) -> p c k", k=128))
        p.step("v", opev, waits=[("optr", p.v("optr"))], inc="opve")

        def opmm(t, tt=tt):
            for nh in range(2):
                pbo = pb[2 + nh]
                t.matmul(pbo[:], ones_rowb[:], outb_row[0:1, nh * 512:nh * 512 + 512],
                         start=True, stop=False, skip_group_check=True)
                for ci in range(8):
                    inst = t.matmul(pbo[:], attnF_t[:, ci, :], ow_b[:, ci, nh * 512:nh * 512 + 512],
                                    start=False, stop=(ci == 7), skip_group_check=True)
            return inst
        p.step("t", opmm, waits=[("opve", p.v("opve")), ("gd", gd_ow),
                                 ("opac", p.v("opac"))], inc="opmm")

        def opadd(v, tt=tt):
            v.tensor_tensor(acc[:, tt, 0:512], acc[:, tt, 0:512], pb[2][:], op=ALU.add)
            return v.tensor_tensor(acc[:, tt, 512:1024], acc[:, tt, 512:1024], pb[3][:], op=ALU.add)
        p.step("v", opadd, waits=[("opmm", p.v("opmm")), ("qd", qd_acc)], inc="opac")

    es_OP.close()
    p.barrier()

    es_F = contextlib.ExitStack()
    w1e = es_F.enter_context(nc.sbuf_tensor("w1e_s", [128, 2, 8, 512], BF16, side="right"))
    w2e = es_F.enter_context(nc.sbuf_tensor("w2e_s", [128, 2, 4, 1024], BF16, side="right"))
    h_sb = es_F.enter_context(nc.sbuf_tensor("h_sb_s", [128, 2, 256], BF16, side="right"))
    b2_row = es_F.enter_context(nc.sbuf_tensor("b2_row_s", [1, NE * D], BF16, side="right"))

    gdma(lambda g: g.dma_start(out=b2_row[:], in_=b2Row))
    fmarks = {}
    for e in range(NE):
        for fe in range(8):
            qi = e * 8 + fe
            wp = qi % 2
            fq, half = fe // 2, fe % 2
            wdw = [("fm2", fmarks[qi - 2])] if qi >= 2 else []
            gdma(lambda g, e=e, fq=fq, half=half, wp=wp:
                 g.dma_start(out=w1e[:, wp], in_=w1_st[e, fq][:, :, half * 512:half * 512 + 512]),
                 waits=wdw)
            gdma(lambda g, e=e, fq=fq, half=half, wp=wp:
                 g.dma_start(out=w2e[:, wp], in_=w2_st[e, fq][:, half * 4:half * 4 + 4, :]))
            gd_f = p.v("gd")
            for tq in range(8):
                def mk_mm1(fc, tq=tq, wp=wp):
                    def mm1(t, fc=fc, tq=tq, wp=wp):
                        pbh = pb[fc % 2]
                        for ci in range(8):
                            inst = t.matmul(pbh[:, 0:256], w1e[:, wp, ci, fc * 128:fc * 128 + 128],
                                            xnTq[:, ci, tq * 256:tq * 256 + 256],
                                            start=(ci == 0), stop=(ci == 7),
                                            skip_group_check=True)
                        return inst
                    return mm1

                def mk_mm2(fc, e=e, fe=fe, wp=wp):
                    def mm2(t, fc=fc, e=e, fe=fe, wp=wp):
                        inst = None
                        for stt in range(2):
                            for nh in range(2):
                                pby = pb[2 + stt * 2 + nh]
                                if fe == 0 and fc == 0:
                                    t.matmul(pby[:], ones_rowb[:],
                                             b2_row[0:1, e * D + nh * 512:e * D + nh * 512 + 512],
                                             start=True, stop=False, skip_group_check=True)
                                inst = t.matmul(pby[:], h_sb[:, fc % 2, stt * 128:stt * 128 + 128],
                                                w2e[:, wp, fc, nh * 512:nh * 512 + 512],
                                                start=(fe > 0 and fc == 0), stop=(fc == 3),
                                                skip_group_check=True)
                        return inst
                    return mm2

                def mk_gelu(fc, e=e, fe=fe):
                    def gelu(a, fc=fc, e=e, fe=fe):
                        j = e * 32 + fe * 4 + fc
                        return a.activation(h_sb[:, fc % 2, :], pb[fc % 2][:, 0:256],
                                            AF.Gelu_apprx_tanh, bias=b1T[:, j:j + 1])
                    return gelu

                for fc in range(4):
                    tw = [("gd", gd_f)]
                    if fc >= 2:
                        tw.append(("fge", p.v("fge") - 1))
                    p.step("t", mk_mm1(fc), waits=tw, inc="fmm")
                    p.step("a", mk_gelu(fc), waits=[("fmm", p.v("fmm")), ("fm2", p.v("fm2") - 1)],
                           inc="fge")
                    if fc >= 1:
                        w2w = [("fge", p.v("fge") - 1)]
                        if fc == 1 and qi + tq > 0:
                            w2w.append(("fev", p.v("fev")))
                        p.step("t", mk_mm2(fc - 1), waits=w2w, inc="fm2")
                p.step("t", mk_mm2(3), waits=[("fge", p.v("fge"))], inc="fm2")

                def fevac(v, e=e, tq=tq):
                    inst = None
                    for stx in range(2):
                        tt = tq * 2 + stx
                        cav = c_coef[:, tt * 8 + e:tt * 8 + e + 1]
                        for nh in range(2):
                            pby = pb[2 + stx * 2 + nh]
                            inst = v.scalar_tensor_tensor(
                                acc[:, tt, nh * 512:nh * 512 + 512], pby[:], cav,
                                acc[:, tt, nh * 512:nh * 512 + 512],
                                op0=ALU.mult, op1=ALU.add)
                    return inst
                p.step("v", fevac, waits=[("fm2", p.v("fm2"))], inc="fev")
            fmarks[qi] = p.v("fm2")

    es_F.close()
    p.barrier()

    qd(lambda q: q.dma_start(out=outp, in_=acc[:]))
    p.step("q", lambda q: q.nop(), waits=[("qd", p.v("qd"))])

    with nc.Block() as blk:
        p.emit(blk)
    es_ACC.close()
    es_glob.close()
    return nc


def _stage(core, x, qkv_w, qkv_b, out_w, out_b, Wg, bg, W1, b1, W2, b2):
    b = core // 2
    h = core % 2
    xf = np.ascontiguousarray(x.reshape(128, 128, D))
    rot = np.roll(np.arange(NTF), -16 * core)
    xrot = np.ascontiguousarray(xf[rot])
    xb = x.reshape(4, 4096, D)[b]
    own = xb[h * 2048:(h + 1) * 2048]
    oth = xb[(1 - h) * 2048:(2 - h) * 2048]
    xbatch = np.ascontiguousarray(
        np.concatenate([own, oth], axis=0).reshape(NTB, 128, D))
    xown = np.ascontiguousarray(own.reshape(NTO, 128, D).transpose(1, 0, 2))

    perm = np.arange(3 * D)
    for hh in range(NH):
        base = hh * HD
        for sec in (0, D):
            perm[sec + base:sec + base + 32] = sec + base + 2 * np.arange(32)
            perm[sec + base + 32:sec + base + 64] = sec + base + 2 * np.arange(32) + 1
    qkvw_p = qkv_w[:, perm]
    qkvb_p = qkv_b[perm]
    qkvw_st = np.ascontiguousarray(
        qkvw_p.reshape(8, 128, 3, 8, 128).transpose(2, 3, 1, 0, 4))
    qkvbT = np.ascontiguousarray(qkvb_p.reshape(24, 128).T)
    qkvbRow = np.ascontiguousarray(qkvb_p.reshape(1, 3 * D))
    outw_st = np.ascontiguousarray(out_w.reshape(8, 128, D).transpose(1, 0, 2))
    outbRow = np.ascontiguousarray(out_b.reshape(1, D))
    wg_stv = np.ascontiguousarray(Wg.reshape(8, 128, 8).transpose(1, 0, 2).reshape(128, 64))
    bgRow = np.ascontiguousarray(bg.reshape(1, 8))
    w1_stv = np.ascontiguousarray(
        W1.reshape(NE, 8, 128, 4, 1024).transpose(0, 3, 2, 1, 4))
    w2_stv = np.ascontiguousarray(
        W2.reshape(NE, 4, 8, 128, D).transpose(0, 1, 3, 2, 4))
    b1T = np.ascontiguousarray(b1.reshape(NE, 32, 128).transpose(2, 0, 1).reshape(128, 256))
    b2Row = np.ascontiguousarray(b2.reshape(1, NE * D))

    identf = np.eye(128, dtype=np.float32)
    identb = np.eye(128, dtype=np.float32).astype(ml_dtypes.bfloat16)
    swap = np.arange(128)
    swap = np.where((swap % 64) < 32, swap + 32, swap - 32)
    psw = np.zeros((128, 128), np.float32)
    psw[swap, np.arange(128)] = 1.0
    pswap = psw.astype(ml_dtypes.bfloat16)
    triu_tok = np.triu(np.ones((128, 128), np.float32), 1).astype(ml_dtypes.bfloat16)
    canon = rot
    triu_rot = (canon[:, None] < canon[None, :]).astype(np.float32)
    freqs = (10000.0 ** (np.arange(0, HD, 2.0, dtype=np.float32) / HD)).astype(np.float32)
    tpos = np.arange(4096, dtype=np.float32)
    stagedpos = np.concatenate([tpos[h * 2048:(h + 1) * 2048],
                                tpos[(1 - h) * 2048:(2 - h) * 2048]])
    ang = (stagedpos[:, None] * freqs[None, :]).astype(np.float32)
    cosv = np.cos(ang.astype(np.float64)).astype(np.float32)
    sinv = np.sin(ang.astype(np.float64)).astype(np.float32)
    prow = np.arange(128)
    cosq = np.ascontiguousarray(cosv[:, prow % 32].T).astype(ml_dtypes.bfloat16)
    sgn = np.where((prow % 64) < 32, -1.0, 1.0).astype(np.float32)
    sinq = np.ascontiguousarray((sinv[:, prow % 32] * sgn[None, :]).T).astype(ml_dtypes.bfloat16)

    return {
        "xrot": xrot, "xbatch": xbatch, "xown": xown,
        "qkvw_st": qkvw_st, "outw_st": outw_st, "qkvbT": qkvbT,
        "qkvbRow": qkvbRow, "outbRow": outbRow, "wg_st": wg_stv, "bgRow": bgRow,
        "w1_st": w1_stv, "w2_st": w2_stv, "b1T": b1T, "b2Row": b2Row,
        "identf": identf, "identb": identb, "pswap": pswap,
        "triu_tok": triu_tok, "triu_rot": triu_rot, "cosq": cosq, "sinq": sinq,
    }


def kernel(L0_attn_x, L0_attn_qkv_w, L0_attn_qkv_b, L0_attn_out_w, L0_attn_out_b,
           L0_ffn_Wg, L0_ffn_bg, L0_ffn_W1, L0_ffn_b1, L0_ffn_W2, L0_ffn_b2):
    x = np.ascontiguousarray(np.asarray(L0_attn_x, np.float32).reshape(16384, D))
    args = [np.asarray(a, np.float32) for a in
            (L0_attn_qkv_w, L0_attn_qkv_b, L0_attn_out_w, L0_attn_out_b,
             L0_ffn_Wg, L0_ffn_bg, L0_ffn_W1, L0_ffn_b1, L0_ffn_W2, L0_ffn_b2)]
    nc = build()
    in_maps = [_stage(c, x, *args) for c in range(NCORES)]
    res = run_bass_kernel_spmd(nc, in_maps, core_ids=list(range(NCORES)))
    shards = []
    for c in range(NCORES):
        o = np.asarray(res.results[c]["outp"])
        shards.append(o.transpose(1, 0, 2).reshape(2048, D))
    return np.concatenate(shards, axis=0).reshape(4, 4096, D).astype(np.float32)


# version 05
# speedup vs baseline: 1.4482x; 1.0515x over previous
import contextlib
import numpy as np
import ml_dtypes
import concourse.bass as bass
import concourse.mybir as mybir
from concourse.bass_utils import run_bass_kernel_spmd

F32 = mybir.dt.float32
BF16 = mybir.dt.bfloat16
AF = mybir.ActivationFunctionType
ALU = mybir.AluOpType
AX = mybir.AxisListType

D = 1024
NH = 16
HD = 64
FF = 4096
NE = 8
CAP = 4916.0
NCORES = 8
NTF = 128
NTO = 16
NTB = 32
EPS = 1e-5

CB_EPS, CB_NHALF, CB_SCALE, CB_EBIAS, CB_INVD, CB_NEG1 = 0, 1, 2, 3, 4, 5


class Plan:
    ORDER = (("q", "sync"), ("g", "gpsimd"), ("t", "tensor"), ("v", "vector"), ("a", "scalar"))

    def __init__(self, nc, es):
        self.nc = nc
        self.es = es
        self.steps = {e: [] for e, _ in self.ORDER}
        self.sems = {}
        self.cnt = {}

    def sem(self, name):
        if name not in self.sems:
            self.sems[name] = self.es.enter_context(self.nc.semaphore(name))
            self.cnt[name] = 0
        return self.sems[name]

    def step(self, eng, fn, waits=(), inc=None, amt=1):
        waits = tuple((s, v) for s, v in waits if v > 0)
        for s, _ in waits:
            self.sem(s)
        if inc is not None:
            self.sem(inc)
            self.cnt[inc] += amt
        self.steps[eng].append((fn, waits, inc, amt))
        return self.cnt[inc] if inc is not None else None

    def v(self, name):
        self.sem(name)
        return self.cnt[name]

    def barrier(self):
        for e in ("q", "g", "t", "v", "a"):
            w = []
            if e == "q":
                w.append(("qd", self.v("qd")))
            if e == "g":
                w.append(("gd", self.v("gd")))
            self.step(e, lambda en: en.nop(), waits=w, inc="bar")
        tot = self.v("bar")
        for e in ("q", "g", "t", "v", "a"):
            self.step(e, lambda en: en.nop(), waits=[("bar", tot)])

    def emit(self, blk):
        for engname, attr in self.ORDER:
            steps = self.steps[engname]

            def body(e, steps=steps):
                for fn, waits, inc, amt in steps:
                    for s, val in waits:
                        e.wait_ge(self.sems[s], val)
                    inst = fn(e)
                    if inc is not None:
                        assert inst is not None
                        inst.then_inc(self.sems[inc], amt)
            getattr(blk, attr)(body)


def build(debug=False):
    nc = bass.Bass("TRN2", target_bir_lowering=False)

    dt_in = lambda n, s, d=F32: nc.dram_tensor(n, s, d, kind="ExternalInput").ap()
    xrot = dt_in("xrot", [NTF, 128, D])
    xbatch = dt_in("xbatch", [NTB, 128, D])
    xown = dt_in("xown", [128, NTO, D])
    qkvw_st = dt_in("qkvw_st", [3, 8, 128, 8, 128])
    outw_st = dt_in("outw_st", [128, 8, D])
    qkvbT = dt_in("qkvbT", [128, 24])
    qkvbRow = dt_in("qkvbRow", [1, 3 * D])
    outbRow = dt_in("outbRow", [1, D])
    wg_st = dt_in("wg_st", [128, 64])
    bgRow = dt_in("bgRow", [1, 8])
    w1_st = dt_in("w1_st", [NE, 4, 128, 8, 1024])
    w2_st = dt_in("w2_st", [NE, 4, 128, 8, 1024])
    b1T_d = dt_in("b1T", [128, 256])
    b2Row = dt_in("b2Row", [1, NE * D])
    identf_d = dt_in("identf", [128, 128])
    identb_d = dt_in("identb", [128, 128], BF16)
    pswap_d = dt_in("pswap", [128, 128], BF16)
    triu_tok_d = dt_in("triu_tok", [128, 128], BF16)
    triu_rot_d = dt_in("triu_rot", [128, 128])
    cosq_d = dt_in("cosq", [128, 4096], BF16)
    sinq_d = dt_in("sinq", [128, 4096], BF16)
    outp = nc.dram_tensor("outp", [128, NTO, D], F32, kind="ExternalOutput").ap()
    scr_d = nc.dram_tensor("scr", [1, 4096], F32, kind="Internal").ap()
    dbg = {}
    if debug:
        for n, s, d in [("d_ccoef", [128, 128], F32), ("d_cnts", [128, 16], F32),
                        ("d_pref", [128, 16], F32), ("d_xnTq", [128, 2048], BF16),
                        ("d_attn", [128, NTO * D], BF16),
                        ("d_krot", [128, 4096], BF16), ("d_oh", [128, 32], F32)]:
            dbg[n] = nc.dram_tensor(n, s, d, kind="ExternalOutput").ap()

    es_glob = contextlib.ExitStack()
    p = Plan(nc, es_glob)
    sb = lambda n, s, d=F32: es_glob.enter_context(nc.sbuf_tensor(n, s, d))

    identf = sb("identf_s", [128, 128])
    identb = sb("identb_s", [128, 128], BF16)
    pswap = sb("pswap_s", [128, 128], BF16)
    triu_tok = sb("triu_tok_s", [128, 128], BF16)
    triu_rot = sb("triu_rot_s", [128, 128])
    wg_sb = sb("wg_sb_s", [128, 64])
    bg_sb = sb("bg_sb_s", [1, 8])
    qkvbT_sb = sb("qkvbT_sb_s", [128, 24])
    b1T = sb("b1T_sb_s", [128, 256])
    cbank = sb("cbank_s", [128, 8])
    ones_col = sb("ones_col_s", [128, 1])
    ones_colb = sb("ones_colb_s", [128, 1], BF16)
    ones_rowf = sb("ones_rowf_s", [1, 128])
    ones_rowb = sb("ones_rowb_s", [1, 128], BF16)
    zeros8 = sb("zeros8_s", [128, 8])
    c_coef = sb("c_coef_s", [128, NTO * 8])
    xnTq = sb("xnTq_s", [128, 8, 2048], BF16)
    attn_tok = sb("attn_tok_s", [128, NTO, D], BF16)

    pbig = es_glob.enter_context(nc.psum_tensor("pbig", [128, 2048], F32))
    pb = [pbig[:, i * 512:(i + 1) * 512] for i in range(4)]
    pb += [es_glob.enter_context(nc.psum_tensor(f"pb{i}", [128, 512], F32))[:] for i in range(4, 8)]

    def qd(fn, waits=()):
        return p.step("q", fn, waits=waits, inc="qd", amt=16)

    def gdma(fn, waits=()):
        return p.step("g", fn, waits=waits, inc="gd", amt=16)

    def vseq(fns, waits=(), inc=None):
        for i, fn in enumerate(fns):
            w = list(waits) + [("vch", p.v("vch"))] if i == 0 else [("vch", p.v("vch"))]
            p.step("v", fn, waits=w, inc=(inc if (i == len(fns) - 1 and inc) else "vch"))

    qd(lambda q: q.dma_start(out=identf[:], in_=identf_d))
    qd(lambda q: q.dma_start(out=triu_tok[:], in_=triu_tok_d))
    qd(lambda q: q.dma_start(out=triu_rot[:], in_=triu_rot_d))
    qd(lambda q: q.dma_start(out=wg_sb[:], in_=wg_st))
    qd(lambda q: q.dma_start(out=bg_sb[:], in_=bgRow))
    qd(lambda q: q.dma_start(out=qkvbT_sb[:], in_=qkvbT))
    qd(lambda q: q.dma_start(out=b1T[:], in_=b1T_d))
    qd(lambda q: q.dma_start(out=identb[:], in_=identb_d))
    qd(lambda q: q.dma_start(out=pswap[:], in_=pswap_d))

    def memsets(g):
        g.memset(cbank[:, CB_EPS:CB_EPS + 1], EPS)
        g.memset(cbank[:, CB_NHALF:CB_NHALF + 1], -0.5)
        g.memset(cbank[:, CB_SCALE:CB_SCALE + 1], 0.125)
        g.memset(cbank[:, CB_EBIAS:CB_EBIAS + 1], -12.0)
        g.memset(cbank[:, CB_INVD:CB_INVD + 1], 1.0 / D)
        g.memset(cbank[:, CB_NEG1:CB_NEG1 + 1], -1.0)
        g.memset(ones_col[:], 1.0)
        g.memset(ones_colb[:], 1.0)
        g.memset(ones_rowf[:], 1.0)
        g.memset(ones_rowb[:], 1.0)
        return g.memset(zeros8[:], 0.0)
    p.step("g", memsets, inc="gcst")
    p.barrier()

    es_XK = contextlib.ExitStack()
    xnTk = es_XK.enter_context(nc.sbuf_tensor("xnTk_s", [128, 8, 2048], BF16, side="right"))
    es_RX = contextlib.ExitStack()
    rsb = lambda n, s, d=F32: es_RX.enter_context(nc.sbuf_tensor(n, s, d, side="right"))
    xt2 = rsb("xt2_s", [128, 2, D])
    stat = rsb("stat_s", [128, 2, 12])
    xn32 = rsb("xn32_s", [128, 2, D])
    xnbf2 = rsb("xnbf2_s", [128, 2, D], BF16)
    xT32 = rsb("xT32_s", [128, D])
    l8 = rsb("l8_s", [128, 8])
    top8 = rsb("top8_s", [128, 8])
    tsc1 = rsb("tsc1_s", [128, 8])
    tsc2 = rsb("tsc2_s", [128, 8])
    gat = rsb("gat_s", [128, 8])
    oh1_all = rsb("oh1_all_s", [128, NTF * 8], BF16)
    oh2_all = rsb("oh2_all_s", [128, NTF * 8], BF16)
    c1_all = rsb("c1_all_s", [128, NTF * 8], BF16)
    c2_all = rsb("c2_all_s", [128, NTF * 8], BF16)
    cnts_row = rsb("cnts_row_s", [1, NTF * 16])
    pref_rows = cnts_row
    cnts_t = rsb("cnts_t_s", [128, 16])
    pref_sb = rsb("pref_sb_s", [128, 16])
    tot_sb = rsb("tot_sb_s", [1, 16])
    ktmp = rsb("ktmp_s", [128, 32])

    def ln_chain(n, src_tile, dump, aw=()):
        par = n % 2
        st = lambda c: stat[:, par, c:c + 1]
        wv = [("qd", p.v("qd")), ("lnarst", p.v("lnarst") - 1)]
        vseq([lambda v: v.reduce_sum(st(0), src_tile, axis=AX.X),
              lambda v: v.tensor_scalar(st(2), st(0), 1.0 / D, None, op0=ALU.mult),
              lambda v: v.scalar_tensor_tensor(st(3), st(2), -1.0, st(2),
                                               op0=ALU.mult, op1=ALU.mult)],
             waits=wv, inc="lnvst")
        p.step("a", lambda a: a.activation(dump, src_tile, AF.Square, accum_out=st(1)),
               waits=[("qd", p.v("qd"))] + list(aw), inc="lnassq")
        vseq([lambda v: v.scalar_tensor_tensor(st(4), st(1), 1.0 / D, st(3),
                                               op0=ALU.mult, op1=ALU.add)],
             waits=[("lnassq", p.v("lnassq"))], inc="lnvvar")

        def archain(a):
            a.activation(st(5), st(4), AF.Ln, bias=cbank[:, CB_EPS:CB_EPS + 1])
            return a.activation(st(6), st(5), AF.Exp, scale=cbank[:, CB_NHALF:CB_NHALF + 1])
        p.step("a", archain, waits=[("lnvvar", p.v("lnvvar"))], inc="lnarst")
        return st

    for bt in range(NTB):
        par = bt % 2
        xt = xt2[:, par, :]
        qd(lambda q, bt=bt, par=par: q.dma_start(out=xt2[:, par, :], in_=xbatch[bt]),
           waits=[("lnaxn", p.v("lnaxn") - 1)])
        st = ln_chain(bt, xt, xn32[:, par, :])
        vseq([lambda v, xt=xt, par=par, st=st:
              v.tensor_scalar(xnbf2[:, par, :], xt, st(2), st(6),
                              op0=ALU.subtract, op1=ALU.mult)],
             waits=[("xttr", p.v("xttr") - 1), ("lnarst", p.v("lnarst"))], inc="lnaxn")
        b0 = 0 if par == 0 else 2

        def transb(t, b0=b0, par=par):
            pbB = pb[b0].bitcast(BF16)
            for ci in range(8):
                inst = t.transpose(pbB[:, ci * 128:ci * 128 + 128],
                                   xnbf2[:, par, ci * 128:ci * 128 + 128], identb[:])
            return inst
        p.step("t", transb, waits=[("lnaxn", p.v("lnaxn")), ("xvev", p.v("xvev") - 1)],
               inc="xttr")
        tgt = xnTq if bt < NTO else xnTk
        off = (bt % NTO) * 128

        def evtrb(v, b0=b0, tgt=tgt, off=off):
            return v.tensor_copy(tgt[:, :, off:off + 128],
                                 pb[b0].bitcast(BF16).rearrange("p (c k) -> p c k", k=128))
        p.step("v", evtrb, waits=[("xttr", p.v("xttr"))], inc="xvev")

    if debug:
        qd(lambda q: q.dma_start(out=dbg["d_xnTq"], in_=xnTq[:, 0, :]),
           waits=[("xvev", p.v("xvev"))])

    p.barrier()

    rmark = {"rtlg": {}, "rvoh": {}, "rtcnt": {}}

    def rt_part1(g, epoch_first):
        par = g % 2
        xt = xt2[:, par, :]
        qd(lambda q, g=g, par=par: q.dma_start(out=xt2[:, par, :], in_=xrot[g]),
           waits=[("lnaxn", p.v("lnaxn") - 1)])
        aw = [("xttr", p.v("xttr"))] if g < 2 else ()
        st = ln_chain(NTB + g, xt, xnbf2[:, par, :], aw)
        vseq([lambda v, xt=xt, par=par, st=st:
              v.tensor_scalar(xn32[:, par, :], xt, st(2), st(6),
                              op0=ALU.subtract, op1=ALU.mult)],
             waits=[("rttr", p.v("rttr") - 2), ("lnarst", p.v("lnarst"))], inc="lnaxn")

        for hfi in range(2):
            def trh(t, hfi=hfi, par=par):
                for ci in range(4):
                    cc = hfi * 4 + ci
                    inst = t.transpose(pb[6][:, ci * 128:ci * 128 + 128],
                                       xn32[:, par, cc * 128:cc * 128 + 128], identf[:])
                return inst
            tw = [("rvev", p.v("rvev"))]
            if hfi == 0:
                tw.append(("lnaxn", p.v("lnaxn")))
            p.step("t", trh, waits=tw, inc="rttr")
            p.step("v", lambda v, hfi=hfi: v.tensor_copy(xT32[:, hfi * 512:hfi * 512 + 512], pb[6]),
                   waits=[("rttr", p.v("rttr"))], inc="rvev")

        sl = (g % 4) * 8

        def lgmm(t, sl=sl):
            for ci in range(8):
                t.matmul(pb[7][:, sl:sl + 8], xT32[:, ci * 128:ci * 128 + 128],
                         wg_sb[:, ci * 8:ci * 8 + 8], start=(ci == 0), stop=False)
            return t.matmul(pb[7][:, sl:sl + 8], ones_rowf[:], bg_sb[:], start=False, stop=True)
        lw = [("rvev", p.v("rvev"))]
        if epoch_first:
            lw += [("rvcp", p.v("rvcp")), ("rvcv", p.v("rvcv"))]
        p.step("t", lgmm, waits=lw, inc="rtlg")
        rmark["rtlg"][g] = p.v("rtlg")

    def rt_part2(g):
        sl = (g % 4) * 8
        p.step("v", lambda v, sl=sl: v.tensor_copy(l8[:], pb[7][:, sl:sl + 8]),
               waits=[("rtlg", rmark["rtlg"][g]), ("rtcnt", p.v("rtcnt"))], inc="rvcp")
        p.step("v", lambda v: v.max(top8[:], l8[:]),
               waits=[("rvcp", p.v("rvcp"))], inc="rvmx")
        p.step("v", lambda v: v.tensor_sub(gat[:, 0:1], top8[:, 1:2], top8[:, 0:1]),
               waits=[("rvmx", p.v("rvmx"))], inc="rvl8")
        p.step("a", lambda a: a.activation(gat[:, 3:4], gat[:, 0:1], AF.Exp),
               waits=[("rvl8", p.v("rvl8"))], inc="raeg")

        vseq([lambda v: (v.tensor_sub(tsc1[:], l8[:], top8[:, 0:1].to_broadcast([128, 8])),
                         v.tensor_scalar(gat[:, 1:2], gat[:, 0:1], 0.0, None, op0=ALU.is_equal))[-1],
              lambda v: (v.tensor_scalar(tsc1[:], tsc1[:], 0.0, None, op0=ALU.is_equal),
                         v.tensor_scalar(gat[:, 2:3], gat[:, 1:2], 1.0, None, op0=ALU.add))[-1],
              lambda v: v.tensor_tensor_scan(tsc2[:], tsc1[:], zeros8[:], 0.0, op0=ALU.add, op1=ALU.add),
              lambda v: v.tensor_scalar(tsc2[:], tsc2[:], 1.0, None, op0=ALU.is_equal),
              lambda v, g=g: v.tensor_tensor(oh1_all[:, g * 8:g * 8 + 8], tsc1[:], tsc2[:], op=ALU.mult)],
             waits=[("raeg", p.v("raeg"))], inc="rvo1")

        vseq([lambda v: (v.tensor_sub(tsc1[:], l8[:], top8[:, 1:2].to_broadcast([128, 8])),
                         v.tensor_scalar(gat[:, 4:5], gat[:, 3:4], 1.0, None, op0=ALU.add))[-1],
              lambda v: (v.tensor_scalar(tsc1[:], tsc1[:], 0.0, None, op0=ALU.is_equal),
                         v.reciprocal(gat[:, 5:6], gat[:, 4:5]))[-1],
              lambda v: v.tensor_tensor_scan(tsc2[:], tsc1[:], zeros8[:], 0.0, op0=ALU.add, op1=ALU.add),
              lambda v: v.tensor_sub(tsc2[:], tsc2[:], gat[:, 2:3].to_broadcast([128, 8])),
              lambda v: v.tensor_scalar(tsc2[:], tsc2[:], 0.0, None, op0=ALU.is_equal),
              lambda v, g=g: v.tensor_tensor(oh2_all[:, g * 8:g * 8 + 8], tsc1[:], tsc2[:], op=ALU.mult)],
             waits=[("rvo1", p.v("rvo1"))], inc="rvo2")

        vseq([lambda v: v.tensor_tensor(gat[:, 6:7], gat[:, 3:4], gat[:, 5:6], op=ALU.mult),
              lambda v, g=g: (v.tensor_mul(c1_all[:, g * 8:g * 8 + 8], oh1_all[:, g * 8:g * 8 + 8],
                                           gat[:, 5:6].to_broadcast([128, 8])),
                              v.tensor_mul(c2_all[:, g * 8:g * 8 + 8], oh2_all[:, g * 8:g * 8 + 8],
                                           gat[:, 6:7].to_broadcast([128, 8])))[-1]],
             waits=[("rvo2", p.v("rvo2"))], inc="rvoh")
        rmark["rvoh"][g] = p.v("rvoh")

    def rt_part3(g):
        sl = 32 + (g % 4) * 16

        def cntmm(t, g=g, sl=sl):
            t.matmul(pb[7][0:1, sl:sl + 8], ones_colb[:], oh1_all[:, g * 8:g * 8 + 8],
                     start=True, stop=True, skip_group_check=True)
            return t.matmul(pb[7][0:1, sl + 8:sl + 16], ones_colb[:], oh2_all[:, g * 8:g * 8 + 8],
                            start=True, stop=True, skip_group_check=True)
        p.step("t", cntmm, waits=[("rvoh", p.v("rvoh"))], inc="rtcnt")
        rmark["rtcnt"][g] = p.v("rtcnt")
        p.step("v", lambda v, g=g, sl=sl:
               v.tensor_copy(cnts_row[0:1, g * 16:g * 16 + 16], pb[7][0:1, sl:sl + 16]),
               waits=[("rtcnt", rmark["rtcnt"][g])], inc="rvcv")

    es_A = contextlib.ExitStack()
    asb = lambda n, s, d=F32: es_A.enter_context(nc.sbuf_tensor(n, s, d, side="right"))
    cosq = asb("cosq_s", [128, 4096], BF16)
    sinq = asb("sinq_s", [128, 4096], BF16)
    wq_b = asb("wq_b_s", [128, 8, 128], BF16)
    wk_b = asb("wk_b_s", [128, 8, 128], BF16)
    wv_b = asb("wv_b_s", [128, 8, 128], BF16)
    kq_sb = asb("kq_sb_s", [128, 2, 512], BF16)
    ttmp = asb("ttmp_s", [128, 512], BF16)
    krot = asb("krot_s", [128, 4096], BF16)
    qrot = asb("qrot_s", [128, 2048], BF16)
    vt = asb("vt_s", [128, 32, 130], BF16)
    pt_sb = asb("pt_sb_s", [128, 2, 2, 1024], BF16)
    recip_sb = asb("recip_sb_s", [128, 8])
    qkvb_vrow = asb("qkvb_vrow_s", [1, D], BF16)

    qd(lambda q: q.dma_start(out=cosq[:], in_=cosq_d))
    qd(lambda q: q.dma_start(out=sinq[:], in_=sinq_d))
    gdma(lambda g: g.dma_start(out=qkvb_vrow[:], in_=qkvbRow[0:1, 2 * D:3 * D]))
    p.step("v", lambda v: v.memset(vt[:], 1.0), inc="avones")

    def xnT_slice(ci, t0, nn):
        if t0 < 2048:
            return xnTq[:, ci, t0:t0 + nn]
        return xnTk[:, ci, t0 - 2048:t0 - 2048 + nn]

    rg = 0
    for hp in range(8):
        par = hp % 2
        wdw = [("atdone", p.v("atdone"))]
        for s, wt in ((0, wq_b), (1, wk_b), (2, wv_b)):
            gdma(lambda g, s=s, wt=wt, hp=hp:
                 g.dma_start(out=wt[:], in_=qkvw_st[s, hp]), waits=wdw)
        gd_w = p.v("gd")

        for kind in range(12):
            isq = kind >= 8
            it = kind - 8 if isq else kind
            pbm = pb[kind % 2]
            pbs = pb[2 + kind % 2]

            def kqmm(t, isq=isq, it=it, pbm=pbm):
                w = wq_b if isq else wk_b
                for ci in range(8):
                    rhs = xnTq[:, ci, it * 512:it * 512 + 512] if isq else xnT_slice(ci, it * 512, 512)
                    inst = t.matmul(pbm[:], w[:, ci, :], rhs,
                                    start=(ci == 0), stop=(ci == 7))
                return inst
            tw = [("gd", gd_w), ("aaev", p.v("aaev") - 1), ("avrp", p.v("avrp") - 1)]
            if hp == 0 and kind < 4:
                tw.append(("xvev", p.v("xvev")))
            p.step("t", kqmm, waits=tw, inc="atkm")
            bcol = hp if isq else 8 + hp
            p.step("a", lambda a, kind=kind, pbm=pbm, bcol=bcol:
                   a.activation(kq_sb[:, kind % 2, :], pbm[:], AF.Identity,
                                bias=qkvbT_sb[:, bcol:bcol + 1]),
                   waits=[("atkm", p.v("atkm"))], inc="aaev")

            def swmm(t, kind=kind, pbs=pbs):
                return t.matmul(pbs[:], pswap[:], kq_sb[:, kind % 2, :], start=True, stop=True)
            p.step("t", swmm, waits=[("aaev", p.v("aaev")), ("avrp", p.v("avrp") - 1)], inc="atsw")
            dst = qrot if isq else krot

            vseq([lambda v, kind=kind, it=it, pbs=pbs, dst=dst:
                  (v.tensor_tensor(ttmp[:], kq_sb[:, kind % 2, :],
                                   cosq[:, it * 512:it * 512 + 512], op=ALU.mult),
                   v.tensor_tensor(dst[:, it * 512:it * 512 + 512], pbs[:],
                                   sinq[:, it * 512:it * 512 + 512], op=ALU.mult))[-1],
                  lambda v, kind=kind, it=it, dst=dst:
                  v.tensor_tensor(dst[:, it * 512:it * 512 + 512],
                                  dst[:, it * 512:it * 512 + 512], ttmp[:], op=ALU.add)],
                 waits=[("atsw", p.v("atsw"))], inc="avrp")

        for tt in range(32):
            pbv = pb[4 + tt % 2]

            def vmm(t, tt=tt, pbv=pbv, hp=hp):
                for ci in range(8):
                    t.matmul(pbv[:, 0:128], xnT_slice(ci, tt * 128, 128), wv_b[:, ci, :],
                             start=(ci == 0), stop=False)
                return t.matmul(pbv[:, 0:128], ones_rowb[:],
                                qkvb_vrow[0:1, hp * 128:hp * 128 + 128],
                                start=False, stop=True)
            vw = [("gd", gd_w), ("avve", p.v("avve") - 1)]
            if hp == 0 and tt < 2:
                vw.append(("gd", p.v("gd")))
            p.step("t", vmm, waits=vw, inc="atvm")

            def vev(v, tt=tt, pbv=pbv):
                v.tensor_copy(vt[:, tt, 0:64], pbv[:, 0:64])
                return v.tensor_copy(vt[:, tt, 65:129], pbv[:, 64:128])
            p.step("v", vev, waits=[("atvm", p.v("atvm")), ("avones", p.v("avones"))], inc="avve")

        avrp_done = p.v("avrp")
        avve_done = p.v("avve")
        for qt in range(4):
            fs0 = p.v("afx")
            for kc in range(32):
                sb0 = pb[kc % 2]
                sb1 = pb[2 + kc % 2]

                def smm(t, kc=kc, qt=qt, sb0=sb0, sb1=sb1):
                    t.matmul(sb0[:], krot[0:64, kc * 128:kc * 128 + 128],
                             qrot[0:64, qt * 512:qt * 512 + 512], start=True, stop=True,
                             skip_group_check=True)
                    return t.matmul(sb1[:], krot[64:128, kc * 128:kc * 128 + 128],
                                    qrot[64:128, qt * 512:qt * 512 + 512], start=True, stop=True,
                                    skip_group_check=True)
                tw = [("avrp", avrp_done), ("avve", avve_done),
                      ("afx", fs0 + 2 * ((kc - 2) // 2 + 1) if kc >= 2 else fs0)]
                if kc == 0 and qt > 0:
                    tw.append(("avno", p.v("avno")))
                p.step("t", smm, waits=tw, inc="afs", amt=2)

                if kc % 2 == 1:
                    pj = kc // 2

                    def expev(a, pj=pj):
                        a.activation(pt_sb[:, 0, pj % 2, :], pbig[:, 0:1024], AF.Exp,
                                     scale=cbank[:, CB_SCALE:CB_SCALE + 1],
                                     bias=cbank[:, CB_EBIAS:CB_EBIAS + 1])
                        return a.activation(pt_sb[:, 1, pj % 2, :], pbig[:, 1024:2048], AF.Exp,
                                            scale=cbank[:, CB_SCALE:CB_SCALE + 1],
                                            bias=cbank[:, CB_EBIAS:CB_EBIAS + 1])
                    p.step("a", expev,
                           waits=[("afs", p.v("afs")), ("afa", p.v("afa") - 1)], inc="afx", amt=2)

                    def avmm(t, kc=kc, pj=pj):
                        for k2 in (kc - 1, kc):
                            for h in range(2):
                                pba = pb[4 + h]
                                for qs in range(4):
                                    inst = t.matmul(pba[:, qs * 65:qs * 65 + 65],
                                                    pt_sb[:, h, pj % 2,
                                                          (k2 % 2) * 512 + qs * 128:(k2 % 2) * 512 + qs * 128 + 128],
                                                    vt[:, k2, h * 65:h * 65 + 65],
                                                    start=(k2 == 0), stop=(k2 == 31),
                                                    skip_group_check=True)
                        return inst
                    p.step("t", avmm, waits=[("afx", p.v("afx"))], inc="afa")

            def norm1(v):
                for h in range(2):
                    pba = pb[4 + h]
                    for qs in range(4):
                        inst = v.reciprocal(recip_sb[:, h * 4 + qs:h * 4 + qs + 1],
                                            pba[:, qs * 65 + 64:qs * 65 + 65])
                return inst
            p.step("v", norm1, waits=[("afa", p.v("afa"))], inc="avn1")

            def norm2(v, qt=qt, hp=hp):
                for h in range(2):
                    pba = pb[4 + h]
                    for qs in range(4):
                        tt = qt * 4 + qs
                        inst = v.tensor_mul(
                            attn_tok[:, tt, (2 * hp + h) * 64:(2 * hp + h) * 64 + 64],
                            pba[:, qs * 65:qs * 65 + 64],
                            recip_sb[:, h * 4 + qs:h * 4 + qs + 1].to_broadcast([128, 64]))
                return inst
            p.step("v", norm2, waits=[("avn1", p.v("avn1"))], inc="avno")

            blk4 = range(rg, rg + 4)
            for jj, g in enumerate(blk4):
                rt_part1(g, epoch_first=(jj == 0))
            for g in range(rg - 4, rg):
                if g >= 0:
                    rt_part3(g)
            for g in blk4:
                rt_part2(g)
            rg += 4
        p.step("t", lambda t: t.nop(), waits=[("avno", p.v("avno"))], inc="atdone")

    assert rg == NTF
    for g in range(NTF - 4, NTF):
        rt_part3(g)

    gdma(lambda g: g.dma_start(out=scr_d[0:1, 0:2048], in_=cnts_row[:]),
         waits=[("rvcv", p.v("rvcv"))])
    gdma(lambda g: g.dma_start(out=cnts_t[:],
                               in_=scr_d[0:1, 0:2048].rearrange("o (g j) -> (o g) j", j=16)),
         waits=[("gd", p.v("gd"))])

    def prefmm(t):
        t.matmul(pb[6][:, 0:16], triu_rot[:], cnts_t[:], start=True, stop=True,
                 skip_group_check=True)
        return t.matmul(pb[7][0:1, 0:16], ones_col[:], cnts_t[:], start=True, stop=True,
                        skip_group_check=True)
    p.step("t", prefmm, waits=[("gd", p.v("gd")), ("rvcv", p.v("rvcv"))], inc="rtpre")

    def evpre(v):
        v.tensor_copy(pref_sb[:], pb[6][:, 0:16])
        return v.tensor_copy(tot_sb[:], pb[7][0:1, 0:16])
    p.step("v", evpre, waits=[("rtpre", p.v("rtpre"))], inc="rvpre")
    gdma(lambda g: g.dma_start(out=scr_d[0:1, 0:2048].rearrange("o (g j) -> (o g) j", j=16),
                               in_=pref_sb[:]),
         waits=[("rvpre", p.v("rvpre"))])
    gdma(lambda g: g.dma_start(out=pref_rows[0:1, 0:2048], in_=scr_d[0:1, 0:2048]),
         waits=[("gd", p.v("gd"))])
    gd_pref = p.v("gd")

    for ti in range(NTO):
        bb = pb[6 + ti % 2]

        def posmm(t, ti=ti, bb=bb):
            t.matmul(bb[:, 0:8], triu_tok[:], oh1_all[:, ti * 8:ti * 8 + 8],
                     start=True, stop=False, skip_group_check=True)
            t.matmul(bb[:, 0:8], ones_rowf[:], pref_rows[0:1, ti * 16:ti * 16 + 8],
                     start=False, stop=True, skip_group_check=True)
            t.matmul(bb[:, 8:16], triu_tok[:], oh2_all[:, ti * 8:ti * 8 + 8],
                     start=True, stop=False, skip_group_check=True)
            t.matmul(bb[:, 8:16], ones_rowf[:], pref_rows[0:1, ti * 16 + 8:ti * 16 + 16],
                     start=False, stop=False, skip_group_check=True)
            return t.matmul(bb[:, 8:16], ones_rowf[:], tot_sb[0:1, 0:8],
                            start=False, stop=True, skip_group_check=True)
        p.step("t", posmm, waits=[("gd", gd_pref), ("rvcc", p.v("rvcc") - 1)], inc="rtpos")

        vseq([lambda v, ti=ti, bb=bb:
              (v.tensor_scalar(ktmp[:, 0:8], bb[:, 0:8], CAP, None, op0=ALU.is_lt),
               v.tensor_scalar(ktmp[:, 8:16], bb[:, 8:16], CAP, None, op0=ALU.is_lt))[-1],
              lambda v, ti=ti: (v.tensor_tensor(ktmp[:, 16:24], c1_all[:, ti * 8:ti * 8 + 8],
                                                ktmp[:, 0:8], op=ALU.mult),
                                v.tensor_tensor(ktmp[:, 24:32], c2_all[:, ti * 8:ti * 8 + 8],
                                                ktmp[:, 8:16], op=ALU.mult))[-1],
              lambda v, ti=ti: v.tensor_tensor(c_coef[:, ti * 8:ti * 8 + 8],
                                               ktmp[:, 16:24], ktmp[:, 24:32], op=ALU.add)],
             waits=[("rtpos", p.v("rtpos"))], inc="rvcc")

    if debug:
        qd(lambda q: q.dma_start(out=dbg["d_cnts"], in_=cnts_t[:]),
           waits=[("rvcc", p.v("rvcc"))])
        qd(lambda q: q.dma_start(out=dbg["d_pref"], in_=pref_sb[:]))
        qd(lambda q: q.dma_start(out=dbg["d_oh"], in_=oh1_all[:, 0:32]))
        qd(lambda q: q.dma_start(out=dbg["d_ccoef"], in_=c_coef[:]))
        qd(lambda q: q.dma_start(out=dbg["d_krot"], in_=krot[:]))
        qd(lambda q: q.dma_start(out=dbg["d_attn"],
                                 in_=attn_tok[:].rearrange("p a b -> p (a b)")),
           waits=[("avno", p.v("avno"))])

    es_A.close()
    es_RX.close()
    es_XK.close()
    p.barrier()

    es_ACC = contextlib.ExitStack()
    acc = es_ACC.enter_context(nc.sbuf_tensor("acc_s", [128, NTO, D], F32, side="right"))
    es_OP = contextlib.ExitStack()
    ow_b = es_OP.enter_context(nc.sbuf_tensor("ow_b_s", [128, 8, D], BF16, side="right"))
    attnF_t = es_OP.enter_context(nc.sbuf_tensor("attnF_t_s", [128, 8, 128], BF16, side="right"))
    outb_row = es_OP.enter_context(nc.sbuf_tensor("outb_row_s", [1, D], BF16, side="right"))

    gdma(lambda g: g.dma_start(out=ow_b[:], in_=outw_st))
    gdma(lambda g: g.dma_start(out=outb_row[:], in_=outbRow))
    gd_ow = p.v("gd")
    qd(lambda q: q.dma_start(out=acc[:], in_=xown))
    qd_acc = p.v("qd")

    for tt in range(NTO):
        def optr(t, tt=tt):
            pbB = pb[0].bitcast(BF16)
            for ci in range(8):
                inst = t.transpose(pbB[:, ci * 128:ci * 128 + 128],
                                   attn_tok[:, tt, ci * 128:ci * 128 + 128], identb[:])
            return inst
        p.step("t", optr, waits=[("opve", p.v("opve"))], inc="optr")

        def opev(v):
            return v.tensor_copy(attnF_t[:, :, :],
                                 pb[0].bitcast(BF16).rearrange("p (c k) -> p c k", k=128))
        p.step("v", opev, waits=[("optr", p.v("optr"))], inc="opve")

        def opmm(t, tt=tt):
            for nh in range(2):
                pbo = pb[2 + nh]
                t.matmul(pbo[:], ones_rowb[:], outb_row[0:1, nh * 512:nh * 512 + 512],
                         start=True, stop=False, skip_group_check=True)
                for ci in range(8):
                    inst = t.matmul(pbo[:], attnF_t[:, ci, :], ow_b[:, ci, nh * 512:nh * 512 + 512],
                                    start=False, stop=(ci == 7), skip_group_check=True)
            return inst
        p.step("t", opmm, waits=[("opve", p.v("opve")), ("gd", gd_ow),
                                 ("opac", p.v("opac"))], inc="opmm")

        def opadd(v, tt=tt):
            v.tensor_tensor(acc[:, tt, 0:512], acc[:, tt, 0:512], pb[2][:], op=ALU.add)
            return v.tensor_tensor(acc[:, tt, 512:1024], acc[:, tt, 512:1024], pb[3][:], op=ALU.add)
        p.step("v", opadd, waits=[("opmm", p.v("opmm")), ("qd", qd_acc)], inc="opac")

    es_OP.close()
    p.barrier()

    es_F = contextlib.ExitStack()
    w1e = es_F.enter_context(nc.sbuf_tensor("w1e_s", [128, 2, 8, 512], BF16, side="right"))
    w2e = es_F.enter_context(nc.sbuf_tensor("w2e_s", [128, 2, 4, 1024], BF16, side="right"))
    h_sb = es_F.enter_context(nc.sbuf_tensor("h_sb_s", [128, 2, 256], BF16, side="right"))
    b2_row = es_F.enter_context(nc.sbuf_tensor("b2_row_s", [1, NE * D], BF16, side="right"))

    gdma(lambda g: g.dma_start(out=b2_row[:], in_=b2Row))
    fmarks = {}
    for e in range(NE):
        for fe in range(8):
            qi = e * 8 + fe
            wp = qi % 2
            fq, half = fe // 2, fe % 2
            wdw = [("fm2", fmarks[qi - 2])] if qi >= 2 else []
            gdma(lambda g, e=e, fq=fq, half=half, wp=wp:
                 g.dma_start(out=w1e[:, wp], in_=w1_st[e, fq][:, :, half * 512:half * 512 + 512]),
                 waits=wdw)
            gdma(lambda g, e=e, fq=fq, half=half, wp=wp:
                 g.dma_start(out=w2e[:, wp], in_=w2_st[e, fq][:, half * 4:half * 4 + 4, :]))
            gd_f = p.v("gd")
            for tq in range(8):
                def mk_mm1(fc, tq=tq, wp=wp):
                    def mm1(t, fc=fc, tq=tq, wp=wp):
                        pbh = pb[fc % 2]
                        for ci in range(8):
                            inst = t.matmul(pbh[:, 0:256], w1e[:, wp, ci, fc * 128:fc * 128 + 128],
                                            xnTq[:, ci, tq * 256:tq * 256 + 256],
                                            start=(ci == 0), stop=(ci == 7),
                                            skip_group_check=True)
                        return inst
                    return mm1

                def mk_mm2(fc, e=e, fe=fe, wp=wp):
                    def mm2(t, fc=fc, e=e, fe=fe, wp=wp):
                        inst = None
                        for stt in range(2):
                            for nh in range(2):
                                pby = pb[2 + stt * 2 + nh]
                                if fe == 0 and fc == 0:
                                    t.matmul(pby[:], ones_rowb[:],
                                             b2_row[0:1, e * D + nh * 512:e * D + nh * 512 + 512],
                                             start=True, stop=False, skip_group_check=True)
                                inst = t.matmul(pby[:], h_sb[:, fc % 2, stt * 128:stt * 128 + 128],
                                                w2e[:, wp, fc, nh * 512:nh * 512 + 512],
                                                start=(fe > 0 and fc == 0), stop=(fc == 3),
                                                skip_group_check=True)
                        return inst
                    return mm2

                def mk_gelu(fc, e=e, fe=fe):
                    def gelu(a, fc=fc, e=e, fe=fe):
                        j = e * 32 + fe * 4 + fc
                        return a.activation(h_sb[:, fc % 2, :], pb[fc % 2][:, 0:256],
                                            AF.Gelu_apprx_tanh, bias=b1T[:, j:j + 1])
                    return gelu

                for fc in range(4):
                    tw = [("gd", gd_f)]
                    if fc >= 2:
                        tw.append(("fge", p.v("fge") - 1))
                    p.step("t", mk_mm1(fc), waits=tw, inc="fmm")
                    p.step("a", mk_gelu(fc), waits=[("fmm", p.v("fmm")), ("fm2", p.v("fm2") - 1)],
                           inc="fge")
                    if fc >= 1:
                        w2w = [("fge", p.v("fge") - 1)]
                        if fc == 1 and qi + tq > 0:
                            w2w.append(("fev", p.v("fev")))
                        p.step("t", mk_mm2(fc - 1), waits=w2w, inc="fm2")
                p.step("t", mk_mm2(3), waits=[("fge", p.v("fge"))], inc="fm2")

                def fevac(v, e=e, tq=tq):
                    inst = None
                    for stx in range(2):
                        tt = tq * 2 + stx
                        cav = c_coef[:, tt * 8 + e:tt * 8 + e + 1]
                        for nh in range(2):
                            pby = pb[2 + stx * 2 + nh]
                            inst = v.scalar_tensor_tensor(
                                acc[:, tt, nh * 512:nh * 512 + 512], pby[:], cav,
                                acc[:, tt, nh * 512:nh * 512 + 512],
                                op0=ALU.mult, op1=ALU.add)
                    return inst
                p.step("v", fevac, waits=[("fm2", p.v("fm2"))], inc="fev")
            fmarks[qi] = p.v("fm2")

    es_F.close()
    p.barrier()

    qd(lambda q: q.dma_start(out=outp, in_=acc[:]))
    p.step("q", lambda q: q.nop(), waits=[("qd", p.v("qd"))])

    with nc.Block() as blk:
        p.emit(blk)
    es_ACC.close()
    es_glob.close()
    return nc


def _stage(core, x, qkv_w, qkv_b, out_w, out_b, Wg, bg, W1, b1, W2, b2):
    b = core // 2
    h = core % 2
    xf = np.ascontiguousarray(x.reshape(128, 128, D))
    rot = np.roll(np.arange(NTF), -16 * core)
    xrot = np.ascontiguousarray(xf[rot])
    xb = x.reshape(4, 4096, D)[b]
    own = xb[h * 2048:(h + 1) * 2048]
    oth = xb[(1 - h) * 2048:(2 - h) * 2048]
    xbatch = np.ascontiguousarray(
        np.concatenate([own, oth], axis=0).reshape(NTB, 128, D))
    xown = np.ascontiguousarray(own.reshape(NTO, 128, D).transpose(1, 0, 2))

    perm = np.arange(3 * D)
    for hh in range(NH):
        base = hh * HD
        for sec in (0, D):
            perm[sec + base:sec + base + 32] = sec + base + 2 * np.arange(32)
            perm[sec + base + 32:sec + base + 64] = sec + base + 2 * np.arange(32) + 1
    qkvw_p = qkv_w[:, perm]
    qkvb_p = qkv_b[perm]
    qkvw_st = np.ascontiguousarray(
        qkvw_p.reshape(8, 128, 3, 8, 128).transpose(2, 3, 1, 0, 4))
    qkvbT = np.ascontiguousarray(qkvb_p.reshape(24, 128).T)
    qkvbRow = np.ascontiguousarray(qkvb_p.reshape(1, 3 * D))
    outw_st = np.ascontiguousarray(out_w.reshape(8, 128, D).transpose(1, 0, 2))
    outbRow = np.ascontiguousarray(out_b.reshape(1, D))
    wg_stv = np.ascontiguousarray(Wg.reshape(8, 128, 8).transpose(1, 0, 2).reshape(128, 64))
    bgRow = np.ascontiguousarray(bg.reshape(1, 8))
    w1_stv = np.ascontiguousarray(
        W1.reshape(NE, 8, 128, 4, 1024).transpose(0, 3, 2, 1, 4))
    w2_stv = np.ascontiguousarray(
        W2.reshape(NE, 4, 8, 128, D).transpose(0, 1, 3, 2, 4))
    b1T = np.ascontiguousarray(b1.reshape(NE, 32, 128).transpose(2, 0, 1).reshape(128, 256))
    b2Row = np.ascontiguousarray(b2.reshape(1, NE * D))

    identf = np.eye(128, dtype=np.float32)
    identb = np.eye(128, dtype=np.float32).astype(ml_dtypes.bfloat16)
    swap = np.arange(128)
    swap = np.where((swap % 64) < 32, swap + 32, swap - 32)
    psw = np.zeros((128, 128), np.float32)
    psw[swap, np.arange(128)] = 1.0
    pswap = psw.astype(ml_dtypes.bfloat16)
    triu_tok = np.triu(np.ones((128, 128), np.float32), 1).astype(ml_dtypes.bfloat16)
    canon = rot
    triu_rot = (canon[:, None] < canon[None, :]).astype(np.float32)
    freqs = (10000.0 ** (np.arange(0, HD, 2.0, dtype=np.float32) / HD)).astype(np.float32)
    tpos = np.arange(4096, dtype=np.float32)
    stagedpos = np.concatenate([tpos[h * 2048:(h + 1) * 2048],
                                tpos[(1 - h) * 2048:(2 - h) * 2048]])
    ang = (stagedpos[:, None] * freqs[None, :]).astype(np.float32)
    cosv = np.cos(ang.astype(np.float64)).astype(np.float32)
    sinv = np.sin(ang.astype(np.float64)).astype(np.float32)
    prow = np.arange(128)
    cosq = np.ascontiguousarray(cosv[:, prow % 32].T).astype(ml_dtypes.bfloat16)
    sgn = np.where((prow % 64) < 32, -1.0, 1.0).astype(np.float32)
    sinq = np.ascontiguousarray((sinv[:, prow % 32] * sgn[None, :]).T).astype(ml_dtypes.bfloat16)

    return {
        "xrot": xrot, "xbatch": xbatch, "xown": xown,
        "qkvw_st": qkvw_st, "outw_st": outw_st, "qkvbT": qkvbT,
        "qkvbRow": qkvbRow, "outbRow": outbRow, "wg_st": wg_stv, "bgRow": bgRow,
        "w1_st": w1_stv, "w2_st": w2_stv, "b1T": b1T, "b2Row": b2Row,
        "identf": identf, "identb": identb, "pswap": pswap,
        "triu_tok": triu_tok, "triu_rot": triu_rot, "cosq": cosq, "sinq": sinq,
    }


def kernel(L0_attn_x, L0_attn_qkv_w, L0_attn_qkv_b, L0_attn_out_w, L0_attn_out_b,
           L0_ffn_Wg, L0_ffn_bg, L0_ffn_W1, L0_ffn_b1, L0_ffn_W2, L0_ffn_b2):
    x = np.ascontiguousarray(np.asarray(L0_attn_x, np.float32).reshape(16384, D))
    args = [np.asarray(a, np.float32) for a in
            (L0_attn_qkv_w, L0_attn_qkv_b, L0_attn_out_w, L0_attn_out_b,
             L0_ffn_Wg, L0_ffn_bg, L0_ffn_W1, L0_ffn_b1, L0_ffn_W2, L0_ffn_b2)]
    nc = build()
    in_maps = [_stage(c, x, *args) for c in range(NCORES)]
    res = run_bass_kernel_spmd(nc, in_maps, core_ids=list(range(NCORES)))
    shards = []
    for c in range(NCORES):
        o = np.asarray(res.results[c]["outp"])
        shards.append(o.transpose(1, 0, 2).reshape(2048, D))
    return np.concatenate(shards, axis=0).reshape(4, 4096, D).astype(np.float32)


# version 06
# speedup vs baseline: 1.4597x; 1.0079x over previous
import contextlib
import numpy as np
import ml_dtypes
import concourse.bass as bass
import concourse.mybir as mybir
from concourse.bass_utils import run_bass_kernel_spmd

F32 = mybir.dt.float32
BF16 = mybir.dt.bfloat16
AF = mybir.ActivationFunctionType
ALU = mybir.AluOpType
AX = mybir.AxisListType

D = 1024
NH = 16
HD = 64
FF = 4096
NE = 8
CAP = 4916.0
NCORES = 8
NTF = 128
NTO = 16
NTB = 32
EPS = 1e-5

CB_EPS, CB_NHALF, CB_SCALE, CB_EBIAS, CB_INVD, CB_NEG1 = 0, 1, 2, 3, 4, 5


class Plan:
    ORDER = (("q", "sync"), ("g", "gpsimd"), ("t", "tensor"), ("v", "vector"), ("a", "scalar"))

    def __init__(self, nc, es):
        self.nc = nc
        self.es = es
        self.steps = {e: [] for e, _ in self.ORDER}
        self.sems = {}
        self.cnt = {}

    def sem(self, name):
        if name not in self.sems:
            self.sems[name] = self.es.enter_context(self.nc.semaphore(name))
            self.cnt[name] = 0
        return self.sems[name]

    def step(self, eng, fn, waits=(), inc=None, amt=1):
        waits = tuple((s, v) for s, v in waits if v > 0)
        for s, _ in waits:
            self.sem(s)
        if inc is not None:
            self.sem(inc)
            self.cnt[inc] += amt
        self.steps[eng].append((fn, waits, inc, amt))
        return self.cnt[inc] if inc is not None else None

    def v(self, name):
        self.sem(name)
        return self.cnt[name]

    def barrier(self):
        for e in ("q", "g", "t", "v", "a"):
            w = []
            if e == "q":
                w.append(("qd", self.v("qd")))
            if e == "g":
                w.append(("gd", self.v("gd")))
            self.step(e, lambda en: en.nop(), waits=w, inc="bar")
        tot = self.v("bar")
        for e in ("q", "g", "t", "v", "a"):
            self.step(e, lambda en: en.nop(), waits=[("bar", tot)])

    def emit(self, blk):
        for engname, attr in self.ORDER:
            steps = self.steps[engname]

            def body(e, steps=steps):
                for fn, waits, inc, amt in steps:
                    for s, val in waits:
                        e.wait_ge(self.sems[s], val)
                    inst = fn(e)
                    if inc is not None:
                        assert inst is not None
                        inst.then_inc(self.sems[inc], amt)
            getattr(blk, attr)(body)


def build(debug=False):
    nc = bass.Bass("TRN2", target_bir_lowering=False)

    dt_in = lambda n, s, d=F32: nc.dram_tensor(n, s, d, kind="ExternalInput").ap()
    xrot = dt_in("xrot", [NTF, 128, D])
    xbatch = dt_in("xbatch", [NTB, 128, D])
    xown = dt_in("xown", [128, NTO, D])
    qkvw_st = dt_in("qkvw_st", [3, 8, 128, 8, 128])
    outw_st = dt_in("outw_st", [128, 8, D])
    qkvbT = dt_in("qkvbT", [128, 24])
    qkvbRow = dt_in("qkvbRow", [1, 3 * D])
    outbRow = dt_in("outbRow", [1, D])
    wg_st = dt_in("wg_st", [128, 64])
    bgRow = dt_in("bgRow", [1, 8])
    w1_st = dt_in("w1_st", [NE, 4, 128, 8, 1024])
    w2_st = dt_in("w2_st", [NE, 4, 128, 8, 1024])
    b1T_d = dt_in("b1T", [128, 256])
    b2Row = dt_in("b2Row", [1, NE * D])
    identf_d = dt_in("identf", [128, 128])
    identb_d = dt_in("identb", [128, 128], BF16)
    pswap_d = dt_in("pswap", [128, 128], BF16)
    triu_tok_d = dt_in("triu_tok", [128, 128], BF16)
    triu_rot_d = dt_in("triu_rot", [128, 128])
    cosq_d = dt_in("cosq", [128, 4096], BF16)
    sinq_d = dt_in("sinq", [128, 4096], BF16)
    outp = nc.dram_tensor("outp", [128, NTO, D], F32, kind="ExternalOutput").ap()
    scr_d = nc.dram_tensor("scr", [1, 4096], F32, kind="Internal").ap()
    dbg = {}
    if debug:
        for n, s, d in [("d_ccoef", [128, 128], F32), ("d_cnts", [128, 16], F32),
                        ("d_pref", [128, 16], F32), ("d_xnTq", [128, 2048], BF16),
                        ("d_attn", [128, NTO * D], BF16),
                        ("d_krot", [128, 4096], BF16), ("d_oh", [128, 32], F32)]:
            dbg[n] = nc.dram_tensor(n, s, d, kind="ExternalOutput").ap()

    es_glob = contextlib.ExitStack()
    p = Plan(nc, es_glob)
    sb = lambda n, s, d=F32: es_glob.enter_context(nc.sbuf_tensor(n, s, d))

    identf = sb("identf_s", [128, 128])
    identb = sb("identb_s", [128, 128], BF16)
    pswap = sb("pswap_s", [128, 128], BF16)
    triu_tok = sb("triu_tok_s", [128, 128], BF16)
    triu_rot = sb("triu_rot_s", [128, 128])
    wg_sb = sb("wg_sb_s", [128, 64])
    bg_sb = sb("bg_sb_s", [1, 8])
    qkvbT_sb = sb("qkvbT_sb_s", [128, 24])
    b1T = sb("b1T_sb_s", [128, 256])
    cbank = sb("cbank_s", [128, 8])
    ones_col = sb("ones_col_s", [128, 1])
    ones_colb = sb("ones_colb_s", [128, 1], BF16)
    ones_rowf = sb("ones_rowf_s", [1, 128])
    ones_rowb = sb("ones_rowb_s", [1, 128], BF16)
    zeros8 = sb("zeros8_s", [128, 8])
    c_coef = sb("c_coef_s", [128, NTO * 8])
    xnTq = sb("xnTq_s", [128, 8, 2048], BF16)
    attn_tok = sb("attn_tok_s", [128, NTO, D], BF16)

    pbig = es_glob.enter_context(nc.psum_tensor("pbig", [128, 2048], F32))
    pb = [pbig[:, i * 512:(i + 1) * 512] for i in range(4)]
    pb += [es_glob.enter_context(nc.psum_tensor(f"pb{i}", [128, 512], F32))[:] for i in range(4, 8)]

    def qd(fn, waits=()):
        return p.step("q", fn, waits=waits, inc="qd", amt=16)

    def gdma(fn, waits=()):
        return p.step("g", fn, waits=waits, inc="gd", amt=16)

    def vseq(fns, waits=(), inc=None):
        for i, fn in enumerate(fns):
            w = list(waits) + [("vch", p.v("vch"))] if i == 0 else [("vch", p.v("vch"))]
            p.step("v", fn, waits=w, inc=(inc if (i == len(fns) - 1 and inc) else "vch"))

    qd(lambda q: q.dma_start(out=identf[:], in_=identf_d))
    qd(lambda q: q.dma_start(out=triu_tok[:], in_=triu_tok_d))
    qd(lambda q: q.dma_start(out=triu_rot[:], in_=triu_rot_d))
    qd(lambda q: q.dma_start(out=wg_sb[:], in_=wg_st))
    qd(lambda q: q.dma_start(out=bg_sb[:], in_=bgRow))
    qd(lambda q: q.dma_start(out=qkvbT_sb[:], in_=qkvbT))
    qd(lambda q: q.dma_start(out=b1T[:], in_=b1T_d))
    qd(lambda q: q.dma_start(out=identb[:], in_=identb_d))
    qd(lambda q: q.dma_start(out=pswap[:], in_=pswap_d))

    def memsets(g):
        g.memset(cbank[:, CB_EPS:CB_EPS + 1], EPS)
        g.memset(cbank[:, CB_NHALF:CB_NHALF + 1], -0.5)
        g.memset(cbank[:, CB_SCALE:CB_SCALE + 1], 0.125)
        g.memset(cbank[:, CB_EBIAS:CB_EBIAS + 1], -12.0)
        g.memset(cbank[:, CB_INVD:CB_INVD + 1], 1.0 / D)
        g.memset(cbank[:, CB_NEG1:CB_NEG1 + 1], -1.0)
        g.memset(ones_col[:], 1.0)
        g.memset(ones_colb[:], 1.0)
        g.memset(ones_rowf[:], 1.0)
        g.memset(ones_rowb[:], 1.0)
        return g.memset(zeros8[:], 0.0)
    p.step("g", memsets, inc="gcst")
    p.barrier()

    es_XK = contextlib.ExitStack()
    xnTk = es_XK.enter_context(nc.sbuf_tensor("xnTk_s", [128, 8, 2048], BF16, side="right"))
    es_RX = contextlib.ExitStack()
    rsb = lambda n, s, d=F32: es_RX.enter_context(nc.sbuf_tensor(n, s, d, side="right"))
    xt2 = rsb("xt2_s", [128, 2, D])
    stat = rsb("stat_s", [128, 2, 12])
    xn32 = rsb("xn32_s", [128, 2, D])
    xnbf2 = rsb("xnbf2_s", [128, 2, D], BF16)
    xT32 = rsb("xT32_s", [128, D])
    l8 = rsb("l8_s", [128, 8])
    top8 = rsb("top8_s", [128, 8])
    tsc1 = rsb("tsc1_s", [128, 8])
    tsc2 = rsb("tsc2_s", [128, 8])
    gat = rsb("gat_s", [128, 8])
    oh1_all = rsb("oh1_all_s", [128, NTF * 8], BF16)
    oh2_all = rsb("oh2_all_s", [128, NTF * 8], BF16)
    c1_all = rsb("c1_all_s", [128, NTF * 8], BF16)
    c2_all = rsb("c2_all_s", [128, NTF * 8], BF16)
    cnts_row = rsb("cnts_row_s", [1, NTF * 16])
    pref_rows = cnts_row
    cnts_t = rsb("cnts_t_s", [128, 16])
    pref_sb = rsb("pref_sb_s", [128, 16])
    tot_sb = rsb("tot_sb_s", [1, 16])
    ktmp = rsb("ktmp_s", [128, 32])

    def ln_chain(n, src_tile, dump, aw=()):
        par = n % 2
        st = lambda c: stat[:, par, c:c + 1]
        wv = [("qd", p.v("qd")), ("lnarst", p.v("lnarst") - 1)]
        vseq([lambda v: v.reduce_sum(st(0), src_tile, axis=AX.X),
              lambda v: v.tensor_scalar(st(2), st(0), 1.0 / D, None, op0=ALU.mult),
              lambda v: v.scalar_tensor_tensor(st(3), st(2), -1.0, st(2),
                                               op0=ALU.mult, op1=ALU.mult)],
             waits=wv, inc="lnvst")
        p.step("a", lambda a: a.activation(dump, src_tile, AF.Square, accum_out=st(1)),
               waits=[("qd", p.v("qd"))] + list(aw), inc="lnassq")
        vseq([lambda v: v.scalar_tensor_tensor(st(4), st(1), 1.0 / D, st(3),
                                               op0=ALU.mult, op1=ALU.add)],
             waits=[("lnassq", p.v("lnassq"))], inc="lnvvar")

        def archain(a):
            a.activation(st(5), st(4), AF.Ln, bias=cbank[:, CB_EPS:CB_EPS + 1])
            return a.activation(st(6), st(5), AF.Exp, scale=cbank[:, CB_NHALF:CB_NHALF + 1])
        p.step("a", archain, waits=[("lnvvar", p.v("lnvvar"))], inc="lnarst")
        return st

    for bt in range(NTB):
        par = bt % 2
        xt = xt2[:, par, :]
        qd(lambda q, bt=bt, par=par: q.dma_start(out=xt2[:, par, :], in_=xbatch[bt]),
           waits=[("lnaxn", p.v("lnaxn") - 1)])
        st = ln_chain(bt, xt, xn32[:, par, :])
        vseq([lambda v, xt=xt, par=par, st=st:
              v.tensor_scalar(xnbf2[:, par, :], xt, st(2), st(6),
                              op0=ALU.subtract, op1=ALU.mult)],
             waits=[("xttr", p.v("xttr") - 1), ("lnarst", p.v("lnarst"))], inc="lnaxn")
        b0 = 0 if par == 0 else 2

        def transb(t, b0=b0, par=par):
            pbB = pb[b0].bitcast(BF16)
            for ci in range(8):
                inst = t.transpose(pbB[:, ci * 128:ci * 128 + 128],
                                   xnbf2[:, par, ci * 128:ci * 128 + 128], identb[:])
            return inst
        p.step("t", transb, waits=[("lnaxn", p.v("lnaxn")), ("xvev", p.v("xvev") - 1)],
               inc="xttr")
        tgt = xnTq if bt < NTO else xnTk
        off = (bt % NTO) * 128

        def evtrb(v, b0=b0, tgt=tgt, off=off):
            return v.tensor_copy(tgt[:, :, off:off + 128],
                                 pb[b0].bitcast(BF16).rearrange("p (c k) -> p c k", k=128))
        p.step("v", evtrb, waits=[("xttr", p.v("xttr"))], inc="xvev")

    if debug:
        qd(lambda q: q.dma_start(out=dbg["d_xnTq"], in_=xnTq[:, 0, :]),
           waits=[("xvev", p.v("xvev"))])

    p.barrier()

    rmark = {"rtlg": {}, "rvoh": {}, "rtcnt": {}}

    def rt_part1(g, epoch_first):
        par = g % 2
        xt = xt2[:, par, :]
        qd(lambda q, g=g, par=par: q.dma_start(out=xt2[:, par, :], in_=xrot[g]),
           waits=[("lnaxn", p.v("lnaxn") - 1)])
        aw = [("xttr", p.v("xttr"))] if g < 2 else ()
        st = ln_chain(NTB + g, xt, xnbf2[:, par, :], aw)
        vseq([lambda v, xt=xt, par=par, st=st:
              v.tensor_scalar(xn32[:, par, :], xt, st(2), st(6),
                              op0=ALU.subtract, op1=ALU.mult)],
             waits=[("rttr", p.v("rttr") - 2), ("lnarst", p.v("lnarst"))], inc="lnaxn")

        for hfi in range(2):
            def trh(t, hfi=hfi, par=par):
                for ci in range(4):
                    cc = hfi * 4 + ci
                    inst = t.transpose(pb[6][:, ci * 128:ci * 128 + 128],
                                       xn32[:, par, cc * 128:cc * 128 + 128], identf[:])
                return inst
            tw = [("rvev", p.v("rvev"))]
            if hfi == 0:
                tw.append(("lnaxn", p.v("lnaxn")))
            p.step("t", trh, waits=tw, inc="rttr")
            p.step("v", lambda v, hfi=hfi: v.tensor_copy(xT32[:, hfi * 512:hfi * 512 + 512], pb[6]),
                   waits=[("rttr", p.v("rttr"))], inc="rvev")

        sl = (g % 4) * 8

        def lgmm(t, sl=sl):
            for ci in range(8):
                t.matmul(pb[7][:, sl:sl + 8], xT32[:, ci * 128:ci * 128 + 128],
                         wg_sb[:, ci * 8:ci * 8 + 8], start=(ci == 0), stop=False)
            return t.matmul(pb[7][:, sl:sl + 8], ones_rowf[:], bg_sb[:], start=False, stop=True)
        lw = [("rvev", p.v("rvev"))]
        if epoch_first:
            lw += [("rvcp", p.v("rvcp")), ("rvcv", p.v("rvcv"))]
        p.step("t", lgmm, waits=lw, inc="rtlg")
        rmark["rtlg"][g] = p.v("rtlg")

    def rt_part2(g):
        sl = (g % 4) * 8
        p.step("v", lambda v, sl=sl: v.tensor_copy(l8[:], pb[7][:, sl:sl + 8]),
               waits=[("rtlg", rmark["rtlg"][g]), ("rtcnt", p.v("rtcnt"))], inc="rvcp")
        p.step("v", lambda v: v.max(top8[:], l8[:]),
               waits=[("rvcp", p.v("rvcp"))], inc="rvmx")
        p.step("v", lambda v: v.tensor_sub(gat[:, 0:1], top8[:, 1:2], top8[:, 0:1]),
               waits=[("rvmx", p.v("rvmx"))], inc="rvl8")
        p.step("a", lambda a: a.activation(gat[:, 3:4], gat[:, 0:1], AF.Exp),
               waits=[("rvl8", p.v("rvl8"))], inc="raeg")

        vseq([lambda v: (v.tensor_sub(tsc1[:], l8[:], top8[:, 0:1].to_broadcast([128, 8])),
                         v.tensor_scalar(gat[:, 1:2], gat[:, 0:1], 0.0, None, op0=ALU.is_equal))[-1],
              lambda v: (v.tensor_scalar(tsc1[:], tsc1[:], 0.0, None, op0=ALU.is_equal),
                         v.tensor_scalar(gat[:, 2:3], gat[:, 1:2], 1.0, None, op0=ALU.add))[-1],
              lambda v: v.tensor_tensor_scan(tsc2[:], tsc1[:], zeros8[:], 0.0, op0=ALU.add, op1=ALU.add),
              lambda v: v.tensor_scalar(tsc2[:], tsc2[:], 1.0, None, op0=ALU.is_equal),
              lambda v, g=g: v.tensor_tensor(oh1_all[:, g * 8:g * 8 + 8], tsc1[:], tsc2[:], op=ALU.mult)],
             waits=[("raeg", p.v("raeg"))], inc="rvo1")

        vseq([lambda v: (v.tensor_sub(tsc1[:], l8[:], top8[:, 1:2].to_broadcast([128, 8])),
                         v.tensor_scalar(gat[:, 4:5], gat[:, 3:4], 1.0, None, op0=ALU.add))[-1],
              lambda v: (v.tensor_scalar(tsc1[:], tsc1[:], 0.0, None, op0=ALU.is_equal),
                         v.reciprocal(gat[:, 5:6], gat[:, 4:5]))[-1],
              lambda v: v.tensor_tensor_scan(tsc2[:], tsc1[:], zeros8[:], 0.0, op0=ALU.add, op1=ALU.add),
              lambda v: v.tensor_sub(tsc2[:], tsc2[:], gat[:, 2:3].to_broadcast([128, 8])),
              lambda v: v.tensor_scalar(tsc2[:], tsc2[:], 0.0, None, op0=ALU.is_equal),
              lambda v, g=g: v.tensor_tensor(oh2_all[:, g * 8:g * 8 + 8], tsc1[:], tsc2[:], op=ALU.mult)],
             waits=[("rvo1", p.v("rvo1"))], inc="rvo2")

        vseq([lambda v: v.tensor_tensor(gat[:, 6:7], gat[:, 3:4], gat[:, 5:6], op=ALU.mult),
              lambda v, g=g: (v.tensor_mul(c1_all[:, g * 8:g * 8 + 8], oh1_all[:, g * 8:g * 8 + 8],
                                           gat[:, 5:6].to_broadcast([128, 8])),
                              v.tensor_mul(c2_all[:, g * 8:g * 8 + 8], oh2_all[:, g * 8:g * 8 + 8],
                                           gat[:, 6:7].to_broadcast([128, 8])))[-1]],
             waits=[("rvo2", p.v("rvo2"))], inc="rvoh")
        rmark["rvoh"][g] = p.v("rvoh")

    def rt_part3(g):
        sl = 32 + (g % 4) * 16

        def cntmm(t, g=g, sl=sl):
            t.matmul(pb[7][0:1, sl:sl + 8], ones_colb[:], oh1_all[:, g * 8:g * 8 + 8],
                     start=True, stop=True, skip_group_check=True)
            return t.matmul(pb[7][0:1, sl + 8:sl + 16], ones_colb[:], oh2_all[:, g * 8:g * 8 + 8],
                            start=True, stop=True, skip_group_check=True)
        p.step("t", cntmm, waits=[("rvoh", p.v("rvoh"))], inc="rtcnt")
        rmark["rtcnt"][g] = p.v("rtcnt")
        p.step("v", lambda v, g=g, sl=sl:
               v.tensor_copy(cnts_row[0:1, g * 16:g * 16 + 16], pb[7][0:1, sl:sl + 16]),
               waits=[("rtcnt", rmark["rtcnt"][g])], inc="rvcv")

    es_A = contextlib.ExitStack()
    asb = lambda n, s, d=F32: es_A.enter_context(nc.sbuf_tensor(n, s, d, side="right"))
    cosq = asb("cosq_s", [128, 4096], BF16)
    sinq = asb("sinq_s", [128, 4096], BF16)
    wq_b = asb("wq_b_s", [128, 8, 128], BF16)
    wk_b = asb("wk_b_s", [128, 8, 128], BF16)
    wv_b = asb("wv_b_s", [128, 8, 128], BF16)
    kq_sb = asb("kq_sb_s", [128, 2, 512], BF16)
    ttmp = asb("ttmp_s", [128, 512], BF16)
    krot = asb("krot_s", [128, 4096], BF16)
    qrot = asb("qrot_s", [128, 2048], BF16)
    vt = asb("vt_s", [128, 32, 130], BF16)
    pt_sb = asb("pt_sb_s", [128, 2, 2, 1024], BF16)
    recip_sb = asb("recip_sb_s", [128, 8])
    qkvb_vrow = asb("qkvb_vrow_s", [1, D], BF16)

    qd(lambda q: q.dma_start(out=cosq[:], in_=cosq_d))
    qd(lambda q: q.dma_start(out=sinq[:], in_=sinq_d))
    gdma(lambda g: g.dma_start(out=qkvb_vrow[:], in_=qkvbRow[0:1, 2 * D:3 * D]))
    p.step("v", lambda v: v.memset(vt[:], 1.0), inc="avones")

    def xnT_slice(ci, t0, nn):
        if t0 < 2048:
            return xnTq[:, ci, t0:t0 + nn]
        return xnTk[:, ci, t0 - 2048:t0 - 2048 + nn]

    rg = 0
    for hp in range(8):
        par = hp % 2
        wdw = [("atdone", p.v("atdone"))]
        for s, wt in ((0, wq_b), (1, wk_b), (2, wv_b)):
            gdma(lambda g, s=s, wt=wt, hp=hp:
                 g.dma_start(out=wt[:], in_=qkvw_st[s, hp]), waits=wdw)
        gd_w = p.v("gd")

        for kind in range(12):
            isq = kind >= 8
            it = kind - 8 if isq else kind
            pbm = pb[kind % 2]
            pbs = pb[2 + kind % 2]

            def kqmm(t, isq=isq, it=it, pbm=pbm):
                w = wq_b if isq else wk_b
                for ci in range(8):
                    rhs = xnTq[:, ci, it * 512:it * 512 + 512] if isq else xnT_slice(ci, it * 512, 512)
                    inst = t.matmul(pbm[:], w[:, ci, :], rhs,
                                    start=(ci == 0), stop=(ci == 7))
                return inst
            tw = [("gd", gd_w), ("aaev", p.v("aaev") - 1), ("avrp", p.v("avrp") - 1)]
            if hp == 0 and kind < 4:
                tw.append(("xvev", p.v("xvev")))
            p.step("t", kqmm, waits=tw, inc="atkm")
            bcol = hp if isq else 8 + hp
            p.step("a", lambda a, kind=kind, pbm=pbm, bcol=bcol:
                   a.activation(kq_sb[:, kind % 2, :], pbm[:], AF.Identity,
                                bias=qkvbT_sb[:, bcol:bcol + 1]),
                   waits=[("atkm", p.v("atkm"))], inc="aaev")

            def swmm(t, kind=kind, pbs=pbs):
                return t.matmul(pbs[:], pswap[:], kq_sb[:, kind % 2, :], start=True, stop=True)
            p.step("t", swmm, waits=[("aaev", p.v("aaev")), ("avrp", p.v("avrp") - 1)], inc="atsw")
            dst = qrot if isq else krot

            vseq([lambda v, kind=kind, it=it, pbs=pbs, dst=dst:
                  (v.tensor_tensor(ttmp[:], kq_sb[:, kind % 2, :],
                                   cosq[:, it * 512:it * 512 + 512], op=ALU.mult),
                   v.tensor_tensor(dst[:, it * 512:it * 512 + 512], pbs[:],
                                   sinq[:, it * 512:it * 512 + 512], op=ALU.mult))[-1],
                  lambda v, kind=kind, it=it, dst=dst:
                  v.tensor_tensor(dst[:, it * 512:it * 512 + 512],
                                  dst[:, it * 512:it * 512 + 512], ttmp[:], op=ALU.add)],
                 waits=[("atsw", p.v("atsw"))], inc="avrp")

        for tt in range(32):
            pbv = pb[4 + tt % 2]

            def vmm(t, tt=tt, pbv=pbv, hp=hp):
                for ci in range(8):
                    t.matmul(pbv[:, 0:128], xnT_slice(ci, tt * 128, 128), wv_b[:, ci, :],
                             start=(ci == 0), stop=False)
                return t.matmul(pbv[:, 0:128], ones_rowb[:],
                                qkvb_vrow[0:1, hp * 128:hp * 128 + 128],
                                start=False, stop=True)
            vw = [("gd", gd_w), ("avve", p.v("avve") - 1)]
            if hp == 0 and tt < 2:
                vw.append(("gd", p.v("gd")))
            p.step("t", vmm, waits=vw, inc="atvm")

            def vev(v, tt=tt, pbv=pbv):
                v.tensor_copy(vt[:, tt, 0:64], pbv[:, 0:64])
                return v.tensor_copy(vt[:, tt, 65:129], pbv[:, 64:128])
            p.step("v", vev, waits=[("atvm", p.v("atvm")), ("avones", p.v("avones"))], inc="avve")

        avrp_done = p.v("avrp")
        avve_done = p.v("avve")
        for qt in range(4):
            fs0 = p.v("afx")
            for kc in range(32):
                sb0 = pb[kc % 2]
                sb1 = pb[2 + kc % 2]

                def smm(t, kc=kc, qt=qt, sb0=sb0, sb1=sb1):
                    t.matmul(sb0[:], krot[0:64, kc * 128:kc * 128 + 128],
                             qrot[0:64, qt * 512:qt * 512 + 512], start=True, stop=True,
                             skip_group_check=True)
                    return t.matmul(sb1[:], krot[64:128, kc * 128:kc * 128 + 128],
                                    qrot[64:128, qt * 512:qt * 512 + 512], start=True, stop=True,
                                    skip_group_check=True)
                tw = [("avrp", avrp_done), ("avve", avve_done),
                      ("afx", fs0 + 2 * ((kc - 2) // 2 + 1) if kc >= 2 else fs0)]
                if kc == 0 and qt > 0:
                    tw.append(("avno", p.v("avno")))
                p.step("t", smm, waits=tw, inc="afs", amt=2)

                if kc % 2 == 1:
                    pj = kc // 2

                    def expev(a, pj=pj):
                        a.activation(pt_sb[:, 0, pj % 2, :], pbig[:, 0:1024], AF.Exp,
                                     scale=cbank[:, CB_SCALE:CB_SCALE + 1],
                                     bias=cbank[:, CB_EBIAS:CB_EBIAS + 1])
                        return a.activation(pt_sb[:, 1, pj % 2, :], pbig[:, 1024:2048], AF.Exp,
                                            scale=cbank[:, CB_SCALE:CB_SCALE + 1],
                                            bias=cbank[:, CB_EBIAS:CB_EBIAS + 1])
                    p.step("a", expev,
                           waits=[("afs", p.v("afs")), ("afa", p.v("afa") - 1)], inc="afx", amt=2)

                    def avmm(t, kc=kc, pj=pj):
                        for k2 in (kc - 1, kc):
                            for h in range(2):
                                pba = pb[4 + h]
                                for qs in range(4):
                                    inst = t.matmul(pba[:, qs * 65:qs * 65 + 65],
                                                    pt_sb[:, h, pj % 2,
                                                          (k2 % 2) * 512 + qs * 128:(k2 % 2) * 512 + qs * 128 + 128],
                                                    vt[:, k2, h * 65:h * 65 + 65],
                                                    start=(k2 == 0), stop=(k2 == 31),
                                                    skip_group_check=True)
                        return inst
                    p.step("t", avmm, waits=[("afx", p.v("afx"))], inc="afa")

            def norm1(v):
                for h in range(2):
                    pba = pb[4 + h]
                    for qs in range(4):
                        inst = v.reciprocal(recip_sb[:, h * 4 + qs:h * 4 + qs + 1],
                                            pba[:, qs * 65 + 64:qs * 65 + 65])
                return inst
            p.step("v", norm1, waits=[("afa", p.v("afa"))], inc="avn1")

            def norm2(v, qt=qt, hp=hp):
                for h in range(2):
                    pba = pb[4 + h]
                    for qs in range(4):
                        tt = qt * 4 + qs
                        inst = v.tensor_mul(
                            attn_tok[:, tt, (2 * hp + h) * 64:(2 * hp + h) * 64 + 64],
                            pba[:, qs * 65:qs * 65 + 64],
                            recip_sb[:, h * 4 + qs:h * 4 + qs + 1].to_broadcast([128, 64]))
                return inst
            p.step("v", norm2, waits=[("avn1", p.v("avn1"))], inc="avno")

            blk4 = range(rg, rg + 4)
            for jj, g in enumerate(blk4):
                rt_part1(g, epoch_first=(jj == 0))
            for g in range(rg - 4, rg):
                if g >= 0:
                    rt_part3(g)
            for g in blk4:
                rt_part2(g)
            rg += 4
        p.step("t", lambda t: t.nop(), waits=[("avno", p.v("avno"))], inc="atdone")

    assert rg == NTF
    for g in range(NTF - 4, NTF):
        rt_part3(g)

    gdma(lambda g: g.dma_start(out=scr_d[0:1, 0:2048], in_=cnts_row[:]),
         waits=[("rvcv", p.v("rvcv"))])
    gdma(lambda g: g.dma_start(out=cnts_t[:],
                               in_=scr_d[0:1, 0:2048].rearrange("o (g j) -> (o g) j", j=16)),
         waits=[("gd", p.v("gd"))])

    def prefmm(t):
        t.matmul(pb[6][:, 0:16], triu_rot[:], cnts_t[:], start=True, stop=True,
                 skip_group_check=True)
        return t.matmul(pb[7][0:1, 0:16], ones_col[:], cnts_t[:], start=True, stop=True,
                        skip_group_check=True)
    p.step("t", prefmm, waits=[("gd", p.v("gd")), ("rvcv", p.v("rvcv"))], inc="rtpre")

    def evpre(v):
        v.tensor_copy(pref_sb[:], pb[6][:, 0:16])
        return v.tensor_copy(tot_sb[:], pb[7][0:1, 0:16])
    p.step("v", evpre, waits=[("rtpre", p.v("rtpre"))], inc="rvpre")
    gdma(lambda g: g.dma_start(out=scr_d[0:1, 0:2048].rearrange("o (g j) -> (o g) j", j=16),
                               in_=pref_sb[:]),
         waits=[("rvpre", p.v("rvpre"))])
    gdma(lambda g: g.dma_start(out=pref_rows[0:1, 0:2048], in_=scr_d[0:1, 0:2048]),
         waits=[("gd", p.v("gd"))])
    gd_pref = p.v("gd")

    for ti in range(NTO):
        bb = pb[6 + ti % 2]

        def posmm(t, ti=ti, bb=bb):
            t.matmul(bb[:, 0:8], triu_tok[:], oh1_all[:, ti * 8:ti * 8 + 8],
                     start=True, stop=False, skip_group_check=True)
            t.matmul(bb[:, 0:8], ones_rowf[:], pref_rows[0:1, ti * 16:ti * 16 + 8],
                     start=False, stop=True, skip_group_check=True)
            t.matmul(bb[:, 8:16], triu_tok[:], oh2_all[:, ti * 8:ti * 8 + 8],
                     start=True, stop=False, skip_group_check=True)
            t.matmul(bb[:, 8:16], ones_rowf[:], pref_rows[0:1, ti * 16 + 8:ti * 16 + 16],
                     start=False, stop=False, skip_group_check=True)
            return t.matmul(bb[:, 8:16], ones_rowf[:], tot_sb[0:1, 0:8],
                            start=False, stop=True, skip_group_check=True)
        p.step("t", posmm, waits=[("gd", gd_pref), ("rvcc", p.v("rvcc") - 1)], inc="rtpos")

        vseq([lambda v, ti=ti, bb=bb:
              (v.tensor_scalar(ktmp[:, 0:8], bb[:, 0:8], CAP, None, op0=ALU.is_lt),
               v.tensor_scalar(ktmp[:, 8:16], bb[:, 8:16], CAP, None, op0=ALU.is_lt))[-1],
              lambda v, ti=ti: (v.tensor_tensor(ktmp[:, 16:24], c1_all[:, ti * 8:ti * 8 + 8],
                                                ktmp[:, 0:8], op=ALU.mult),
                                v.tensor_tensor(ktmp[:, 24:32], c2_all[:, ti * 8:ti * 8 + 8],
                                                ktmp[:, 8:16], op=ALU.mult))[-1],
              lambda v, ti=ti: v.tensor_tensor(c_coef[:, ti * 8:ti * 8 + 8],
                                               ktmp[:, 16:24], ktmp[:, 24:32], op=ALU.add)],
             waits=[("rtpos", p.v("rtpos"))], inc="rvcc")

    if debug:
        qd(lambda q: q.dma_start(out=dbg["d_cnts"], in_=cnts_t[:]),
           waits=[("rvcc", p.v("rvcc"))])
        qd(lambda q: q.dma_start(out=dbg["d_pref"], in_=pref_sb[:]))
        qd(lambda q: q.dma_start(out=dbg["d_oh"], in_=oh1_all[:, 0:32]))
        qd(lambda q: q.dma_start(out=dbg["d_ccoef"], in_=c_coef[:]))
        qd(lambda q: q.dma_start(out=dbg["d_krot"], in_=krot[:]))
        qd(lambda q: q.dma_start(out=dbg["d_attn"],
                                 in_=attn_tok[:].rearrange("p a b -> p (a b)")),
           waits=[("avno", p.v("avno"))])

    es_A.close()
    es_RX.close()
    es_XK.close()
    p.barrier()

    es_ACC = contextlib.ExitStack()
    acc = es_ACC.enter_context(nc.sbuf_tensor("acc_s", [128, NTO, D], F32, side="right"))
    es_OP = contextlib.ExitStack()
    ow_b = es_OP.enter_context(nc.sbuf_tensor("ow_b_s", [128, 8, D], BF16, side="right"))
    attnF_t = es_OP.enter_context(nc.sbuf_tensor("attnF_t_s", [128, 8, 128], BF16, side="right"))
    outb_row = es_OP.enter_context(nc.sbuf_tensor("outb_row_s", [1, D], BF16, side="right"))

    gdma(lambda g: g.dma_start(out=ow_b[:], in_=outw_st))
    gdma(lambda g: g.dma_start(out=outb_row[:], in_=outbRow))
    gd_ow = p.v("gd")
    qd(lambda q: q.dma_start(out=acc[:], in_=xown))
    qd_acc = p.v("qd")

    for tt in range(NTO):
        def optr(t, tt=tt):
            pbB = pb[0].bitcast(BF16)
            for ci in range(8):
                inst = t.transpose(pbB[:, ci * 128:ci * 128 + 128],
                                   attn_tok[:, tt, ci * 128:ci * 128 + 128], identb[:])
            return inst
        p.step("t", optr, waits=[("opve", p.v("opve"))], inc="optr")

        def opev(v):
            return v.tensor_copy(attnF_t[:, :, :],
                                 pb[0].bitcast(BF16).rearrange("p (c k) -> p c k", k=128))
        p.step("v", opev, waits=[("optr", p.v("optr"))], inc="opve")

        def opmm(t, tt=tt):
            for nh in range(2):
                pbo = pb[2 + nh]
                t.matmul(pbo[:], ones_rowb[:], outb_row[0:1, nh * 512:nh * 512 + 512],
                         start=True, stop=False, skip_group_check=True)
                for ci in range(8):
                    inst = t.matmul(pbo[:], attnF_t[:, ci, :], ow_b[:, ci, nh * 512:nh * 512 + 512],
                                    start=False, stop=(ci == 7), skip_group_check=True)
            return inst
        p.step("t", opmm, waits=[("opve", p.v("opve")), ("gd", gd_ow),
                                 ("opac", p.v("opac"))], inc="opmm")

        def opadd(v, tt=tt):
            v.tensor_tensor(acc[:, tt, 0:512], acc[:, tt, 0:512], pb[2][:], op=ALU.add)
            return v.tensor_tensor(acc[:, tt, 512:1024], acc[:, tt, 512:1024], pb[3][:], op=ALU.add)
        p.step("v", opadd, waits=[("opmm", p.v("opmm")), ("qd", qd_acc)], inc="opac")

    es_OP.close()
    p.barrier()

    es_F = contextlib.ExitStack()
    w1e = es_F.enter_context(nc.sbuf_tensor("w1e_s", [128, 2, 8, 512], BF16, side="right"))
    w2e = es_F.enter_context(nc.sbuf_tensor("w2e_s", [128, 2, 4, 1024], BF16, side="right"))
    h_sb = es_F.enter_context(nc.sbuf_tensor("h_sb_s", [128, 4, 256], BF16, side="right"))
    b2_row = es_F.enter_context(nc.sbuf_tensor("b2_row_s", [1, NE * D], BF16, side="right"))

    gdma(lambda g: g.dma_start(out=b2_row[:], in_=b2Row))
    fmarks = {}
    for e in range(NE):
        for fe in range(8):
            qi = e * 8 + fe
            wp = qi % 2
            fq, half = fe // 2, fe % 2
            wdw = [("fm2", fmarks[qi - 2])] if qi >= 2 else []
            gdma(lambda g, e=e, fq=fq, half=half, wp=wp:
                 g.dma_start(out=w1e[:, wp], in_=w1_st[e, fq][:, :, half * 512:half * 512 + 512]),
                 waits=wdw)
            gdma(lambda g, e=e, fq=fq, half=half, wp=wp:
                 g.dma_start(out=w2e[:, wp], in_=w2_st[e, fq][:, half * 4:half * 4 + 4, :]))
            gd_f = p.v("gd")
            for tq in range(8):
                HB = (0, 1, 6, 7)

                def mk_mm1(fc, tq=tq, wp=wp):
                    def mm1(t, fc=fc, tq=tq, wp=wp):
                        pbh = pb[HB[fc]]
                        for ci in range(8):
                            inst = t.matmul(pbh[:, 0:256], w1e[:, wp, ci, fc * 128:fc * 128 + 128],
                                            xnTq[:, ci, tq * 256:tq * 256 + 256],
                                            start=(ci == 0), stop=(ci == 7),
                                            skip_group_check=True)
                        return inst
                    return mm1

                def mk_mm2(fc, e=e, fe=fe, wp=wp):
                    def mm2(t, fc=fc, e=e, fe=fe, wp=wp):
                        inst = None
                        for stt in range(2):
                            for nh in range(2):
                                pby = pb[2 + stt * 2 + nh]
                                if fe == 0 and fc == 0:
                                    t.matmul(pby[:], ones_rowb[:],
                                             b2_row[0:1, e * D + nh * 512:e * D + nh * 512 + 512],
                                             start=True, stop=False, skip_group_check=True)
                                inst = t.matmul(pby[:], h_sb[:, fc, stt * 128:stt * 128 + 128],
                                                w2e[:, wp, fc, nh * 512:nh * 512 + 512],
                                                start=(fe > 0 and fc == 0), stop=(fc == 3),
                                                skip_group_check=True)
                        return inst
                    return mm2

                def mk_gelu(fc, e=e, fe=fe):
                    def gelu(a, fc=fc, e=e, fe=fe):
                        j = e * 32 + fe * 4 + fc
                        return a.activation(h_sb[:, fc, :], pb[HB[fc]][:, 0:256],
                                            AF.Gelu_apprx_tanh, bias=b1T[:, j:j + 1])
                    return gelu

                for fc in range(4):
                    tw = [("gd", gd_f), ("fge", p.v("fge") - 3)]
                    p.step("t", mk_mm1(fc), waits=tw, inc="fmm")
                    p.step("a", mk_gelu(fc),
                           waits=[("fmm", p.v("fmm")),
                                  ("fm2", p.v("fm2") - (3 if fc == 0 else 2))],
                           inc="fge")
                    if fc >= 1:
                        w2w = [("fge", p.v("fge") - 1)]
                        if fc == 1 and qi + tq > 0:
                            w2w.append(("fev", p.v("fev")))
                        p.step("t", mk_mm2(fc - 1), waits=w2w, inc="fm2")
                p.step("t", mk_mm2(3), waits=[("fge", p.v("fge"))], inc="fm2")

                def fevac(v, e=e, tq=tq):
                    inst = None
                    for stx in range(2):
                        tt = tq * 2 + stx
                        cav = c_coef[:, tt * 8 + e:tt * 8 + e + 1]
                        for nh in range(2):
                            pby = pb[2 + stx * 2 + nh]
                            inst = v.scalar_tensor_tensor(
                                acc[:, tt, nh * 512:nh * 512 + 512], pby[:], cav,
                                acc[:, tt, nh * 512:nh * 512 + 512],
                                op0=ALU.mult, op1=ALU.add)
                    return inst
                p.step("v", fevac, waits=[("fm2", p.v("fm2"))], inc="fev")
            fmarks[qi] = p.v("fm2")

    es_F.close()
    p.barrier()

    qd(lambda q: q.dma_start(out=outp, in_=acc[:]))
    p.step("q", lambda q: q.nop(), waits=[("qd", p.v("qd"))])

    with nc.Block() as blk:
        p.emit(blk)
    es_ACC.close()
    es_glob.close()
    return nc


def _stage(core, x, qkv_w, qkv_b, out_w, out_b, Wg, bg, W1, b1, W2, b2):
    b = core // 2
    h = core % 2
    xf = np.ascontiguousarray(x.reshape(128, 128, D))
    rot = np.roll(np.arange(NTF), -16 * core)
    xrot = np.ascontiguousarray(xf[rot])
    xb = x.reshape(4, 4096, D)[b]
    own = xb[h * 2048:(h + 1) * 2048]
    oth = xb[(1 - h) * 2048:(2 - h) * 2048]
    xbatch = np.ascontiguousarray(
        np.concatenate([own, oth], axis=0).reshape(NTB, 128, D))
    xown = np.ascontiguousarray(own.reshape(NTO, 128, D).transpose(1, 0, 2))

    perm = np.arange(3 * D)
    for hh in range(NH):
        base = hh * HD
        for sec in (0, D):
            perm[sec + base:sec + base + 32] = sec + base + 2 * np.arange(32)
            perm[sec + base + 32:sec + base + 64] = sec + base + 2 * np.arange(32) + 1
    qkvw_p = qkv_w[:, perm]
    qkvb_p = qkv_b[perm]
    qkvw_st = np.ascontiguousarray(
        qkvw_p.reshape(8, 128, 3, 8, 128).transpose(2, 3, 1, 0, 4))
    qkvbT = np.ascontiguousarray(qkvb_p.reshape(24, 128).T)
    qkvbRow = np.ascontiguousarray(qkvb_p.reshape(1, 3 * D))
    outw_st = np.ascontiguousarray(out_w.reshape(8, 128, D).transpose(1, 0, 2))
    outbRow = np.ascontiguousarray(out_b.reshape(1, D))
    wg_stv = np.ascontiguousarray(Wg.reshape(8, 128, 8).transpose(1, 0, 2).reshape(128, 64))
    bgRow = np.ascontiguousarray(bg.reshape(1, 8))
    w1_stv = np.ascontiguousarray(
        W1.reshape(NE, 8, 128, 4, 1024).transpose(0, 3, 2, 1, 4))
    w2_stv = np.ascontiguousarray(
        W2.reshape(NE, 4, 8, 128, D).transpose(0, 1, 3, 2, 4))
    b1T = np.ascontiguousarray(b1.reshape(NE, 32, 128).transpose(2, 0, 1).reshape(128, 256))
    b2Row = np.ascontiguousarray(b2.reshape(1, NE * D))

    identf = np.eye(128, dtype=np.float32)
    identb = np.eye(128, dtype=np.float32).astype(ml_dtypes.bfloat16)
    swap = np.arange(128)
    swap = np.where((swap % 64) < 32, swap + 32, swap - 32)
    psw = np.zeros((128, 128), np.float32)
    psw[swap, np.arange(128)] = 1.0
    pswap = psw.astype(ml_dtypes.bfloat16)
    triu_tok = np.triu(np.ones((128, 128), np.float32), 1).astype(ml_dtypes.bfloat16)
    canon = rot
    triu_rot = (canon[:, None] < canon[None, :]).astype(np.float32)
    freqs = (10000.0 ** (np.arange(0, HD, 2.0, dtype=np.float32) / HD)).astype(np.float32)
    tpos = np.arange(4096, dtype=np.float32)
    stagedpos = np.concatenate([tpos[h * 2048:(h + 1) * 2048],
                                tpos[(1 - h) * 2048:(2 - h) * 2048]])
    ang = (stagedpos[:, None] * freqs[None, :]).astype(np.float32)
    cosv = np.cos(ang.astype(np.float64)).astype(np.float32)
    sinv = np.sin(ang.astype(np.float64)).astype(np.float32)
    prow = np.arange(128)
    cosq = np.ascontiguousarray(cosv[:, prow % 32].T).astype(ml_dtypes.bfloat16)
    sgn = np.where((prow % 64) < 32, -1.0, 1.0).astype(np.float32)
    sinq = np.ascontiguousarray((sinv[:, prow % 32] * sgn[None, :]).T).astype(ml_dtypes.bfloat16)

    return {
        "xrot": xrot, "xbatch": xbatch, "xown": xown,
        "qkvw_st": qkvw_st, "outw_st": outw_st, "qkvbT": qkvbT,
        "qkvbRow": qkvbRow, "outbRow": outbRow, "wg_st": wg_stv, "bgRow": bgRow,
        "w1_st": w1_stv, "w2_st": w2_stv, "b1T": b1T, "b2Row": b2Row,
        "identf": identf, "identb": identb, "pswap": pswap,
        "triu_tok": triu_tok, "triu_rot": triu_rot, "cosq": cosq, "sinq": sinq,
    }


def kernel(L0_attn_x, L0_attn_qkv_w, L0_attn_qkv_b, L0_attn_out_w, L0_attn_out_b,
           L0_ffn_Wg, L0_ffn_bg, L0_ffn_W1, L0_ffn_b1, L0_ffn_W2, L0_ffn_b2):
    x = np.ascontiguousarray(np.asarray(L0_attn_x, np.float32).reshape(16384, D))
    args = [np.asarray(a, np.float32) for a in
            (L0_attn_qkv_w, L0_attn_qkv_b, L0_attn_out_w, L0_attn_out_b,
             L0_ffn_Wg, L0_ffn_bg, L0_ffn_W1, L0_ffn_b1, L0_ffn_W2, L0_ffn_b2)]
    nc = build()
    in_maps = [_stage(c, x, *args) for c in range(NCORES)]
    res = run_bass_kernel_spmd(nc, in_maps, core_ids=list(range(NCORES)))
    shards = []
    for c in range(NCORES):
        o = np.asarray(res.results[c]["outp"])
        shards.append(o.transpose(1, 0, 2).reshape(2048, D))
    return np.concatenate(shards, axis=0).reshape(4, 4096, D).astype(np.float32)


# version 07
# speedup vs baseline: 1.4890x; 1.0201x over previous
import contextlib
import numpy as np
import ml_dtypes
import concourse.bass as bass
import concourse.mybir as mybir
from concourse.bass_utils import run_bass_kernel_spmd

F32 = mybir.dt.float32
BF16 = mybir.dt.bfloat16
AF = mybir.ActivationFunctionType
ALU = mybir.AluOpType
AX = mybir.AxisListType

D = 1024
NH = 16
HD = 64
FF = 4096
NE = 8
CAP = 4916.0
NCORES = 8
NTF = 128
NTO = 16
NTB = 32
EPS = 1e-5

CB_EPS, CB_NHALF, CB_SCALE, CB_EBIAS, CB_INVD, CB_NEG1 = 0, 1, 2, 3, 4, 5


class Plan:
    ORDER = (("q", "sync"), ("g", "gpsimd"), ("t", "tensor"), ("v", "vector"), ("a", "scalar"))

    def __init__(self, nc, es):
        self.nc = nc
        self.es = es
        self.steps = {e: [] for e, _ in self.ORDER}
        self.sems = {}
        self.cnt = {}

    def sem(self, name):
        if name not in self.sems:
            self.sems[name] = self.es.enter_context(self.nc.semaphore(name))
            self.cnt[name] = 0
        return self.sems[name]

    def step(self, eng, fn, waits=(), inc=None, amt=1):
        waits = tuple((s, v) for s, v in waits if v > 0)
        for s, _ in waits:
            self.sem(s)
        if inc is not None:
            self.sem(inc)
            self.cnt[inc] += amt
        self.steps[eng].append((fn, waits, inc, amt))
        return self.cnt[inc] if inc is not None else None

    def v(self, name):
        self.sem(name)
        return self.cnt[name]

    def barrier(self):
        for e in ("q", "g", "t", "v", "a"):
            w = []
            if e == "q":
                w.append(("qd", self.v("qd")))
            if e == "g":
                w.append(("gd", self.v("gd")))
            self.step(e, lambda en: en.nop(), waits=w, inc="bar")
        tot = self.v("bar")
        for e in ("q", "g", "t", "v", "a"):
            self.step(e, lambda en: en.nop(), waits=[("bar", tot)])

    def emit(self, blk):
        for engname, attr in self.ORDER:
            steps = self.steps[engname]

            def body(e, steps=steps):
                for fn, waits, inc, amt in steps:
                    for s, val in waits:
                        e.wait_ge(self.sems[s], val)
                    inst = fn(e)
                    if inc is not None:
                        assert inst is not None
                        inst.then_inc(self.sems[inc], amt)
            getattr(blk, attr)(body)


def build(debug=False):
    nc = bass.Bass("TRN2", target_bir_lowering=False)

    dt_in = lambda n, s, d=F32: nc.dram_tensor(n, s, d, kind="ExternalInput").ap()
    xrot = dt_in("xrot", [NTF, 128, D])
    xbatch = dt_in("xbatch", [NTB, 128, D])
    xown = dt_in("xown", [128, NTO, D])
    qkvw_st = dt_in("qkvw_st", [3, 8, 128, 8, 128])
    outw_st = dt_in("outw_st", [128, 8, D])
    qkvbT = dt_in("qkvbT", [128, 24])
    qkvbRow = dt_in("qkvbRow", [1, 3 * D])
    outbRow = dt_in("outbRow", [1, D])
    wg_st = dt_in("wg_st", [128, 64])
    bgRow = dt_in("bgRow", [1, 8])
    w1_st = dt_in("w1_st", [NE, 4, 128, 8, 1024])
    w2_st = dt_in("w2_st", [NE, 4, 128, 8, 1024])
    b1T_d = dt_in("b1T", [128, 256])
    b2Row = dt_in("b2Row", [1, NE * D])
    identf_d = dt_in("identf", [128, 128])
    identb_d = dt_in("identb", [128, 128], BF16)
    pswap_d = dt_in("pswap", [128, 128], BF16)
    triu_tok_d = dt_in("triu_tok", [128, 128], BF16)
    triu_rot_d = dt_in("triu_rot", [128, 128])
    cosq_d = dt_in("cosq", [128, 4096], BF16)
    sinq_d = dt_in("sinq", [128, 4096], BF16)
    outp = nc.dram_tensor("outp", [128, NTO, D], F32, kind="ExternalOutput").ap()
    scr_d = nc.dram_tensor("scr", [1, 4096], F32, kind="Internal").ap()
    dbg = {}
    if debug:
        for n, s, d in [("d_ccoef", [128, 128], F32), ("d_cnts", [128, 16], F32),
                        ("d_pref", [128, 16], F32), ("d_xnTq", [128, 2048], BF16),
                        ("d_attn", [128, NTO * D], BF16),
                        ("d_krot", [128, 4096], BF16), ("d_oh", [128, 32], F32)]:
            dbg[n] = nc.dram_tensor(n, s, d, kind="ExternalOutput").ap()

    es_glob = contextlib.ExitStack()
    p = Plan(nc, es_glob)
    sb = lambda n, s, d=F32: es_glob.enter_context(nc.sbuf_tensor(n, s, d))

    identf = sb("identf_s", [128, 128])
    identb = sb("identb_s", [128, 128], BF16)
    pswap = sb("pswap_s", [128, 128], BF16)
    triu_tok = sb("triu_tok_s", [128, 128], BF16)
    triu_rot = sb("triu_rot_s", [128, 128])
    wg_sb = sb("wg_sb_s", [128, 64])
    bg_sb = sb("bg_sb_s", [1, 8])
    qkvbT_sb = sb("qkvbT_sb_s", [128, 24])
    b1T = sb("b1T_sb_s", [128, 256])
    cbank = sb("cbank_s", [128, 8])
    ones_col = sb("ones_col_s", [128, 1])
    ones_colb = sb("ones_colb_s", [128, 1], BF16)
    ones_rowf = sb("ones_rowf_s", [1, 128])
    ones_rowb = sb("ones_rowb_s", [1, 128], BF16)
    zeros8 = sb("zeros8_s", [128, 8])
    c_coef = sb("c_coef_s", [128, NTO * 8])
    xnTq = sb("xnTq_s", [128, 8, 2048], BF16)
    attn_tok = sb("attn_tok_s", [128, NTO, D], BF16)

    pbig = es_glob.enter_context(nc.psum_tensor("pbig", [128, 2048], F32))
    pb = [pbig[:, i * 512:(i + 1) * 512] for i in range(4)]
    pb += [es_glob.enter_context(nc.psum_tensor(f"pb{i}", [128, 512], F32))[:] for i in range(4, 8)]

    def qd(fn, waits=()):
        return p.step("q", fn, waits=waits, inc="qd", amt=16)

    def gdma(fn, waits=()):
        return p.step("g", fn, waits=waits, inc="gd", amt=16)

    def vseq(fns, waits=(), inc=None):
        for i, fn in enumerate(fns):
            w = list(waits) + [("vch", p.v("vch"))] if i == 0 else [("vch", p.v("vch"))]
            p.step("v", fn, waits=w, inc=(inc if (i == len(fns) - 1 and inc) else "vch"))

    qd(lambda q: q.dma_start(out=identf[:], in_=identf_d))
    qd(lambda q: q.dma_start(out=triu_tok[:], in_=triu_tok_d))
    qd(lambda q: q.dma_start(out=triu_rot[:], in_=triu_rot_d))
    qd(lambda q: q.dma_start(out=wg_sb[:], in_=wg_st))
    qd(lambda q: q.dma_start(out=bg_sb[:], in_=bgRow))
    qd(lambda q: q.dma_start(out=qkvbT_sb[:], in_=qkvbT))
    qd(lambda q: q.dma_start(out=b1T[:], in_=b1T_d))
    qd(lambda q: q.dma_start(out=identb[:], in_=identb_d))
    qd(lambda q: q.dma_start(out=pswap[:], in_=pswap_d))

    def memsets(g):
        g.memset(cbank[:, CB_EPS:CB_EPS + 1], EPS)
        g.memset(cbank[:, CB_NHALF:CB_NHALF + 1], -0.5)
        g.memset(cbank[:, CB_SCALE:CB_SCALE + 1], 0.125)
        g.memset(cbank[:, CB_EBIAS:CB_EBIAS + 1], -12.0)
        g.memset(cbank[:, CB_INVD:CB_INVD + 1], 1.0 / D)
        g.memset(cbank[:, CB_NEG1:CB_NEG1 + 1], -1.0)
        g.memset(ones_col[:], 1.0)
        g.memset(ones_colb[:], 1.0)
        g.memset(ones_rowf[:], 1.0)
        g.memset(ones_rowb[:], 1.0)
        return g.memset(zeros8[:], 0.0)
    p.step("g", memsets, inc="gcst")
    p.barrier()

    es_XK = contextlib.ExitStack()
    xnTk = es_XK.enter_context(nc.sbuf_tensor("xnTk_s", [128, 8, 2048], BF16, side="right"))
    es_RX = contextlib.ExitStack()
    rsb = lambda n, s, d=F32: es_RX.enter_context(nc.sbuf_tensor(n, s, d, side="right"))
    xt2 = rsb("xt2_s", [128, 2, D])
    stat = rsb("stat_s", [128, 2, 12])
    xn32 = rsb("xn32_s", [128, 2, D])
    xnbf2 = rsb("xnbf2_s", [128, 2, D], BF16)
    xT32 = rsb("xT32_s", [128, D])
    l8 = rsb("l8_s", [128, 8])
    top8 = rsb("top8_s", [128, 8])
    tsc1 = rsb("tsc1_s", [128, 8])
    tsc2 = rsb("tsc2_s", [128, 8])
    gat = rsb("gat_s", [128, 8])
    oh1_all = rsb("oh1_all_s", [128, NTF * 8], BF16)
    oh2_all = rsb("oh2_all_s", [128, NTF * 8], BF16)
    c1_all = rsb("c1_all_s", [128, NTF * 8], BF16)
    c2_all = rsb("c2_all_s", [128, NTF * 8], BF16)
    cnts_row = rsb("cnts_row_s", [1, NTF * 16])
    pref_rows = cnts_row
    cnts_t = rsb("cnts_t_s", [128, 16])
    pref_sb = rsb("pref_sb_s", [128, 16])
    tot_sb = rsb("tot_sb_s", [1, 16])
    ktmp = rsb("ktmp_s", [128, 32])

    def ln_chain(n, src_tile, dump, aw=()):
        par = n % 2
        st = lambda c: stat[:, par, c:c + 1]
        wv = [("qd", p.v("qd")), ("lnarst", p.v("lnarst") - 1)]
        vseq([lambda v: v.reduce_sum(st(0), src_tile, axis=AX.X),
              lambda v: v.tensor_scalar(st(2), st(0), 1.0 / D, None, op0=ALU.mult),
              lambda v: v.scalar_tensor_tensor(st(3), st(2), -1.0, st(2),
                                               op0=ALU.mult, op1=ALU.mult)],
             waits=wv, inc="lnvst")
        p.step("a", lambda a: a.activation(dump, src_tile, AF.Square, accum_out=st(1)),
               waits=[("qd", p.v("qd"))] + list(aw), inc="lnassq")
        vseq([lambda v: v.scalar_tensor_tensor(st(4), st(1), 1.0 / D, st(3),
                                               op0=ALU.mult, op1=ALU.add)],
             waits=[("lnassq", p.v("lnassq"))], inc="lnvvar")

        def archain(a):
            a.activation(st(5), st(4), AF.Ln, bias=cbank[:, CB_EPS:CB_EPS + 1])
            return a.activation(st(6), st(5), AF.Exp, scale=cbank[:, CB_NHALF:CB_NHALF + 1])
        p.step("a", archain, waits=[("lnvvar", p.v("lnvvar"))], inc="lnarst")
        return st

    for bt in range(NTB):
        par = bt % 2
        xt = xt2[:, par, :]
        qd(lambda q, bt=bt, par=par: q.dma_start(out=xt2[:, par, :], in_=xbatch[bt]),
           waits=[("lnaxn", p.v("lnaxn") - 1)])
        st = ln_chain(bt, xt, xn32[:, par, :])
        vseq([lambda v, xt=xt, par=par, st=st:
              v.tensor_scalar(xnbf2[:, par, :], xt, st(2), st(6),
                              op0=ALU.subtract, op1=ALU.mult)],
             waits=[("xttr", p.v("xttr") - 1), ("lnarst", p.v("lnarst"))], inc="lnaxn")
        b0 = 0 if par == 0 else 2

        def transb(t, b0=b0, par=par):
            pbB = pb[b0].bitcast(BF16)
            for ci in range(8):
                inst = t.transpose(pbB[:, ci * 128:ci * 128 + 128],
                                   xnbf2[:, par, ci * 128:ci * 128 + 128], identb[:])
            return inst
        p.step("t", transb, waits=[("lnaxn", p.v("lnaxn")), ("xvev", p.v("xvev") - 1)],
               inc="xttr")
        tgt = xnTq if bt < NTO else xnTk
        off = (bt % NTO) * 128

        def evtrb(v, b0=b0, tgt=tgt, off=off):
            return v.tensor_copy(tgt[:, :, off:off + 128],
                                 pb[b0].bitcast(BF16).rearrange("p (c k) -> p c k", k=128))
        p.step("v", evtrb, waits=[("xttr", p.v("xttr"))], inc="xvev")

    if debug:
        qd(lambda q: q.dma_start(out=dbg["d_xnTq"], in_=xnTq[:, 0, :]),
           waits=[("xvev", p.v("xvev"))])

    p.barrier()

    rmark = {"rtlg": {}, "rvoh": {}, "rtcnt": {}}

    def rt_part1(g, epoch_first):
        par = g % 2
        xt = xt2[:, par, :]
        qd(lambda q, g=g, par=par: q.dma_start(out=xt2[:, par, :], in_=xrot[g]),
           waits=[("lnaxn", p.v("lnaxn") - 1)])
        aw = [("xttr", p.v("xttr"))] if g < 2 else ()
        st = ln_chain(NTB + g, xt, xnbf2[:, par, :], aw)
        vseq([lambda v, xt=xt, par=par, st=st:
              v.tensor_scalar(xn32[:, par, :], xt, st(2), st(6),
                              op0=ALU.subtract, op1=ALU.mult)],
             waits=[("rttr", p.v("rttr") - 2), ("lnarst", p.v("lnarst"))], inc="lnaxn")

        for hfi in range(2):
            def trh(t, hfi=hfi, par=par):
                for ci in range(4):
                    cc = hfi * 4 + ci
                    inst = t.transpose(pb[6][:, ci * 128:ci * 128 + 128],
                                       xn32[:, par, cc * 128:cc * 128 + 128], identf[:])
                return inst
            tw = [("rvev", p.v("rvev"))]
            if hfi == 0:
                tw.append(("lnaxn", p.v("lnaxn")))
            p.step("t", trh, waits=tw, inc="rttr")
            p.step("v", lambda v, hfi=hfi: v.tensor_copy(xT32[:, hfi * 512:hfi * 512 + 512], pb[6]),
                   waits=[("rttr", p.v("rttr"))], inc="rvev")

        sl = (g % 8) * 8

        def lgmm(t, sl=sl):
            for ci in range(8):
                t.matmul(pb[7][:, sl:sl + 8], xT32[:, ci * 128:ci * 128 + 128],
                         wg_sb[:, ci * 8:ci * 8 + 8], start=(ci == 0), stop=False)
            return t.matmul(pb[7][:, sl:sl + 8], ones_rowf[:], bg_sb[:], start=False, stop=True)
        lw = [("rvev", p.v("rvev"))]
        if epoch_first:
            lw += [("rvcp", p.v("rvcp")), ("rvcv", p.v("rvcv"))]
        p.step("t", lgmm, waits=lw, inc="rtlg")
        rmark["rtlg"][g] = p.v("rtlg")

    def rt_part2(g):
        sl = (g % 8) * 8
        p.step("v", lambda v, sl=sl: v.tensor_copy(l8[:], pb[7][:, sl:sl + 8]),
               waits=[("rtlg", p.v("rtlg")), ("rtcnt", p.v("rtcnt"))], inc="rvcp")
        p.step("v", lambda v: v.max(top8[:], l8[:]),
               waits=[("rvcp", p.v("rvcp"))], inc="rvmx")
        p.step("v", lambda v: v.tensor_sub(gat[:, 0:1], top8[:, 1:2], top8[:, 0:1]),
               waits=[("rvmx", p.v("rvmx"))], inc="rvl8")
        p.step("a", lambda a: a.activation(gat[:, 3:4], gat[:, 0:1], AF.Exp),
               waits=[("rvl8", p.v("rvl8"))], inc="raeg")

        vseq([lambda v: (v.tensor_sub(tsc1[:], l8[:], top8[:, 0:1].to_broadcast([128, 8])),
                         v.tensor_scalar(gat[:, 1:2], gat[:, 0:1], 0.0, None, op0=ALU.is_equal))[-1],
              lambda v: (v.tensor_scalar(tsc1[:], tsc1[:], 0.0, None, op0=ALU.is_equal),
                         v.tensor_scalar(gat[:, 2:3], gat[:, 1:2], 1.0, None, op0=ALU.add))[-1],
              lambda v: v.tensor_tensor_scan(tsc2[:], tsc1[:], zeros8[:], 0.0, op0=ALU.add, op1=ALU.add),
              lambda v: v.tensor_scalar(tsc2[:], tsc2[:], 1.0, None, op0=ALU.is_equal),
              lambda v, g=g: v.tensor_tensor(oh1_all[:, g * 8:g * 8 + 8], tsc1[:], tsc2[:], op=ALU.mult)],
             waits=[("raeg", p.v("raeg"))], inc="rvo1")

        vseq([lambda v: (v.tensor_sub(tsc1[:], l8[:], top8[:, 1:2].to_broadcast([128, 8])),
                         v.tensor_scalar(gat[:, 4:5], gat[:, 3:4], 1.0, None, op0=ALU.add))[-1],
              lambda v: (v.tensor_scalar(tsc1[:], tsc1[:], 0.0, None, op0=ALU.is_equal),
                         v.reciprocal(gat[:, 5:6], gat[:, 4:5]))[-1],
              lambda v: v.tensor_tensor_scan(tsc2[:], tsc1[:], zeros8[:], 0.0, op0=ALU.add, op1=ALU.add),
              lambda v: v.tensor_sub(tsc2[:], tsc2[:], gat[:, 2:3].to_broadcast([128, 8])),
              lambda v: v.tensor_scalar(tsc2[:], tsc2[:], 0.0, None, op0=ALU.is_equal),
              lambda v, g=g: v.tensor_tensor(oh2_all[:, g * 8:g * 8 + 8], tsc1[:], tsc2[:], op=ALU.mult)],
             waits=[("rvo1", p.v("rvo1"))], inc="rvo2")

        vseq([lambda v: v.tensor_tensor(gat[:, 6:7], gat[:, 3:4], gat[:, 5:6], op=ALU.mult),
              lambda v, g=g: (v.tensor_mul(c1_all[:, g * 8:g * 8 + 8], oh1_all[:, g * 8:g * 8 + 8],
                                           gat[:, 5:6].to_broadcast([128, 8])),
                              v.tensor_mul(c2_all[:, g * 8:g * 8 + 8], oh2_all[:, g * 8:g * 8 + 8],
                                           gat[:, 6:7].to_broadcast([128, 8])))[-1]],
             waits=[("rvo2", p.v("rvo2"))], inc="rvoh")
        rmark["rvoh"][g] = p.v("rvoh")

    def rt_part3(g):
        sl = 64 + (g % 8) * 16

        def cntmm(t, g=g, sl=sl):
            t.matmul(pb[7][0:1, sl:sl + 8], ones_colb[:], oh1_all[:, g * 8:g * 8 + 8],
                     start=True, stop=True, skip_group_check=True)
            return t.matmul(pb[7][0:1, sl + 8:sl + 16], ones_colb[:], oh2_all[:, g * 8:g * 8 + 8],
                            start=True, stop=True, skip_group_check=True)
        p.step("t", cntmm, waits=[("rvoh", p.v("rvoh"))], inc="rtcnt")
        rmark["rtcnt"][g] = p.v("rtcnt")
        p.step("v", lambda v, g=g, sl=sl:
               v.tensor_copy(cnts_row[0:1, g * 16:g * 16 + 16], pb[7][0:1, sl:sl + 16]),
               waits=[("rtcnt", rmark["rtcnt"][g])], inc="rvcv")

    es_A = contextlib.ExitStack()
    asb = lambda n, s, d=F32: es_A.enter_context(nc.sbuf_tensor(n, s, d, side="right"))
    cosq = asb("cosq_s", [128, 4096], BF16)
    sinq = asb("sinq_s", [128, 4096], BF16)
    wq_b = asb("wq_b_s", [128, 8, 128], BF16)
    wk_b = asb("wk_b_s", [128, 8, 128], BF16)
    wv_b = asb("wv_b_s", [128, 8, 128], BF16)
    kq_sb = asb("kq_sb_s", [128, 2, 512], BF16)
    ttmp = asb("ttmp_s", [128, 512], BF16)
    krot = asb("krot_s", [128, 4096], BF16)
    qrot = asb("qrot_s", [128, 2048], BF16)
    vt = asb("vt_s", [128, 32, 130], BF16)
    pt_sb = asb("pt_sb_s", [128, 2, 2, 1024], BF16)
    recip_sb = asb("recip_sb_s", [128, 8])
    qkvb_vrow = asb("qkvb_vrow_s", [1, D], BF16)

    qd(lambda q: q.dma_start(out=cosq[:], in_=cosq_d))
    qd(lambda q: q.dma_start(out=sinq[:], in_=sinq_d))
    gdma(lambda g: g.dma_start(out=qkvb_vrow[:], in_=qkvbRow[0:1, 2 * D:3 * D]))
    p.step("v", lambda v: v.memset(vt[:], 1.0), inc="avones")

    def xnT_slice(ci, t0, nn):
        if t0 < 2048:
            return xnTq[:, ci, t0:t0 + nn]
        return xnTk[:, ci, t0 - 2048:t0 - 2048 + nn]

    rg = 0
    for hp in range(8):
        par = hp % 2
        wdw = [("atdone", p.v("atdone"))]
        for s, wt in ((0, wq_b), (1, wk_b), (2, wv_b)):
            gdma(lambda g, s=s, wt=wt, hp=hp:
                 g.dma_start(out=wt[:], in_=qkvw_st[s, hp]), waits=wdw)
        gd_w = p.v("gd")

        for kind in range(12):
            isq = kind >= 8
            it = kind - 8 if isq else kind
            pbm = pb[kind % 2]
            pbs = pb[2 + kind % 2]

            def kqmm(t, isq=isq, it=it, pbm=pbm):
                w = wq_b if isq else wk_b
                for ci in range(8):
                    rhs = xnTq[:, ci, it * 512:it * 512 + 512] if isq else xnT_slice(ci, it * 512, 512)
                    inst = t.matmul(pbm[:], w[:, ci, :], rhs,
                                    start=(ci == 0), stop=(ci == 7))
                return inst
            tw = [("gd", gd_w), ("aaev", p.v("aaev") - 1), ("avrp", p.v("avrp") - 1)]
            if hp == 0 and kind < 4:
                tw.append(("xvev", p.v("xvev")))
            p.step("t", kqmm, waits=tw, inc="atkm")
            bcol = hp if isq else 8 + hp
            p.step("a", lambda a, kind=kind, pbm=pbm, bcol=bcol:
                   a.activation(kq_sb[:, kind % 2, :], pbm[:], AF.Identity,
                                bias=qkvbT_sb[:, bcol:bcol + 1]),
                   waits=[("atkm", p.v("atkm"))], inc="aaev")

            def swmm(t, kind=kind, pbs=pbs):
                return t.matmul(pbs[:], pswap[:], kq_sb[:, kind % 2, :], start=True, stop=True)
            p.step("t", swmm, waits=[("aaev", p.v("aaev")), ("avrp", p.v("avrp") - 1)], inc="atsw")
            dst = qrot if isq else krot

            vseq([lambda v, kind=kind, it=it, pbs=pbs, dst=dst:
                  (v.tensor_tensor(ttmp[:], kq_sb[:, kind % 2, :],
                                   cosq[:, it * 512:it * 512 + 512], op=ALU.mult),
                   v.tensor_tensor(dst[:, it * 512:it * 512 + 512], pbs[:],
                                   sinq[:, it * 512:it * 512 + 512], op=ALU.mult))[-1],
                  lambda v, kind=kind, it=it, dst=dst:
                  v.tensor_tensor(dst[:, it * 512:it * 512 + 512],
                                  dst[:, it * 512:it * 512 + 512], ttmp[:], op=ALU.add)],
                 waits=[("atsw", p.v("atsw"))], inc="avrp")

        for tt in range(32):
            pbv = pb[4 + tt % 2]

            def vmm(t, tt=tt, pbv=pbv, hp=hp):
                for ci in range(8):
                    t.matmul(pbv[:, 0:128], xnT_slice(ci, tt * 128, 128), wv_b[:, ci, :],
                             start=(ci == 0), stop=False)
                return t.matmul(pbv[:, 0:128], ones_rowb[:],
                                qkvb_vrow[0:1, hp * 128:hp * 128 + 128],
                                start=False, stop=True)
            vw = [("gd", gd_w), ("avve", p.v("avve") - 1)]
            if hp == 0 and tt < 2:
                vw.append(("gd", p.v("gd")))
            p.step("t", vmm, waits=vw, inc="atvm")

            def vev(v, tt=tt, pbv=pbv):
                v.tensor_copy(vt[:, tt, 0:64], pbv[:, 0:64])
                return v.tensor_copy(vt[:, tt, 65:129], pbv[:, 64:128])
            p.step("v", vev, waits=[("atvm", p.v("atvm")), ("avones", p.v("avones"))], inc="avve")

        avrp_done = p.v("avrp")
        avve_done = p.v("avve")
        for qt in range(4):
            fs0 = p.v("afx")
            for kc in range(32):
                sb0 = pb[kc % 2]
                sb1 = pb[2 + kc % 2]

                def smm(t, kc=kc, qt=qt, sb0=sb0, sb1=sb1):
                    t.matmul(sb0[:], krot[0:64, kc * 128:kc * 128 + 128],
                             qrot[0:64, qt * 512:qt * 512 + 512], start=True, stop=True,
                             skip_group_check=True)
                    return t.matmul(sb1[:], krot[64:128, kc * 128:kc * 128 + 128],
                                    qrot[64:128, qt * 512:qt * 512 + 512], start=True, stop=True,
                                    skip_group_check=True)
                tw = [("avrp", avrp_done), ("avve", avve_done),
                      ("afx", fs0 + 2 * ((kc - 2) // 2 + 1) if kc >= 2 else fs0)]
                if kc == 0 and qt > 0:
                    tw.append(("avno", p.v("avno")))
                p.step("t", smm, waits=tw, inc="afs", amt=2)

                if kc % 2 == 1:
                    pj = kc // 2

                    def expev(a, pj=pj):
                        a.activation(pt_sb[:, 0, pj % 2, :], pbig[:, 0:1024], AF.Exp,
                                     scale=cbank[:, CB_SCALE:CB_SCALE + 1],
                                     bias=cbank[:, CB_EBIAS:CB_EBIAS + 1])
                        return a.activation(pt_sb[:, 1, pj % 2, :], pbig[:, 1024:2048], AF.Exp,
                                            scale=cbank[:, CB_SCALE:CB_SCALE + 1],
                                            bias=cbank[:, CB_EBIAS:CB_EBIAS + 1])
                    p.step("a", expev,
                           waits=[("afs", p.v("afs")), ("afa", p.v("afa") - 1)], inc="afx", amt=2)

                    def avmm(t, kc=kc, pj=pj):
                        for k2 in (kc - 1, kc):
                            for h in range(2):
                                pba = pb[4 + h]
                                for qs in range(4):
                                    inst = t.matmul(pba[:, qs * 65:qs * 65 + 65],
                                                    pt_sb[:, h, pj % 2,
                                                          (k2 % 2) * 512 + qs * 128:(k2 % 2) * 512 + qs * 128 + 128],
                                                    vt[:, k2, h * 65:h * 65 + 65],
                                                    start=(k2 == 0), stop=(k2 == 31),
                                                    skip_group_check=True)
                        return inst
                    p.step("t", avmm, waits=[("afx", p.v("afx"))], inc="afa")

            def norm1(v):
                for h in range(2):
                    pba = pb[4 + h]
                    for qs in range(4):
                        inst = v.reciprocal(recip_sb[:, h * 4 + qs:h * 4 + qs + 1],
                                            pba[:, qs * 65 + 64:qs * 65 + 65])
                return inst
            p.step("v", norm1, waits=[("afa", p.v("afa"))], inc="avn1")

            def norm2(v, qt=qt, hp=hp):
                for h in range(2):
                    pba = pb[4 + h]
                    for qs in range(4):
                        tt = qt * 4 + qs
                        inst = v.tensor_mul(
                            attn_tok[:, tt, (2 * hp + h) * 64:(2 * hp + h) * 64 + 64],
                            pba[:, qs * 65:qs * 65 + 64],
                            recip_sb[:, h * 4 + qs:h * 4 + qs + 1].to_broadcast([128, 64]))
                return inst
            p.step("v", norm2, waits=[("avn1", p.v("avn1"))], inc="avno")

            blk4 = range(rg, rg + 4)
            for jj, g in enumerate(blk4):
                rt_part1(g, epoch_first=(jj == 0))
            for g in range(rg - 8, rg - 4):
                if g >= 0:
                    rt_part3(g)
            for g in range(rg - 4, rg):
                if g >= 0:
                    rt_part2(g)
            rg += 4
        p.step("t", lambda t: t.nop(), waits=[("avno", p.v("avno"))], inc="atdone")

    assert rg == NTF
    for g in range(NTF - 8, NTF - 4):
        rt_part3(g)
    for g in range(NTF - 4, NTF):
        rt_part2(g)
    for g in range(NTF - 4, NTF):
        rt_part3(g)

    gdma(lambda g: g.dma_start(out=scr_d[0:1, 0:2048], in_=cnts_row[:]),
         waits=[("rvcv", p.v("rvcv"))])
    gdma(lambda g: g.dma_start(out=cnts_t[:],
                               in_=scr_d[0:1, 0:2048].rearrange("o (g j) -> (o g) j", j=16)),
         waits=[("gd", p.v("gd"))])

    def prefmm(t):
        t.matmul(pb[6][:, 0:16], triu_rot[:], cnts_t[:], start=True, stop=True,
                 skip_group_check=True)
        return t.matmul(pb[7][0:1, 0:16], ones_col[:], cnts_t[:], start=True, stop=True,
                        skip_group_check=True)
    p.step("t", prefmm, waits=[("gd", p.v("gd")), ("rvcv", p.v("rvcv"))], inc="rtpre")

    def evpre(v):
        v.tensor_copy(pref_sb[:], pb[6][:, 0:16])
        return v.tensor_copy(tot_sb[:], pb[7][0:1, 0:16])
    p.step("v", evpre, waits=[("rtpre", p.v("rtpre"))], inc="rvpre")
    gdma(lambda g: g.dma_start(out=scr_d[0:1, 0:2048].rearrange("o (g j) -> (o g) j", j=16),
                               in_=pref_sb[:]),
         waits=[("rvpre", p.v("rvpre"))])
    gdma(lambda g: g.dma_start(out=pref_rows[0:1, 0:2048], in_=scr_d[0:1, 0:2048]),
         waits=[("gd", p.v("gd"))])
    gd_pref = p.v("gd")

    for ti in range(NTO):
        bb = pb[6 + ti % 2]

        def posmm(t, ti=ti, bb=bb):
            t.matmul(bb[:, 0:8], triu_tok[:], oh1_all[:, ti * 8:ti * 8 + 8],
                     start=True, stop=False, skip_group_check=True)
            t.matmul(bb[:, 0:8], ones_rowf[:], pref_rows[0:1, ti * 16:ti * 16 + 8],
                     start=False, stop=True, skip_group_check=True)
            t.matmul(bb[:, 8:16], triu_tok[:], oh2_all[:, ti * 8:ti * 8 + 8],
                     start=True, stop=False, skip_group_check=True)
            t.matmul(bb[:, 8:16], ones_rowf[:], pref_rows[0:1, ti * 16 + 8:ti * 16 + 16],
                     start=False, stop=False, skip_group_check=True)
            return t.matmul(bb[:, 8:16], ones_rowf[:], tot_sb[0:1, 0:8],
                            start=False, stop=True, skip_group_check=True)
        p.step("t", posmm, waits=[("gd", gd_pref), ("rvcc", p.v("rvcc") - 1)], inc="rtpos")

        vseq([lambda v, ti=ti, bb=bb:
              (v.tensor_scalar(ktmp[:, 0:8], bb[:, 0:8], CAP, None, op0=ALU.is_lt),
               v.tensor_scalar(ktmp[:, 8:16], bb[:, 8:16], CAP, None, op0=ALU.is_lt))[-1],
              lambda v, ti=ti: (v.tensor_tensor(ktmp[:, 16:24], c1_all[:, ti * 8:ti * 8 + 8],
                                                ktmp[:, 0:8], op=ALU.mult),
                                v.tensor_tensor(ktmp[:, 24:32], c2_all[:, ti * 8:ti * 8 + 8],
                                                ktmp[:, 8:16], op=ALU.mult))[-1],
              lambda v, ti=ti: v.tensor_tensor(c_coef[:, ti * 8:ti * 8 + 8],
                                               ktmp[:, 16:24], ktmp[:, 24:32], op=ALU.add)],
             waits=[("rtpos", p.v("rtpos"))], inc="rvcc")

    if debug:
        qd(lambda q: q.dma_start(out=dbg["d_cnts"], in_=cnts_t[:]),
           waits=[("rvcc", p.v("rvcc"))])
        qd(lambda q: q.dma_start(out=dbg["d_pref"], in_=pref_sb[:]))
        qd(lambda q: q.dma_start(out=dbg["d_oh"], in_=oh1_all[:, 0:32]))
        qd(lambda q: q.dma_start(out=dbg["d_ccoef"], in_=c_coef[:]))
        qd(lambda q: q.dma_start(out=dbg["d_krot"], in_=krot[:]))
        qd(lambda q: q.dma_start(out=dbg["d_attn"],
                                 in_=attn_tok[:].rearrange("p a b -> p (a b)")),
           waits=[("avno", p.v("avno"))])

    es_A.close()
    es_RX.close()
    es_XK.close()
    p.barrier()

    es_ACC = contextlib.ExitStack()
    acc = es_ACC.enter_context(nc.sbuf_tensor("acc_s", [128, NTO, D], F32, side="right"))
    es_OP = contextlib.ExitStack()
    ow_b = es_OP.enter_context(nc.sbuf_tensor("ow_b_s", [128, 8, D], BF16, side="right"))
    attnF_t = es_OP.enter_context(nc.sbuf_tensor("attnF_t_s", [128, 8, 128], BF16, side="right"))
    outb_row = es_OP.enter_context(nc.sbuf_tensor("outb_row_s", [1, D], BF16, side="right"))

    gdma(lambda g: g.dma_start(out=ow_b[:], in_=outw_st))
    gdma(lambda g: g.dma_start(out=outb_row[:], in_=outbRow))
    gd_ow = p.v("gd")
    qd(lambda q: q.dma_start(out=acc[:], in_=xown))
    qd_acc = p.v("qd")

    for tt in range(NTO):
        def optr(t, tt=tt):
            pbB = pb[0].bitcast(BF16)
            for ci in range(8):
                inst = t.transpose(pbB[:, ci * 128:ci * 128 + 128],
                                   attn_tok[:, tt, ci * 128:ci * 128 + 128], identb[:])
            return inst
        p.step("t", optr, waits=[("opve", p.v("opve"))], inc="optr")

        def opev(v):
            return v.tensor_copy(attnF_t[:, :, :],
                                 pb[0].bitcast(BF16).rearrange("p (c k) -> p c k", k=128))
        p.step("v", opev, waits=[("optr", p.v("optr"))], inc="opve")

        def opmm(t, tt=tt):
            for nh in range(2):
                pbo = pb[2 + nh]
                t.matmul(pbo[:], ones_rowb[:], outb_row[0:1, nh * 512:nh * 512 + 512],
                         start=True, stop=False, skip_group_check=True)
                for ci in range(8):
                    inst = t.matmul(pbo[:], attnF_t[:, ci, :], ow_b[:, ci, nh * 512:nh * 512 + 512],
                                    start=False, stop=(ci == 7), skip_group_check=True)
            return inst
        p.step("t", opmm, waits=[("opve", p.v("opve")), ("gd", gd_ow),
                                 ("opac", p.v("opac"))], inc="opmm")

        def opadd(v, tt=tt):
            v.tensor_tensor(acc[:, tt, 0:512], acc[:, tt, 0:512], pb[2][:], op=ALU.add)
            return v.tensor_tensor(acc[:, tt, 512:1024], acc[:, tt, 512:1024], pb[3][:], op=ALU.add)
        p.step("v", opadd, waits=[("opmm", p.v("opmm")), ("qd", qd_acc)], inc="opac")

    es_OP.close()
    p.barrier()

    es_F = contextlib.ExitStack()
    w1e = es_F.enter_context(nc.sbuf_tensor("w1e_s", [128, 2, 8, 512], BF16, side="right"))
    w2e = es_F.enter_context(nc.sbuf_tensor("w2e_s", [128, 2, 4, 1024], BF16, side="right"))
    h_sb = es_F.enter_context(nc.sbuf_tensor("h_sb_s", [128, 4, 256], BF16, side="right"))
    b2_row = es_F.enter_context(nc.sbuf_tensor("b2_row_s", [1, NE * D], BF16, side="right"))

    gdma(lambda g: g.dma_start(out=b2_row[:], in_=b2Row))
    fmarks = {}
    for e in range(NE):
        for fe in range(8):
            qi = e * 8 + fe
            wp = qi % 2
            fq, half = fe // 2, fe % 2
            wdw = [("fm2", fmarks[qi - 2])] if qi >= 2 else []
            gdma(lambda g, e=e, fq=fq, half=half, wp=wp:
                 g.dma_start(out=w1e[:, wp], in_=w1_st[e, fq][:, :, half * 512:half * 512 + 512]),
                 waits=wdw)
            gdma(lambda g, e=e, fq=fq, half=half, wp=wp:
                 g.dma_start(out=w2e[:, wp], in_=w2_st[e, fq][:, half * 4:half * 4 + 4, :]))
            gd_f = p.v("gd")
            for tq in range(8):
                HB = (0, 1, 6, 7)

                def mk_mm1(fc, tq=tq, wp=wp):
                    def mm1(t, fc=fc, tq=tq, wp=wp):
                        pbh = pb[HB[fc]]
                        for ci in range(8):
                            inst = t.matmul(pbh[:, 0:256], w1e[:, wp, ci, fc * 128:fc * 128 + 128],
                                            xnTq[:, ci, tq * 256:tq * 256 + 256],
                                            start=(ci == 0), stop=(ci == 7),
                                            skip_group_check=True)
                        return inst
                    return mm1

                def mk_mm2(fc, e=e, fe=fe, wp=wp):
                    def mm2(t, fc=fc, e=e, fe=fe, wp=wp):
                        inst = None
                        for stt in range(2):
                            for nh in range(2):
                                pby = pb[2 + stt * 2 + nh]
                                if fe == 0 and fc == 0:
                                    t.matmul(pby[:], ones_rowb[:],
                                             b2_row[0:1, e * D + nh * 512:e * D + nh * 512 + 512],
                                             start=True, stop=False, skip_group_check=True)
                                inst = t.matmul(pby[:], h_sb[:, fc, stt * 128:stt * 128 + 128],
                                                w2e[:, wp, fc, nh * 512:nh * 512 + 512],
                                                start=(fe > 0 and fc == 0), stop=(fc == 3),
                                                skip_group_check=True)
                        return inst
                    return mm2

                def mk_gelu(fc, e=e, fe=fe):
                    def gelu(a, fc=fc, e=e, fe=fe):
                        j = e * 32 + fe * 4 + fc
                        return a.activation(h_sb[:, fc, :], pb[HB[fc]][:, 0:256],
                                            AF.Gelu_apprx_tanh, bias=b1T[:, j:j + 1])
                    return gelu

                for fc in range(4):
                    tw = [("gd", gd_f), ("fge", p.v("fge") - 3)]
                    p.step("t", mk_mm1(fc), waits=tw, inc="fmm")
                    p.step("a", mk_gelu(fc),
                           waits=[("fmm", p.v("fmm")),
                                  ("fm2", p.v("fm2") - (3 if fc == 0 else 2))],
                           inc="fge")
                    if fc >= 1:
                        w2w = [("fge", p.v("fge") - 1)]
                        if fc == 1 and qi + tq > 0:
                            w2w.append(("fev", p.v("fev")))
                        p.step("t", mk_mm2(fc - 1), waits=w2w, inc="fm2")
                p.step("t", mk_mm2(3), waits=[("fge", p.v("fge"))], inc="fm2")

                def fevac(v, e=e, tq=tq):
                    inst = None
                    for stx in range(2):
                        tt = tq * 2 + stx
                        cav = c_coef[:, tt * 8 + e:tt * 8 + e + 1]
                        for nh in range(2):
                            pby = pb[2 + stx * 2 + nh]
                            inst = v.scalar_tensor_tensor(
                                acc[:, tt, nh * 512:nh * 512 + 512], pby[:], cav,
                                acc[:, tt, nh * 512:nh * 512 + 512],
                                op0=ALU.mult, op1=ALU.add)
                    return inst
                p.step("v", fevac, waits=[("fm2", p.v("fm2"))], inc="fev")
            fmarks[qi] = p.v("fm2")

    es_F.close()
    p.barrier()

    qd(lambda q: q.dma_start(out=outp, in_=acc[:]))
    p.step("q", lambda q: q.nop(), waits=[("qd", p.v("qd"))])

    with nc.Block() as blk:
        p.emit(blk)
    es_ACC.close()
    es_glob.close()
    return nc


def _stage(core, x, qkv_w, qkv_b, out_w, out_b, Wg, bg, W1, b1, W2, b2):
    b = core // 2
    h = core % 2
    xf = np.ascontiguousarray(x.reshape(128, 128, D))
    rot = np.roll(np.arange(NTF), -16 * core)
    xrot = np.ascontiguousarray(xf[rot])
    xb = x.reshape(4, 4096, D)[b]
    own = xb[h * 2048:(h + 1) * 2048]
    oth = xb[(1 - h) * 2048:(2 - h) * 2048]
    xbatch = np.ascontiguousarray(
        np.concatenate([own, oth], axis=0).reshape(NTB, 128, D))
    xown = np.ascontiguousarray(own.reshape(NTO, 128, D).transpose(1, 0, 2))

    perm = np.arange(3 * D)
    for hh in range(NH):
        base = hh * HD
        for sec in (0, D):
            perm[sec + base:sec + base + 32] = sec + base + 2 * np.arange(32)
            perm[sec + base + 32:sec + base + 64] = sec + base + 2 * np.arange(32) + 1
    qkvw_p = qkv_w[:, perm]
    qkvb_p = qkv_b[perm]
    qkvw_st = np.ascontiguousarray(
        qkvw_p.reshape(8, 128, 3, 8, 128).transpose(2, 3, 1, 0, 4))
    qkvbT = np.ascontiguousarray(qkvb_p.reshape(24, 128).T)
    qkvbRow = np.ascontiguousarray(qkvb_p.reshape(1, 3 * D))
    outw_st = np.ascontiguousarray(out_w.reshape(8, 128, D).transpose(1, 0, 2))
    outbRow = np.ascontiguousarray(out_b.reshape(1, D))
    wg_stv = np.ascontiguousarray(Wg.reshape(8, 128, 8).transpose(1, 0, 2).reshape(128, 64))
    bgRow = np.ascontiguousarray(bg.reshape(1, 8))
    w1_stv = np.ascontiguousarray(
        W1.reshape(NE, 8, 128, 4, 1024).transpose(0, 3, 2, 1, 4))
    w2_stv = np.ascontiguousarray(
        W2.reshape(NE, 4, 8, 128, D).transpose(0, 1, 3, 2, 4))
    b1T = np.ascontiguousarray(b1.reshape(NE, 32, 128).transpose(2, 0, 1).reshape(128, 256))
    b2Row = np.ascontiguousarray(b2.reshape(1, NE * D))

    identf = np.eye(128, dtype=np.float32)
    identb = np.eye(128, dtype=np.float32).astype(ml_dtypes.bfloat16)
    swap = np.arange(128)
    swap = np.where((swap % 64) < 32, swap + 32, swap - 32)
    psw = np.zeros((128, 128), np.float32)
    psw[swap, np.arange(128)] = 1.0
    pswap = psw.astype(ml_dtypes.bfloat16)
    triu_tok = np.triu(np.ones((128, 128), np.float32), 1).astype(ml_dtypes.bfloat16)
    canon = rot
    triu_rot = (canon[:, None] < canon[None, :]).astype(np.float32)
    freqs = (10000.0 ** (np.arange(0, HD, 2.0, dtype=np.float32) / HD)).astype(np.float32)
    tpos = np.arange(4096, dtype=np.float32)
    stagedpos = np.concatenate([tpos[h * 2048:(h + 1) * 2048],
                                tpos[(1 - h) * 2048:(2 - h) * 2048]])
    ang = (stagedpos[:, None] * freqs[None, :]).astype(np.float32)
    cosv = np.cos(ang.astype(np.float64)).astype(np.float32)
    sinv = np.sin(ang.astype(np.float64)).astype(np.float32)
    prow = np.arange(128)
    cosq = np.ascontiguousarray(cosv[:, prow % 32].T).astype(ml_dtypes.bfloat16)
    sgn = np.where((prow % 64) < 32, -1.0, 1.0).astype(np.float32)
    sinq = np.ascontiguousarray((sinv[:, prow % 32] * sgn[None, :]).T).astype(ml_dtypes.bfloat16)

    return {
        "xrot": xrot, "xbatch": xbatch, "xown": xown,
        "qkvw_st": qkvw_st, "outw_st": outw_st, "qkvbT": qkvbT,
        "qkvbRow": qkvbRow, "outbRow": outbRow, "wg_st": wg_stv, "bgRow": bgRow,
        "w1_st": w1_stv, "w2_st": w2_stv, "b1T": b1T, "b2Row": b2Row,
        "identf": identf, "identb": identb, "pswap": pswap,
        "triu_tok": triu_tok, "triu_rot": triu_rot, "cosq": cosq, "sinq": sinq,
    }


def kernel(L0_attn_x, L0_attn_qkv_w, L0_attn_qkv_b, L0_attn_out_w, L0_attn_out_b,
           L0_ffn_Wg, L0_ffn_bg, L0_ffn_W1, L0_ffn_b1, L0_ffn_W2, L0_ffn_b2):
    x = np.ascontiguousarray(np.asarray(L0_attn_x, np.float32).reshape(16384, D))
    args = [np.asarray(a, np.float32) for a in
            (L0_attn_qkv_w, L0_attn_qkv_b, L0_attn_out_w, L0_attn_out_b,
             L0_ffn_Wg, L0_ffn_bg, L0_ffn_W1, L0_ffn_b1, L0_ffn_W2, L0_ffn_b2)]
    nc = build()
    in_maps = [_stage(c, x, *args) for c in range(NCORES)]
    res = run_bass_kernel_spmd(nc, in_maps, core_ids=list(range(NCORES)))
    shards = []
    for c in range(NCORES):
        o = np.asarray(res.results[c]["outp"])
        shards.append(o.transpose(1, 0, 2).reshape(2048, D))
    return np.concatenate(shards, axis=0).reshape(4, 4096, D).astype(np.float32)


# version 08
# speedup vs baseline: 1.5041x; 1.0101x over previous
import contextlib
import numpy as np
import ml_dtypes
import concourse.bass as bass
import concourse.mybir as mybir
from concourse.bass_utils import run_bass_kernel_spmd

F32 = mybir.dt.float32
BF16 = mybir.dt.bfloat16
AF = mybir.ActivationFunctionType
ALU = mybir.AluOpType
AX = mybir.AxisListType

D = 1024
NH = 16
HD = 64
FF = 4096
NE = 8
CAP = 4916.0
NCORES = 8
NTF = 128
NTO = 16
NTB = 32
EPS = 1e-5

CB_EPS, CB_NHALF, CB_SCALE, CB_EBIAS, CB_INVD, CB_NEG1 = 0, 1, 2, 3, 4, 5


class Plan:
    ORDER = (("q", "sync"), ("g", "gpsimd"), ("t", "tensor"), ("v", "vector"), ("a", "scalar"))

    def __init__(self, nc, es):
        self.nc = nc
        self.es = es
        self.steps = {e: [] for e, _ in self.ORDER}
        self.sems = {}
        self.cnt = {}

    def sem(self, name):
        if name not in self.sems:
            self.sems[name] = self.es.enter_context(self.nc.semaphore(name))
            self.cnt[name] = 0
        return self.sems[name]

    def step(self, eng, fn, waits=(), inc=None, amt=1):
        waits = tuple((s, v) for s, v in waits if v > 0)
        for s, _ in waits:
            self.sem(s)
        if inc is not None:
            self.sem(inc)
            self.cnt[inc] += amt
        self.steps[eng].append((fn, waits, inc, amt))
        return self.cnt[inc] if inc is not None else None

    def v(self, name):
        self.sem(name)
        return self.cnt[name]

    def barrier(self):
        for e in ("q", "g", "t", "v", "a"):
            w = []
            if e == "q":
                w.append(("qd", self.v("qd")))
            if e == "g":
                w.append(("gd", self.v("gd")))
            self.step(e, lambda en: en.nop(), waits=w, inc="bar")
        tot = self.v("bar")
        for e in ("q", "g", "t", "v", "a"):
            self.step(e, lambda en: en.nop(), waits=[("bar", tot)])

    def emit(self, blk):
        for engname, attr in self.ORDER:
            steps = self.steps[engname]

            def body(e, steps=steps):
                for fn, waits, inc, amt in steps:
                    for s, val in waits:
                        e.wait_ge(self.sems[s], val)
                    inst = fn(e)
                    if inc is not None:
                        assert inst is not None
                        inst.then_inc(self.sems[inc], amt)
            getattr(blk, attr)(body)


def build(debug=False):
    nc = bass.Bass("TRN2", target_bir_lowering=False)

    dt_in = lambda n, s, d=F32: nc.dram_tensor(n, s, d, kind="ExternalInput").ap()
    xrot = dt_in("xrot", [NTF, 128, D])
    xbatch = dt_in("xbatch", [NTB, 128, D])
    xown = dt_in("xown", [128, NTO, D])
    qkvw_st = dt_in("qkvw_st", [3, 8, 128, 8, 128])
    outw_st = dt_in("outw_st", [128, 8, D])
    qkvbT = dt_in("qkvbT", [128, 24])
    qkvbRow = dt_in("qkvbRow", [1, 3 * D])
    outbRow = dt_in("outbRow", [1, D])
    wg_st = dt_in("wg_st", [128, 64])
    bgRow = dt_in("bgRow", [1, 8])
    w1_st = dt_in("w1_st", [NE, 4, 128, 8, 1024])
    w2_st = dt_in("w2_st", [NE, 4, 128, 8, 1024])
    b1T_d = dt_in("b1T", [128, 256])
    b2Row = dt_in("b2Row", [1, NE * D])
    identf_d = dt_in("identf", [128, 128])
    identb_d = dt_in("identb", [128, 128], BF16)
    pswap_d = dt_in("pswap", [128, 128], BF16)
    triu_tok_d = dt_in("triu_tok", [128, 128], BF16)
    triu_rot_d = dt_in("triu_rot", [128, 128])
    cosq_d = dt_in("cosq", [128, 4096], BF16)
    sinq_d = dt_in("sinq", [128, 4096], BF16)
    outp = nc.dram_tensor("outp", [128, NTO, D], F32, kind="ExternalOutput").ap()
    scr_d = nc.dram_tensor("scr", [1, 4096], F32, kind="Internal").ap()
    dbg = {}
    if debug:
        for n, s, d in [("d_ccoef", [128, 128], F32), ("d_cnts", [128, 16], F32),
                        ("d_pref", [128, 16], F32), ("d_xnTq", [128, 2048], BF16),
                        ("d_attn", [128, NTO * D], BF16),
                        ("d_krot", [128, 4096], BF16), ("d_oh", [128, 32], F32)]:
            dbg[n] = nc.dram_tensor(n, s, d, kind="ExternalOutput").ap()

    es_glob = contextlib.ExitStack()
    p = Plan(nc, es_glob)
    sb = lambda n, s, d=F32: es_glob.enter_context(nc.sbuf_tensor(n, s, d))

    identf = sb("identf_s", [128, 128])
    identb = sb("identb_s", [128, 128], BF16)
    pswap = sb("pswap_s", [128, 128], BF16)
    triu_tok = sb("triu_tok_s", [128, 128], BF16)
    triu_rot = sb("triu_rot_s", [128, 128])
    wg_sb = sb("wg_sb_s", [128, 64])
    bg_sb = sb("bg_sb_s", [1, 8])
    qkvbT_sb = sb("qkvbT_sb_s", [128, 24])
    b1T = sb("b1T_sb_s", [128, 256])
    cbank = sb("cbank_s", [128, 8])
    ones_col = sb("ones_col_s", [128, 1])
    ones_colb = sb("ones_colb_s", [128, 1], BF16)
    ones_rowf = sb("ones_rowf_s", [1, 128])
    ones_rowb = sb("ones_rowb_s", [1, 128], BF16)
    zeros8 = sb("zeros8_s", [128, 8])
    c_coef = sb("c_coef_s", [128, NTO * 8])
    xnTq = sb("xnTq_s", [128, 8, 2048], BF16)
    attn_tok = sb("attn_tok_s", [128, NTO, D], BF16)

    pbig = es_glob.enter_context(nc.psum_tensor("pbig", [128, 2048], F32))
    pb = [pbig[:, i * 512:(i + 1) * 512] for i in range(4)]
    pb += [es_glob.enter_context(nc.psum_tensor(f"pb{i}", [128, 512], F32))[:] for i in range(4, 8)]

    def qd(fn, waits=()):
        return p.step("q", fn, waits=waits, inc="qd", amt=16)

    def gdma(fn, waits=()):
        return p.step("g", fn, waits=waits, inc="gd", amt=16)

    def vseq(fns, waits=(), inc=None):
        for i, fn in enumerate(fns):
            w = list(waits) + [("vch", p.v("vch"))] if i == 0 else [("vch", p.v("vch"))]
            p.step("v", fn, waits=w, inc=(inc if (i == len(fns) - 1 and inc) else "vch"))

    qd(lambda q: q.dma_start(out=identf[:], in_=identf_d))
    qd(lambda q: q.dma_start(out=triu_tok[:], in_=triu_tok_d))
    qd(lambda q: q.dma_start(out=triu_rot[:], in_=triu_rot_d))
    qd(lambda q: q.dma_start(out=wg_sb[:], in_=wg_st))
    qd(lambda q: q.dma_start(out=bg_sb[:], in_=bgRow))
    qd(lambda q: q.dma_start(out=qkvbT_sb[:], in_=qkvbT))
    qd(lambda q: q.dma_start(out=b1T[:], in_=b1T_d))
    qd(lambda q: q.dma_start(out=identb[:], in_=identb_d))
    qd(lambda q: q.dma_start(out=pswap[:], in_=pswap_d))

    def memsets(g):
        g.memset(cbank[:, CB_EPS:CB_EPS + 1], EPS)
        g.memset(cbank[:, CB_NHALF:CB_NHALF + 1], -0.5)
        g.memset(cbank[:, CB_SCALE:CB_SCALE + 1], 0.125)
        g.memset(cbank[:, CB_EBIAS:CB_EBIAS + 1], -12.0)
        g.memset(cbank[:, CB_INVD:CB_INVD + 1], 1.0 / D)
        g.memset(cbank[:, CB_NEG1:CB_NEG1 + 1], -1.0)
        g.memset(ones_col[:], 1.0)
        g.memset(ones_colb[:], 1.0)
        g.memset(ones_rowf[:], 1.0)
        g.memset(ones_rowb[:], 1.0)
        return g.memset(zeros8[:], 0.0)
    p.step("g", memsets, inc="gcst")
    p.barrier()

    es_XK = contextlib.ExitStack()
    xnTk = es_XK.enter_context(nc.sbuf_tensor("xnTk_s", [128, 8, 2048], BF16, side="right"))
    es_RX = contextlib.ExitStack()
    rsb = lambda n, s, d=F32: es_RX.enter_context(nc.sbuf_tensor(n, s, d, side="right"))
    xt2 = rsb("xt2_s", [128, 2, D])
    stat = rsb("stat_s", [128, 2, 12])
    xn32 = rsb("xn32_s", [128, 2, D])
    xnbf2 = rsb("xnbf2_s", [128, 2, D], BF16)
    xT32 = rsb("xT32_s", [128, D])
    l8 = rsb("l8_s", [128, 8])
    top8 = rsb("top8_s", [128, 8])
    tsc1 = rsb("tsc1_s", [128, 8])
    tsc2 = rsb("tsc2_s", [128, 8])
    gat = rsb("gat_s", [128, 8])
    oh1_all = rsb("oh1_all_s", [128, NTF * 8], BF16)
    oh2_all = rsb("oh2_all_s", [128, NTF * 8], BF16)
    c1_all = rsb("c1_all_s", [128, NTF * 8], BF16)
    c2_all = rsb("c2_all_s", [128, NTF * 8], BF16)
    cnts_row = rsb("cnts_row_s", [1, NTF * 16])
    pref_rows = cnts_row
    cnts_t = rsb("cnts_t_s", [128, 16])
    pref_sb = rsb("pref_sb_s", [128, 16])
    tot_sb = rsb("tot_sb_s", [1, 16])
    ktmp = rsb("ktmp_s", [128, 32])

    def ln_chain(n, src_tile, dump, aw=()):
        par = n % 2
        st = lambda c: stat[:, par, c:c + 1]
        wv = [("qd", p.v("qd")), ("lnarst", p.v("lnarst") - 1)]
        vseq([lambda v: v.reduce_sum(st(0), src_tile, axis=AX.X),
              lambda v: v.tensor_scalar(st(2), st(0), 1.0 / D, None, op0=ALU.mult),
              lambda v: v.scalar_tensor_tensor(st(3), st(2), -1.0, st(2),
                                               op0=ALU.mult, op1=ALU.mult)],
             waits=wv, inc="lnvst")
        p.step("a", lambda a: a.activation(dump, src_tile, AF.Square, accum_out=st(1)),
               waits=[("qd", p.v("qd"))] + list(aw), inc="lnassq")
        vseq([lambda v: v.scalar_tensor_tensor(st(4), st(1), 1.0 / D, st(3),
                                               op0=ALU.mult, op1=ALU.add)],
             waits=[("lnassq", p.v("lnassq"))], inc="lnvvar")

        def archain(a):
            a.activation(st(5), st(4), AF.Ln, bias=cbank[:, CB_EPS:CB_EPS + 1])
            return a.activation(st(6), st(5), AF.Exp, scale=cbank[:, CB_NHALF:CB_NHALF + 1])
        p.step("a", archain, waits=[("lnvvar", p.v("lnvvar"))], inc="lnarst")
        return st

    for bt in range(NTB):
        par = bt % 2
        xt = xt2[:, par, :]
        qd(lambda q, bt=bt, par=par: q.dma_start(out=xt2[:, par, :], in_=xbatch[bt]),
           waits=[("lnaxn", p.v("lnaxn") - 1)])
        st = ln_chain(bt, xt, xn32[:, par, :])
        vseq([lambda v, xt=xt, par=par, st=st:
              v.tensor_scalar(xnbf2[:, par, :], xt, st(2), st(6),
                              op0=ALU.subtract, op1=ALU.mult)],
             waits=[("xttr", p.v("xttr") - 1), ("lnarst", p.v("lnarst"))], inc="lnaxn")
        b0 = 0 if par == 0 else 2

        def transb(t, b0=b0, par=par):
            pbB = pb[b0].bitcast(BF16)
            for ci in range(8):
                inst = t.transpose(pbB[:, ci * 128:ci * 128 + 128],
                                   xnbf2[:, par, ci * 128:ci * 128 + 128], identb[:])
            return inst
        p.step("t", transb, waits=[("lnaxn", p.v("lnaxn")), ("xvev", p.v("xvev") - 1)],
               inc="xttr")
        tgt = xnTq if bt < NTO else xnTk
        off = (bt % NTO) * 128

        def evtrb(v, b0=b0, tgt=tgt, off=off):
            return v.tensor_copy(tgt[:, :, off:off + 128],
                                 pb[b0].bitcast(BF16).rearrange("p (c k) -> p c k", k=128))
        p.step("v", evtrb, waits=[("xttr", p.v("xttr"))], inc="xvev")

    if debug:
        qd(lambda q: q.dma_start(out=dbg["d_xnTq"], in_=xnTq[:, 0, :]),
           waits=[("xvev", p.v("xvev"))])

    p.barrier()

    rmark = {"rtlg": {}, "rvoh": {}, "rtcnt": {}}

    def rt_part1(g, epoch_first):
        par = g % 2
        xt = xt2[:, par, :]
        qd(lambda q, g=g, par=par: q.dma_start(out=xt2[:, par, :], in_=xrot[g]),
           waits=[("lnaxn", p.v("lnaxn") - 1)])
        aw = [("xttr", p.v("xttr"))] if g < 2 else ()
        st = ln_chain(NTB + g, xt, xnbf2[:, par, :], aw)
        vseq([lambda v, xt=xt, par=par, st=st:
              v.tensor_scalar(xn32[:, par, :], xt, st(2), st(6),
                              op0=ALU.subtract, op1=ALU.mult)],
             waits=[("rttr", p.v("rttr") - 2), ("lnarst", p.v("lnarst"))], inc="lnaxn")

        for hfi in range(2):
            def trh(t, hfi=hfi, par=par):
                for ci in range(4):
                    cc = hfi * 4 + ci
                    inst = t.transpose(pb[6][:, ci * 128:ci * 128 + 128],
                                       xn32[:, par, cc * 128:cc * 128 + 128], identf[:])
                return inst
            tw = [("rvev", p.v("rvev"))]
            if hfi == 0:
                tw.append(("lnaxn", p.v("lnaxn")))
            p.step("t", trh, waits=tw, inc="rttr")
            p.step("v", lambda v, hfi=hfi: v.tensor_copy(xT32[:, hfi * 512:hfi * 512 + 512], pb[6]),
                   waits=[("rttr", p.v("rttr"))], inc="rvev")

        sl = (g % 8) * 8

        def lgmm(t, sl=sl):
            for ci in range(8):
                t.matmul(pb[7][:, sl:sl + 8], xT32[:, ci * 128:ci * 128 + 128],
                         wg_sb[:, ci * 8:ci * 8 + 8], start=(ci == 0), stop=False)
            return t.matmul(pb[7][:, sl:sl + 8], ones_rowf[:], bg_sb[:], start=False, stop=True)
        lw = [("rvev", p.v("rvev"))]
        if epoch_first:
            lw += [("rvcp", p.v("rvcp")), ("rvcv", p.v("rvcv"))]
        p.step("t", lgmm, waits=lw, inc="rtlg")
        rmark["rtlg"][g] = p.v("rtlg")

    def rt_part2(g):
        sl = (g % 8) * 8
        p.step("v", lambda v, sl=sl: v.tensor_copy(l8[:], pb[7][:, sl:sl + 8]),
               waits=[("rtlg", p.v("rtlg")), ("rtcnt", p.v("rtcnt"))], inc="rvcp")
        p.step("v", lambda v: v.max(top8[:], l8[:]),
               waits=[("rvcp", p.v("rvcp"))], inc="rvmx")
        p.step("v", lambda v: v.tensor_sub(gat[:, 0:1], top8[:, 1:2], top8[:, 0:1]),
               waits=[("rvmx", p.v("rvmx"))], inc="rvl8")
        p.step("a", lambda a: a.activation(gat[:, 3:4], gat[:, 0:1], AF.Exp),
               waits=[("rvl8", p.v("rvl8"))], inc="raeg")

        vseq([lambda v: (v.tensor_sub(tsc1[:], l8[:], top8[:, 0:1].to_broadcast([128, 8])),
                         v.tensor_scalar(gat[:, 1:2], gat[:, 0:1], 0.0, None, op0=ALU.is_equal))[-1],
              lambda v: (v.tensor_scalar(tsc1[:], tsc1[:], 0.0, None, op0=ALU.is_equal),
                         v.tensor_scalar(gat[:, 2:3], gat[:, 1:2], 1.0, None, op0=ALU.add))[-1],
              lambda v: v.tensor_tensor_scan(tsc2[:], tsc1[:], zeros8[:], 0.0, op0=ALU.add, op1=ALU.add),
              lambda v: v.tensor_scalar(tsc2[:], tsc2[:], 1.0, None, op0=ALU.is_equal),
              lambda v, g=g: v.tensor_tensor(oh1_all[:, g * 8:g * 8 + 8], tsc1[:], tsc2[:], op=ALU.mult)],
             waits=[("raeg", p.v("raeg"))], inc="rvo1")

        vseq([lambda v: (v.tensor_sub(tsc1[:], l8[:], top8[:, 1:2].to_broadcast([128, 8])),
                         v.tensor_scalar(gat[:, 4:5], gat[:, 3:4], 1.0, None, op0=ALU.add))[-1],
              lambda v: (v.tensor_scalar(tsc1[:], tsc1[:], 0.0, None, op0=ALU.is_equal),
                         v.reciprocal(gat[:, 5:6], gat[:, 4:5]))[-1],
              lambda v: v.tensor_tensor_scan(tsc2[:], tsc1[:], zeros8[:], 0.0, op0=ALU.add, op1=ALU.add),
              lambda v: v.tensor_sub(tsc2[:], tsc2[:], gat[:, 2:3].to_broadcast([128, 8])),
              lambda v: v.tensor_scalar(tsc2[:], tsc2[:], 0.0, None, op0=ALU.is_equal),
              lambda v, g=g: v.tensor_tensor(oh2_all[:, g * 8:g * 8 + 8], tsc1[:], tsc2[:], op=ALU.mult)],
             waits=[("rvo1", p.v("rvo1"))], inc="rvo2")

        vseq([lambda v: v.tensor_tensor(gat[:, 6:7], gat[:, 3:4], gat[:, 5:6], op=ALU.mult),
              lambda v, g=g: (v.tensor_mul(c1_all[:, g * 8:g * 8 + 8], oh1_all[:, g * 8:g * 8 + 8],
                                           gat[:, 5:6].to_broadcast([128, 8])),
                              v.tensor_mul(c2_all[:, g * 8:g * 8 + 8], oh2_all[:, g * 8:g * 8 + 8],
                                           gat[:, 6:7].to_broadcast([128, 8])))[-1]],
             waits=[("rvo2", p.v("rvo2"))], inc="rvoh")
        rmark["rvoh"][g] = p.v("rvoh")

    def rt_part3(g):
        sl = 64 + (g % 8) * 16

        def cntmm(t, g=g, sl=sl):
            t.matmul(pb[7][0:1, sl:sl + 8], ones_colb[:], oh1_all[:, g * 8:g * 8 + 8],
                     start=True, stop=True, skip_group_check=True)
            return t.matmul(pb[7][0:1, sl + 8:sl + 16], ones_colb[:], oh2_all[:, g * 8:g * 8 + 8],
                            start=True, stop=True, skip_group_check=True)
        p.step("t", cntmm, waits=[("rvoh", p.v("rvoh"))], inc="rtcnt")
        rmark["rtcnt"][g] = p.v("rtcnt")
        p.step("v", lambda v, g=g, sl=sl:
               v.tensor_copy(cnts_row[0:1, g * 16:g * 16 + 16], pb[7][0:1, sl:sl + 16]),
               waits=[("rtcnt", rmark["rtcnt"][g])], inc="rvcv")

    es_A = contextlib.ExitStack()
    asb = lambda n, s, d=F32: es_A.enter_context(nc.sbuf_tensor(n, s, d, side="right"))
    cosq = asb("cosq_s", [128, 4096], BF16)
    sinq = asb("sinq_s", [128, 4096], BF16)
    wq_b = asb("wq_b_s", [128, 8, 128], BF16)
    wk_b = asb("wk_b_s", [128, 8, 128], BF16)
    wv_b = asb("wv_b_s", [128, 8, 128], BF16)
    kq_sb = asb("kq_sb_s", [128, 2, 512], BF16)
    ttmp = asb("ttmp_s", [128, 512], BF16)
    krot = asb("krot_s", [128, 4096], BF16)
    qrot = asb("qrot_s", [128, 2048], BF16)
    vt = asb("vt_s", [128, 32, 130], BF16)
    pt_sb = asb("pt_sb_s", [128, 2, 2, 1024], BF16)
    recip_sb = asb("recip_sb_s", [128, 8])
    qkvb_vrow = asb("qkvb_vrow_s", [1, D], BF16)

    qd(lambda q: q.dma_start(out=cosq[:], in_=cosq_d))
    qd(lambda q: q.dma_start(out=sinq[:], in_=sinq_d))
    gdma(lambda g: g.dma_start(out=qkvb_vrow[:], in_=qkvbRow[0:1, 2 * D:3 * D]))
    p.step("v", lambda v: v.memset(vt[:], 1.0), inc="avones")

    def xnT_slice(ci, t0, nn):
        if t0 < 2048:
            return xnTq[:, ci, t0:t0 + nn]
        return xnTk[:, ci, t0 - 2048:t0 - 2048 + nn]

    rg = 0
    for hp in range(8):
        par = hp % 2
        wdw = [("atdone", p.v("atdone"))]
        for s, wt in ((0, wq_b), (1, wk_b), (2, wv_b)):
            gdma(lambda g, s=s, wt=wt, hp=hp:
                 g.dma_start(out=wt[:], in_=qkvw_st[s, hp]), waits=wdw)
        gd_w = p.v("gd")

        for kind in range(12):
            isq = kind >= 8
            it = kind - 8 if isq else kind
            pbm = pb[kind % 2]
            pbs = pb[2 + kind % 2]

            def kqmm(t, isq=isq, it=it, pbm=pbm):
                w = wq_b if isq else wk_b
                for ci in range(8):
                    rhs = xnTq[:, ci, it * 512:it * 512 + 512] if isq else xnT_slice(ci, it * 512, 512)
                    inst = t.matmul(pbm[:], w[:, ci, :], rhs,
                                    start=(ci == 0), stop=(ci == 7))
                return inst
            tw = [("gd", gd_w), ("aaev", p.v("aaev") - 1), ("avrp", p.v("avrp") - 1)]
            if hp == 0 and kind < 4:
                tw.append(("xvev", p.v("xvev")))
            p.step("t", kqmm, waits=tw, inc="atkm")
            bcol = hp if isq else 8 + hp
            p.step("a", lambda a, kind=kind, pbm=pbm, bcol=bcol:
                   a.activation(kq_sb[:, kind % 2, :], pbm[:], AF.Identity,
                                bias=qkvbT_sb[:, bcol:bcol + 1]),
                   waits=[("atkm", p.v("atkm"))], inc="aaev")

            def swmm(t, kind=kind, pbs=pbs):
                return t.matmul(pbs[:], pswap[:], kq_sb[:, kind % 2, :], start=True, stop=True)
            p.step("t", swmm, waits=[("aaev", p.v("aaev")), ("avrp", p.v("avrp") - 1)], inc="atsw")
            dst = qrot if isq else krot

            vseq([lambda v, kind=kind, it=it, pbs=pbs, dst=dst:
                  (v.tensor_tensor(ttmp[:], kq_sb[:, kind % 2, :],
                                   cosq[:, it * 512:it * 512 + 512], op=ALU.mult),
                   v.tensor_tensor(dst[:, it * 512:it * 512 + 512], pbs[:],
                                   sinq[:, it * 512:it * 512 + 512], op=ALU.mult))[-1],
                  lambda v, kind=kind, it=it, dst=dst:
                  v.tensor_tensor(dst[:, it * 512:it * 512 + 512],
                                  dst[:, it * 512:it * 512 + 512], ttmp[:], op=ALU.add)],
                 waits=[("atsw", p.v("atsw"))], inc="avrp")

        for tt in range(32):
            pbv = pb[4 + tt % 2]

            def vmm(t, tt=tt, pbv=pbv, hp=hp):
                for ci in range(8):
                    t.matmul(pbv[:, 0:128], xnT_slice(ci, tt * 128, 128), wv_b[:, ci, :],
                             start=(ci == 0), stop=False)
                return t.matmul(pbv[:, 0:128], ones_rowb[:],
                                qkvb_vrow[0:1, hp * 128:hp * 128 + 128],
                                start=False, stop=True)
            vw = [("gd", gd_w), ("avve", p.v("avve") - 1)]
            if hp == 0 and tt < 2:
                vw.append(("gd", p.v("gd")))
            p.step("t", vmm, waits=vw, inc="atvm")

            def vev(v, tt=tt, pbv=pbv):
                v.tensor_copy(vt[:, tt, 0:64], pbv[:, 0:64])
                return v.tensor_copy(vt[:, tt, 65:129], pbv[:, 64:128])
            p.step("v", vev, waits=[("atvm", p.v("atvm")), ("avones", p.v("avones"))], inc="avve")

        avrp_done = p.v("avrp")
        avve_done = p.v("avve")
        for qt in range(4):
            fs0 = p.v("afx")
            for kc in range(32):
                sb0 = pb[kc % 2]
                sb1 = pb[2 + kc % 2]

                def smm(t, kc=kc, qt=qt, sb0=sb0, sb1=sb1):
                    t.matmul(sb0[:], krot[0:64, kc * 128:kc * 128 + 128],
                             qrot[0:64, qt * 512:qt * 512 + 512], start=True, stop=True,
                             skip_group_check=True)
                    return t.matmul(sb1[:], krot[64:128, kc * 128:kc * 128 + 128],
                                    qrot[64:128, qt * 512:qt * 512 + 512], start=True, stop=True,
                                    skip_group_check=True)
                tw = [("avrp", avrp_done), ("avve", avve_done),
                      ("afx", fs0 + 2 * ((kc - 2) // 2 + 1) if kc >= 2 else fs0)]
                if kc == 0 and qt > 0:
                    tw.append(("avno", p.v("avno")))
                p.step("t", smm, waits=tw, inc="afs", amt=2)

                if kc % 2 == 1:
                    pj = kc // 2

                    def expev(a, pj=pj):
                        return a.activation(pt_sb[:, :, pj % 2, :], pbig[:, 0:2048], AF.Exp,
                                            scale=cbank[:, CB_SCALE:CB_SCALE + 1],
                                            bias=cbank[:, CB_EBIAS:CB_EBIAS + 1])
                    p.step("a", expev,
                           waits=[("afs", p.v("afs")), ("afa", p.v("afa") - 1)], inc="afx", amt=2)

                    def avmm(t, kc=kc, pj=pj):
                        for k2 in (kc - 1, kc):
                            for h in range(2):
                                pba = pb[4 + h]
                                for qs in range(4):
                                    inst = t.matmul(pba[:, qs * 65:qs * 65 + 65],
                                                    pt_sb[:, h, pj % 2,
                                                          (k2 % 2) * 512 + qs * 128:(k2 % 2) * 512 + qs * 128 + 128],
                                                    vt[:, k2, h * 65:h * 65 + 65],
                                                    start=(k2 == 0), stop=(k2 == 31),
                                                    skip_group_check=True)
                        return inst
                    p.step("t", avmm, waits=[("afx", p.v("afx"))], inc="afa")

            def norm1(v):
                for h in range(2):
                    pba = pb[4 + h]
                    for qs in range(4):
                        inst = v.reciprocal(recip_sb[:, h * 4 + qs:h * 4 + qs + 1],
                                            pba[:, qs * 65 + 64:qs * 65 + 65])
                return inst
            p.step("v", norm1, waits=[("afa", p.v("afa"))], inc="avn1")

            def norm2(v, qt=qt, hp=hp):
                for h in range(2):
                    pba = pb[4 + h]
                    for qs in range(4):
                        tt = qt * 4 + qs
                        inst = v.tensor_mul(
                            attn_tok[:, tt, (2 * hp + h) * 64:(2 * hp + h) * 64 + 64],
                            pba[:, qs * 65:qs * 65 + 64],
                            recip_sb[:, h * 4 + qs:h * 4 + qs + 1].to_broadcast([128, 64]))
                return inst
            p.step("v", norm2, waits=[("avn1", p.v("avn1"))], inc="avno")

            blk4 = range(rg, rg + 4)
            for jj, g in enumerate(blk4):
                rt_part1(g, epoch_first=(jj == 0))
            for g in range(rg - 8, rg - 4):
                if g >= 0:
                    rt_part3(g)
            for g in range(rg - 4, rg):
                if g >= 0:
                    rt_part2(g)
            rg += 4
        p.step("t", lambda t: t.nop(), waits=[("avno", p.v("avno"))], inc="atdone")

    assert rg == NTF
    for g in range(NTF - 8, NTF - 4):
        rt_part3(g)
    for g in range(NTF - 4, NTF):
        rt_part2(g)
    for g in range(NTF - 4, NTF):
        rt_part3(g)

    gdma(lambda g: g.dma_start(out=scr_d[0:1, 0:2048], in_=cnts_row[:]),
         waits=[("rvcv", p.v("rvcv"))])
    gdma(lambda g: g.dma_start(out=cnts_t[:],
                               in_=scr_d[0:1, 0:2048].rearrange("o (g j) -> (o g) j", j=16)),
         waits=[("gd", p.v("gd"))])

    def prefmm(t):
        t.matmul(pb[6][:, 0:16], triu_rot[:], cnts_t[:], start=True, stop=True,
                 skip_group_check=True)
        return t.matmul(pb[7][0:1, 0:16], ones_col[:], cnts_t[:], start=True, stop=True,
                        skip_group_check=True)
    p.step("t", prefmm, waits=[("gd", p.v("gd")), ("rvcv", p.v("rvcv"))], inc="rtpre")

    def evpre(v):
        v.tensor_copy(pref_sb[:], pb[6][:, 0:16])
        return v.tensor_copy(tot_sb[:], pb[7][0:1, 0:16])
    p.step("v", evpre, waits=[("rtpre", p.v("rtpre"))], inc="rvpre")
    gdma(lambda g: g.dma_start(out=scr_d[0:1, 0:2048].rearrange("o (g j) -> (o g) j", j=16),
                               in_=pref_sb[:]),
         waits=[("rvpre", p.v("rvpre"))])
    gdma(lambda g: g.dma_start(out=pref_rows[0:1, 0:2048], in_=scr_d[0:1, 0:2048]),
         waits=[("gd", p.v("gd"))])
    gd_pref = p.v("gd")

    for ti in range(NTO):
        bb = pb[6 + ti % 2]

        def posmm(t, ti=ti, bb=bb):
            t.matmul(bb[:, 0:8], triu_tok[:], oh1_all[:, ti * 8:ti * 8 + 8],
                     start=True, stop=False, skip_group_check=True)
            t.matmul(bb[:, 0:8], ones_rowf[:], pref_rows[0:1, ti * 16:ti * 16 + 8],
                     start=False, stop=True, skip_group_check=True)
            t.matmul(bb[:, 8:16], triu_tok[:], oh2_all[:, ti * 8:ti * 8 + 8],
                     start=True, stop=False, skip_group_check=True)
            t.matmul(bb[:, 8:16], ones_rowf[:], pref_rows[0:1, ti * 16 + 8:ti * 16 + 16],
                     start=False, stop=False, skip_group_check=True)
            return t.matmul(bb[:, 8:16], ones_rowf[:], tot_sb[0:1, 0:8],
                            start=False, stop=True, skip_group_check=True)
        p.step("t", posmm, waits=[("gd", gd_pref), ("rvcc", p.v("rvcc") - 1)], inc="rtpos")

        vseq([lambda v, ti=ti, bb=bb:
              (v.tensor_scalar(ktmp[:, 0:8], bb[:, 0:8], CAP, None, op0=ALU.is_lt),
               v.tensor_scalar(ktmp[:, 8:16], bb[:, 8:16], CAP, None, op0=ALU.is_lt))[-1],
              lambda v, ti=ti: (v.tensor_tensor(ktmp[:, 16:24], c1_all[:, ti * 8:ti * 8 + 8],
                                                ktmp[:, 0:8], op=ALU.mult),
                                v.tensor_tensor(ktmp[:, 24:32], c2_all[:, ti * 8:ti * 8 + 8],
                                                ktmp[:, 8:16], op=ALU.mult))[-1],
              lambda v, ti=ti: v.tensor_tensor(c_coef[:, ti * 8:ti * 8 + 8],
                                               ktmp[:, 16:24], ktmp[:, 24:32], op=ALU.add)],
             waits=[("rtpos", p.v("rtpos"))], inc="rvcc")

    if debug:
        qd(lambda q: q.dma_start(out=dbg["d_cnts"], in_=cnts_t[:]),
           waits=[("rvcc", p.v("rvcc"))])
        qd(lambda q: q.dma_start(out=dbg["d_pref"], in_=pref_sb[:]))
        qd(lambda q: q.dma_start(out=dbg["d_oh"], in_=oh1_all[:, 0:32]))
        qd(lambda q: q.dma_start(out=dbg["d_ccoef"], in_=c_coef[:]))
        qd(lambda q: q.dma_start(out=dbg["d_krot"], in_=krot[:]))
        qd(lambda q: q.dma_start(out=dbg["d_attn"],
                                 in_=attn_tok[:].rearrange("p a b -> p (a b)")),
           waits=[("avno", p.v("avno"))])

    es_A.close()
    es_RX.close()
    es_XK.close()
    p.barrier()

    es_ACC = contextlib.ExitStack()
    acc = es_ACC.enter_context(nc.sbuf_tensor("acc_s", [128, NTO, D], F32, side="right"))
    es_OP = contextlib.ExitStack()
    ow_b = es_OP.enter_context(nc.sbuf_tensor("ow_b_s", [128, 8, D], BF16, side="right"))
    attnF_t = es_OP.enter_context(nc.sbuf_tensor("attnF_t_s", [128, 8, 128], BF16, side="right"))
    outb_row = es_OP.enter_context(nc.sbuf_tensor("outb_row_s", [1, D], BF16, side="right"))

    gdma(lambda g: g.dma_start(out=ow_b[:], in_=outw_st))
    gdma(lambda g: g.dma_start(out=outb_row[:], in_=outbRow))
    gd_ow = p.v("gd")
    qd(lambda q: q.dma_start(out=acc[:], in_=xown))
    qd_acc = p.v("qd")

    for tt in range(NTO):
        def optr(t, tt=tt):
            pbB = pb[0].bitcast(BF16)
            for ci in range(8):
                inst = t.transpose(pbB[:, ci * 128:ci * 128 + 128],
                                   attn_tok[:, tt, ci * 128:ci * 128 + 128], identb[:])
            return inst
        p.step("t", optr, waits=[("opve", p.v("opve"))], inc="optr")

        def opev(v):
            return v.tensor_copy(attnF_t[:, :, :],
                                 pb[0].bitcast(BF16).rearrange("p (c k) -> p c k", k=128))
        p.step("v", opev, waits=[("optr", p.v("optr"))], inc="opve")

        def opmm(t, tt=tt):
            for nh in range(2):
                pbo = pb[2 + nh]
                t.matmul(pbo[:], ones_rowb[:], outb_row[0:1, nh * 512:nh * 512 + 512],
                         start=True, stop=False, skip_group_check=True)
                for ci in range(8):
                    inst = t.matmul(pbo[:], attnF_t[:, ci, :], ow_b[:, ci, nh * 512:nh * 512 + 512],
                                    start=False, stop=(ci == 7), skip_group_check=True)
            return inst
        p.step("t", opmm, waits=[("opve", p.v("opve")), ("gd", gd_ow),
                                 ("opac", p.v("opac"))], inc="opmm")

        def opadd(v, tt=tt):
            v.tensor_tensor(acc[:, tt, 0:512], acc[:, tt, 0:512], pb[2][:], op=ALU.add)
            return v.tensor_tensor(acc[:, tt, 512:1024], acc[:, tt, 512:1024], pb[3][:], op=ALU.add)
        p.step("v", opadd, waits=[("opmm", p.v("opmm")), ("qd", qd_acc)], inc="opac")

    es_OP.close()
    p.barrier()

    es_F = contextlib.ExitStack()
    w1e = es_F.enter_context(nc.sbuf_tensor("w1e_s", [128, 2, 8, 512], BF16, side="right"))
    w2e = es_F.enter_context(nc.sbuf_tensor("w2e_s", [128, 2, 4, 1024], BF16, side="right"))
    h_sb = es_F.enter_context(nc.sbuf_tensor("h_sb_s", [128, 4, 256], BF16, side="right"))
    b2_row = es_F.enter_context(nc.sbuf_tensor("b2_row_s", [1, NE * D], BF16, side="right"))

    gdma(lambda g: g.dma_start(out=b2_row[:], in_=b2Row))
    fmarks = {}
    for e in range(NE):
        for fe in range(8):
            qi = e * 8 + fe
            wp = qi % 2
            fq, half = fe // 2, fe % 2
            wdw = [("fm2", fmarks[qi - 2])] if qi >= 2 else []
            gdma(lambda g, e=e, fq=fq, half=half, wp=wp:
                 g.dma_start(out=w1e[:, wp], in_=w1_st[e, fq][:, :, half * 512:half * 512 + 512]),
                 waits=wdw)
            gdma(lambda g, e=e, fq=fq, half=half, wp=wp:
                 g.dma_start(out=w2e[:, wp], in_=w2_st[e, fq][:, half * 4:half * 4 + 4, :]))
            gd_f = p.v("gd")
            for tq in range(8):
                HB = (0, 1, 6, 7)

                def mk_mm1(fc, tq=tq, wp=wp):
                    def mm1(t, fc=fc, tq=tq, wp=wp):
                        pbh = pb[HB[fc]]
                        for ci in range(8):
                            inst = t.matmul(pbh[:, 0:256], w1e[:, wp, ci, fc * 128:fc * 128 + 128],
                                            xnTq[:, ci, tq * 256:tq * 256 + 256],
                                            start=(ci == 0), stop=(ci == 7),
                                            skip_group_check=True)
                        return inst
                    return mm1

                def mk_mm2(fc, e=e, fe=fe, wp=wp):
                    def mm2(t, fc=fc, e=e, fe=fe, wp=wp):
                        inst = None
                        for stt in range(2):
                            for nh in range(2):
                                pby = pb[2 + stt * 2 + nh]
                                if fe == 0 and fc == 0:
                                    t.matmul(pby[:], ones_rowb[:],
                                             b2_row[0:1, e * D + nh * 512:e * D + nh * 512 + 512],
                                             start=True, stop=False, skip_group_check=True)
                                inst = t.matmul(pby[:], h_sb[:, fc, stt * 128:stt * 128 + 128],
                                                w2e[:, wp, fc, nh * 512:nh * 512 + 512],
                                                start=(fe > 0 and fc == 0), stop=(fc == 3),
                                                skip_group_check=True)
                        return inst
                    return mm2

                def mk_gelu(fc, e=e, fe=fe):
                    def gelu(a, fc=fc, e=e, fe=fe):
                        j = e * 32 + fe * 4 + fc
                        return a.activation(h_sb[:, fc, :], pb[HB[fc]][:, 0:256],
                                            AF.Gelu_apprx_tanh, bias=b1T[:, j:j + 1])
                    return gelu

                for fc in range(4):
                    tw = [("gd", gd_f), ("fge", p.v("fge") - 3)]
                    p.step("t", mk_mm1(fc), waits=tw, inc="fmm")
                    p.step("a", mk_gelu(fc),
                           waits=[("fmm", p.v("fmm")),
                                  ("fm2", p.v("fm2") - (3 if fc == 0 else 2))],
                           inc="fge")
                    if fc >= 1:
                        w2w = [("fge", p.v("fge") - 1)]
                        if fc == 1 and qi + tq > 0:
                            w2w.append(("fev", p.v("fev")))
                        p.step("t", mk_mm2(fc - 1), waits=w2w, inc="fm2")
                p.step("t", mk_mm2(3), waits=[("fge", p.v("fge"))], inc="fm2")

                def fevac(v, e=e, tq=tq):
                    inst = None
                    for stx in range(2):
                        tt = tq * 2 + stx
                        cav = c_coef[:, tt * 8 + e:tt * 8 + e + 1]
                        for nh in range(2):
                            pby = pb[2 + stx * 2 + nh]
                            inst = v.scalar_tensor_tensor(
                                acc[:, tt, nh * 512:nh * 512 + 512], pby[:], cav,
                                acc[:, tt, nh * 512:nh * 512 + 512],
                                op0=ALU.mult, op1=ALU.add)
                    return inst
                p.step("v", fevac, waits=[("fm2", p.v("fm2"))], inc="fev")
            fmarks[qi] = p.v("fm2")

    es_F.close()
    p.barrier()

    qd(lambda q: q.dma_start(out=outp, in_=acc[:]))
    p.step("q", lambda q: q.nop(), waits=[("qd", p.v("qd"))])

    with nc.Block() as blk:
        p.emit(blk)
    es_ACC.close()
    es_glob.close()
    return nc


def _stage(core, x, qkv_w, qkv_b, out_w, out_b, Wg, bg, W1, b1, W2, b2):
    b = core // 2
    h = core % 2
    xf = np.ascontiguousarray(x.reshape(128, 128, D))
    rot = np.roll(np.arange(NTF), -16 * core)
    xrot = np.ascontiguousarray(xf[rot])
    xb = x.reshape(4, 4096, D)[b]
    own = xb[h * 2048:(h + 1) * 2048]
    oth = xb[(1 - h) * 2048:(2 - h) * 2048]
    xbatch = np.ascontiguousarray(
        np.concatenate([own, oth], axis=0).reshape(NTB, 128, D))
    xown = np.ascontiguousarray(own.reshape(NTO, 128, D).transpose(1, 0, 2))

    perm = np.arange(3 * D)
    for hh in range(NH):
        base = hh * HD
        for sec in (0, D):
            perm[sec + base:sec + base + 32] = sec + base + 2 * np.arange(32)
            perm[sec + base + 32:sec + base + 64] = sec + base + 2 * np.arange(32) + 1
    qkvw_p = qkv_w[:, perm]
    qkvb_p = qkv_b[perm]
    qkvw_st = np.ascontiguousarray(
        qkvw_p.reshape(8, 128, 3, 8, 128).transpose(2, 3, 1, 0, 4))
    qkvbT = np.ascontiguousarray(qkvb_p.reshape(24, 128).T)
    qkvbRow = np.ascontiguousarray(qkvb_p.reshape(1, 3 * D))
    outw_st = np.ascontiguousarray(out_w.reshape(8, 128, D).transpose(1, 0, 2))
    outbRow = np.ascontiguousarray(out_b.reshape(1, D))
    wg_stv = np.ascontiguousarray(Wg.reshape(8, 128, 8).transpose(1, 0, 2).reshape(128, 64))
    bgRow = np.ascontiguousarray(bg.reshape(1, 8))
    w1_stv = np.ascontiguousarray(
        W1.reshape(NE, 8, 128, 4, 1024).transpose(0, 3, 2, 1, 4))
    w2_stv = np.ascontiguousarray(
        W2.reshape(NE, 4, 8, 128, D).transpose(0, 1, 3, 2, 4))
    b1T = np.ascontiguousarray(b1.reshape(NE, 32, 128).transpose(2, 0, 1).reshape(128, 256))
    b2Row = np.ascontiguousarray(b2.reshape(1, NE * D))

    identf = np.eye(128, dtype=np.float32)
    identb = np.eye(128, dtype=np.float32).astype(ml_dtypes.bfloat16)
    swap = np.arange(128)
    swap = np.where((swap % 64) < 32, swap + 32, swap - 32)
    psw = np.zeros((128, 128), np.float32)
    psw[swap, np.arange(128)] = 1.0
    pswap = psw.astype(ml_dtypes.bfloat16)
    triu_tok = np.triu(np.ones((128, 128), np.float32), 1).astype(ml_dtypes.bfloat16)
    canon = rot
    triu_rot = (canon[:, None] < canon[None, :]).astype(np.float32)
    freqs = (10000.0 ** (np.arange(0, HD, 2.0, dtype=np.float32) / HD)).astype(np.float32)
    tpos = np.arange(4096, dtype=np.float32)
    stagedpos = np.concatenate([tpos[h * 2048:(h + 1) * 2048],
                                tpos[(1 - h) * 2048:(2 - h) * 2048]])
    ang = (stagedpos[:, None] * freqs[None, :]).astype(np.float32)
    cosv = np.cos(ang.astype(np.float64)).astype(np.float32)
    sinv = np.sin(ang.astype(np.float64)).astype(np.float32)
    prow = np.arange(128)
    cosq = np.ascontiguousarray(cosv[:, prow % 32].T).astype(ml_dtypes.bfloat16)
    sgn = np.where((prow % 64) < 32, -1.0, 1.0).astype(np.float32)
    sinq = np.ascontiguousarray((sinv[:, prow % 32] * sgn[None, :]).T).astype(ml_dtypes.bfloat16)

    return {
        "xrot": xrot, "xbatch": xbatch, "xown": xown,
        "qkvw_st": qkvw_st, "outw_st": outw_st, "qkvbT": qkvbT,
        "qkvbRow": qkvbRow, "outbRow": outbRow, "wg_st": wg_stv, "bgRow": bgRow,
        "w1_st": w1_stv, "w2_st": w2_stv, "b1T": b1T, "b2Row": b2Row,
        "identf": identf, "identb": identb, "pswap": pswap,
        "triu_tok": triu_tok, "triu_rot": triu_rot, "cosq": cosq, "sinq": sinq,
    }


def kernel(L0_attn_x, L0_attn_qkv_w, L0_attn_qkv_b, L0_attn_out_w, L0_attn_out_b,
           L0_ffn_Wg, L0_ffn_bg, L0_ffn_W1, L0_ffn_b1, L0_ffn_W2, L0_ffn_b2):
    x = np.ascontiguousarray(np.asarray(L0_attn_x, np.float32).reshape(16384, D))
    args = [np.asarray(a, np.float32) for a in
            (L0_attn_qkv_w, L0_attn_qkv_b, L0_attn_out_w, L0_attn_out_b,
             L0_ffn_Wg, L0_ffn_bg, L0_ffn_W1, L0_ffn_b1, L0_ffn_W2, L0_ffn_b2)]
    nc = build()
    in_maps = [_stage(c, x, *args) for c in range(NCORES)]
    res = run_bass_kernel_spmd(nc, in_maps, core_ids=list(range(NCORES)))
    shards = []
    for c in range(NCORES):
        o = np.asarray(res.results[c]["outp"])
        shards.append(o.transpose(1, 0, 2).reshape(2048, D))
    return np.concatenate(shards, axis=0).reshape(4, 4096, D).astype(np.float32)


# version 09 (final)
# speedup vs baseline: 1.5146x; 1.0070x over previous
import contextlib
import numpy as np
import ml_dtypes
import concourse.bass as bass
import concourse.mybir as mybir
from concourse.bass_utils import run_bass_kernel_spmd

F32 = mybir.dt.float32
BF16 = mybir.dt.bfloat16
AF = mybir.ActivationFunctionType
ALU = mybir.AluOpType
AX = mybir.AxisListType

D = 1024
NH = 16
HD = 64
FF = 4096
NE = 8
CAP = 4916.0
NCORES = 8
NTF = 128
NTO = 16
NTB = 32
EPS = 1e-5

CB_EPS, CB_NHALF, CB_SCALE, CB_EBIAS, CB_INVD, CB_NEG1 = 0, 1, 2, 3, 4, 5


class Plan:
    ORDER = (("q", "sync"), ("g", "gpsimd"), ("t", "tensor"), ("v", "vector"), ("a", "scalar"))

    def __init__(self, nc, es):
        self.nc = nc
        self.es = es
        self.steps = {e: [] for e, _ in self.ORDER}
        self.sems = {}
        self.cnt = {}

    def sem(self, name):
        if name not in self.sems:
            self.sems[name] = self.es.enter_context(self.nc.semaphore(name))
            self.cnt[name] = 0
        return self.sems[name]

    def step(self, eng, fn, waits=(), inc=None, amt=1):
        waits = tuple((s, v) for s, v in waits if v > 0)
        for s, _ in waits:
            self.sem(s)
        if inc is not None:
            self.sem(inc)
            self.cnt[inc] += amt
        self.steps[eng].append((fn, waits, inc, amt))
        return self.cnt[inc] if inc is not None else None

    def v(self, name):
        self.sem(name)
        return self.cnt[name]

    def barrier(self):
        for e in ("q", "g", "t", "v", "a"):
            w = []
            if e == "q":
                w.append(("qd", self.v("qd")))
            if e == "g":
                w.append(("gd", self.v("gd")))
            self.step(e, lambda en: en.nop(), waits=w, inc="bar")
        tot = self.v("bar")
        for e in ("q", "g", "t", "v", "a"):
            self.step(e, lambda en: en.nop(), waits=[("bar", tot)])

    def emit(self, blk):
        for engname, attr in self.ORDER:
            steps = self.steps[engname]

            def body(e, steps=steps):
                for fn, waits, inc, amt in steps:
                    for s, val in waits:
                        e.wait_ge(self.sems[s], val)
                    inst = fn(e)
                    if inc is not None:
                        assert inst is not None
                        inst.then_inc(self.sems[inc], amt)
            getattr(blk, attr)(body)


def build(debug=False):
    nc = bass.Bass("TRN2", target_bir_lowering=False)

    dt_in = lambda n, s, d=F32: nc.dram_tensor(n, s, d, kind="ExternalInput").ap()
    xrot = dt_in("xrot", [NTF, 128, D])
    xbatch = dt_in("xbatch", [NTB, 128, D])
    xown = dt_in("xown", [128, NTO, D])
    qkvw_st = dt_in("qkvw_st", [3, 8, 128, 8, 128])
    outw_st = dt_in("outw_st", [128, 8, D])
    qkvbT = dt_in("qkvbT", [128, 24])
    qkvbRow = dt_in("qkvbRow", [1, 3 * D])
    outbRow = dt_in("outbRow", [1, D])
    wg_st = dt_in("wg_st", [128, 64])
    bgRow = dt_in("bgRow", [1, 8])
    w1_st = dt_in("w1_st", [NE, 4, 128, 8, 1024])
    w2_st = dt_in("w2_st", [NE, 4, 128, 8, 1024])
    b1T_d = dt_in("b1T", [128, 256])
    b2Row = dt_in("b2Row", [1, NE * D])
    identf_d = dt_in("identf", [128, 128])
    identb_d = dt_in("identb", [128, 128], BF16)
    pswap_d = dt_in("pswap", [128, 128], BF16)
    triu_tok_d = dt_in("triu_tok", [128, 128], BF16)
    triu_rot_d = dt_in("triu_rot", [128, 128])
    cosq_d = dt_in("cosq", [128, 4096], BF16)
    sinq_d = dt_in("sinq", [128, 4096], BF16)
    outp = nc.dram_tensor("outp", [128, NTO, D], F32, kind="ExternalOutput").ap()
    scr_d = nc.dram_tensor("scr", [1, 4096], F32, kind="Internal").ap()
    dbg = {}
    if debug:
        for n, s, d in [("d_ccoef", [128, 128], F32), ("d_cnts", [128, 16], F32),
                        ("d_pref", [128, 16], F32), ("d_xnTq", [128, 2048], BF16),
                        ("d_attn", [128, NTO * D], BF16),
                        ("d_krot", [128, 4096], BF16), ("d_oh", [128, 32], F32)]:
            dbg[n] = nc.dram_tensor(n, s, d, kind="ExternalOutput").ap()

    es_glob = contextlib.ExitStack()
    p = Plan(nc, es_glob)
    sb = lambda n, s, d=F32: es_glob.enter_context(nc.sbuf_tensor(n, s, d))

    identf = sb("identf_s", [128, 128])
    identb = sb("identb_s", [128, 128], BF16)
    pswap = sb("pswap_s", [128, 128], BF16)
    triu_tok = sb("triu_tok_s", [128, 128], BF16)
    triu_rot = sb("triu_rot_s", [128, 128])
    wg_sb = sb("wg_sb_s", [128, 64])
    bg_sb = sb("bg_sb_s", [1, 8])
    qkvbT_sb = sb("qkvbT_sb_s", [128, 24])
    b1T = sb("b1T_sb_s", [128, 256])
    cbank = sb("cbank_s", [128, 8])
    ones_col = sb("ones_col_s", [128, 1])
    ones_colb = sb("ones_colb_s", [128, 1], BF16)
    ones_rowf = sb("ones_rowf_s", [1, 128])
    ones_rowb = sb("ones_rowb_s", [1, 128], BF16)
    zeros8 = sb("zeros8_s", [128, 8])
    c_coef = sb("c_coef_s", [128, NTO * 8])
    xnTq = sb("xnTq_s", [128, 8, 2048], BF16)
    attn_tok = sb("attn_tok_s", [128, NTO, D], BF16)

    pbig = es_glob.enter_context(nc.psum_tensor("pbig", [128, 2048], F32))
    pb = [pbig[:, i * 512:(i + 1) * 512] for i in range(4)]
    pb += [es_glob.enter_context(nc.psum_tensor(f"pb{i}", [128, 512], F32))[:] for i in range(4, 8)]

    def qd(fn, waits=()):
        return p.step("q", fn, waits=waits, inc="qd", amt=16)

    def gdma(fn, waits=()):
        return p.step("g", fn, waits=waits, inc="gd", amt=16)

    def vseq(fns, waits=(), inc=None):
        for i, fn in enumerate(fns):
            w = list(waits) + [("vch", p.v("vch"))] if i == 0 else [("vch", p.v("vch"))]
            p.step("v", fn, waits=w, inc=(inc if (i == len(fns) - 1 and inc) else "vch"))

    qd(lambda q: q.dma_start(out=identf[:], in_=identf_d))
    qd(lambda q: q.dma_start(out=triu_tok[:], in_=triu_tok_d))
    qd(lambda q: q.dma_start(out=triu_rot[:], in_=triu_rot_d))
    qd(lambda q: q.dma_start(out=wg_sb[:], in_=wg_st))
    qd(lambda q: q.dma_start(out=bg_sb[:], in_=bgRow))
    qd(lambda q: q.dma_start(out=qkvbT_sb[:], in_=qkvbT))
    qd(lambda q: q.dma_start(out=b1T[:], in_=b1T_d))
    qd(lambda q: q.dma_start(out=identb[:], in_=identb_d))
    qd(lambda q: q.dma_start(out=pswap[:], in_=pswap_d))

    def memsets(g):
        g.memset(cbank[:, CB_EPS:CB_EPS + 1], EPS)
        g.memset(cbank[:, CB_NHALF:CB_NHALF + 1], -0.5)
        g.memset(cbank[:, CB_SCALE:CB_SCALE + 1], 0.125)
        g.memset(cbank[:, CB_EBIAS:CB_EBIAS + 1], -12.0)
        g.memset(cbank[:, CB_INVD:CB_INVD + 1], 1.0 / D)
        g.memset(cbank[:, CB_NEG1:CB_NEG1 + 1], -1.0)
        g.memset(ones_col[:], 1.0)
        g.memset(ones_colb[:], 1.0)
        g.memset(ones_rowf[:], 1.0)
        g.memset(ones_rowb[:], 1.0)
        return g.memset(zeros8[:], 0.0)
    p.step("g", memsets, inc="gcst")
    p.barrier()

    es_XK = contextlib.ExitStack()
    xnTk = es_XK.enter_context(nc.sbuf_tensor("xnTk_s", [128, 8, 2048], BF16, side="right"))
    es_RX = contextlib.ExitStack()
    rsb = lambda n, s, d=F32: es_RX.enter_context(nc.sbuf_tensor(n, s, d, side="right"))
    xt2 = rsb("xt2_s", [128, 2, D])
    stat = rsb("stat_s", [128, 2, 12])
    xn32 = rsb("xn32_s", [128, 2, D])
    xnbf2 = rsb("xnbf2_s", [128, 2, D], BF16)
    xT32 = rsb("xT32_s", [128, D])
    l8 = rsb("l8_s", [128, 8])
    top8 = rsb("top8_s", [128, 8])
    tsc1 = rsb("tsc1_s", [128, 8])
    tsc2 = rsb("tsc2_s", [128, 8])
    gat = rsb("gat_s", [128, 8])
    oh1_all = rsb("oh1_all_s", [128, NTF * 8], BF16)
    oh2_all = rsb("oh2_all_s", [128, NTF * 8], BF16)
    c1_all = rsb("c1_all_s", [128, NTF * 8], BF16)
    c2_all = rsb("c2_all_s", [128, NTF * 8], BF16)
    cnts_row = rsb("cnts_row_s", [1, NTF * 16])
    pref_rows = cnts_row
    cnts_t = rsb("cnts_t_s", [128, 16])
    pref_sb = rsb("pref_sb_s", [128, 16])
    tot_sb = rsb("tot_sb_s", [1, 16])
    ktmp = rsb("ktmp_s", [128, 32])

    def ln_chain(n, src_tile, dump, aw=()):
        par = n % 2
        st = lambda c: stat[:, par, c:c + 1]
        wv = [("qd", p.v("qd")), ("lnarst", p.v("lnarst") - 1)]
        vseq([lambda v: v.reduce_sum(st(0), src_tile, axis=AX.X),
              lambda v: v.tensor_scalar(st(2), st(0), 1.0 / D, None, op0=ALU.mult),
              lambda v: v.scalar_tensor_tensor(st(3), st(2), -1.0, st(2),
                                               op0=ALU.mult, op1=ALU.mult)],
             waits=wv, inc="lnvst")
        p.step("a", lambda a: a.activation(dump, src_tile, AF.Square, accum_out=st(1)),
               waits=[("qd", p.v("qd"))] + list(aw), inc="lnassq")
        vseq([lambda v: v.scalar_tensor_tensor(st(4), st(1), 1.0 / D, st(3),
                                               op0=ALU.mult, op1=ALU.add)],
             waits=[("lnassq", p.v("lnassq"))], inc="lnvvar")

        def archain(a):
            a.activation(st(5), st(4), AF.Ln, bias=cbank[:, CB_EPS:CB_EPS + 1])
            return a.activation(st(6), st(5), AF.Exp, scale=cbank[:, CB_NHALF:CB_NHALF + 1])
        p.step("a", archain, waits=[("lnvvar", p.v("lnvvar"))], inc="lnarst")
        return st

    for bt in range(NTB):
        par = bt % 2
        xt = xt2[:, par, :]
        qd(lambda q, bt=bt, par=par: q.dma_start(out=xt2[:, par, :], in_=xbatch[bt]),
           waits=[("lnaxn", p.v("lnaxn") - 1)])
        st = ln_chain(bt, xt, xn32[:, par, :])
        vseq([lambda v, xt=xt, par=par, st=st:
              v.tensor_scalar(xnbf2[:, par, :], xt, st(2), st(6),
                              op0=ALU.subtract, op1=ALU.mult)],
             waits=[("xttr", p.v("xttr") - 1), ("lnarst", p.v("lnarst"))], inc="lnaxn")
        b0 = 0 if par == 0 else 2

        def transb(t, b0=b0, par=par):
            pbB = pb[b0].bitcast(BF16)
            for ci in range(8):
                inst = t.transpose(pbB[:, ci * 128:ci * 128 + 128],
                                   xnbf2[:, par, ci * 128:ci * 128 + 128], identb[:])
            return inst
        p.step("t", transb, waits=[("lnaxn", p.v("lnaxn")), ("xvev", p.v("xvev") - 1)],
               inc="xttr")
        tgt = xnTq if bt < NTO else xnTk
        off = (bt % NTO) * 128

        def evtrb(v, b0=b0, tgt=tgt, off=off):
            return v.tensor_copy(tgt[:, :, off:off + 128],
                                 pb[b0].bitcast(BF16).rearrange("p (c k) -> p c k", k=128))
        p.step("v", evtrb, waits=[("xttr", p.v("xttr"))], inc="xvev")

    if debug:
        qd(lambda q: q.dma_start(out=dbg["d_xnTq"], in_=xnTq[:, 0, :]),
           waits=[("xvev", p.v("xvev"))])

    p.barrier()

    rmark = {"rtlg": {}, "rvoh": {}, "rtcnt": {}}

    def rt_part1(g, epoch_first):
        par = g % 2
        xt = xt2[:, par, :]
        qd(lambda q, g=g, par=par: q.dma_start(out=xt2[:, par, :], in_=xrot[g]),
           waits=[("lnaxn", p.v("lnaxn") - 1)])
        aw = [("xttr", p.v("xttr"))] if g < 2 else ()
        st = ln_chain(NTB + g, xt, xnbf2[:, par, :], aw)
        vseq([lambda v, xt=xt, par=par, st=st:
              v.tensor_scalar(xn32[:, par, :], xt, st(2), st(6),
                              op0=ALU.subtract, op1=ALU.mult)],
             waits=[("rttr", p.v("rttr") - 2), ("lnarst", p.v("lnarst"))], inc="lnaxn")

        for hfi in range(2):
            def trh(t, hfi=hfi, par=par):
                for ci in range(4):
                    cc = hfi * 4 + ci
                    inst = t.transpose(pb[6][:, ci * 128:ci * 128 + 128],
                                       xn32[:, par, cc * 128:cc * 128 + 128], identf[:])
                return inst
            tw = [("rvev", p.v("rvev"))]
            if hfi == 0:
                tw.append(("lnaxn", p.v("lnaxn")))
            p.step("t", trh, waits=tw, inc="rttr")
            p.step("v", lambda v, hfi=hfi: v.tensor_copy(xT32[:, hfi * 512:hfi * 512 + 512], pb[6]),
                   waits=[("rttr", p.v("rttr"))], inc="rvev")

        sl = (g % 8) * 8

        def lgmm(t, sl=sl):
            for ci in range(8):
                t.matmul(pb[7][:, sl:sl + 8], xT32[:, ci * 128:ci * 128 + 128],
                         wg_sb[:, ci * 8:ci * 8 + 8], start=(ci == 0), stop=False)
            return t.matmul(pb[7][:, sl:sl + 8], ones_rowf[:], bg_sb[:], start=False, stop=True)
        lw = [("rvev", p.v("rvev"))]
        if epoch_first:
            lw += [("rvcp", p.v("rvcp")), ("rvcv", p.v("rvcv"))]
        p.step("t", lgmm, waits=lw, inc="rtlg")
        rmark["rtlg"][g] = p.v("rtlg")

    def rt_part2(g):
        sl = (g % 8) * 8
        p.step("v", lambda v, sl=sl: v.tensor_copy(l8[:], pb[7][:, sl:sl + 8]),
               waits=[("rtlg", p.v("rtlg")), ("rtcnt", p.v("rtcnt"))], inc="rvcp")
        p.step("v", lambda v: v.max(top8[:], l8[:]),
               waits=[("rvcp", p.v("rvcp"))], inc="rvmx")
        p.step("v", lambda v: v.tensor_sub(gat[:, 0:1], top8[:, 1:2], top8[:, 0:1]),
               waits=[("rvmx", p.v("rvmx"))], inc="rvl8")
        p.step("a", lambda a: a.activation(gat[:, 3:4], gat[:, 0:1], AF.Exp),
               waits=[("rvl8", p.v("rvl8"))], inc="raeg")

        vseq([lambda v: (v.tensor_sub(tsc1[:], l8[:], top8[:, 0:1].to_broadcast([128, 8])),
                         v.tensor_scalar(gat[:, 1:2], gat[:, 0:1], 0.0, None, op0=ALU.is_equal))[-1],
              lambda v: (v.tensor_scalar(tsc1[:], tsc1[:], 0.0, None, op0=ALU.is_equal),
                         v.tensor_scalar(gat[:, 2:3], gat[:, 1:2], 1.0, None, op0=ALU.add))[-1],
              lambda v: v.tensor_tensor_scan(tsc2[:], tsc1[:], zeros8[:], 0.0, op0=ALU.add, op1=ALU.add),
              lambda v: v.tensor_scalar(tsc2[:], tsc2[:], 1.0, None, op0=ALU.is_equal),
              lambda v, g=g: v.tensor_tensor(oh1_all[:, g * 8:g * 8 + 8], tsc1[:], tsc2[:], op=ALU.mult)],
             waits=[("raeg", p.v("raeg"))], inc="rvo1")

        vseq([lambda v: (v.tensor_sub(tsc1[:], l8[:], top8[:, 1:2].to_broadcast([128, 8])),
                         v.tensor_scalar(gat[:, 4:5], gat[:, 3:4], 1.0, None, op0=ALU.add))[-1],
              lambda v: (v.tensor_scalar(tsc1[:], tsc1[:], 0.0, None, op0=ALU.is_equal),
                         v.reciprocal(gat[:, 5:6], gat[:, 4:5]))[-1],
              lambda v: v.tensor_tensor_scan(tsc2[:], tsc1[:], zeros8[:], 0.0, op0=ALU.add, op1=ALU.add),
              lambda v: v.tensor_sub(tsc2[:], tsc2[:], gat[:, 2:3].to_broadcast([128, 8])),
              lambda v: v.tensor_scalar(tsc2[:], tsc2[:], 0.0, None, op0=ALU.is_equal),
              lambda v, g=g: v.tensor_tensor(oh2_all[:, g * 8:g * 8 + 8], tsc1[:], tsc2[:], op=ALU.mult)],
             waits=[("rvo1", p.v("rvo1"))], inc="rvo2")

        vseq([lambda v: v.tensor_tensor(gat[:, 6:7], gat[:, 3:4], gat[:, 5:6], op=ALU.mult),
              lambda v, g=g: (v.tensor_mul(c1_all[:, g * 8:g * 8 + 8], oh1_all[:, g * 8:g * 8 + 8],
                                           gat[:, 5:6].to_broadcast([128, 8])),
                              v.tensor_mul(c2_all[:, g * 8:g * 8 + 8], oh2_all[:, g * 8:g * 8 + 8],
                                           gat[:, 6:7].to_broadcast([128, 8])))[-1]],
             waits=[("rvo2", p.v("rvo2"))], inc="rvoh")
        rmark["rvoh"][g] = p.v("rvoh")

    def rt_part3(g):
        sl = 64 + (g % 8) * 16

        def cntmm(t, g=g, sl=sl):
            t.matmul(pb[7][0:1, sl:sl + 8], ones_colb[:], oh1_all[:, g * 8:g * 8 + 8],
                     start=True, stop=True, skip_group_check=True)
            return t.matmul(pb[7][0:1, sl + 8:sl + 16], ones_colb[:], oh2_all[:, g * 8:g * 8 + 8],
                            start=True, stop=True, skip_group_check=True)
        p.step("t", cntmm, waits=[("rvoh", p.v("rvoh"))], inc="rtcnt")
        rmark["rtcnt"][g] = p.v("rtcnt")
        p.step("v", lambda v, g=g, sl=sl:
               v.tensor_copy(cnts_row[0:1, g * 16:g * 16 + 16], pb[7][0:1, sl:sl + 16]),
               waits=[("rtcnt", rmark["rtcnt"][g])], inc="rvcv")

    es_A = contextlib.ExitStack()
    asb = lambda n, s, d=F32: es_A.enter_context(nc.sbuf_tensor(n, s, d, side="right"))
    cosq = asb("cosq_s", [128, 4096], BF16)
    sinq = asb("sinq_s", [128, 4096], BF16)
    wq_b = asb("wq_b_s", [128, 8, 128], BF16)
    wk_b = asb("wk_b_s", [128, 8, 128], BF16)
    wv_b = asb("wv_b_s", [128, 8, 128], BF16)
    kq_sb = asb("kq_sb_s", [128, 2, 512], BF16)
    ttmp = asb("ttmp_s", [128, 512], BF16)
    krot = asb("krot_s", [128, 4096], BF16)
    qrot = asb("qrot_s", [128, 2048], BF16)
    vt = asb("vt_s", [128, 32, 130], BF16)
    pt_sb = asb("pt_sb_s", [128, 2, 2, 1024], BF16)
    recip_sb = asb("recip_sb_s", [128, 8])
    qkvb_vrow = asb("qkvb_vrow_s", [1, D], BF16)

    qd(lambda q: q.dma_start(out=cosq[:], in_=cosq_d))
    qd(lambda q: q.dma_start(out=sinq[:], in_=sinq_d))
    gdma(lambda g: g.dma_start(out=qkvb_vrow[:], in_=qkvbRow[0:1, 2 * D:3 * D]))
    p.step("v", lambda v: v.memset(vt[:], 1.0), inc="avones")

    def xnT_slice(ci, t0, nn):
        if t0 < 2048:
            return xnTq[:, ci, t0:t0 + nn]
        return xnTk[:, ci, t0 - 2048:t0 - 2048 + nn]

    rg = 0
    for hp in range(8):
        par = hp % 2
        wdw = [("atdone", p.v("atdone"))]
        for s, wt in ((0, wq_b), (1, wk_b), (2, wv_b)):
            gdma(lambda g, s=s, wt=wt, hp=hp:
                 g.dma_start(out=wt[:], in_=qkvw_st[s, hp]), waits=wdw)
        gd_w = p.v("gd")

        for kind in range(12):
            isq = kind >= 8
            it = kind - 8 if isq else kind
            pbm = pb[kind % 2]
            pbs = pb[2 + kind % 2]

            def kqmm(t, isq=isq, it=it, pbm=pbm):
                w = wq_b if isq else wk_b
                for ci in range(8):
                    rhs = xnTq[:, ci, it * 512:it * 512 + 512] if isq else xnT_slice(ci, it * 512, 512)
                    inst = t.matmul(pbm[:], w[:, ci, :], rhs,
                                    start=(ci == 0), stop=(ci == 7))
                return inst
            tw = [("gd", gd_w), ("aaev", p.v("aaev") - 1), ("avrp", p.v("avrp") - 1)]
            if hp == 0 and kind < 4:
                tw.append(("xvev", p.v("xvev")))
            p.step("t", kqmm, waits=tw, inc="atkm")
            bcol = hp if isq else 8 + hp
            p.step("a", lambda a, kind=kind, pbm=pbm, bcol=bcol:
                   a.activation(kq_sb[:, kind % 2, :], pbm[:], AF.Identity,
                                bias=qkvbT_sb[:, bcol:bcol + 1]),
                   waits=[("atkm", p.v("atkm"))], inc="aaev")

            def swmm(t, kind=kind, pbs=pbs):
                return t.matmul(pbs[:], pswap[:], kq_sb[:, kind % 2, :], start=True, stop=True)
            p.step("t", swmm, waits=[("aaev", p.v("aaev")), ("avrp", p.v("avrp") - 1)], inc="atsw")
            dst = qrot if isq else krot

            vseq([lambda v, kind=kind, it=it, pbs=pbs, dst=dst:
                  (v.tensor_tensor(ttmp[:], kq_sb[:, kind % 2, :],
                                   cosq[:, it * 512:it * 512 + 512], op=ALU.mult),
                   v.tensor_tensor(dst[:, it * 512:it * 512 + 512], pbs[:],
                                   sinq[:, it * 512:it * 512 + 512], op=ALU.mult))[-1],
                  lambda v, kind=kind, it=it, dst=dst:
                  v.tensor_tensor(dst[:, it * 512:it * 512 + 512],
                                  dst[:, it * 512:it * 512 + 512], ttmp[:], op=ALU.add)],
                 waits=[("atsw", p.v("atsw"))], inc="avrp")

        for tt in range(32):
            pbv = pb[4 + tt % 2]

            def vmm(t, tt=tt, pbv=pbv, hp=hp):
                for ci in range(8):
                    t.matmul(pbv[:, 0:128], xnT_slice(ci, tt * 128, 128), wv_b[:, ci, :],
                             start=(ci == 0), stop=False)
                return t.matmul(pbv[:, 0:128], ones_rowb[:],
                                qkvb_vrow[0:1, hp * 128:hp * 128 + 128],
                                start=False, stop=True)
            vw = [("gd", gd_w), ("avve", p.v("avve") - 1)]
            if hp == 0 and tt < 2:
                vw.append(("gd", p.v("gd")))
            p.step("t", vmm, waits=vw, inc="atvm")

            def vev(v, tt=tt, pbv=pbv):
                v.tensor_copy(vt[:, tt, 0:64], pbv[:, 0:64])
                return v.tensor_copy(vt[:, tt, 65:129], pbv[:, 64:128])
            p.step("v", vev, waits=[("atvm", p.v("atvm")), ("avones", p.v("avones"))], inc="avve")

        avrp_done = p.v("avrp")
        avve_done = p.v("avve")
        pbig4 = pbig[:, 0:2048].rearrange("p (h k q) -> p h k q", h=2, k=2)
        for qt in range(4):
            fs0 = p.v("afx")
            for kc in range(32):
                sb0 = pb[kc % 2]
                sb1 = pb[2 + kc % 2]

                def smm(t, kc=kc, qt=qt, sb0=sb0, sb1=sb1):
                    t.matmul(sb0[:], krot[0:64, kc * 128:kc * 128 + 128],
                             qrot[0:64, qt * 512:qt * 512 + 512], start=True, stop=True,
                             skip_group_check=True)
                    return t.matmul(sb1[:], krot[64:128, kc * 128:kc * 128 + 128],
                                    qrot[64:128, qt * 512:qt * 512 + 512], start=True, stop=True,
                                    skip_group_check=True)
                tw = [("avrp", avrp_done), ("avve", avve_done),
                      ("afx", fs0 + (kc - 1) if kc >= 2 else fs0)]
                if kc == 0 and qt > 0:
                    tw.append(("avno", p.v("avno")))
                p.step("t", smm, waits=tw, inc="afs", amt=2)

                pj = kc // 2
                par = kc % 2

                def expev(a, pj=pj, par=par):
                    return a.activation(pt_sb[:, :, pj % 2, par * 512:par * 512 + 512],
                                        pbig4[:, :, par, :], AF.Exp,
                                        scale=cbank[:, CB_SCALE:CB_SCALE + 1],
                                        bias=cbank[:, CB_EBIAS:CB_EBIAS + 1])
                p.step("a", expev,
                       waits=[("afs", p.v("afs")), ("afa", p.v("afa") - 1)], inc="afx", amt=1)

                if kc % 2 == 1:
                    def avmm(t, kc=kc, pj=pj):
                        for k2 in (kc - 1, kc):
                            for h in range(2):
                                pba = pb[4 + h]
                                for qs in range(4):
                                    inst = t.matmul(pba[:, qs * 65:qs * 65 + 65],
                                                    pt_sb[:, h, pj % 2,
                                                          (k2 % 2) * 512 + qs * 128:(k2 % 2) * 512 + qs * 128 + 128],
                                                    vt[:, k2, h * 65:h * 65 + 65],
                                                    start=(k2 == 0), stop=(k2 == 31),
                                                    skip_group_check=True)
                        return inst
                    p.step("t", avmm, waits=[("afx", p.v("afx"))], inc="afa")

            def norm1(v):
                for h in range(2):
                    pba = pb[4 + h]
                    for qs in range(4):
                        inst = v.reciprocal(recip_sb[:, h * 4 + qs:h * 4 + qs + 1],
                                            pba[:, qs * 65 + 64:qs * 65 + 65])
                return inst
            p.step("v", norm1, waits=[("afa", p.v("afa"))], inc="avn1")

            def norm2(v, qt=qt, hp=hp):
                for h in range(2):
                    pba = pb[4 + h]
                    for qs in range(4):
                        tt = qt * 4 + qs
                        inst = v.tensor_mul(
                            attn_tok[:, tt, (2 * hp + h) * 64:(2 * hp + h) * 64 + 64],
                            pba[:, qs * 65:qs * 65 + 64],
                            recip_sb[:, h * 4 + qs:h * 4 + qs + 1].to_broadcast([128, 64]))
                return inst
            p.step("v", norm2, waits=[("avn1", p.v("avn1"))], inc="avno")

            blk4 = range(rg, rg + 4)
            for jj, g in enumerate(blk4):
                rt_part1(g, epoch_first=(jj == 0))
            for g in range(rg - 8, rg - 4):
                if g >= 0:
                    rt_part3(g)
            for g in range(rg - 4, rg):
                if g >= 0:
                    rt_part2(g)
            rg += 4
        p.step("t", lambda t: t.nop(), waits=[("avno", p.v("avno"))], inc="atdone")

    assert rg == NTF
    for g in range(NTF - 8, NTF - 4):
        rt_part3(g)
    for g in range(NTF - 4, NTF):
        rt_part2(g)
    for g in range(NTF - 4, NTF):
        rt_part3(g)

    gdma(lambda g: g.dma_start(out=scr_d[0:1, 0:2048], in_=cnts_row[:]),
         waits=[("rvcv", p.v("rvcv"))])
    gdma(lambda g: g.dma_start(out=cnts_t[:],
                               in_=scr_d[0:1, 0:2048].rearrange("o (g j) -> (o g) j", j=16)),
         waits=[("gd", p.v("gd"))])

    def prefmm(t):
        t.matmul(pb[6][:, 0:16], triu_rot[:], cnts_t[:], start=True, stop=True,
                 skip_group_check=True)
        return t.matmul(pb[7][0:1, 0:16], ones_col[:], cnts_t[:], start=True, stop=True,
                        skip_group_check=True)
    p.step("t", prefmm, waits=[("gd", p.v("gd")), ("rvcv", p.v("rvcv"))], inc="rtpre")

    def evpre(v):
        v.tensor_copy(pref_sb[:], pb[6][:, 0:16])
        return v.tensor_copy(tot_sb[:], pb[7][0:1, 0:16])
    p.step("v", evpre, waits=[("rtpre", p.v("rtpre"))], inc="rvpre")
    gdma(lambda g: g.dma_start(out=scr_d[0:1, 0:2048].rearrange("o (g j) -> (o g) j", j=16),
                               in_=pref_sb[:]),
         waits=[("rvpre", p.v("rvpre"))])
    gdma(lambda g: g.dma_start(out=pref_rows[0:1, 0:2048], in_=scr_d[0:1, 0:2048]),
         waits=[("gd", p.v("gd"))])
    gd_pref = p.v("gd")

    for ti in range(NTO):
        bb = pb[6 + ti % 2]

        def posmm(t, ti=ti, bb=bb):
            t.matmul(bb[:, 0:8], triu_tok[:], oh1_all[:, ti * 8:ti * 8 + 8],
                     start=True, stop=False, skip_group_check=True)
            t.matmul(bb[:, 0:8], ones_rowf[:], pref_rows[0:1, ti * 16:ti * 16 + 8],
                     start=False, stop=True, skip_group_check=True)
            t.matmul(bb[:, 8:16], triu_tok[:], oh2_all[:, ti * 8:ti * 8 + 8],
                     start=True, stop=False, skip_group_check=True)
            t.matmul(bb[:, 8:16], ones_rowf[:], pref_rows[0:1, ti * 16 + 8:ti * 16 + 16],
                     start=False, stop=False, skip_group_check=True)
            return t.matmul(bb[:, 8:16], ones_rowf[:], tot_sb[0:1, 0:8],
                            start=False, stop=True, skip_group_check=True)
        p.step("t", posmm, waits=[("gd", gd_pref), ("rvcc", p.v("rvcc") - 1)], inc="rtpos")

        vseq([lambda v, ti=ti, bb=bb:
              (v.tensor_scalar(ktmp[:, 0:8], bb[:, 0:8], CAP, None, op0=ALU.is_lt),
               v.tensor_scalar(ktmp[:, 8:16], bb[:, 8:16], CAP, None, op0=ALU.is_lt))[-1],
              lambda v, ti=ti: (v.tensor_tensor(ktmp[:, 16:24], c1_all[:, ti * 8:ti * 8 + 8],
                                                ktmp[:, 0:8], op=ALU.mult),
                                v.tensor_tensor(ktmp[:, 24:32], c2_all[:, ti * 8:ti * 8 + 8],
                                                ktmp[:, 8:16], op=ALU.mult))[-1],
              lambda v, ti=ti: v.tensor_tensor(c_coef[:, ti * 8:ti * 8 + 8],
                                               ktmp[:, 16:24], ktmp[:, 24:32], op=ALU.add)],
             waits=[("rtpos", p.v("rtpos"))], inc="rvcc")

    if debug:
        qd(lambda q: q.dma_start(out=dbg["d_cnts"], in_=cnts_t[:]),
           waits=[("rvcc", p.v("rvcc"))])
        qd(lambda q: q.dma_start(out=dbg["d_pref"], in_=pref_sb[:]))
        qd(lambda q: q.dma_start(out=dbg["d_oh"], in_=oh1_all[:, 0:32]))
        qd(lambda q: q.dma_start(out=dbg["d_ccoef"], in_=c_coef[:]))
        qd(lambda q: q.dma_start(out=dbg["d_krot"], in_=krot[:]))
        qd(lambda q: q.dma_start(out=dbg["d_attn"],
                                 in_=attn_tok[:].rearrange("p a b -> p (a b)")),
           waits=[("avno", p.v("avno"))])

    es_A.close()
    es_RX.close()
    es_XK.close()
    p.barrier()

    es_ACC = contextlib.ExitStack()
    acc = es_ACC.enter_context(nc.sbuf_tensor("acc_s", [128, NTO, D], F32, side="right"))
    es_OP = contextlib.ExitStack()
    ow_b = es_OP.enter_context(nc.sbuf_tensor("ow_b_s", [128, 8, D], BF16, side="right"))
    attnF_t = es_OP.enter_context(nc.sbuf_tensor("attnF_t_s", [128, 8, 128], BF16, side="right"))
    outb_row = es_OP.enter_context(nc.sbuf_tensor("outb_row_s", [1, D], BF16, side="right"))

    gdma(lambda g: g.dma_start(out=ow_b[:], in_=outw_st))
    gdma(lambda g: g.dma_start(out=outb_row[:], in_=outbRow))
    gd_ow = p.v("gd")
    qd(lambda q: q.dma_start(out=acc[:], in_=xown))
    qd_acc = p.v("qd")

    for tt in range(NTO):
        def optr(t, tt=tt):
            pbB = pb[0].bitcast(BF16)
            for ci in range(8):
                inst = t.transpose(pbB[:, ci * 128:ci * 128 + 128],
                                   attn_tok[:, tt, ci * 128:ci * 128 + 128], identb[:])
            return inst
        p.step("t", optr, waits=[("opve", p.v("opve"))], inc="optr")

        def opev(v):
            return v.tensor_copy(attnF_t[:, :, :],
                                 pb[0].bitcast(BF16).rearrange("p (c k) -> p c k", k=128))
        p.step("v", opev, waits=[("optr", p.v("optr"))], inc="opve")

        def opmm(t, tt=tt):
            for nh in range(2):
                pbo = pb[2 + nh]
                t.matmul(pbo[:], ones_rowb[:], outb_row[0:1, nh * 512:nh * 512 + 512],
                         start=True, stop=False, skip_group_check=True)
                for ci in range(8):
                    inst = t.matmul(pbo[:], attnF_t[:, ci, :], ow_b[:, ci, nh * 512:nh * 512 + 512],
                                    start=False, stop=(ci == 7), skip_group_check=True)
            return inst
        p.step("t", opmm, waits=[("opve", p.v("opve")), ("gd", gd_ow),
                                 ("opac", p.v("opac"))], inc="opmm")

        def opadd(v, tt=tt):
            v.tensor_tensor(acc[:, tt, 0:512], acc[:, tt, 0:512], pb[2][:], op=ALU.add)
            return v.tensor_tensor(acc[:, tt, 512:1024], acc[:, tt, 512:1024], pb[3][:], op=ALU.add)
        p.step("v", opadd, waits=[("opmm", p.v("opmm")), ("qd", qd_acc)], inc="opac")

    es_OP.close()
    p.barrier()

    es_F = contextlib.ExitStack()
    w1e = es_F.enter_context(nc.sbuf_tensor("w1e_s", [128, 2, 8, 512], BF16, side="right"))
    w2e = es_F.enter_context(nc.sbuf_tensor("w2e_s", [128, 2, 4, 1024], BF16, side="right"))
    h_sb = es_F.enter_context(nc.sbuf_tensor("h_sb_s", [128, 4, 256], BF16, side="right"))
    b2_row = es_F.enter_context(nc.sbuf_tensor("b2_row_s", [1, NE * D], BF16, side="right"))

    gdma(lambda g: g.dma_start(out=b2_row[:], in_=b2Row))
    fmarks = {}
    for e in range(NE):
        for fe in range(8):
            qi = e * 8 + fe
            wp = qi % 2
            fq, half = fe // 2, fe % 2
            wdw = [("fm2", fmarks[qi - 2])] if qi >= 2 else []
            gdma(lambda g, e=e, fq=fq, half=half, wp=wp:
                 g.dma_start(out=w1e[:, wp], in_=w1_st[e, fq][:, :, half * 512:half * 512 + 512]),
                 waits=wdw)
            gdma(lambda g, e=e, fq=fq, half=half, wp=wp:
                 g.dma_start(out=w2e[:, wp], in_=w2_st[e, fq][:, half * 4:half * 4 + 4, :]))
            gd_f = p.v("gd")
            for tq in range(8):
                HB = (0, 1, 6, 7)

                def mk_mm1(fc, tq=tq, wp=wp):
                    def mm1(t, fc=fc, tq=tq, wp=wp):
                        pbh = pb[HB[fc]]
                        for ci in range(8):
                            inst = t.matmul(pbh[:, 0:256], w1e[:, wp, ci, fc * 128:fc * 128 + 128],
                                            xnTq[:, ci, tq * 256:tq * 256 + 256],
                                            start=(ci == 0), stop=(ci == 7),
                                            skip_group_check=True)
                        return inst
                    return mm1

                def mk_mm2(fc, e=e, fe=fe, wp=wp):
                    def mm2(t, fc=fc, e=e, fe=fe, wp=wp):
                        inst = None
                        for stt in range(2):
                            for nh in range(2):
                                pby = pb[2 + stt * 2 + nh]
                                if fe == 0 and fc == 0:
                                    t.matmul(pby[:], ones_rowb[:],
                                             b2_row[0:1, e * D + nh * 512:e * D + nh * 512 + 512],
                                             start=True, stop=False, skip_group_check=True)
                                inst = t.matmul(pby[:], h_sb[:, fc, stt * 128:stt * 128 + 128],
                                                w2e[:, wp, fc, nh * 512:nh * 512 + 512],
                                                start=(fe > 0 and fc == 0), stop=(fc == 3),
                                                skip_group_check=True)
                        return inst
                    return mm2

                def mk_gelu(fc, e=e, fe=fe):
                    def gelu(a, fc=fc, e=e, fe=fe):
                        j = e * 32 + fe * 4 + fc
                        return a.activation(h_sb[:, fc, :], pb[HB[fc]][:, 0:256],
                                            AF.Gelu_apprx_tanh, bias=b1T[:, j:j + 1])
                    return gelu

                for fc in range(4):
                    tw = [("gd", gd_f), ("fge", p.v("fge") - 3)]
                    p.step("t", mk_mm1(fc), waits=tw, inc="fmm")
                    p.step("a", mk_gelu(fc),
                           waits=[("fmm", p.v("fmm")),
                                  ("fm2", p.v("fm2") - (3 if fc == 0 else 2))],
                           inc="fge")
                    if fc >= 1:
                        w2w = [("fge", p.v("fge") - 1)]
                        if fc == 1 and qi + tq > 0:
                            w2w.append(("fev", p.v("fev")))
                        p.step("t", mk_mm2(fc - 1), waits=w2w, inc="fm2")
                p.step("t", mk_mm2(3), waits=[("fge", p.v("fge"))], inc="fm2")

                def fevac(v, e=e, tq=tq):
                    inst = None
                    for stx in range(2):
                        tt = tq * 2 + stx
                        cav = c_coef[:, tt * 8 + e:tt * 8 + e + 1]
                        for nh in range(2):
                            pby = pb[2 + stx * 2 + nh]
                            inst = v.scalar_tensor_tensor(
                                acc[:, tt, nh * 512:nh * 512 + 512], pby[:], cav,
                                acc[:, tt, nh * 512:nh * 512 + 512],
                                op0=ALU.mult, op1=ALU.add)
                    return inst
                p.step("v", fevac, waits=[("fm2", p.v("fm2"))], inc="fev")
            fmarks[qi] = p.v("fm2")

    es_F.close()
    p.barrier()

    qd(lambda q: q.dma_start(out=outp, in_=acc[:]))
    p.step("q", lambda q: q.nop(), waits=[("qd", p.v("qd"))])

    with nc.Block() as blk:
        p.emit(blk)
    es_ACC.close()
    es_glob.close()
    return nc


def _stage(core, x, qkv_w, qkv_b, out_w, out_b, Wg, bg, W1, b1, W2, b2):
    b = core // 2
    h = core % 2
    xf = np.ascontiguousarray(x.reshape(128, 128, D))
    rot = np.roll(np.arange(NTF), -16 * core)
    xrot = np.ascontiguousarray(xf[rot])
    xb = x.reshape(4, 4096, D)[b]
    own = xb[h * 2048:(h + 1) * 2048]
    oth = xb[(1 - h) * 2048:(2 - h) * 2048]
    xbatch = np.ascontiguousarray(
        np.concatenate([own, oth], axis=0).reshape(NTB, 128, D))
    xown = np.ascontiguousarray(own.reshape(NTO, 128, D).transpose(1, 0, 2))

    perm = np.arange(3 * D)
    for hh in range(NH):
        base = hh * HD
        for sec in (0, D):
            perm[sec + base:sec + base + 32] = sec + base + 2 * np.arange(32)
            perm[sec + base + 32:sec + base + 64] = sec + base + 2 * np.arange(32) + 1
    qkvw_p = qkv_w[:, perm]
    qkvb_p = qkv_b[perm]
    qkvw_st = np.ascontiguousarray(
        qkvw_p.reshape(8, 128, 3, 8, 128).transpose(2, 3, 1, 0, 4))
    qkvbT = np.ascontiguousarray(qkvb_p.reshape(24, 128).T)
    qkvbRow = np.ascontiguousarray(qkvb_p.reshape(1, 3 * D))
    outw_st = np.ascontiguousarray(out_w.reshape(8, 128, D).transpose(1, 0, 2))
    outbRow = np.ascontiguousarray(out_b.reshape(1, D))
    wg_stv = np.ascontiguousarray(Wg.reshape(8, 128, 8).transpose(1, 0, 2).reshape(128, 64))
    bgRow = np.ascontiguousarray(bg.reshape(1, 8))
    w1_stv = np.ascontiguousarray(
        W1.reshape(NE, 8, 128, 4, 1024).transpose(0, 3, 2, 1, 4))
    w2_stv = np.ascontiguousarray(
        W2.reshape(NE, 4, 8, 128, D).transpose(0, 1, 3, 2, 4))
    b1T = np.ascontiguousarray(b1.reshape(NE, 32, 128).transpose(2, 0, 1).reshape(128, 256))
    b2Row = np.ascontiguousarray(b2.reshape(1, NE * D))

    identf = np.eye(128, dtype=np.float32)
    identb = np.eye(128, dtype=np.float32).astype(ml_dtypes.bfloat16)
    swap = np.arange(128)
    swap = np.where((swap % 64) < 32, swap + 32, swap - 32)
    psw = np.zeros((128, 128), np.float32)
    psw[swap, np.arange(128)] = 1.0
    pswap = psw.astype(ml_dtypes.bfloat16)
    triu_tok = np.triu(np.ones((128, 128), np.float32), 1).astype(ml_dtypes.bfloat16)
    canon = rot
    triu_rot = (canon[:, None] < canon[None, :]).astype(np.float32)
    freqs = (10000.0 ** (np.arange(0, HD, 2.0, dtype=np.float32) / HD)).astype(np.float32)
    tpos = np.arange(4096, dtype=np.float32)
    stagedpos = np.concatenate([tpos[h * 2048:(h + 1) * 2048],
                                tpos[(1 - h) * 2048:(2 - h) * 2048]])
    ang = (stagedpos[:, None] * freqs[None, :]).astype(np.float32)
    cosv = np.cos(ang.astype(np.float64)).astype(np.float32)
    sinv = np.sin(ang.astype(np.float64)).astype(np.float32)
    prow = np.arange(128)
    cosq = np.ascontiguousarray(cosv[:, prow % 32].T).astype(ml_dtypes.bfloat16)
    sgn = np.where((prow % 64) < 32, -1.0, 1.0).astype(np.float32)
    sinq = np.ascontiguousarray((sinv[:, prow % 32] * sgn[None, :]).T).astype(ml_dtypes.bfloat16)

    return {
        "xrot": xrot, "xbatch": xbatch, "xown": xown,
        "qkvw_st": qkvw_st, "outw_st": outw_st, "qkvbT": qkvbT,
        "qkvbRow": qkvbRow, "outbRow": outbRow, "wg_st": wg_stv, "bgRow": bgRow,
        "w1_st": w1_stv, "w2_st": w2_stv, "b1T": b1T, "b2Row": b2Row,
        "identf": identf, "identb": identb, "pswap": pswap,
        "triu_tok": triu_tok, "triu_rot": triu_rot, "cosq": cosq, "sinq": sinq,
    }


def kernel(L0_attn_x, L0_attn_qkv_w, L0_attn_qkv_b, L0_attn_out_w, L0_attn_out_b,
           L0_ffn_Wg, L0_ffn_bg, L0_ffn_W1, L0_ffn_b1, L0_ffn_W2, L0_ffn_b2):
    x = np.ascontiguousarray(np.asarray(L0_attn_x, np.float32).reshape(16384, D))
    args = [np.asarray(a, np.float32) for a in
            (L0_attn_qkv_w, L0_attn_qkv_b, L0_attn_out_w, L0_attn_out_b,
             L0_ffn_Wg, L0_ffn_bg, L0_ffn_W1, L0_ffn_b1, L0_ffn_W2, L0_ffn_b2)]
    nc = build()
    in_maps = [_stage(c, x, *args) for c in range(NCORES)]
    res = run_bass_kernel_spmd(nc, in_maps, core_ids=list(range(NCORES)))
    shards = []
    for c in range(NCORES):
        o = np.asarray(res.results[c]["outp"])
        shards.append(o.transpose(1, 0, 2).reshape(2048, D))
    return np.concatenate(shards, axis=0).reshape(4, 4096, D).astype(np.float32)
